# version 14
# baseline (speedup 1.0000x reference)
import numpy as np
import concourse.bass as bass
import concourse.mybir as mybir
from concourse.bass_utils import run_bass_kernel_spmd

F32 = mybir.dt.float32
BF16 = mybir.dt.bfloat16
AF = mybir.ActivationFunctionType
ALU = mybir.AluOpType
AX = mybir.AxisListType

D = 1024
LATD = 256
KR = 512
NCORES = 8
NSEQ = 2
NEG = -1.0e30
LN1_8 = float(np.log(1.0 / 8.0))

SEQT = 2048
NBLOCKS = 4
DETECT = False
SYNC_SER = False



def _host_consts(T):
    c = {}
    j = np.arange(128)
    c["ones"] = np.ones((128, 128), np.float32)
    c["id"] = np.eye(128, dtype=np.float32)
    f = (j % 64) % 32
    inv = (1.0 / 10000.0 ** (f / 32.0))[:, None]
    ang = inv * np.arange(T)[None, :]
    c["ropeC"] = np.cos(ang).astype(np.float32)
    c["ropeS"] = np.sin(ang).astype(np.float32)
    Pm = np.zeros((128, 128), np.float32)
    for r in range(128):
        rb = r % 64
        base = r - rb
        if rb < 32:
            Pm[r, base + rb + 32] = -1.0
        else:
            Pm[r, base + rb - 32] = 1.0
    c["pm0T"] = np.ascontiguousarray(Pm.T)
    kk, qq = np.meshgrid(np.arange(128), np.arange(128), indexing="ij")
    m = np.zeros((128, 128), np.float32)
    m[kk > qq] = NEG
    c["mask"] = m
    E = np.zeros((8, 512), np.float32)
    for col in range(512):
        E[col // 64, col] = 1.0
    c["ec"] = E
    nv = (1.0 / np.arange(1, T + 1, dtype=np.float64)).astype(np.float32)
    c["nvinv"] = np.ascontiguousarray(np.broadcast_to(nv, (128, T)))
    n = np.arange(D, dtype=np.int64)
    th = 2.0 * np.pi * (np.outer(n, n) % D).astype(np.float64) / D
    c["dftC"] = np.cos(th).astype(np.float32)
    c["dftS"] = np.sin(th).astype(np.float32)
    return c


def _blk_lhsT(w, kb=128, mb=128):
    K, M = w.shape
    nk, nm = K // kb, M // mb
    a = w.reshape(nk, kb, nm, mb).transpose(2, 1, 0, 3).reshape(nm, kb, nk * mb)
    return np.ascontiguousarray(a)


def _flat_one(w):
    K, M = w.shape
    nk = K // 128
    a = w.reshape(nk, 128, M).transpose(1, 0, 2).reshape(128, nk * M)
    return np.ascontiguousarray(a)


def _host_prep(inputs):
    f32 = lambda a: np.ascontiguousarray(np.asarray(a, np.float32))
    iw = {}
    iw["w0q"] = _blk_lhsT(f32(inputs["L0_attn_W_q"]))
    iw["w0dkv"] = _flat_one(f32(inputs["L0_attn_W_dkv"]))
    iw["w0uk"] = _blk_lhsT(f32(inputs["L0_attn_W_uk"]))
    iw["w0uv"] = _blk_lhsT(f32(inputs["L0_attn_W_uv"]))
    iw["w0o"] = f32(inputs["L0_attn_W_o"])
    iw["g0"] = _flat_one(f32(inputs["L0_attn_ln_gamma"]).reshape(LATD, 1))
    iw["b0"] = _flat_one(f32(inputs["L0_attn_ln_beta"]).reshape(LATD, 1))
    iw["wfQ"] = _blk_lhsT(f32(inputs["L0_ffn_W_Q"]).T)
    iw["wfK"] = _blk_lhsT(f32(inputs["L0_ffn_W_K"]).T)
    iw["wfO"] = _blk_lhsT(f32(inputs["L0_ffn_W_O"]).T)
    iw["wfI"] = _blk_lhsT(f32(inputs["L0_ffn_W_inv"]).T)
    iw["wfr1"] = _blk_lhsT(f32(inputs["L0_ffn_router_w1"]).T)
    iw["wfr2"] = _blk_lhsT(f32(inputs["L0_ffn_router_w2"]).T, mb=8)
    vflat = lambda a: _blk_lhsT(f32(a).transpose(1, 0, 2).reshape(D, KR))
    uflat = lambda a: _blk_lhsT(f32(a).transpose(0, 2, 1).reshape(KR, D))
    iw["vfw"] = vflat(inputs["L0_ffn_V_fwd"])
    iw["wfw"] = vflat(inputs["L0_ffn_W_fwd"])
    iw["ufw"] = uflat(inputs["L0_ffn_U_fwd"])
    iw["vin"] = vflat(inputs["L0_ffn_W_inv_exp"])
    iw["win"] = vflat(inputs["L0_ffn_V_inv"])
    iw["uin"] = uflat(inputs["L0_ffn_U_inv"])
    iw["bfw"] = f32(inputs["L0_ffn_b_fwd"])
    iw["bin"] = f32(inputs["L0_ffn_b_inv"])
    iw["rb1"] = _flat_one(f32(inputs["L0_ffn_router_b1"]).reshape(D, 1))
    iw["rb2"] = f32(inputs["L0_ffn_router_b2"]).reshape(8, 1)
    iw["eb"] = f32(inputs["L0_ffn_expert_bias"]).reshape(8, 1)
    iw["alpha"] = f32(inputs["L0_ffn_alpha_bi"]).reshape(1, 1)
    iw["w1q"] = _blk_lhsT(f32(inputs["L1_attn_wq"]).T)
    iw["w1ka"] = _blk_lhsT(f32(inputs["L1_attn_wk_a"]).T, mb=256)
    iw["w1va"] = _blk_lhsT(f32(inputs["L1_attn_wv_a"]).T, mb=256)
    iw["w1kb"] = _blk_lhsT(f32(inputs["L1_attn_wk_b"]).T)
    iw["w1vb"] = _blk_lhsT(f32(inputs["L1_attn_wv_b"]).T)
    iw["w1o"] = np.ascontiguousarray(f32(inputs["L1_attn_wo"]).T)
    iw["wrB"] = _blk_lhsT(f32(inputs["L1_ffn_w_freq_real"]))
    iw["wiB"] = _blk_lhsT(f32(inputs["L1_ffn_w_freq_imag"]))
    iw["bias1"] = _flat_one(f32(inputs["L1_ffn_bias"]).reshape(D, 1))
    return iw


_SHAPES = {
    "w0q": [8, 128, 1024], "w0dkv": [128, 2048], "w0uk": [8, 128, 256],
    "w0uv": [8, 128, 256], "w0o": [1024, 1024], "g0": [128, 2], "b0": [128, 2],
    "wfQ": [8, 128, 1024], "wfK": [8, 128, 1024], "wfO": [8, 128, 1024],
    "wfI": [8, 128, 1024], "wfr1": [8, 128, 1024], "wfr2": [1, 128, 64],
    "vfw": [4, 128, 1024], "wfw": [4, 128, 1024], "ufw": [8, 128, 512],
    "vin": [4, 128, 1024], "win": [4, 128, 1024], "uin": [8, 128, 512],
    "bfw": [8, 1024], "bin": [8, 1024],
    "rb1": [128, 8], "rb2": [8, 1], "eb": [8, 1], "alpha": [1, 1],
    "w1q": [8, 128, 1024], "w1ka": [1, 128, 2048], "w1va": [1, 128, 2048],
    "w1kb": [8, 128, 256], "w1vb": [8, 128, 256], "w1o": [1024, 1024],
    "wrB": [8, 128, 1024], "wiB": [8, 128, 1024], "bias1": [128, 8],
}

ENGS = ("sync", "tensor", "vector", "scalar", "gpsimd")


class Sem:
    def __init__(self, h):
        self.h = h
        self.n = 0

    def inc(self, k=1):
        self.n += k
        return self.n


class B:
    def __init__(self, T, nblocks):
        self.T = T
        self.TT = min(512, T)
        self.NTT = T // self.TT
        self.NKT = T // 128
        self.NQT = T // self.TT
        self.nqsub = self.TT // 128
        self.nblocks = nblocks
        nc = bass.Bass("TRN2", target_bir_lowering=False,
                       detect_race_conditions=DETECT)
        self.nc = nc
        self.ops = {e: [] for e in ENGS}

        dr = lambda n, s, k: nc.dram_tensor(n, s, F32, kind=k)
        self.xin = dr("x", [NSEQ, D, T], "ExternalInput")
        self.yout = dr("y", [NSEQ, D, T], "ExternalOutput")
        self.d = {n: dr(n, s, "ExternalInput") for n, s in _SHAPES.items()}
        for n, s in (("ropeC", [128, T]), ("ropeS", [128, T]),
                     ("pm0T", [128, 128]), ("mask", [128, 128]),
                     ("ones", [128, 128]), ("id", [128, 128]),
                     ("ec", [8, 512]), ("nvinv", [128, T]),
                     ("dftC", [1024, 1024]), ("dftS", [1024, 1024])):
            self.d[n] = dr(n, s, "ExternalInput")
        self.mcache = dr("mcache", [128, 8192], "Internal")

        T8 = 8 * T
        self.OFF = {}
        cur = 0
        for name, n in (("XSB", T8), ("H1", max(T8, 32 * self.TT, 16384)),
                        ("R2", 14368),
                        ("WB", 2560), ("CB", 896), ("ST", 1024), ("SP", 1024)):
            self.OFF[name] = cur
            cur += n
        self.NCOLS = cur
        assert cur <= 53200, cur
        self.heap = nc.alloc_sbuf_tensor("heap", [128, self.NCOLS], F32)
        ha = nc.lookup_mloc(self.heap).addr
        TTl = self.TT
        self._bfregs = []
        for i, (st, ln) in enumerate((
                (self.OFF["R2"], 8 * TTl),
                (self.OFF["R2"] + 20 * TTl, 4 * TTl),
                (self.OFF["H1"], 8 * TTl),
                (self.OFF["H1"] + 8 * TTl, 8 * TTl),
                (self.OFF["H1"] + 16 * TTl, 8 * TTl),
                (self.OFF["H1"] + 24 * TTl, 8 * TTl),
                (self.OFF["WB"], 2048),
        )):
            t = nc.alloc_sbuf_tensor_at(f"ab{i}", [128, 2 * ln], BF16,
                                        offset=ha + 4 * st)
            self._bfregs.append((st, ln, t))
        self.pb = [nc.alloc_psum_tensor(f"pb{i}", [128, 512], F32) for i in range(8)]
        self.sem = {n: Sem(nc.alloc_semaphore(n)) for n in
                    ("TEN", "VEC", "SCA", "GPS", "BAR")}
        for i in range(8):
            self.sem[f"LD{i}"] = Sem(nc.alloc_semaphore(f"LD{i}"))
        self._dmai = 0
        self._ld_last = {}
        CB = self.OFF["CB"]
        self.C_ONES = CB + 0
        self.C_ID = CB + 128
        self.C_OVL = CB + 256
        self.C_MISC = CB + 768
        M = self.C_MISC
        self.M_RB2EB = M + 0
        self.M_ALPHA = M + 1
        self.M_G0 = M + 2
        self.M_B0 = M + 4
        self.M_BIAS1 = M + 6
        self.M_CARRY = M + 16
        self.M_NEGB = M + 24
        self.M_MQ = M + 25
        self.M_MK = M + 26
        self.M_SC1 = M + 27
        self.M_SC2 = M + 28
        self.M_RECIP = M + 29
        self.M_ALPHA1 = M + 33
        self.M_EPS = M + 34
        self.M_LN18 = M + 35
        self.bank_free = {i: None for i in range(8)}
        self._stage_busy = {}
        self._pend = 0
        self._pcell = None
        self.phase = "init"
        self.mmlab = []

    def v(self, col, n, p0=0, p1=128):
        return self.heap[p0:p1, col:col + n]

    def hap(self, col, dims, np_=128):
        return bass.AP(self.heap, col, [[self.NCOLS, np_]] + [list(d) for d in dims])

    def pap(self, pbt, col, dims, np_=128):
        return bass.AP(pbt, col, [[512, np_]] + [list(d) for d in dims])

    def _bfreg(self, col, n):
        for st, ln, t in self._bfregs:
            if st <= col and col + n <= st + ln:
                return st, t
        raise KeyError((col, n))

    def vb(self, col, n, p0=0, p1=128, lo=False):
        st, t = self._bfreg(col, n)
        base = 2 * (col - st) + (n if lo else 0)
        return t[p0:p1, base:base + n]

    def split(self, col, n, src_ap, waits=(), p0=0, p1=128):
        h_ = self.vb(col, n, p0, p1)
        l_ = self.vb(col, n, p0, p1, lo=True)
        self.op("vector", "tensor_copy", h_, src_ap, waits=waits)
        return self.op("vector", "tensor_tensor", l_, src_ap, h_, ALU.subtract)

    def load_pair(self, pair_col, ncols, dram_ap, stage_col, after=None):
        if after is not None:
            self.wait("sync", after[0], after[1])
        sb = self._stage_busy.get(stage_col)
        if sb:
            self.wait("sync", sb[0], sb[1])
        lv = self.dma(self.v(stage_col, ncols), dram_ap)
        sv = self.split(pair_col, ncols, self.v(stage_col, ncols), waits=[lv])
        self._stage_busy[stage_col] = ("VEC", sv)
        return sv

    def mm3(self, psum_ap, wcol, nwe, wc, rcol, n_r, start, stop, waits=(),
            wspan=128, p0=0, p1=128):
        wst, wt = self._bfreg(wcol, nwe)
        b0 = 2 * (wcol - wst)
        lh = wt[p0:p1, b0 + wc * wspan:b0 + (wc + 1) * wspan]
        ll = wt[p0:p1, b0 + nwe + wc * wspan:b0 + nwe + (wc + 1) * wspan]
        rh = self.vb(rcol, n_r, p0, p1)
        rl = self.vb(rcol, n_r, p0, p1, lo=True)
        self.mm(psum_ap, lh, rh, start, False, waits=waits)
        self.mm(psum_ap, lh, rl, False, False)
        return self.mm(psum_ap, ll, rh, False, stop)

    def on(self, eng, fn):
        self.ops[eng].append(fn)

    def ten_flush(self):
        if self._pend:
            self._pcell[0] += self._pend
            self._pend = 0

    def tmark(self):
        self.ten_flush()
        return self.sem["TEN"].n

    def wait(self, eng, sem, val):
        if not val:
            return
        if eng == "tensor":
            self.ten_flush()
        self.on(eng, lambda e, s=self.sem[sem].h, v=val: e.wait_ge(s, v))

    def ready(self):
        return [("VEC", self.sem["VEC"].n), ("SCA", self.sem["SCA"].n),
                ("GPS", self.sem["GPS"].n)]

    def barrier(self):
        self.ten_flush()
        bar = self.sem["BAR"]
        v = bar.inc(5)
        for eng in ENGS:
            self.on(eng, lambda e, s=bar.h, v=v:
                    (e.nop().then_inc(s, 1), e.wait_ge(s, v)))
        for i in range(8):
            self.bank_free[i] = None

    def dma(self, dst_ap, src_ap, after=None):
        name = f"LD{self._dmai % 8}"
        self._dmai += 1
        LD = self.sem[name]
        if after is not None:
            self.wait("sync", after[0], after[1])
        if name in self._ld_last:
            self.wait("sync", name, self._ld_last[name])
        val = LD.inc(16)
        self._ld_last[name] = val
        self.on("sync", lambda e, d=dst_ap, s=src_ap, h=LD.h:
                e.dma_start(out=d, in_=s).then_inc(h, 16))
        if SYNC_SER:
            self.wait("sync", name, val)
        return (name, val)

    def t_bank(self, bank):
        bf = self.bank_free[bank]
        if bf is not None:
            self.wait("tensor", bf[0], bf[1])

    def mm(self, psum_ap, lhsT, rhs, start, stop, waits=(), tile_position=None,
           inc=False):
        for (s, v) in waits:
            self.wait("tensor", s, v)
        TEN = self.sem["TEN"]
        val = TEN.inc()
        try:
            shp = (lhsT.ap[0][1], lhsT.free_size(), rhs.free_size(),
                   "f32" if lhsT.tensor.dtype == F32 else "bf16")
        except (AttributeError, IndexError):
            shp = None
        self.mmlab.append((self.phase, shp))
        cell = [0]
        self._pend += 1
        self._pcell = cell
        if inc:
            self.ten_flush()
        def fn(e, o=psum_ap, l=lhsT, r=rhs, st=start, sp=stop, h=TEN.h,
               c=cell, tpos=tile_position):
            i = e.matmul(o, l, r, start=st, stop=sp, tile_position=tpos)
            if c[0]:
                i.then_inc(h, c[0])
        self.on("tensor", fn)
        return val

    def tp(self, out_ap, in_ap, waits=(), kdim=128, tile_position=None):
        for (s, v) in waits:
            self.wait("tensor", s, v)
        self.ten_flush()
        TEN = self.sem["TEN"]
        val = TEN.inc()
        self.mmlab.append((self.phase + "/tp", "tp"))
        def fn(e, o=out_ap, i_=in_ap,
               idn=self.heap[0:kdim, self.C_ID:self.C_ID + kdim], h=TEN.h,
               tpos=tile_position):
            e.transpose(o, i_, idn).then_inc(h, 1)
        self.on("tensor", fn)
        return val

    def op(self, eng, meth, *args, waits=(), **kw):
        S = {"vector": "VEC", "scalar": "SCA", "gpsimd": "GPS"}[eng]
        for (s, v) in waits:
            self.wait(eng, s, v)
        sem = self.sem[S]
        prev = sem.n
        val = sem.inc()
        def fn(e, m=meth, a=args, k=kw, h=sem.h, p=prev):
            i = getattr(e, m)(*a, **k).then_inc(h, 1)
            if p:
                i._wait_ge(h, p)
        self.on(eng, fn)
        return val

    def ln_tile(self, src_off, dst_off, tt, inv_d, dst_tile, sq_off):
        self.phase = self.phase.split("|")[0] + "|ln"
        """LayerNorm of one t-tile: dst = (src - mean)/sqrt(var+eps).
        src chunk c at src_off + c*T + tt*TT; dst chunk c at
        dst_off + c*(TT if dst_tile else T) (+ tt*TT if not dst_tile).
        sq_off: 8*TT scratch cols for squares. Returns GPS val of completion."""
        T, TT = self.T, self.TT
        ST = self.OFF["ST"]
        t0 = tt * TT
        meanB = self.v(ST, TT)
        rsd = self.v(ST + 512, TT)
        src3 = self.hap(src_off + t0, [[T, 8], [1, TT]])
        dst3 = self.hap(sq_off, [[TT, 8], [1, TT]])
        sq = self.op("scalar", "activation", dst3, src3, AF.Square,
                     waits=self.ready())
        for b_, off, step, w in ((0, src_off + t0, T, None), (1, sq_off, TT, sq)):
            self.t_bank(b_)
            for c in range(8):
                self.mm(self.pb[b_][:, 0:TT], self.v(self.C_ONES, 128),
                        self.v(off + c * step, TT), c == 0, c == 7,
                        waits=([("SCA", w)] if (w and c == 0) else ()))
        vsum = self.tmark()
        self.op("scalar", "activation", meanB, self.pb[0][:, 0:TT], AF.Copy,
                scale=inv_d, waits=[("TEN", vsum)])
        e2 = self.op("scalar", "activation", rsd, self.pb[1][:, 0:TT], AF.Copy,
                     scale=inv_d)
        m2 = self.op("vector", "tensor_mul", self.pb[0][:, 0:TT], meanB, meanB,
                     waits=[("SCA", e2)])
        v2 = self.op("vector", "tensor_sub", rsd, rsd, self.pb[0][:, 0:TT],
                     waits=[("VEC", m2)])
        self.op("scalar", "activation", rsd, rsd, AF.Ln, bias=self.v(self.M_EPS, 1),
                waits=[("VEC", v2)])
        r2 = self.op("scalar", "activation", rsd, rsd, AF.Exp, scale=-0.5)
        self.bank_free[0] = ("VEC", v2)
        self.bank_free[1] = ("SCA", e2)
        gv = None
        for c in range(8):
            xc = self.v(src_off + c * T + t0, TT)
            dc = self.v(dst_off + c * (TT if dst_tile else T) +
                        (0 if dst_tile else t0), TT)
            sv = self.op("vector", "tensor_sub", dc, xc, meanB,
                         waits=([("SCA", r2)] if c == 0 else ()))
            gv = self.op("gpsimd", "tensor_mul", dc, dc, rsd,
                         waits=[("VEC", sv)])
        return gv

    def preln(self, dst_off, sq_off):
        gv = None
        for tt in range(self.NTT):
            gv = self.ln_tile(self.OFF["XSB"], dst_off, tt, 1.0 / D, False, sq_off)
        self.barrier()
        return gv

    def bound_tile(self, src_off, acc_col, first, wv):
        TT = self.TT
        sq = self.v(self.OFF["ST"], TT)
        sv = self.op("scalar", "activation", sq, self.v(src_off, TT), AF.Square,
                     waits=[("VEC", wv)] if wv else ())
        self.t_bank(5)
        mv = self.mm(self.pb[5][0:1, 0:TT], self.v(self.C_ONES, 1), sq,
                     True, True, waits=[("SCA", sv)], inc=True)
        if first:
            rv = self.op("vector", "reduce_max", self.v(acc_col, 1, 0, 1),
                         self.pb[5][0:1, 0:TT], AX.X, waits=[("TEN", mv)])
        else:
            self.op("vector", "reduce_max", self.v(self.M_SC1, 1, 0, 1),
                    self.pb[5][0:1, 0:TT], AX.X, waits=[("TEN", mv)])
            rv = self.op("vector", "tensor_max", self.v(acc_col, 1, 0, 1),
                         self.v(acc_col, 1, 0, 1), self.v(self.M_SC1, 1, 0, 1))
        self.bank_free[5] = ("VEC", rv)
        return rv

    def finish_bound(self):
        mq = self.v(self.M_MQ, 1, 0, 1)
        mk = self.v(self.M_MK, 1, 0, 1)
        self.op("scalar", "activation", mq, mq, AF.Ln, waits=self.ready())
        l2 = self.op("scalar", "activation", mk, mk, AF.Ln)
        a1 = self.op("vector", "tensor_add", mq, mq, mk, waits=[("SCA", l2)])
        e1 = self.op("scalar", "activation", mq, mq, AF.Exp, scale=0.5,
                     bias=self.v(self.M_LN18, 1, 0, 1), waits=[("VEC", a1)])
        n1 = self.op("vector", "tensor_scalar", mq, mq, -1.0, 25.0,
                     ALU.mult, ALU.add, waits=[("SCA", e1)])
        self.t_bank(5)
        mv = self.mm(self.pb[5][:, 0:1],
                     self.heap[0:1, self.C_ONES:self.C_ONES + 128], mq,
                     True, True, waits=[("VEC", n1)], inc=True)
        cv = self.op("vector", "tensor_copy", self.v(self.M_NEGB, 1),
                     self.pb[5][:, 0:1], waits=[("TEN", mv)])
        self.bank_free[5] = ("VEC", cv)
        return cv

    def attn_group(self, QG, KG, VG, causal, bound_wait, wo_cols, wo_ld):
        base = self.phase.split("|")[0]
        """attention for 2 heads + per-q-tile W_o partial accumulated into x.
        QG/KG feature-major [128 x T]; VG token-major aug [128 x 129*NKT]
        per k-tile laid out [V_h0(64)|ones|V_h1(64)].  PV is feature-major:
        O_aug^T [65 x TT] = V_augT.T @ P^T accumulated over k-tiles; the aug
        row gives the softmax denominator (row 64 for head 0, row 0 for head
        1).  Normalization multiplies by exp(-ln(denom)) broadcast across 64
        partitions via a K=1 matmul.  W_o is applied as two K=64 matmuls
        against AT[64 x 2*TT]; wo_cols = WB cols of the two 64-row halves."""
        T, TT, NKT, NQT, nqsub = self.T, self.TT, self.NKT, self.NQT, self.nqsub
        SP, ST, XS = self.OFF["SP"], self.OFF["ST"], self.OFF["XSB"]
        negb = self.v(self.M_NEGB, 1)
        AT = ST
        last_at = 0
        if causal:
            sbank, pslot = (0, 1), (SP, SP + 512)
        else:
            sbank, pslot = (0, 1, 3), (SP, SP + 512, self.C_OVL)
        nbank = len(sbank)
        for qt in range(NQT):
            at_done = 0
            for hl in range(2):
                kts = list(range(min(NKT, nqsub * qt + nqsub))) if causal \
                    else list(range(NKT))
                drow = 64
                vrow0 = 0
                last_pv = None
                prev_pv = {}
                rd = self.ready()
                self.t_bank(2)
                def emit_S(i, hl=hl, qt=qt, kts=kts, rd=rd):
                    kt = kts[i]
                    bank = sbank[i % nbank]
                    qoffb = max(0, kt - nqsub * qt) if causal else 0
                    self.t_bank(bank)
                    self.phase = base + "|S"
                    return self.mm(
                        self.pb[bank][:, qoffb * 128:TT],
                        self.heap[hl * 64:hl * 64 + 64,
                                  KG + kt * 128:KG + kt * 128 + 128],
                        self.heap[hl * 64:hl * 64 + 64,
                                  QG + qt * TT + qoffb * 128:QG + (qt + 1) * TT],
                        True, True, waits=(rd if i == 0 and hl == 0 else ()),
                        inc=True)
                svs = {}
                for i in range(min(nbank, len(kts))):
                    svs[i] = emit_S(i)
                for i, kt in enumerate(kts):
                    par = i % nbank
                    bank = sbank[par]
                    qoffb = max(0, kt - nqsub * qt) if causal else 0
                    n_q = TT - qoffb * 128
                    sv = svs[i]
                    mwait = [("TEN", sv)]
                    jd = kt - nqsub * qt
                    if causal and 0 <= jd < nqsub:
                        mv = self.op("vector", "tensor_add",
                                     self.pb[bank][:, jd * 128:jd * 128 + 128],
                                     self.pb[bank][:, jd * 128:jd * 128 + 128],
                                     self.v(self.C_OVL + 128, 128),
                                     waits=[("TEN", sv)])
                        mwait = [("VEC", mv)]
                    ew = list(mwait)
                    if i == 0:
                        ew.append(("VEC", bound_wait))
                    if i <= 1 and last_at:
                        ew.append(("VEC", last_at))
                    if par in prev_pv:
                        ew.append(("TEN", prev_pv[par]))
                    pexp = self.op("scalar", "activation",
                                   self.v(pslot[par], n_q),
                                   self.pb[bank][:, qoffb * 128:TT], AF.Exp,
                                   bias=negb, scale=0.125, waits=ew)
                    self.bank_free[bank] = ("SCA", pexp)
                    self.phase = base + "|PV"
                    last_pv = self.mm(
                        self.pb[2][0:65, qoffb * 128:TT],
                        self.hap(VG + kt * 130 + 65 * hl, [[1, 65]]),
                        self.v(pslot[par], n_q),
                        i == 0, i == len(kts) - 1,
                        waits=[("SCA", pexp)], inc=True)
                    prev_pv[par] = last_pv
                    if i + nbank < len(kts):
                        svs[i + nbank] = emit_S(i + nbank)
                u1 = self.op("scalar", "activation", self.v(SP, TT, 0, 1),
                             self.pb[2][drow:drow + 1, 0:TT], AF.Ln,
                             waits=[("TEN", last_pv)])
                u2 = self.op("scalar", "activation", self.v(SP, TT, 0, 1),
                             self.v(SP, TT, 0, 1), AF.Exp, scale=-1.0)
                self.t_bank(3)
                self.phase = base + "|norm"
                bc = self.mm(self.pb[3][0:64, 0:TT],
                             self.heap[0:1, self.C_ONES:self.C_ONES + 64],
                             self.v(SP, TT, 0, 1), True, True,
                             waits=[("SCA", u2)], inc=True)
                cpv = self.op("vector", "tensor_copy", self.v(SP + 512, TT, 0, 64),
                              self.pb[3][0:64, 0:TT], waits=[("TEN", bc)])
                at_done = self.op("vector", "scalar_tensor_tensor",
                                  self.heap[0:64, AT + hl * TT:AT + (hl + 1) * TT],
                                  self.pb[2][vrow0:vrow0 + 64, 0:TT], 1.0,
                                  self.v(SP + 512, TT, 0, 64),
                                  ALU.mult, ALU.mult, waits=[("VEC", cpv)])
                self.bank_free[2] = ("VEC", at_done)
                self.bank_free[3] = ("VEC", cpv)
                last_at = at_done
            self.phase = base + "|WO"
            for oc in range(8):
                bank = 4 + (oc % 4)
                self.t_bank(bank)
                for hl in range(2):
                    w0 = []
                    if oc == 0 and hl == 0:
                        w0 = [("VEC", at_done)] + list(wo_ld)
                    mv = self.mm(self.pb[bank][:, 0:TT],
                                 self.heap[0:64, wo_cols[hl] + oc * 128:
                                           wo_cols[hl] + oc * 128 + 128],
                                 self.heap[0:64, AT + hl * TT:AT + (hl + 1) * TT],
                                 hl == 0, hl == 1, waits=w0, inc=(hl == 1))
                av = self.op("vector", "tensor_add",
                             self.v(XS + oc * T + qt * TT, TT),
                             self.v(XS + oc * T + qt * TT, TT),
                             self.pb[bank][:, 0:TT], waits=[("TEN", mv)])
                self.bank_free[bank] = ("VEC", av)

    def rope(self, off, lc1, lcs, src_ready):
        TT = self.TT
        SP, ST = self.OFF["SP"], self.OFF["ST"]
        self.t_bank(4)
        mv = self.mm(self.pb[4][:, 0:TT], self.v(self.C_OVL, 128),
                     self.v(off, TT), True, True, waits=[("VEC", src_ready)],
                     inc=True)
        tmp = self.v(ST, TT)
        v1 = self.op("vector", "tensor_mul", tmp, self.pb[4][:, 0:TT],
                     self.v(SP + 512, TT), waits=[("TEN", mv), lcs])
        self.bank_free[4] = ("VEC", v1)
        v2 = self.op("gpsimd", "tensor_mul", self.v(off, TT), self.v(off, TT),
                     self.v(SP, TT),
                     waits=[lc1, ("VEC", src_ready), ("TEN", mv)])
        v3 = self.op("vector", "tensor_add", self.v(off, TT), self.v(off, TT),
                     tmp, waits=[("GPS", v2)])
        return v3

    def emit_l0a(self, s):
        self.phase = "l0a.front"
        T, TT, NTT, NKT = self.T, self.TT, self.NTT, self.NKT
        XS, H1, R2, WB, SP = (self.OFF[k] for k in
                              ("XSB", "H1", "R2", "WB", "SP"))
        for c in range(8):
            lv = self.dma(self.v(XS + c * T, T),
                          self.xin.ap()[s, c * 128:(c + 1) * 128, :])
        self.dma(self.v(self.C_OVL, 128), self.d["pm0T"].ap())
        lov = self.dma(self.v(self.C_OVL + 128, 128), self.d["mask"].ap())
        wall = [(f"LD{i}", self.sem[f"LD{i}"].n) for i in range(8)]
        for eng in ("scalar", "vector", "tensor", "gpsimd"):
            for w in wall:
                self.wait(eng, *w)
        self.barrier()
        self.preln(H1, R2)
        LAT = R2
        SQ2 = R2 + 2 * T
        ldkv = self.dma(self.v(WB, 2048), self.d["w0dkv"].ap())
        for tt in range(NTT):
            t0 = tt * TT
            rd = self.ready()
            for lc in range(2):
                self.t_bank(lc)
                for c in range(8):
                    self.mm(self.pb[lc][:, 0:TT],
                            self.v(WB + c * 256 + lc * 128, 128),
                            self.v(H1 + c * T + t0, TT), c == 0, c == 7,
                            waits=(rd + [ldkv]) if (c == 0 and lc == 0) else ())
                ev = self.op("vector", "tensor_copy", self.v(LAT + lc * T + t0, TT),
                             self.pb[lc][:, 0:TT], waits=[("TEN", self.tmark())])
                self.bank_free[lc] = ("VEC", ev)
            lat_ev = ev
            lsq = self.op("scalar", "activation",
                          self.hap(SQ2, [[TT, 2], [1, TT]]),
                          self.hap(LAT + t0, [[T, 2], [1, TT]]),
                          AF.Square, waits=[("VEC", self.sem["VEC"].n)])
            for b_, off, step, w in ((2, LAT + t0, T, None), (3, SQ2, TT, lsq)):
                self.t_bank(b_)
                for lc in range(2):
                    self.mm(self.pb[b_][:, 0:TT], self.v(self.C_ONES, 128),
                            self.v(off + lc * step, TT), lc == 0, lc == 1,
                            waits=([("SCA", w)] if (w and lc == 0) else
                                   ([("VEC", lat_ev)] if lc == 0 else ())))
            vsum = self.tmark()
            ST = self.OFF["ST"]
            meanB = self.v(ST, TT)
            rsd = self.v(ST + 512, TT)
            self.op("scalar", "activation", meanB, self.pb[2][:, 0:TT], AF.Copy,
                    scale=1.0 / LATD, waits=[("TEN", vsum)])
            e2 = self.op("scalar", "activation", rsd, self.pb[3][:, 0:TT],
                         AF.Copy, scale=1.0 / LATD)
            m2 = self.op("vector", "tensor_mul", self.pb[2][:, 0:TT], meanB, meanB,
                         waits=[("SCA", e2)])
            v2 = self.op("vector", "tensor_sub", rsd, rsd, self.pb[2][:, 0:TT],
                         waits=[("VEC", m2)])
            self.op("scalar", "activation", rsd, rsd, AF.Ln, bias=self.v(self.M_EPS, 1),
                    waits=[("VEC", v2)])
            r2 = self.op("scalar", "activation", rsd, rsd, AF.Exp, scale=-0.5)
            self.bank_free[2] = ("VEC", v2)
            self.bank_free[3] = ("SCA", e2)
            for lc in range(2):
                la = self.v(LAT + lc * T + t0, TT)
                self.op("vector", "tensor_sub", la, la, meanB,
                        waits=([("SCA", r2)] if lc == 0 else ()))
                mv = self.op("vector", "tensor_mul", la, la, rsd)
                self.op("scalar", "activation", la, la, AF.Identity,
                        scale=self.v(self.M_G0 + lc, 1),
                        bias=self.v(self.M_B0 + lc, 1), waits=[("VEC", mv)])
        self.barrier()
        QG = R2 + 2 * T
        KG = QG + T
        VG = KG + T
        for g in range(8):
            self._l0a_group(g, LAT, QG, KG, VG)
        self.barrier()

    def _l0a_group(self, g, LAT, QG, KG, VG):
        self.phase = "l0a.proj"
        T, TT, NTT, NKT = self.T, self.TT, self.NTT, self.NKT
        H1, WB, SP = self.OFF["H1"], self.OFF["WB"], self.OFF["SP"]
        lq = self.dma(self.v(WB, 1024), self.d["w0q"].ap()[g])
        luk = self.dma(self.v(WB + 1024, 256), self.d["w0uk"].ap()[g])
        luv = self.dma(self.v(WB + 1280, 256), self.d["w0uv"].ap()[g])
        lwo = self.dma(self.heap[0:64, WB + 1536:WB + 2560],
                       self.d["w0o"].ap()[g * 128:g * 128 + 64, :])
        rd = self.ready()
        for tt in range(NTT):
            t0 = tt * TT
            qb = 6 + (tt % 2)
            kb = 7 - (tt % 2)
            lc1 = self.dma(self.v(SP, TT), self.d["ropeC"].ap()[:, t0:t0 + TT],
                           after=("VEC", self.rope_done) if getattr(self, "rope_done", 0) else None)
            lcs = self.dma(self.v(SP + 512, TT), self.d["ropeS"].ap()[:, t0:t0 + TT])
            self.t_bank(qb)
            for c in range(8):
                self.mm(self.pb[qb][:, 0:TT], self.v(WB + c * 128, 128),
                        self.v(H1 + c * T + t0, TT), c == 0, c == 7,
                        waits=(rd + [lq]) if (c == 0 and tt == 0) else ())
            qv = self.op("vector", "tensor_copy", self.v(QG + t0, TT),
                         self.pb[qb][:, 0:TT], waits=[("TEN", self.tmark())])
            self.bank_free[qb] = ("VEC", qv)
            rq = self.rope(QG + t0, lc1, lcs, qv)
            self.bound_tile(QG + t0, self.M_MQ, tt == 0, rq)
            self.t_bank(kb)
            for lc in range(2):
                self.mm(self.pb[kb][:, 0:TT], self.v(WB + 1024 + lc * 128, 128),
                        self.v(LAT + lc * T + t0, TT), lc == 0, lc == 1,
                        waits=[luk] if lc == 0 else ())
            kv = self.op("vector", "tensor_copy", self.v(KG + t0, TT),
                         self.pb[kb][:, 0:TT], waits=[("TEN", self.tmark())])
            self.bank_free[kb] = ("VEC", kv)
            rk = self.rope(KG + t0, lc1, lcs, kv)
            self.rope_done = self.bound_tile(KG + t0, self.M_MK, tt == 0, rk)
        lwo2 = self.dma(self.heap[0:64, WB:WB + 1024],
                        self.d["w0o"].ap()[g * 128 + 64:g * 128 + 128, :],
                        after=("TEN", self.tmark()))
        bv = self.finish_bound()
        mz = self.op("vector", "memset",
                     self.hap(VG + 64, [[130, NKT], [65, 2]]), 1.0)
        for kt in range(NKT):
            vb_ = 6 + (kt % 2)
            self.t_bank(vb_)
            for hl in range(2):
                for lc in range(2):
                    self.mm(self.pb[vb_][:, hl * 64:hl * 64 + 64],
                            self.v(LAT + lc * T + kt * 128, 128),
                            self.v(WB + 1280 + lc * 128 + hl * 64, 64),
                            lc == 0, lc == 1,
                            waits=[luv] if (kt == 0 and hl == 0 and lc == 0) else ())
            ev = self.op("vector", "tensor_copy",
                         self.hap(VG + kt * 130, [[65, 2], [1, 64]]),
                         self.pb[vb_][:, 0:128],
                         waits=[("TEN", self.tmark()), ("VEC", mz)])
            self.bank_free[vb_] = ("VEC", ev)
        self.attn_group(QG, KG, VG, True, bv, (WB + 1536, WB), (lwo, lwo2))
        self.barrier()
        self.rope_done = 0

    def emit_l0f(self, s):
        lec = self.dma(self.v(self.C_OVL, 512, 0, 8), self.d["ec"].ap())
        self.wait("tensor", *lec)
        for tt in range(self.NTT):
            self._l0f_tile(tt)
        self.barrier()

    def ln_tile_pair(self, src_off, dst_pair, tt, scr_off):
        self.phase = "l0f.ln"
        """LayerNorm of one t-tile of chunked fp32 src into a bf16 PAIR at
        dst_pair (+ c*TT per chunk).  scr_off: 8*TT fp32 scratch."""
        T, TT = self.T, self.TT
        ST = self.OFF["ST"]
        t0 = tt * TT
        meanB = self.v(ST, TT)
        rsd = self.v(ST + 512, TT)
        src3 = self.hap(src_off + t0, [[T, 8], [1, TT]])
        dst3 = self.hap(scr_off, [[TT, 8], [1, TT]])
        sq = self.op("scalar", "activation", dst3, src3, AF.Square,
                     waits=self.ready())
        for b_, off, step, w in ((0, src_off + t0, T, None),
                                 (1, scr_off, TT, sq)):
            self.t_bank(b_)
            for c in range(8):
                self.mm(self.pb[b_][:, 0:TT], self.v(self.C_ONES, 128),
                        self.v(off + c * step, TT), c == 0, c == 7,
                        waits=([("SCA", w)] if (w and c == 0) else ()))
        vsum = self.tmark()
        self.op("scalar", "activation", meanB, self.pb[0][:, 0:TT], AF.Copy,
                scale=1.0 / D, waits=[("TEN", vsum)])
        e2 = self.op("scalar", "activation", rsd, self.pb[1][:, 0:TT], AF.Copy,
                     scale=1.0 / D)
        m2 = self.op("vector", "tensor_mul", self.pb[0][:, 0:TT], meanB, meanB,
                     waits=[("SCA", e2)])
        v2 = self.op("vector", "tensor_sub", rsd, rsd, self.pb[0][:, 0:TT],
                     waits=[("VEC", m2)])
        self.op("scalar", "activation", rsd, rsd, AF.Ln,
                bias=self.v(self.M_EPS, 1), waits=[("VEC", v2)])
        r2 = self.op("scalar", "activation", rsd, rsd, AF.Exp, scale=-0.5)
        self.bank_free[0] = ("VEC", v2)
        self.bank_free[1] = ("SCA", e2)
        gv = None
        for c in range(8):
            xc = self.v(src_off + c * T + t0, TT)
            uc = self.v(scr_off + c * TT, TT)
            sv = self.op("vector", "tensor_sub", uc, xc, meanB,
                         waits=([("SCA", r2)] if c == 0 else ()))
            gm = self.op("gpsimd", "tensor_mul", uc, uc, rsd,
                         waits=[("VEC", sv)])
            gv = self.split(dst_pair + c * TT, TT, uc, waits=[("GPS", gm)])
        return gv

    def _mmblk(self, dst_off, rhs_pair, wname, nblk, nk, blkelems, stage,
               extra=(), evac="pair", evac_args=None):
        TT = self.TT
        WB = self.OFF["WB"]
        self.phase = self.phase.split("|")[0] + "|mb." + wname
        evs = []
        for ob in range(nblk):
            sl = ob % 2
            after = ("TEN", self.wlast[sl]) if self.wlast[sl] else None
            wsv = self.load_pair(WB + sl * 1024, blkelems,
                                 self.d[wname].ap()[ob], stage, after=after)
            bank = 4 + sl
            self.t_bank(bank)
            w0 = (list(extra) if ob == 0 else []) + [("VEC", wsv)]
            for c in range(nk):
                self.mm3(self.pb[bank][:, 0:TT], WB + sl * 1024, blkelems, c,
                         rhs_pair + c * TT, TT, c == 0, c == nk - 1,
                         waits=w0 if c == 0 else ())
            self.wlast[sl] = self.tmark()
            mv = self.tmark()
            if evac == "pair":
                ev = self.split(dst_off + ob * TT, TT, self.pb[bank][:, 0:TT],
                                waits=[("TEN", mv)])
                self.bank_free[bank] = ("VEC", ev)
            elif evac == "copy":
                ev = self.op("vector", "tensor_copy",
                             self.v(dst_off + ob * TT, TT),
                             self.pb[bank][:, 0:TT], waits=[("TEN", mv)])
                self.bank_free[bank] = ("VEC", ev)
            elif evac == "gelu":
                ev = self.op("scalar", "activation",
                             self.v(dst_off + ob * TT, TT),
                             self.pb[bank][:, 0:TT], AF.Gelu,
                             bias=self.v(evac_args + ob, 1), waits=[("TEN", mv)])
                self.bank_free[bank] = ("SCA", ev)
            evs.append(ev)
        return evs

    def _l0f_tile(self, tt):
        T, TT = self.T, self.TT
        XS, H1, R2, WB, SP, ST = (self.OFF[k] for k in
                                  ("XSB", "H1", "R2", "WB", "SP", "ST"))
        t0 = tt * TT
        XNP = R2
        HH = R2 + 8 * TT
        XV = R2 + 16 * TT
        YW = R2 + 20 * TT
        CU = R2 + 24 * TT
        BF = R2 + 1024
        QQ, KK, IQ, IK = H1, H1 + 8 * TT, H1 + 16 * TT, H1 + 24 * TT
        GF, GI = KK, QQ
        CMB = IQ
        WF, WI = ST, ST + 512
        stageA = HH
        stageB = XNP
        self.wlast = [None, None]
        lrw2 = self.dma(self.v(WB + 2048, 64), self.d["wfr2"].ap()[0])
        lrb1 = self.dma(self.v(WB + 2112, 8), self.d["rb1"].ap())
        lnv = self.dma(self.v(SP, TT), self.d["nvinv"].ap()[:, t0:t0 + TT])
        gv = self.ln_tile_pair(XS, XNP, tt, HH)
        self.phase = "l0f"
        self._stage_busy[stageA] = ("VEC", gv)
        self._mmblk(QQ, XNP, "wfQ", 8, 8, 1024, stageA,
                    extra=[("VEC", gv), ("GPS", self.sem["GPS"].n)])
        self._mmblk(KK, XNP, "wfK", 8, 8, 1024, stageA)
        self._mmblk(IQ, QQ, "wfI", 8, 8, 1024, stageA)
        self._mmblk(IK, KK, "wfI", 8, 8, 1024, stageA)
        self._stage_busy[stageB] = ("TEN", self.tmark())
        self.phase = "l0f|rtr"
        for srcq, wdst in ((QQ, WF), (IQ, WI)):
            self._mmblk(HH, srcq, "wfr1", 8, 8, 1024, stageB,
                        evac="gelu", evac_args=WB + 2112)
            self.wait("scalar", *lrb1)
            self.t_bank(6)
            rd = self.ready()
            for c in range(8):
                self.mm(self.pb[6][0:8, 0:TT], self.v(WB + 2048 + c * 8, 8),
                        self.v(HH + c * TT, TT), c == 0, c == 7,
                        waits=(rd + [lrw2]) if c == 0 else ())
            lg = self.op("scalar", "activation", self.v(wdst, TT, 0, 8),
                         self.pb[6][0:8, 0:TT], AF.Identity,
                         bias=self.v(self.M_RB2EB, 1, 0, 8),
                         waits=[("TEN", self.tmark())])
            self.bank_free[6] = ("SCA", lg)
            wtm = self.v(SP + 512, 128)
            for j in range(TT // 128):
                self.t_bank(7)
                v1 = self.tp(self.pb[7][0:128, 0:8],
                             self.heap[0:8, wdst + j * 128:wdst + (j + 1) * 128],
                             waits=[("SCA", lg)], kdim=8)
                nm = self.op("vector", "tensor_reduce", self.v(self.M_SC1, 1),
                             self.pb[7][:, 0:8], AX.X, ALU.max, negate=True,
                             waits=[("TEN", v1)])
                e1 = self.op("scalar", "activation", wtm[:, 0:8],
                             self.pb[7][:, 0:8], AF.Exp,
                             bias=self.v(self.M_SC1, 1), waits=[("VEC", nm)])
                self.op("vector", "reduce_sum", self.v(self.M_SC1, 1),
                        wtm[:, 0:8], AX.X, waits=[("SCA", e1)])
                self.op("vector", "reciprocal", self.v(self.M_SC2, 1),
                        self.v(self.M_SC1, 1))
                m1 = self.op("vector", "tensor_scalar_mul", wtm[:, 0:8],
                             wtm[:, 0:8], self.v(self.M_SC2, 1))
                self.bank_free[7] = ("VEC", m1)
                self.t_bank(7)
                v2 = self.tp(self.pb[7][0:8, 0:128], wtm[:, 0:8],
                             waits=[("VEC", m1)])
                cv = self.op("vector", "tensor_copy",
                             self.heap[0:8, wdst + j * 128:wdst + (j + 1) * 128],
                             self.pb[7][0:8, 0:128], waits=[("TEN", v2)])
                self.bank_free[7] = ("VEC", cv)
        routers_done = self.tmark()
        lbf = self.dma(self.v(BF, 1024, 0, 8), self.d["bfw"].ap(),
                       after=("TEN", routers_done))
        lbi = self.dma(self.v(BF + 1024, 1024, 0, 8), self.d["bin"].ap())
        gps_prev = 0
        for br, (qsrc, ksrc, vn, wn, un, boff, wrow, c0) in enumerate(
                ((QQ, KK, "vfw", "wfw", "ufw", BF, WF, 0),
                 (IQ, IK, "vin", "win", "uin", BF + 1024, WI, 4))):
            self._mmblk(XV, qsrc, vn, 4, 8, 1024, stageB, evac="copy",
                        extra=[("TEN", self.tmark())])
            self._mmblk(YW, ksrc, wn, 4, 8, 1024, stageB, evac="copy",
                        extra=[("TEN", self.tmark())])
            self.phase = "l0f|scan"
            for krb in range(4):
                cc = self.v(self.M_CARRY + c0 + krb, 1)
                ini = 0.0 if tt == 0 else cc
                self.op("vector", "tensor_tensor_scan",
                        self.v(CU + krb * TT, TT), self.v(YW + krb * TT, TT),
                        self.v(YW + krb * TT, TT), ini, ALU.add, ALU.bypass,
                        waits=[("TEN", routers_done), ("GPS", gps_prev)])
                self.op("vector", "tensor_copy", cc,
                        self.v(CU + krb * TT + TT - 1, 1))
            for krb in range(4):
                p1 = self.op("gpsimd", "tensor_mul", self.v(XV + krb * TT, TT),
                             self.v(XV + krb * TT, TT), self.v(CU + krb * TT, TT),
                             waits=[("VEC", self.sem["VEC"].n)])
                gps_prev = p1
                self.t_bank(6)
                mv = self.mm(self.pb[6][:, 0:TT],
                             self.heap[0:8, self.C_OVL + krb * 128:
                                       self.C_OVL + (krb + 1) * 128],
                             self.v(wrow, TT, 0, 8), True, True,
                             waits=[("VEC", self.sem["VEC"].n)], inc=True)
                p2 = self.op("vector", "tensor_mul", self.v(XV + krb * TT, TT),
                             self.v(XV + krb * TT, TT), self.pb[6][:, 0:TT],
                             waits=[("TEN", mv), ("GPS", p1)])
                self.bank_free[6] = ("VEC", p2)
                self.split(YW + krb * TT, TT, self.v(XV + krb * TT, TT))
            pv = self.sem["VEC"].n
            gdst = GF if br == 0 else GI
            self.phase = "l0f|U"
            for oc in range(8):
                sl = oc % 2
                after = ("TEN", self.wlast[sl]) if self.wlast[sl] else None
                wsv = self.load_pair(WB + sl * 1024, 512, self.d[un].ap()[oc],
                                     stageB, after=after)
                bank = 4 + sl
                self.t_bank(bank)
                for krc in range(4):
                    self.mm3(self.pb[bank][:, 0:TT], WB + sl * 1024, 512, krc,
                             YW + krc * TT, TT, krc == 0, False,
                             waits=[("VEC", wsv), ("VEC", pv), lbf, lbi]
                             if krc == 0 else ())
                self.mm(self.pb[bank][:, 0:TT],
                        self.heap[0:8, boff + oc * 128:boff + oc * 128 + 128],
                        self.v(wrow, TT, 0, 8), False, True)
                self.wlast[sl] = self.tmark()
                ev = self.op("vector", "tensor_copy", self.v(gdst + oc * TT, TT),
                             self.pb[bank][:, 0:TT],
                             waits=[("TEN", self.tmark())])
                self.bank_free[bank] = ("VEC", ev)
        cvs = None
        for oc in range(8):
            self.op("vector", "scalar_tensor_tensor", self.v(GF + oc * TT, TT),
                    self.v(GI + oc * TT, TT), self.v(self.M_ALPHA, 1),
                    self.v(GF + oc * TT, TT), ALU.mult, ALU.add)
            cvs = self.split(CMB + oc * TT, TT, self.v(GF + oc * TT, TT))
        self.phase = "l0f|WO"
        for oc in range(8):
            sl = oc % 2
            after = ("TEN", self.wlast[sl]) if self.wlast[sl] else None
            wsv = self.load_pair(WB + sl * 1024, 1024, self.d["wfO"].ap()[oc],
                                 stageB, after=after)
            bank = 4 + sl
            self.t_bank(bank)
            for c in range(8):
                self.mm3(self.pb[bank][:, 0:TT], WB + sl * 1024, 1024, c,
                         CMB + c * TT, TT, c == 0, c == 7,
                         waits=[("VEC", wsv), ("VEC", cvs)] if c == 0 else ())
            self.wlast[sl] = self.tmark()
            mv = self.tmark()
            e1 = self.op("vector", "tensor_mul", self.v(GI + oc * TT, TT),
                         self.pb[bank][:, 0:TT], self.v(SP, TT),
                         waits=[("TEN", mv), lnv])
            self.op("vector", "tensor_add", self.v(XS + oc * T + t0, TT),
                    self.v(XS + oc * T + t0, TT), self.v(GI + oc * TT, TT))
            self.bank_free[bank] = ("VEC", e1)
        self.barrier()
        self._stage_busy = {}

    def emit_l1a(self, s):
        LATK, LATV, QG, KG, VG = self._l1a_front(s)
        for g in range(8):
            self._l1a_group(g, LATK, LATV, QG, KG, VG)
        self.barrier()

    def _l1a_front(self, s):
        self.phase = "l1a.front"
        T, TT, NTT, NKT = self.T, self.TT, self.NTT, self.NKT
        XS, H1, R2, WB = (self.OFF[k] for k in ("XSB", "H1", "R2", "WB"))
        self.preln(H1, R2)
        LATK, LATV = R2, R2 + 2 * T
        for wn, dsto in (("w1ka", LATK), ("w1va", LATV)):
            lv = self.dma(self.v(WB, 2048), self.d[wn].ap()[0])
            rd = self.ready()
            for tt in range(NTT):
                t0 = tt * TT
                for lc in range(2):
                    self.t_bank(lc)
                    for c in range(8):
                        self.mm(self.pb[lc][:, 0:TT],
                                self.v(WB + c * 256 + lc * 128, 128),
                                self.v(H1 + c * T + t0, TT), c == 0, c == 7,
                                waits=(rd + [lv]) if (c == 0 and tt == 0 and lc == 0) else ())
                    ev = self.op("vector", "tensor_copy",
                                 self.v(dsto + lc * T + t0, TT),
                                 self.pb[lc][:, 0:TT],
                                 waits=[("TEN", self.tmark())])
                    self.bank_free[lc] = ("VEC", ev)
            self.barrier()
        QG = R2 + 4 * T
        KG = QG + T
        VG = KG + T
        return LATK, LATV, QG, KG, VG

    def _l1a_group(self, g, LATK, LATV, QG, KG, VG):
        self.phase = "l1a.proj"
        T, TT, NTT, NKT = self.T, self.TT, self.NTT, self.NKT
        H1, WB = self.OFF["H1"], self.OFF["WB"]
        lq = self.dma(self.v(WB, 1024), self.d["w1q"].ap()[g])
        lkb = self.dma(self.v(WB + 1024, 256), self.d["w1kb"].ap()[g])
        lvb = self.dma(self.v(WB + 1280, 256), self.d["w1vb"].ap()[g])
        lwo = self.dma(self.heap[0:64, WB + 1536:WB + 2560],
                       self.d["w1o"].ap()[g * 128:g * 128 + 64, :])
        rd = self.ready()
        for tt in range(NTT):
            t0 = tt * TT
            qb = 6 + (tt % 2)
            kb = 7 - (tt % 2)
            self.t_bank(qb)
            for c in range(8):
                self.mm(self.pb[qb][:, 0:TT], self.v(WB + c * 128, 128),
                        self.v(H1 + c * T + t0, TT), c == 0, c == 7,
                        waits=(rd + [lq]) if (c == 0 and tt == 0) else ())
            qv = self.op("vector", "tensor_copy", self.v(QG + t0, TT),
                         self.pb[qb][:, 0:TT], waits=[("TEN", self.tmark())])
            self.bank_free[qb] = ("VEC", qv)
            self.bound_tile(QG + t0, self.M_MQ, tt == 0, qv)
            self.t_bank(kb)
            for lc in range(2):
                self.mm(self.pb[kb][:, 0:TT], self.v(WB + 1024 + lc * 128, 128),
                        self.v(LATK + lc * T + t0, TT), lc == 0, lc == 1,
                        waits=[lkb] if lc == 0 else ())
            kv = self.op("vector", "tensor_copy", self.v(KG + t0, TT),
                         self.pb[kb][:, 0:TT], waits=[("TEN", self.tmark())])
            self.bank_free[kb] = ("VEC", kv)
            self.bound_tile(KG + t0, self.M_MK, tt == 0, kv)
        lwo2 = self.dma(self.heap[0:64, WB:WB + 1024],
                        self.d["w1o"].ap()[g * 128 + 64:g * 128 + 128, :],
                        after=("TEN", self.tmark()))
        bv = self.finish_bound()
        mz = self.op("vector", "memset",
                     self.hap(VG + 64, [[130, NKT], [65, 2]]), 1.0)
        for kt in range(NKT):
            vb_ = 6 + (kt % 2)
            self.t_bank(vb_)
            for hl in range(2):
                for lc in range(2):
                    self.mm(self.pb[vb_][:, hl * 64:hl * 64 + 64],
                            self.v(LATV + lc * T + kt * 128, 128),
                            self.v(WB + 1280 + lc * 128 + hl * 64, 64),
                            lc == 0, lc == 1,
                            waits=[lvb] if (kt == 0 and hl == 0 and lc == 0) else ())
            ev = self.op("vector", "tensor_copy",
                         self.hap(VG + kt * 130, [[65, 2], [1, 64]]),
                         self.pb[vb_][:, 0:128],
                         waits=[("TEN", self.tmark()), ("VEC", mz)])
            self.bank_free[vb_] = ("VEC", ev)
        self.attn_group(QG, KG, VG, False, bv, (WB + 1536, WB), (lwo, lwo2))
        self.barrier()

    def emit_l1f(self, s):
        self.phase = "l1f"
        T, TT, NTT = self.T, self.TT, self.NTT
        XS, H1, R2, WB, SP = (self.OFF[k] for k in
                              ("XSB", "H1", "R2", "WB", "SP"))
        MOFF = R2
        if s == 0:
            AO, BO = H1, H1 + 8192
            w1last = None
            for ob in range(8):
                lwr = self.dma(self.v(WB, 1024), self.d["wrB"].ap()[ob],
                               after=("TEN", w1last) if w1last else None)
                lwi = self.dma(self.v(WB + 1024, 1024), self.d["wiB"].ap()[ob])
                for b in range(6):
                    self.t_bank(b)
                for nt in range(2):
                    for c in range(8):
                        lc_ = self.dma(
                            self.v(SP, 512),
                            self.d["dftC"].ap()[c * 128:(c + 1) * 128,
                                                nt * 512:(nt + 1) * 512],
                            after=("TEN", self.tmark())
                            if (c + nt + ob) else None)
                        self.mm(self.pb[nt][:, 0:512], self.v(WB + c * 128, 128),
                                self.v(SP, 512), c == 0, False,
                                waits=[lc_, lwr] if c == 0 else [lc_])
                        self.mm(self.pb[2 + nt][:, 0:512],
                                self.v(WB + 1024 + c * 128, 128),
                                self.v(SP, 512), c == 0, c == 7,
                                waits=[lwi] if c == 0 else ())
                        ls_ = self.dma(
                            self.v(SP + 512, 512),
                            self.d["dftS"].ap()[c * 128:(c + 1) * 128,
                                                nt * 512:(nt + 1) * 512])
                        self.mm(self.pb[nt][:, 0:512],
                                self.v(WB + 1024 + c * 128, 128),
                                self.v(SP + 512, 512), False, c == 7,
                                waits=[ls_])
                        self.mm(self.pb[4 + nt][:, 0:512], self.v(WB + c * 128, 128),
                                self.v(SP + 512, 512), c == 0, c == 7)
                    w1last = self.tmark()
                    av = self.op("vector", "tensor_copy",
                                 self.v(AO + ob * 1024 + nt * 512, 512),
                                 self.pb[nt][:, 0:512], waits=[("TEN", w1last)])
                    self.op("vector", "tensor_copy",
                            self.v(BO + ob * 1024 + nt * 512, 512),
                            self.pb[2 + nt][:, 0:512])
                    bv = self.op("vector", "tensor_sub",
                                 self.v(BO + ob * 1024 + nt * 512, 512),
                                 self.v(BO + ob * 1024 + nt * 512, 512),
                                 self.pb[4 + nt][:, 0:512])
                    self.bank_free[nt] = ("VEC", av)
                    self.bank_free[2 + nt] = ("VEC", bv)
                    self.bank_free[4 + nt] = ("VEC", bv)
            self.barrier()
            for nb in range(8):
                for mt in range(2):
                    self.t_bank(0)
                    self.t_bank(1)
                    for oc in range(8):
                        lc_ = self.dma(
                            self.v(SP, 512),
                            self.d["dftC"].ap()[oc * 128:(oc + 1) * 128,
                                                mt * 512:(mt + 1) * 512],
                            after=("TEN", self.tmark())
                            if (oc + mt + nb) else None)
                        self.mm(self.pb[0][:, 0:512],
                                self.v(AO + oc * 1024 + nb * 128, 128),
                                self.v(SP, 512), oc == 0, oc == 7,
                                waits=[lc_])
                        ls_ = self.dma(
                            self.v(SP + 512, 512),
                            self.d["dftS"].ap()[oc * 128:(oc + 1) * 128,
                                                mt * 512:(mt + 1) * 512])
                        self.mm(self.pb[1][:, 0:512],
                                self.v(BO + oc * 1024 + nb * 128, 128),
                                self.v(SP + 512, 512), oc == 0, oc == 7,
                                waits=[ls_])
                    mv = self.tmark()
                    c1 = self.op("vector", "tensor_copy",
                                 self.v(MOFF + nb * 1024 + mt * 512, 512),
                                 self.pb[0][:, 0:512], waits=[("TEN", mv)])
                    sv = self.op("vector", "tensor_sub",
                                 self.v(MOFF + nb * 1024 + mt * 512, 512),
                                 self.v(MOFF + nb * 1024 + mt * 512, 512),
                                 self.pb[1][:, 0:512])
                    ev = self.op("scalar", "activation",
                                 self.v(MOFF + nb * 1024 + mt * 512, 512),
                                 self.v(MOFF + nb * 1024 + mt * 512, 512),
                                 AF.Copy, scale=1.0 / 1024.0,
                                 waits=[("VEC", sv)])
                    self.bank_free[0] = ("VEC", c1)
                    self.bank_free[1] = ("VEC", sv)
            self.wait("sync", "SCA", self.sem["SCA"].n)
            self.dma(self.mcache.ap()[:, :], self.v(MOFF, 8192))
            self.barrier()
        else:
            lv = self.dma(self.v(MOFF, 8192), self.mcache.ap()[:, :])
            self.wait("tensor", *lv)
            self.barrier()
        XN = R2 + 8192
        for tt in range(NTT):
            t0 = tt * TT
            gv = self.ln_tile(XS, XN, tt, 1.0 / D, True, XN)
            prev_e2 = [0, 0]
            for oc in range(8):
                bank = 2 + (oc % 4)
                self.t_bank(bank)
                for c in range(8):
                    self.mm(self.pb[bank][:, 0:TT],
                            self.v(MOFF + c * 1024 + oc * 128, 128),
                            self.v(XN + c * TT, TT), c == 0, c == 7,
                            waits=[("GPS", gv)] if (c == 0 and oc == 0) else ())
                mv = self.tmark()
                sl = oc % 2
                e1 = self.op("scalar", "activation", self.v(SP + sl * 512, TT),
                             self.pb[bank][:, 0:TT], AF.Identity,
                             bias=self.v(self.M_BIAS1 + oc, 1),
                             waits=[("TEN", mv), ("VEC", prev_e2[sl])])
                e2 = self.op("vector", "tensor_add",
                             self.v(XS + oc * T + t0, TT),
                             self.v(XS + oc * T + t0, TT),
                             self.v(SP + sl * 512, TT),
                             waits=[("SCA", e1)])
                prev_e2[sl] = e2
                self.bank_free[bank] = ("VEC", e2)
        self.barrier()

    def emit_init(self):
        self.dma(self.v(self.C_ONES, 128), self.d["ones"].ap())
        self.dma(self.v(self.C_ID, 128), self.d["id"].ap())
        self.dma(self.v(self.M_RB2EB, 1, 0, 8), self.d["rb2"].ap())
        self.dma(self.v(self.M_G0, 2), self.d["g0"].ap())
        self.dma(self.v(self.M_B0, 2), self.d["b0"].ap())
        self.dma(self.v(self.M_BIAS1, 8), self.d["bias1"].ap())
        self.dma(self.v(self.M_ALPHA1, 1, 0, 1), self.d["alpha"].ap())
        self.op("vector", "memset", self.v(self.M_EPS, 1), 1e-5)
        self.op("vector", "memset", self.v(self.M_LN18, 1), LN1_8)
        lv = self.dma(self.v(self.M_SC1, 1, 0, 8), self.d["eb"].ap())
        wall = [(f"LD{i}", self.sem[f"LD{i}"].n) for i in range(8)]
        self.op("vector", "tensor_add", self.v(self.M_RB2EB, 1, 0, 8),
                self.v(self.M_RB2EB, 1, 0, 8), self.v(self.M_SC1, 1, 0, 8),
                waits=wall)
        mv = self.mm(self.pb[7][:, 0:1],
                     self.heap[0:1, self.C_ONES:self.C_ONES + 128],
                     self.heap[0:1, self.M_ALPHA1:self.M_ALPHA1 + 1],
                     True, True, waits=wall, inc=True)
        self.op("vector", "tensor_copy", self.v(self.M_ALPHA, 1),
                self.pb[7][:, 0:1], waits=[("TEN", mv)])
        self.barrier()

    def emit_store(self, s):
        T, XS = self.T, self.OFF["XSB"]
        self.wait("sync", "VEC", self.sem["VEC"].n)
        self.wait("sync", "GPS", self.sem["GPS"].n)
        self.wait("sync", "SCA", self.sem["SCA"].n)
        for c in range(8):
            self.dma(self.yout.ap()[s, c * 128:(c + 1) * 128, :],
                     self.v(XS + c * T, T))
        for i in range(8):
            self.wait("sync", f"LD{i}", self.sem[f"LD{i}"].n)
        self.barrier()

    def build(self):
        self.rope_done = 0
        self.emit_init()
        for s in range(NSEQ):
            self.emit_l0a(s)
            if self.nblocks >= 2:
                self.emit_l0f(s)
            if self.nblocks >= 3:
                self.emit_l1a(s)
            if self.nblocks >= 4:
                self.emit_l1f(s)
            self.emit_store(s)
        nc, ops = self.nc, self.ops
        with nc.Block() as block:
            @block.sync
            def _(e):
                for fn in ops["sync"]:
                    fn(e)

            @block.tensor
            def _(e):
                for fn in ops["tensor"]:
                    fn(e)

            @block.vector
            def _(e):
                for fn in ops["vector"]:
                    fn(e)

            @block.scalar
            def _(e):
                for fn in ops["scalar"]:
                    fn(e)

            @block.gpsimd
            def _(e):
                for fn in ops["gpsimd"]:
                    fn(e)
        return nc


_BUILD_CACHE = {}


def build_program(T=2048, nblocks=4):
    key = (T, nblocks)
    if key not in _BUILD_CACHE:
        b = B(T, nblocks)
        _BUILD_CACHE[key] = b.build()
        _BUILD_CACHE[key]._mmlab = b.mmlab
    return _BUILD_CACHE[key]


def kernel(**inputs):
    T = SEQT
    x = np.ascontiguousarray(np.asarray(inputs["L0_attn_x"], np.float32))
    Bt, Tx, Dx = x.shape
    assert Dx == D and Tx == T, (x.shape, T)
    shared = _host_prep(inputs)
    shared.update(_host_consts(T))
    nc = build_program(T, NBLOCKS)
    in_maps = []
    for c in range(NCORES):
        m = dict(shared)
        m["x"] = np.ascontiguousarray(x[2 * c:2 * c + 2].transpose(0, 2, 1))
        in_maps.append(m)
    res = run_bass_kernel_spmd(nc, in_maps, core_ids=list(range(NCORES)))
    out = np.empty((Bt, T, D), np.float32)
    for c in range(NCORES):
        y = np.asarray(res.results[c]["y"])
        out[2 * c:2 * c + 2] = y.transpose(0, 2, 1)
    return out


# version 15
# speedup vs baseline: 1.0525x; 1.0525x over previous
import numpy as np
import concourse.bass as bass
import concourse.mybir as mybir
from concourse.bass_utils import run_bass_kernel_spmd

F32 = mybir.dt.float32
BF16 = mybir.dt.bfloat16
AF = mybir.ActivationFunctionType
ALU = mybir.AluOpType
AX = mybir.AxisListType

D = 1024
LATD = 256
KR = 512
NCORES = 8
NSEQ = 2
NEG = -1.0e30
LN1_8 = float(np.log(1.0 / 8.0))

SEQT = 2048
NBLOCKS = 4
DETECT = False
SYNC_SER = False



def _host_consts(T):
    c = {}
    j = np.arange(128)
    c["ones"] = np.ones((128, 128), np.float32)
    c["id"] = np.eye(128, dtype=np.float32)
    f = (j % 64) % 32
    inv = (1.0 / 10000.0 ** (f / 32.0))[:, None]
    ang = inv * np.arange(T)[None, :]
    c["ropeC"] = np.cos(ang).astype(np.float32)
    c["ropeS"] = np.sin(ang).astype(np.float32)
    Pm = np.zeros((128, 128), np.float32)
    for r in range(128):
        rb = r % 64
        base = r - rb
        if rb < 32:
            Pm[r, base + rb + 32] = -1.0
        else:
            Pm[r, base + rb - 32] = 1.0
    c["pm0T"] = np.ascontiguousarray(Pm.T)
    kk, qq = np.meshgrid(np.arange(128), np.arange(128), indexing="ij")
    m = np.zeros((128, 128), np.float32)
    m[kk > qq] = NEG
    c["mask"] = m
    E = np.zeros((8, 512), np.float32)
    for col in range(512):
        E[col // 64, col] = 1.0
    c["ec"] = E
    nv = (1.0 / np.arange(1, T + 1, dtype=np.float64)).astype(np.float32)
    c["nvinv"] = np.ascontiguousarray(np.broadcast_to(nv, (128, T)))
    n = np.arange(D, dtype=np.int64)
    th = 2.0 * np.pi * (np.outer(n, n) % D).astype(np.float64) / D
    c["dftC"] = np.cos(th).astype(np.float32)
    c["dftS"] = np.sin(th).astype(np.float32)
    return c


def _blk_lhsT(w, kb=128, mb=128):
    K, M = w.shape
    nk, nm = K // kb, M // mb
    a = w.reshape(nk, kb, nm, mb).transpose(2, 1, 0, 3).reshape(nm, kb, nk * mb)
    return np.ascontiguousarray(a)


def _flat_one(w):
    K, M = w.shape
    nk = K // 128
    a = w.reshape(nk, 128, M).transpose(1, 0, 2).reshape(128, nk * M)
    return np.ascontiguousarray(a)


def _host_prep(inputs):
    f32 = lambda a: np.ascontiguousarray(np.asarray(a, np.float32))
    iw = {}
    iw["w0q"] = _blk_lhsT(f32(inputs["L0_attn_W_q"]))
    iw["w0dkv"] = _flat_one(f32(inputs["L0_attn_W_dkv"]))
    iw["w0uk"] = _blk_lhsT(f32(inputs["L0_attn_W_uk"]))
    iw["w0uv"] = _blk_lhsT(f32(inputs["L0_attn_W_uv"]))
    iw["w0o"] = f32(inputs["L0_attn_W_o"])
    iw["g0"] = _flat_one(f32(inputs["L0_attn_ln_gamma"]).reshape(LATD, 1))
    iw["b0"] = _flat_one(f32(inputs["L0_attn_ln_beta"]).reshape(LATD, 1))
    iw["wfQ"] = _blk_lhsT(f32(inputs["L0_ffn_W_Q"]).T)
    iw["wfK"] = _blk_lhsT(f32(inputs["L0_ffn_W_K"]).T)
    iw["wfO"] = _blk_lhsT(f32(inputs["L0_ffn_W_O"]).T)
    iw["wfI"] = _blk_lhsT(f32(inputs["L0_ffn_W_inv"]).T)
    iw["wfr1"] = _blk_lhsT(f32(inputs["L0_ffn_router_w1"]).T)
    iw["wfr2"] = _blk_lhsT(f32(inputs["L0_ffn_router_w2"]).T, mb=8)
    vflat = lambda a: _blk_lhsT(f32(a).transpose(1, 0, 2).reshape(D, KR))
    uflat = lambda a: _blk_lhsT(f32(a).transpose(0, 2, 1).reshape(KR, D))
    iw["vfw"] = vflat(inputs["L0_ffn_V_fwd"])
    iw["wfw"] = vflat(inputs["L0_ffn_W_fwd"])
    iw["ufw"] = uflat(inputs["L0_ffn_U_fwd"])
    iw["vin"] = vflat(inputs["L0_ffn_W_inv_exp"])
    iw["win"] = vflat(inputs["L0_ffn_V_inv"])
    iw["uin"] = uflat(inputs["L0_ffn_U_inv"])
    iw["bfw"] = f32(inputs["L0_ffn_b_fwd"])
    iw["bin"] = f32(inputs["L0_ffn_b_inv"])
    iw["rb1"] = _flat_one(f32(inputs["L0_ffn_router_b1"]).reshape(D, 1))
    iw["rb2"] = f32(inputs["L0_ffn_router_b2"]).reshape(8, 1)
    iw["eb"] = f32(inputs["L0_ffn_expert_bias"]).reshape(8, 1)
    iw["alpha"] = f32(inputs["L0_ffn_alpha_bi"]).reshape(1, 1)
    iw["w1q"] = _blk_lhsT(f32(inputs["L1_attn_wq"]).T)
    iw["w1ka"] = _blk_lhsT(f32(inputs["L1_attn_wk_a"]).T, mb=256)
    iw["w1va"] = _blk_lhsT(f32(inputs["L1_attn_wv_a"]).T, mb=256)
    iw["w1kb"] = _blk_lhsT(f32(inputs["L1_attn_wk_b"]).T)
    iw["w1vb"] = _blk_lhsT(f32(inputs["L1_attn_wv_b"]).T)
    iw["w1o"] = np.ascontiguousarray(f32(inputs["L1_attn_wo"]).T)
    iw["wrB"] = _blk_lhsT(f32(inputs["L1_ffn_w_freq_real"]))
    iw["wiB"] = _blk_lhsT(f32(inputs["L1_ffn_w_freq_imag"]))
    iw["bias1"] = _flat_one(f32(inputs["L1_ffn_bias"]).reshape(D, 1))
    return iw


_SHAPES = {
    "w0q": [8, 128, 1024], "w0dkv": [128, 2048], "w0uk": [8, 128, 256],
    "w0uv": [8, 128, 256], "w0o": [1024, 1024], "g0": [128, 2], "b0": [128, 2],
    "wfQ": [8, 128, 1024], "wfK": [8, 128, 1024], "wfO": [8, 128, 1024],
    "wfI": [8, 128, 1024], "wfr1": [8, 128, 1024], "wfr2": [1, 128, 64],
    "vfw": [4, 128, 1024], "wfw": [4, 128, 1024], "ufw": [8, 128, 512],
    "vin": [4, 128, 1024], "win": [4, 128, 1024], "uin": [8, 128, 512],
    "bfw": [8, 1024], "bin": [8, 1024],
    "rb1": [128, 8], "rb2": [8, 1], "eb": [8, 1], "alpha": [1, 1],
    "w1q": [8, 128, 1024], "w1ka": [1, 128, 2048], "w1va": [1, 128, 2048],
    "w1kb": [8, 128, 256], "w1vb": [8, 128, 256], "w1o": [1024, 1024],
    "wrB": [8, 128, 1024], "wiB": [8, 128, 1024], "bias1": [128, 8],
}

ENGS = ("sync", "tensor", "vector", "scalar", "gpsimd")


class Sem:
    def __init__(self, h):
        self.h = h
        self.n = 0

    def inc(self, k=1):
        self.n += k
        return self.n


class B:
    def __init__(self, T, nblocks):
        self.T = T
        self.TT = min(512, T)
        self.NTT = T // self.TT
        self.NKT = T // 128
        self.NQT = T // self.TT
        self.nqsub = self.TT // 128
        self.nblocks = nblocks
        nc = bass.Bass("TRN2", target_bir_lowering=False,
                       detect_race_conditions=DETECT)
        self.nc = nc
        self.ops = {e: [] for e in ENGS}

        dr = lambda n, s, k: nc.dram_tensor(n, s, F32, kind=k)
        self.xin = dr("x", [NSEQ, D, T], "ExternalInput")
        self.yout = dr("y", [NSEQ, D, T], "ExternalOutput")
        self.d = {n: dr(n, s, "ExternalInput") for n, s in _SHAPES.items()}
        for n, s in (("ropeC", [128, T]), ("ropeS", [128, T]),
                     ("pm0T", [128, 128]), ("mask", [128, 128]),
                     ("ones", [128, 128]), ("id", [128, 128]),
                     ("ec", [8, 512]), ("nvinv", [128, T]),
                     ("dftC", [1024, 1024]), ("dftS", [1024, 1024])):
            self.d[n] = dr(n, s, "ExternalInput")
        self.mcache = dr("mcache", [128, 8192], "Internal")

        T8 = 8 * T
        self.OFF = {}
        cur = 0
        for name, n in (("XSB", T8), ("H1", max(T8, 32 * self.TT, 16384)),
                        ("R2", 14368),
                        ("WB", 2560), ("CB", 896), ("ST", 1024), ("SP", 1024)):
            self.OFF[name] = cur
            cur += n
        self.NCOLS = cur
        assert cur <= 53200, cur
        self.heap = nc.alloc_sbuf_tensor("heap", [128, self.NCOLS], F32)
        ha = nc.lookup_mloc(self.heap).addr
        TTl = self.TT
        self._bfregs = []
        for i, (st, ln) in enumerate((
                (self.OFF["R2"], 8 * TTl),
                (self.OFF["R2"] + 20 * TTl, 4 * TTl),
                (self.OFF["H1"], 8 * TTl),
                (self.OFF["H1"] + 8 * TTl, 8 * TTl),
                (self.OFF["H1"] + 16 * TTl, 8 * TTl),
                (self.OFF["H1"] + 24 * TTl, 8 * TTl),
                (self.OFF["WB"], 2048),
                (self.OFF["R2"] + 16 * TTl, 4 * TTl),
        )):
            t = nc.alloc_sbuf_tensor_at(f"ab{i}", [128, 2 * ln], BF16,
                                        offset=ha + 4 * st)
            self._bfregs.append((st, ln, t))
        self.pb = [nc.alloc_psum_tensor(f"pb{i}", [128, 512], F32) for i in range(8)]
        self.sem = {n: Sem(nc.alloc_semaphore(n)) for n in
                    ("TEN", "VEC", "SCA", "GPS", "BAR")}
        for i in range(8):
            self.sem[f"LD{i}"] = Sem(nc.alloc_semaphore(f"LD{i}"))
        self._dmai = 0
        self._ld_last = {}
        CB = self.OFF["CB"]
        self.C_ONES = CB + 0
        self.C_ID = CB + 128
        self.C_OVL = CB + 256
        self.C_MISC = CB + 768
        M = self.C_MISC
        self.M_RB2EB = M + 0
        self.M_ALPHA = M + 1
        self.M_G0 = M + 2
        self.M_B0 = M + 4
        self.M_BIAS1 = M + 6
        self.M_CARRY = M + 16
        self.M_NEGB = M + 24
        self.M_MQ = M + 25
        self.M_MK = M + 26
        self.M_SC1 = M + 27
        self.M_SC2 = M + 28
        self.M_RECIP = M + 29
        self.M_ALPHA1 = M + 33
        self.M_EPS = M + 34
        self.M_LN18 = M + 35
        self.bank_free = {i: None for i in range(8)}
        self._stage_busy = {}
        self._pend = 0
        self._pcell = None
        self.phase = "init"
        self.mmlab = []

    def v(self, col, n, p0=0, p1=128):
        return self.heap[p0:p1, col:col + n]

    def hap(self, col, dims, np_=128):
        return bass.AP(self.heap, col, [[self.NCOLS, np_]] + [list(d) for d in dims])

    def pap(self, pbt, col, dims, np_=128):
        return bass.AP(pbt, col, [[512, np_]] + [list(d) for d in dims])

    def _bfreg(self, col, n):
        for st, ln, t in self._bfregs:
            if st <= col and col + n <= st + ln:
                return st, t
        raise KeyError((col, n))

    def vb(self, col, n, p0=0, p1=128, lo=False):
        st, t = self._bfreg(col, n)
        base = 2 * (col - st) + (n if lo else 0)
        return t[p0:p1, base:base + n]

    def vbx(self, col, n, soff, slen, p0=0, p1=128, lo=False):
        st, t = self._bfreg(col, n)
        base = 2 * (col - st) + (n if lo else 0) + soff
        return t[p0:p1, base:base + slen]

    def split(self, col, n, src_ap, waits=(), p0=0, p1=128):
        h_ = self.vb(col, n, p0, p1)
        l_ = self.vb(col, n, p0, p1, lo=True)
        self.op("vector", "tensor_copy", h_, src_ap, waits=waits)
        return self.op("vector", "tensor_tensor", l_, src_ap, h_, ALU.subtract)

    def load_pair(self, pair_col, ncols, dram_ap, stage_col, after=None):
        if after is not None:
            self.wait("sync", after[0], after[1])
        sb = self._stage_busy.get(stage_col)
        if sb:
            self.wait("sync", sb[0], sb[1])
        lv = self.dma(self.v(stage_col, ncols), dram_ap)
        sv = self.split(pair_col, ncols, self.v(stage_col, ncols), waits=[lv])
        self._stage_busy[stage_col] = ("VEC", sv)
        return sv

    def mm3(self, psum_ap, wcol, nwe, wc, rcol, n_r, start, stop, waits=(),
            wspan=128, p0=0, p1=128):
        wst, wt = self._bfreg(wcol, nwe)
        b0 = 2 * (wcol - wst)
        lh = wt[p0:p1, b0 + wc * wspan:b0 + (wc + 1) * wspan]
        ll = wt[p0:p1, b0 + nwe + wc * wspan:b0 + nwe + (wc + 1) * wspan]
        rh = self.vb(rcol, n_r, p0, p1)
        rl = self.vb(rcol, n_r, p0, p1, lo=True)
        self.mm(psum_ap, lh, rh, start, False, waits=waits)
        self.mm(psum_ap, lh, rl, False, False)
        return self.mm(psum_ap, ll, rh, False, stop)

    def on(self, eng, fn):
        self.ops[eng].append(fn)

    def ten_flush(self):
        if self._pend:
            self._pcell[0] += self._pend
            self._pend = 0

    def tmark(self):
        self.ten_flush()
        return self.sem["TEN"].n

    def wait(self, eng, sem, val):
        if not val:
            return
        if eng == "tensor":
            self.ten_flush()
        self.on(eng, lambda e, s=self.sem[sem].h, v=val: e.wait_ge(s, v))

    def ready(self):
        return [("VEC", self.sem["VEC"].n), ("SCA", self.sem["SCA"].n),
                ("GPS", self.sem["GPS"].n)]

    def barrier(self):
        self.ten_flush()
        bar = self.sem["BAR"]
        v = bar.inc(5)
        for eng in ENGS:
            self.on(eng, lambda e, s=bar.h, v=v:
                    (e.nop().then_inc(s, 1), e.wait_ge(s, v)))
        for i in range(8):
            self.bank_free[i] = None

    def dma(self, dst_ap, src_ap, after=None):
        name = f"LD{self._dmai % 8}"
        self._dmai += 1
        LD = self.sem[name]
        if after is not None:
            self.wait("sync", after[0], after[1])
        if name in self._ld_last:
            self.wait("sync", name, self._ld_last[name])
        val = LD.inc(16)
        self._ld_last[name] = val
        self.on("sync", lambda e, d=dst_ap, s=src_ap, h=LD.h:
                e.dma_start(out=d, in_=s).then_inc(h, 16))
        if SYNC_SER:
            self.wait("sync", name, val)
        return (name, val)

    def t_bank(self, bank):
        bf = self.bank_free[bank]
        if bf is not None:
            self.wait("tensor", bf[0], bf[1])

    def mm(self, psum_ap, lhsT, rhs, start, stop, waits=(), tile_position=None,
           inc=False):
        for (s, v) in waits:
            self.wait("tensor", s, v)
        TEN = self.sem["TEN"]
        val = TEN.inc()
        try:
            shp = (lhsT.ap[0][1], lhsT.free_size(), rhs.free_size(),
                   "f32" if lhsT.tensor.dtype == F32 else "bf16")
        except (AttributeError, IndexError):
            shp = None
        self.mmlab.append((self.phase, shp))
        cell = [0]
        self._pend += 1
        self._pcell = cell
        if inc:
            self.ten_flush()
        def fn(e, o=psum_ap, l=lhsT, r=rhs, st=start, sp=stop, h=TEN.h,
               c=cell, tpos=tile_position):
            i = e.matmul(o, l, r, start=st, stop=sp, tile_position=tpos)
            if c[0]:
                i.then_inc(h, c[0])
        self.on("tensor", fn)
        return val

    def tp(self, out_ap, in_ap, waits=(), kdim=128, tile_position=None):
        for (s, v) in waits:
            self.wait("tensor", s, v)
        self.ten_flush()
        TEN = self.sem["TEN"]
        val = TEN.inc()
        self.mmlab.append((self.phase + "/tp", "tp"))
        def fn(e, o=out_ap, i_=in_ap,
               idn=self.heap[0:kdim, self.C_ID:self.C_ID + kdim], h=TEN.h,
               tpos=tile_position):
            e.transpose(o, i_, idn).then_inc(h, 1)
        self.on("tensor", fn)
        return val

    def op(self, eng, meth, *args, waits=(), **kw):
        S = {"vector": "VEC", "scalar": "SCA", "gpsimd": "GPS"}[eng]
        for (s, v) in waits:
            self.wait(eng, s, v)
        sem = self.sem[S]
        prev = sem.n
        val = sem.inc()
        def fn(e, m=meth, a=args, k=kw, h=sem.h, p=prev):
            i = getattr(e, m)(*a, **k).then_inc(h, 1)
            if p:
                i._wait_ge(h, p)
        self.on(eng, fn)
        return val

    def ln_tile(self, src_off, dst_off, tt, inv_d, dst_tile, sq_off):
        self.phase = self.phase.split("|")[0] + "|ln"
        """LayerNorm of one t-tile: dst = (src - mean)/sqrt(var+eps).
        src chunk c at src_off + c*T + tt*TT; dst chunk c at
        dst_off + c*(TT if dst_tile else T) (+ tt*TT if not dst_tile).
        sq_off: 8*TT scratch cols for squares. Returns GPS val of completion."""
        T, TT = self.T, self.TT
        ST = self.OFF["ST"]
        t0 = tt * TT
        meanB = self.v(ST, TT)
        rsd = self.v(ST + 512, TT)
        src3 = self.hap(src_off + t0, [[T, 8], [1, TT]])
        dst3 = self.hap(sq_off, [[TT, 8], [1, TT]])
        sq = self.op("scalar", "activation", dst3, src3, AF.Square,
                     waits=self.ready())
        for b_, off, step, w in ((0, src_off + t0, T, None), (1, sq_off, TT, sq)):
            self.t_bank(b_)
            for c in range(8):
                self.mm(self.pb[b_][:, 0:TT], self.v(self.C_ONES, 128),
                        self.v(off + c * step, TT), c == 0, c == 7,
                        waits=([("SCA", w)] if (w and c == 0) else ()))
        vsum = self.tmark()
        self.op("scalar", "activation", meanB, self.pb[0][:, 0:TT], AF.Copy,
                scale=inv_d, waits=[("TEN", vsum)])
        e2 = self.op("scalar", "activation", rsd, self.pb[1][:, 0:TT], AF.Copy,
                     scale=inv_d)
        m2 = self.op("vector", "tensor_mul", self.pb[0][:, 0:TT], meanB, meanB,
                     waits=[("SCA", e2)])
        v2 = self.op("vector", "tensor_sub", rsd, rsd, self.pb[0][:, 0:TT],
                     waits=[("VEC", m2)])
        self.op("scalar", "activation", rsd, rsd, AF.Ln, bias=self.v(self.M_EPS, 1),
                waits=[("VEC", v2)])
        r2 = self.op("scalar", "activation", rsd, rsd, AF.Exp, scale=-0.5)
        self.bank_free[0] = ("VEC", v2)
        self.bank_free[1] = ("SCA", e2)
        gv = None
        for c in range(8):
            xc = self.v(src_off + c * T + t0, TT)
            dc = self.v(dst_off + c * (TT if dst_tile else T) +
                        (0 if dst_tile else t0), TT)
            sv = self.op("vector", "tensor_sub", dc, xc, meanB,
                         waits=([("SCA", r2)] if c == 0 else ()))
            gv = self.op("gpsimd", "tensor_mul", dc, dc, rsd,
                         waits=[("VEC", sv)])
        return gv

    def preln(self, dst_off, sq_off):
        gv = None
        for tt in range(self.NTT):
            gv = self.ln_tile(self.OFF["XSB"], dst_off, tt, 1.0 / D, False, sq_off)
        self.barrier()
        return gv

    def bound_tile(self, src_off, acc_col, first, wv):
        TT = self.TT
        sq = self.v(self.OFF["ST"], TT)
        sv = self.op("scalar", "activation", sq, self.v(src_off, TT), AF.Square,
                     waits=[("VEC", wv)] if wv else ())
        self._last_sq = sv
        self.t_bank(5)
        mv = self.mm(self.pb[5][0:1, 0:TT], self.v(self.C_ONES, 1), sq,
                     True, True, waits=[("SCA", sv)], inc=True)
        if first:
            rv = self.op("vector", "reduce_max", self.v(acc_col, 1, 0, 1),
                         self.pb[5][0:1, 0:TT], AX.X, waits=[("TEN", mv)])
        else:
            self.op("vector", "reduce_max", self.v(self.M_SC1, 1, 0, 1),
                    self.pb[5][0:1, 0:TT], AX.X, waits=[("TEN", mv)])
            rv = self.op("vector", "tensor_max", self.v(acc_col, 1, 0, 1),
                         self.v(acc_col, 1, 0, 1), self.v(self.M_SC1, 1, 0, 1))
        self.bank_free[5] = ("VEC", rv)
        return rv

    def finish_bound(self):
        mq = self.v(self.M_MQ, 1, 0, 1)
        mk = self.v(self.M_MK, 1, 0, 1)
        self.op("scalar", "activation", mq, mq, AF.Ln, waits=self.ready())
        l2 = self.op("scalar", "activation", mk, mk, AF.Ln)
        a1 = self.op("vector", "tensor_add", mq, mq, mk, waits=[("SCA", l2)])
        e1 = self.op("scalar", "activation", mq, mq, AF.Exp, scale=0.5,
                     bias=self.v(self.M_LN18, 1, 0, 1), waits=[("VEC", a1)])
        n1 = self.op("vector", "tensor_scalar", mq, mq, -1.0, 25.0,
                     ALU.mult, ALU.add, waits=[("SCA", e1)])
        self.t_bank(5)
        mv = self.mm(self.pb[5][:, 0:1],
                     self.heap[0:1, self.C_ONES:self.C_ONES + 128], mq,
                     True, True, waits=[("VEC", n1)], inc=True)
        cv = self.op("vector", "tensor_copy", self.v(self.M_NEGB, 1),
                     self.pb[5][:, 0:1], waits=[("TEN", mv)])
        self.bank_free[5] = ("VEC", cv)
        return cv

    def attn_group(self, QG, KG, VG, causal, bound_wait, wo_cols, wo_ld,
                   s_pair=False):
        base = self.phase.split("|")[0]
        """attention for 2 heads + per-q-tile W_o partial accumulated into x.
        QG/KG feature-major [128 x T]; VG token-major aug [128 x 129*NKT]
        per k-tile laid out [V_h0(64)|ones|V_h1(64)].  PV is feature-major:
        O_aug^T [65 x TT] = V_augT.T @ P^T accumulated over k-tiles; the aug
        row gives the softmax denominator (row 64 for head 0, row 0 for head
        1).  Normalization multiplies by exp(-ln(denom)) broadcast across 64
        partitions via a K=1 matmul.  W_o is applied as two K=64 matmuls
        against AT[64 x 2*TT]; wo_cols = WB cols of the two 64-row halves."""
        T, TT, NKT, NQT, nqsub = self.T, self.TT, self.NKT, self.NQT, self.nqsub
        SP, ST, XS = self.OFF["SP"], self.OFF["ST"], self.OFF["XSB"]
        negb = self.v(self.M_NEGB, 1)
        AT = ST
        last_at = 0
        if causal:
            sbank, pslot = (0, 1), (SP, SP + 512)
        else:
            sbank, pslot = (0, 1, 3), (SP, SP + 512, self.C_OVL)
        nbank = len(sbank)
        for qt in range(NQT):
            at_done = 0
            for hl in range(2):
                kts = list(range(min(NKT, nqsub * qt + nqsub))) if causal \
                    else list(range(NKT))
                drow = 64
                vrow0 = 0
                last_pv = None
                prev_pv = {}
                rd = self.ready()
                self.t_bank(2)
                def emit_S(i, hl=hl, qt=qt, kts=kts, rd=rd):
                    kt = kts[i]
                    bank = sbank[i % nbank]
                    qoffb = max(0, kt - nqsub * qt) if causal else 0
                    self.t_bank(bank)
                    self.phase = base + "|S"
                    w0 = rd if i == 0 and hl == 0 else ()
                    if not s_pair:
                        return self.mm(
                            self.pb[bank][:, qoffb * 128:TT],
                            self.heap[hl * 64:hl * 64 + 64,
                                      KG + kt * 128:KG + kt * 128 + 128],
                            self.heap[hl * 64:hl * 64 + 64,
                                      QG + qt * TT + qoffb * 128:
                                      QG + (qt + 1) * TT],
                            True, True, waits=w0, inc=True)
                    p0, p1 = hl * 64, hl * 64 + 64
                    t0k = (kt * 128 // TT) * TT
                    sk = kt * 128 - t0k
                    sq_ = qoffb * 128
                    nq_ = TT - sq_
                    kh = self.vbx(KG + t0k, TT, sk, 128, p0, p1)
                    kl = self.vbx(KG + t0k, TT, sk, 128, p0, p1, lo=True)
                    qh = self.vbx(QG + qt * TT, TT, sq_, nq_, p0, p1)
                    ql = self.vbx(QG + qt * TT, TT, sq_, nq_, p0, p1, lo=True)
                    ps = self.pb[bank][:, sq_:TT]
                    self.mm(ps, kh, qh, True, False, waits=w0)
                    self.mm(ps, kh, ql, False, False)
                    return self.mm(ps, kl, qh, False, True, inc=True)
                svs = {}
                for i in range(min(nbank, len(kts))):
                    svs[i] = emit_S(i)
                for i, kt in enumerate(kts):
                    par = i % nbank
                    bank = sbank[par]
                    qoffb = max(0, kt - nqsub * qt) if causal else 0
                    n_q = TT - qoffb * 128
                    sv = svs[i]
                    mwait = [("TEN", sv)]
                    jd = kt - nqsub * qt
                    if causal and 0 <= jd < nqsub:
                        mv = self.op("vector", "tensor_add",
                                     self.pb[bank][:, jd * 128:jd * 128 + 128],
                                     self.pb[bank][:, jd * 128:jd * 128 + 128],
                                     self.v(self.C_OVL + 128, 128),
                                     waits=[("TEN", sv)])
                        mwait = [("VEC", mv)]
                    ew = list(mwait)
                    if i == 0:
                        ew.append(("VEC", bound_wait))
                    if i <= 1 and last_at:
                        ew.append(("VEC", last_at))
                    if par in prev_pv:
                        ew.append(("TEN", prev_pv[par]))
                    pexp = self.op("scalar", "activation",
                                   self.v(pslot[par], n_q),
                                   self.pb[bank][:, qoffb * 128:TT], AF.Exp,
                                   bias=negb, scale=0.125, waits=ew)
                    self.bank_free[bank] = ("SCA", pexp)
                    self.phase = base + "|PV"
                    last_pv = self.mm(
                        self.pb[2][0:65, qoffb * 128:TT],
                        self.hap(VG + kt * 130 + 65 * hl, [[1, 65]]),
                        self.v(pslot[par], n_q),
                        i == 0, i == len(kts) - 1,
                        waits=[("SCA", pexp)], inc=True)
                    prev_pv[par] = last_pv
                    if i + nbank < len(kts):
                        svs[i + nbank] = emit_S(i + nbank)
                u1 = self.op("scalar", "activation", self.v(SP, TT, 0, 1),
                             self.pb[2][drow:drow + 1, 0:TT], AF.Ln,
                             waits=[("TEN", last_pv)])
                u2 = self.op("scalar", "activation", self.v(SP, TT, 0, 1),
                             self.v(SP, TT, 0, 1), AF.Exp, scale=-1.0)
                self.t_bank(3)
                self.phase = base + "|norm"
                bc = self.mm(self.pb[3][0:64, 0:TT],
                             self.heap[0:1, self.C_ONES:self.C_ONES + 64],
                             self.v(SP, TT, 0, 1), True, True,
                             waits=[("SCA", u2)], inc=True)
                cpv = self.op("vector", "tensor_copy", self.v(SP + 512, TT, 0, 64),
                              self.pb[3][0:64, 0:TT], waits=[("TEN", bc)])
                at_done = self.op("vector", "scalar_tensor_tensor",
                                  self.heap[0:64, AT + hl * TT:AT + (hl + 1) * TT],
                                  self.pb[2][vrow0:vrow0 + 64, 0:TT], 1.0,
                                  self.v(SP + 512, TT, 0, 64),
                                  ALU.mult, ALU.mult, waits=[("VEC", cpv)])
                self.bank_free[2] = ("VEC", at_done)
                self.bank_free[3] = ("VEC", cpv)
                last_at = at_done
                self.attn_guard = [("TEN", last_pv), ("SCA", u2)]
            self.phase = base + "|WO"
            for oc in range(8):
                bank = 4 + (oc % 4)
                self.t_bank(bank)
                for hl in range(2):
                    w0 = []
                    if oc == 0 and hl == 0:
                        w0 = [("VEC", at_done)] + list(wo_ld)
                    mv = self.mm(self.pb[bank][:, 0:TT],
                                 self.heap[0:64, wo_cols[hl] + oc * 128:
                                           wo_cols[hl] + oc * 128 + 128],
                                 self.heap[0:64, AT + hl * TT:AT + (hl + 1) * TT],
                                 hl == 0, hl == 1, waits=w0, inc=(hl == 1))
                av = self.op("vector", "tensor_add",
                             self.v(XS + oc * T + qt * TT, TT),
                             self.v(XS + oc * T + qt * TT, TT),
                             self.pb[bank][:, 0:TT], waits=[("TEN", mv)])
                self.bank_free[bank] = ("VEC", av)

    def rope(self, off, lc1, lcs, src_ready):
        TT = self.TT
        SP, ST = self.OFF["SP"], self.OFF["ST"]
        self.t_bank(4)
        mv = self.mm(self.pb[4][:, 0:TT], self.v(self.C_OVL, 128),
                     self.v(off, TT), True, True, waits=[("VEC", src_ready)],
                     inc=True)
        tmp = self.v(ST, TT)
        v1 = self.op("vector", "tensor_mul", tmp, self.pb[4][:, 0:TT],
                     self.v(SP + 512, TT), waits=[("TEN", mv), lcs])
        self.bank_free[4] = ("VEC", v1)
        v2 = self.op("gpsimd", "tensor_mul", self.v(off, TT), self.v(off, TT),
                     self.v(SP, TT),
                     waits=[lc1, ("VEC", src_ready), ("TEN", mv)])
        v3 = self.op("vector", "tensor_add", self.v(off, TT), self.v(off, TT),
                     tmp, waits=[("GPS", v2)])
        return v3

    def emit_l0a(self, s):
        self.phase = "l0a.front"
        T, TT, NTT, NKT = self.T, self.TT, self.NTT, self.NKT
        XS, H1, R2, WB, SP = (self.OFF[k] for k in
                              ("XSB", "H1", "R2", "WB", "SP"))
        for c in range(8):
            lv = self.dma(self.v(XS + c * T, T),
                          self.xin.ap()[s, c * 128:(c + 1) * 128, :])
        self.dma(self.v(self.C_OVL, 128), self.d["pm0T"].ap())
        lov = self.dma(self.v(self.C_OVL + 128, 128), self.d["mask"].ap())
        wall = [(f"LD{i}", self.sem[f"LD{i}"].n) for i in range(8)]
        for eng in ("scalar", "vector", "tensor", "gpsimd"):
            for w in wall:
                self.wait(eng, *w)
        self.barrier()
        self.preln(H1, R2)
        LAT = R2
        SQ2 = R2 + 2 * T
        ldkv = self.dma(self.v(WB, 2048), self.d["w0dkv"].ap())
        for tt in range(NTT):
            t0 = tt * TT
            rd = self.ready()
            for lc in range(2):
                self.t_bank(lc)
                for c in range(8):
                    self.mm(self.pb[lc][:, 0:TT],
                            self.v(WB + c * 256 + lc * 128, 128),
                            self.v(H1 + c * T + t0, TT), c == 0, c == 7,
                            waits=(rd + [ldkv]) if (c == 0 and lc == 0) else ())
                ev = self.op("vector", "tensor_copy", self.v(LAT + lc * T + t0, TT),
                             self.pb[lc][:, 0:TT], waits=[("TEN", self.tmark())])
                self.bank_free[lc] = ("VEC", ev)
            lat_ev = ev
            lsq = self.op("scalar", "activation",
                          self.hap(SQ2, [[TT, 2], [1, TT]]),
                          self.hap(LAT + t0, [[T, 2], [1, TT]]),
                          AF.Square, waits=[("VEC", self.sem["VEC"].n)])
            for b_, off, step, w in ((2, LAT + t0, T, None), (3, SQ2, TT, lsq)):
                self.t_bank(b_)
                for lc in range(2):
                    self.mm(self.pb[b_][:, 0:TT], self.v(self.C_ONES, 128),
                            self.v(off + lc * step, TT), lc == 0, lc == 1,
                            waits=([("SCA", w)] if (w and lc == 0) else
                                   ([("VEC", lat_ev)] if lc == 0 else ())))
            vsum = self.tmark()
            ST = self.OFF["ST"]
            meanB = self.v(ST, TT)
            rsd = self.v(ST + 512, TT)
            self.op("scalar", "activation", meanB, self.pb[2][:, 0:TT], AF.Copy,
                    scale=1.0 / LATD, waits=[("TEN", vsum)])
            e2 = self.op("scalar", "activation", rsd, self.pb[3][:, 0:TT],
                         AF.Copy, scale=1.0 / LATD)
            m2 = self.op("vector", "tensor_mul", self.pb[2][:, 0:TT], meanB, meanB,
                         waits=[("SCA", e2)])
            v2 = self.op("vector", "tensor_sub", rsd, rsd, self.pb[2][:, 0:TT],
                         waits=[("VEC", m2)])
            self.op("scalar", "activation", rsd, rsd, AF.Ln, bias=self.v(self.M_EPS, 1),
                    waits=[("VEC", v2)])
            r2 = self.op("scalar", "activation", rsd, rsd, AF.Exp, scale=-0.5)
            self.bank_free[2] = ("VEC", v2)
            self.bank_free[3] = ("SCA", e2)
            for lc in range(2):
                la = self.v(LAT + lc * T + t0, TT)
                self.op("vector", "tensor_sub", la, la, meanB,
                        waits=([("SCA", r2)] if lc == 0 else ()))
                mv = self.op("vector", "tensor_mul", la, la, rsd)
                self.op("scalar", "activation", la, la, AF.Identity,
                        scale=self.v(self.M_G0 + lc, 1),
                        bias=self.v(self.M_B0 + lc, 1), waits=[("VEC", mv)])
        self.barrier()
        QG = R2 + 2 * T
        KG = QG + T
        VG = KG + T
        for g in range(8):
            self._l0a_group(g, LAT, QG, KG, VG)
        self.barrier()

    def _l0a_group(self, g, LAT, QG, KG, VG):
        self.phase = "l0a.proj"
        T, TT, NTT, NKT = self.T, self.TT, self.NTT, self.NKT
        H1, WB, SP = self.OFF["H1"], self.OFF["WB"], self.OFF["SP"]
        lq = self.dma(self.v(WB, 1024), self.d["w0q"].ap()[g])
        luk = self.dma(self.v(WB + 1024, 256), self.d["w0uk"].ap()[g])
        luv = self.dma(self.v(WB + 1280, 256), self.d["w0uv"].ap()[g])
        lwo = self.dma(self.heap[0:64, WB + 1536:WB + 2560],
                       self.d["w0o"].ap()[g * 128:g * 128 + 64, :])
        rd = self.ready()
        for tt in range(NTT):
            t0 = tt * TT
            qb = 6 + (tt % 2)
            kb = 7 - (tt % 2)
            lc1 = self.dma(self.v(SP, TT), self.d["ropeC"].ap()[:, t0:t0 + TT],
                           after=("VEC", self.rope_done) if getattr(self, "rope_done", 0) else None)
            lcs = self.dma(self.v(SP + 512, TT), self.d["ropeS"].ap()[:, t0:t0 + TT])
            self.t_bank(qb)
            for c in range(8):
                self.mm(self.pb[qb][:, 0:TT], self.v(WB + c * 128, 128),
                        self.v(H1 + c * T + t0, TT), c == 0, c == 7,
                        waits=(rd + [lq]) if (c == 0 and tt == 0) else ())
            qv = self.op("vector", "tensor_copy", self.v(QG + t0, TT),
                         self.pb[qb][:, 0:TT], waits=[("TEN", self.tmark())])
            self.bank_free[qb] = ("VEC", qv)
            rq = self.rope(QG + t0, lc1, lcs, qv)
            self.bound_tile(QG + t0, self.M_MQ, tt == 0, rq)
            self.t_bank(kb)
            for lc in range(2):
                self.mm(self.pb[kb][:, 0:TT], self.v(WB + 1024 + lc * 128, 128),
                        self.v(LAT + lc * T + t0, TT), lc == 0, lc == 1,
                        waits=[luk] if lc == 0 else ())
            kv = self.op("vector", "tensor_copy", self.v(KG + t0, TT),
                         self.pb[kb][:, 0:TT], waits=[("TEN", self.tmark())])
            self.bank_free[kb] = ("VEC", kv)
            rk = self.rope(KG + t0, lc1, lcs, kv)
            self.rope_done = self.bound_tile(KG + t0, self.M_MK, tt == 0, rk)
        lwo2 = self.dma(self.heap[0:64, WB:WB + 1024],
                        self.d["w0o"].ap()[g * 128 + 64:g * 128 + 128, :],
                        after=("TEN", self.tmark()))
        bv = self.finish_bound()
        mz = self.op("vector", "memset",
                     self.hap(VG + 64, [[130, NKT], [65, 2]]), 1.0)
        for kt in range(NKT):
            vb_ = 6 + (kt % 2)
            self.t_bank(vb_)
            for hl in range(2):
                for lc in range(2):
                    self.mm(self.pb[vb_][:, hl * 64:hl * 64 + 64],
                            self.v(LAT + lc * T + kt * 128, 128),
                            self.v(WB + 1280 + lc * 128 + hl * 64, 64),
                            lc == 0, lc == 1,
                            waits=[luv] if (kt == 0 and hl == 0 and lc == 0) else ())
            ev = self.op("vector", "tensor_copy",
                         self.hap(VG + kt * 130, [[65, 2], [1, 64]]),
                         self.pb[vb_][:, 0:128],
                         waits=[("TEN", self.tmark()), ("VEC", mz)])
            self.bank_free[vb_] = ("VEC", ev)
        self.attn_group(QG, KG, VG, True, bv, (WB + 1536, WB), (lwo, lwo2))
        self.barrier()
        self.rope_done = 0

    def emit_l0f(self, s):
        lec = self.dma(self.v(self.C_OVL, 512, 0, 8), self.d["ec"].ap())
        self.wait("tensor", *lec)
        for tt in range(self.NTT):
            self._l0f_tile(tt)
        self.barrier()

    def ln_tile_pair(self, src_off, dst_pair, tt, scr_off):
        self.phase = "l0f.ln"
        """LayerNorm of one t-tile of chunked fp32 src into a bf16 PAIR at
        dst_pair (+ c*TT per chunk).  scr_off: 8*TT fp32 scratch."""
        T, TT = self.T, self.TT
        ST = self.OFF["ST"]
        t0 = tt * TT
        meanB = self.v(ST, TT)
        rsd = self.v(ST + 512, TT)
        src3 = self.hap(src_off + t0, [[T, 8], [1, TT]])
        dst3 = self.hap(scr_off, [[TT, 8], [1, TT]])
        sq = self.op("scalar", "activation", dst3, src3, AF.Square,
                     waits=self.ready())
        for b_, off, step, w in ((0, src_off + t0, T, None),
                                 (1, scr_off, TT, sq)):
            self.t_bank(b_)
            for c in range(8):
                self.mm(self.pb[b_][:, 0:TT], self.v(self.C_ONES, 128),
                        self.v(off + c * step, TT), c == 0, c == 7,
                        waits=([("SCA", w)] if (w and c == 0) else ()))
        vsum = self.tmark()
        self.op("scalar", "activation", meanB, self.pb[0][:, 0:TT], AF.Copy,
                scale=1.0 / D, waits=[("TEN", vsum)])
        e2 = self.op("scalar", "activation", rsd, self.pb[1][:, 0:TT], AF.Copy,
                     scale=1.0 / D)
        m2 = self.op("vector", "tensor_mul", self.pb[0][:, 0:TT], meanB, meanB,
                     waits=[("SCA", e2)])
        v2 = self.op("vector", "tensor_sub", rsd, rsd, self.pb[0][:, 0:TT],
                     waits=[("VEC", m2)])
        self.op("scalar", "activation", rsd, rsd, AF.Ln,
                bias=self.v(self.M_EPS, 1), waits=[("VEC", v2)])
        r2 = self.op("scalar", "activation", rsd, rsd, AF.Exp, scale=-0.5)
        self.bank_free[0] = ("VEC", v2)
        self.bank_free[1] = ("SCA", e2)
        gv = None
        for c in range(8):
            xc = self.v(src_off + c * T + t0, TT)
            uc = self.v(scr_off + c * TT, TT)
            sv = self.op("vector", "tensor_sub", uc, xc, meanB,
                         waits=([("SCA", r2)] if c == 0 else ()))
            gm = self.op("gpsimd", "tensor_mul", uc, uc, rsd,
                         waits=[("VEC", sv)])
            gv = self.split(dst_pair + c * TT, TT, uc, waits=[("GPS", gm)])
        return gv

    def _mmblk(self, dst_off, rhs_pair, wname, nblk, nk, blkelems, stage,
               extra=(), evac="pair", evac_args=None):
        TT = self.TT
        WB = self.OFF["WB"]
        self.phase = self.phase.split("|")[0] + "|mb." + wname
        evs = []
        for ob in range(nblk):
            sl = ob % 2
            after = ("TEN", self.wlast[sl]) if self.wlast[sl] else None
            wsv = self.load_pair(WB + sl * 1024, blkelems,
                                 self.d[wname].ap()[ob], stage, after=after)
            bank = 4 + sl
            self.t_bank(bank)
            w0 = (list(extra) if ob == 0 else []) + [("VEC", wsv)]
            for c in range(nk):
                self.mm3(self.pb[bank][:, 0:TT], WB + sl * 1024, blkelems, c,
                         rhs_pair + c * TT, TT, c == 0, c == nk - 1,
                         waits=w0 if c == 0 else ())
            self.wlast[sl] = self.tmark()
            mv = self.tmark()
            if evac == "pair":
                ev = self.split(dst_off + ob * TT, TT, self.pb[bank][:, 0:TT],
                                waits=[("TEN", mv)])
                self.bank_free[bank] = ("VEC", ev)
            elif evac == "copy":
                ev = self.op("vector", "tensor_copy",
                             self.v(dst_off + ob * TT, TT),
                             self.pb[bank][:, 0:TT], waits=[("TEN", mv)])
                self.bank_free[bank] = ("VEC", ev)
            elif evac == "gelu":
                ev = self.op("scalar", "activation",
                             self.v(dst_off + ob * TT, TT),
                             self.pb[bank][:, 0:TT], AF.Gelu,
                             bias=self.v(evac_args + ob, 1), waits=[("TEN", mv)])
                self.bank_free[bank] = ("SCA", ev)
            evs.append(ev)
        return evs

    def _l0f_tile(self, tt):
        T, TT = self.T, self.TT
        XS, H1, R2, WB, SP, ST = (self.OFF[k] for k in
                                  ("XSB", "H1", "R2", "WB", "SP", "ST"))
        t0 = tt * TT
        XNP = R2
        HH = R2 + 8 * TT
        XV = R2 + 16 * TT
        YW = R2 + 20 * TT
        CU = R2 + 24 * TT
        BF = R2 + 1024
        QQ, KK, IQ, IK = H1, H1 + 8 * TT, H1 + 16 * TT, H1 + 24 * TT
        GF, GI = KK, QQ
        CMB = IQ
        WF, WI = ST, ST + 512
        stageA = HH
        stageB = XNP
        self.wlast = [None, None]
        lrw2 = self.dma(self.v(WB + 2048, 64), self.d["wfr2"].ap()[0])
        lrb1 = self.dma(self.v(WB + 2112, 8), self.d["rb1"].ap())
        lnv = self.dma(self.v(SP, TT), self.d["nvinv"].ap()[:, t0:t0 + TT])
        gv = self.ln_tile_pair(XS, XNP, tt, HH)
        self.phase = "l0f"
        self._stage_busy[stageA] = ("VEC", gv)
        self._mmblk(QQ, XNP, "wfQ", 8, 8, 1024, stageA,
                    extra=[("VEC", gv), ("GPS", self.sem["GPS"].n)])
        self._mmblk(KK, XNP, "wfK", 8, 8, 1024, stageA)
        self._mmblk(IQ, QQ, "wfI", 8, 8, 1024, stageA)
        self._mmblk(IK, KK, "wfI", 8, 8, 1024, stageA)
        self._stage_busy[stageB] = ("TEN", self.tmark())
        self.phase = "l0f|rtr"
        for srcq, wdst in ((QQ, WF), (IQ, WI)):
            self._mmblk(HH, srcq, "wfr1", 8, 8, 1024, stageB,
                        evac="gelu", evac_args=WB + 2112)
            self.wait("scalar", *lrb1)
            self.t_bank(6)
            rd = self.ready()
            for c in range(8):
                self.mm(self.pb[6][0:8, 0:TT], self.v(WB + 2048 + c * 8, 8),
                        self.v(HH + c * TT, TT), c == 0, c == 7,
                        waits=(rd + [lrw2]) if c == 0 else ())
            lg = self.op("scalar", "activation", self.v(wdst, TT, 0, 8),
                         self.pb[6][0:8, 0:TT], AF.Identity,
                         bias=self.v(self.M_RB2EB, 1, 0, 8),
                         waits=[("TEN", self.tmark())])
            self.bank_free[6] = ("SCA", lg)
            wtm = self.v(SP + 512, 128)
            for j in range(TT // 128):
                self.t_bank(7)
                v1 = self.tp(self.pb[7][0:128, 0:8],
                             self.heap[0:8, wdst + j * 128:wdst + (j + 1) * 128],
                             waits=[("SCA", lg)], kdim=8)
                nm = self.op("vector", "tensor_reduce", self.v(self.M_SC1, 1),
                             self.pb[7][:, 0:8], AX.X, ALU.max, negate=True,
                             waits=[("TEN", v1)])
                e1 = self.op("scalar", "activation", wtm[:, 0:8],
                             self.pb[7][:, 0:8], AF.Exp,
                             bias=self.v(self.M_SC1, 1), waits=[("VEC", nm)])
                self.op("vector", "reduce_sum", self.v(self.M_SC1, 1),
                        wtm[:, 0:8], AX.X, waits=[("SCA", e1)])
                self.op("vector", "reciprocal", self.v(self.M_SC2, 1),
                        self.v(self.M_SC1, 1))
                m1 = self.op("vector", "tensor_scalar_mul", wtm[:, 0:8],
                             wtm[:, 0:8], self.v(self.M_SC2, 1))
                self.bank_free[7] = ("VEC", m1)
                self.t_bank(7)
                v2 = self.tp(self.pb[7][0:8, 0:128], wtm[:, 0:8],
                             waits=[("VEC", m1)])
                cv = self.op("vector", "tensor_copy",
                             self.heap[0:8, wdst + j * 128:wdst + (j + 1) * 128],
                             self.pb[7][0:8, 0:128], waits=[("TEN", v2)])
                self.bank_free[7] = ("VEC", cv)
        routers_done = self.tmark()
        lbf = self.dma(self.v(BF, 1024, 0, 8), self.d["bfw"].ap(),
                       after=("TEN", routers_done))
        lbi = self.dma(self.v(BF + 1024, 1024, 0, 8), self.d["bin"].ap())
        gps_prev = 0
        for br, (qsrc, ksrc, vn, wn, un, boff, wrow, c0) in enumerate(
                ((QQ, KK, "vfw", "wfw", "ufw", BF, WF, 0),
                 (IQ, IK, "vin", "win", "uin", BF + 1024, WI, 4))):
            self._mmblk(XV, qsrc, vn, 4, 8, 1024, stageB, evac="copy",
                        extra=[("TEN", self.tmark())])
            self._mmblk(YW, ksrc, wn, 4, 8, 1024, stageB, evac="copy",
                        extra=[("TEN", self.tmark())])
            self.phase = "l0f|scan"
            for krb in range(4):
                cc = self.v(self.M_CARRY + c0 + krb, 1)
                ini = 0.0 if tt == 0 else cc
                self.op("vector", "tensor_tensor_scan",
                        self.v(CU + krb * TT, TT), self.v(YW + krb * TT, TT),
                        self.v(YW + krb * TT, TT), ini, ALU.add, ALU.bypass,
                        waits=[("TEN", routers_done), ("GPS", gps_prev)])
                self.op("vector", "tensor_copy", cc,
                        self.v(CU + krb * TT + TT - 1, 1))
            for krb in range(4):
                p1 = self.op("gpsimd", "tensor_mul", self.v(XV + krb * TT, TT),
                             self.v(XV + krb * TT, TT), self.v(CU + krb * TT, TT),
                             waits=[("VEC", self.sem["VEC"].n)])
                gps_prev = p1
                self.t_bank(6)
                mv = self.mm(self.pb[6][:, 0:TT],
                             self.heap[0:8, self.C_OVL + krb * 128:
                                       self.C_OVL + (krb + 1) * 128],
                             self.v(wrow, TT, 0, 8), True, True,
                             waits=[("VEC", self.sem["VEC"].n)], inc=True)
                p2 = self.op("vector", "tensor_mul", self.v(XV + krb * TT, TT),
                             self.v(XV + krb * TT, TT), self.pb[6][:, 0:TT],
                             waits=[("TEN", mv), ("GPS", p1)])
                self.bank_free[6] = ("VEC", p2)
                self.split(YW + krb * TT, TT, self.v(XV + krb * TT, TT))
            pv = self.sem["VEC"].n
            gdst = GF if br == 0 else GI
            self.phase = "l0f|U"
            for oc in range(8):
                sl = oc % 2
                after = ("TEN", self.wlast[sl]) if self.wlast[sl] else None
                wsv = self.load_pair(WB + sl * 1024, 512, self.d[un].ap()[oc],
                                     stageB, after=after)
                bank = 4 + sl
                self.t_bank(bank)
                for krc in range(4):
                    self.mm3(self.pb[bank][:, 0:TT], WB + sl * 1024, 512, krc,
                             YW + krc * TT, TT, krc == 0, False,
                             waits=[("VEC", wsv), ("VEC", pv), lbf, lbi]
                             if krc == 0 else ())
                self.mm(self.pb[bank][:, 0:TT],
                        self.heap[0:8, boff + oc * 128:boff + oc * 128 + 128],
                        self.v(wrow, TT, 0, 8), False, True)
                self.wlast[sl] = self.tmark()
                ev = self.op("vector", "tensor_copy", self.v(gdst + oc * TT, TT),
                             self.pb[bank][:, 0:TT],
                             waits=[("TEN", self.tmark())])
                self.bank_free[bank] = ("VEC", ev)
        cvs = None
        for oc in range(8):
            self.op("vector", "scalar_tensor_tensor", self.v(GF + oc * TT, TT),
                    self.v(GI + oc * TT, TT), self.v(self.M_ALPHA, 1),
                    self.v(GF + oc * TT, TT), ALU.mult, ALU.add)
            cvs = self.split(CMB + oc * TT, TT, self.v(GF + oc * TT, TT))
        self.phase = "l0f|WO"
        for oc in range(8):
            sl = oc % 2
            after = ("TEN", self.wlast[sl]) if self.wlast[sl] else None
            wsv = self.load_pair(WB + sl * 1024, 1024, self.d["wfO"].ap()[oc],
                                 stageB, after=after)
            bank = 4 + sl
            self.t_bank(bank)
            for c in range(8):
                self.mm3(self.pb[bank][:, 0:TT], WB + sl * 1024, 1024, c,
                         CMB + c * TT, TT, c == 0, c == 7,
                         waits=[("VEC", wsv), ("VEC", cvs)] if c == 0 else ())
            self.wlast[sl] = self.tmark()
            mv = self.tmark()
            e1 = self.op("vector", "tensor_mul", self.v(GI + oc * TT, TT),
                         self.pb[bank][:, 0:TT], self.v(SP, TT),
                         waits=[("TEN", mv), lnv])
            self.op("vector", "tensor_add", self.v(XS + oc * T + t0, TT),
                    self.v(XS + oc * T + t0, TT), self.v(GI + oc * TT, TT))
            self.bank_free[bank] = ("VEC", e1)
        self.barrier()
        self._stage_busy = {}

    def emit_l1a(self, s):
        LATK, LATV, QG, KG, VG = self._l1a_front(s)
        for g in range(8):
            self._l1a_group(g, LATK, LATV, QG, KG, VG)
        self.barrier()

    def _l1a_front(self, s):
        self.phase = "l1a.front"
        T, TT, NTT, NKT = self.T, self.TT, self.NTT, self.NKT
        XS, H1, R2, WB = (self.OFF[k] for k in ("XSB", "H1", "R2", "WB"))
        self.preln(H1, R2)
        LATK, LATV = R2, R2 + 2 * T
        for wn, dsto in (("w1ka", LATK), ("w1va", LATV)):
            lv = self.dma(self.v(WB, 2048), self.d[wn].ap()[0])
            rd = self.ready()
            for tt in range(NTT):
                t0 = tt * TT
                for lc in range(2):
                    self.t_bank(lc)
                    for c in range(8):
                        self.mm(self.pb[lc][:, 0:TT],
                                self.v(WB + c * 256 + lc * 128, 128),
                                self.v(H1 + c * T + t0, TT), c == 0, c == 7,
                                waits=(rd + [lv]) if (c == 0 and tt == 0 and lc == 0) else ())
                    ev = self.op("vector", "tensor_copy",
                                 self.v(dsto + lc * T + t0, TT),
                                 self.pb[lc][:, 0:TT],
                                 waits=[("TEN", self.tmark())])
                    self.bank_free[lc] = ("VEC", ev)
            self.barrier()
        QG = R2 + 4 * T
        KG = QG + T
        VG = KG + T
        return LATK, LATV, QG, KG, VG

    def _l1a_group(self, g, LATK, LATV, QG, KG, VG):
        self.phase = "l1a.proj"
        T, TT, NTT, NKT = self.T, self.TT, self.NTT, self.NKT
        H1, WB = self.OFF["H1"], self.OFF["WB"]
        lq = self.dma(self.v(WB, 1024), self.d["w1q"].ap()[g])
        lkb = self.dma(self.v(WB + 1024, 256), self.d["w1kb"].ap()[g])
        lvb = self.dma(self.v(WB + 1280, 256), self.d["w1vb"].ap()[g])
        lwo = self.dma(self.heap[0:64, WB + 1536:WB + 2560],
                       self.d["w1o"].ap()[g * 128:g * 128 + 64, :])
        rd = self.ready()
        SCQ = self.C_OVL
        SCK = self.OFF["SP"]
        ag = getattr(self, "attn_guard", [])
        for tt in range(NTT):
            t0 = tt * TT
            qb = 6 + (tt % 2)
            kb = 7 - (tt % 2)
            self.t_bank(qb)
            for c in range(8):
                self.mm(self.pb[qb][:, 0:TT], self.v(WB + c * 128, 128),
                        self.v(H1 + c * T + t0, TT), c == 0, c == 7,
                        waits=(rd + [lq]) if (c == 0 and tt == 0) else ())
            qw = [("TEN", self.tmark())] + (list(ag) if tt == 0 else [])
            if tt > 0:
                qw.append(("SCA", self._last_sq))
            qv = self.op("vector", "tensor_copy", self.v(SCQ, TT),
                         self.pb[qb][:, 0:TT], waits=qw)
            self.bank_free[qb] = ("VEC", qv)
            self.bound_tile(SCQ, self.M_MQ, tt == 0, qv)
            self.split(QG + t0, TT, self.v(SCQ, TT))
            self.t_bank(kb)
            for lc in range(2):
                self.mm(self.pb[kb][:, 0:TT], self.v(WB + 1024 + lc * 128, 128),
                        self.v(LATK + lc * T + t0, TT), lc == 0, lc == 1,
                        waits=[lkb] if lc == 0 else ())
            kw = [("TEN", self.tmark())] + (list(ag) if tt == 0 else [])
            kw.append(("SCA", self._last_sq))
            kv = self.op("vector", "tensor_copy", self.v(SCK, TT),
                         self.pb[kb][:, 0:TT], waits=kw)
            self.bank_free[kb] = ("VEC", kv)
            self.bound_tile(SCK, self.M_MK, tt == 0, kv)
            self.split(KG + t0, TT, self.v(SCK, TT))
        lwo2 = self.dma(self.heap[0:64, WB:WB + 1024],
                        self.d["w1o"].ap()[g * 128 + 64:g * 128 + 128, :],
                        after=("TEN", self.tmark()))
        bv = self.finish_bound()
        mz = self.op("vector", "memset",
                     self.hap(VG + 64, [[130, NKT], [65, 2]]), 1.0)
        for kt in range(NKT):
            vb_ = 6 + (kt % 2)
            self.t_bank(vb_)
            for hl in range(2):
                for lc in range(2):
                    self.mm(self.pb[vb_][:, hl * 64:hl * 64 + 64],
                            self.v(LATV + lc * T + kt * 128, 128),
                            self.v(WB + 1280 + lc * 128 + hl * 64, 64),
                            lc == 0, lc == 1,
                            waits=[lvb] if (kt == 0 and hl == 0 and lc == 0) else ())
            ev = self.op("vector", "tensor_copy",
                         self.hap(VG + kt * 130, [[65, 2], [1, 64]]),
                         self.pb[vb_][:, 0:128],
                         waits=[("TEN", self.tmark()), ("VEC", mz)])
            self.bank_free[vb_] = ("VEC", ev)
        self.attn_group(QG, KG, VG, False, bv, (WB + 1536, WB), (lwo, lwo2),
                        s_pair=True)
        self.barrier()

    def emit_l1f(self, s):
        self.phase = "l1f"
        T, TT, NTT = self.T, self.TT, self.NTT
        XS, H1, R2, WB, SP = (self.OFF[k] for k in
                              ("XSB", "H1", "R2", "WB", "SP"))
        MOFF = R2
        if s == 0:
            AO, BO = H1, H1 + 8192
            w1last = None
            for ob in range(8):
                lwr = self.dma(self.v(WB, 1024), self.d["wrB"].ap()[ob],
                               after=("TEN", w1last) if w1last else None)
                lwi = self.dma(self.v(WB + 1024, 1024), self.d["wiB"].ap()[ob])
                for b in range(6):
                    self.t_bank(b)
                for nt in range(2):
                    for c in range(8):
                        lc_ = self.dma(
                            self.v(SP, 512),
                            self.d["dftC"].ap()[c * 128:(c + 1) * 128,
                                                nt * 512:(nt + 1) * 512],
                            after=("TEN", self.tmark())
                            if (c + nt + ob) else None)
                        self.mm(self.pb[nt][:, 0:512], self.v(WB + c * 128, 128),
                                self.v(SP, 512), c == 0, False,
                                waits=[lc_, lwr] if c == 0 else [lc_])
                        self.mm(self.pb[2 + nt][:, 0:512],
                                self.v(WB + 1024 + c * 128, 128),
                                self.v(SP, 512), c == 0, c == 7,
                                waits=[lwi] if c == 0 else ())
                        ls_ = self.dma(
                            self.v(SP + 512, 512),
                            self.d["dftS"].ap()[c * 128:(c + 1) * 128,
                                                nt * 512:(nt + 1) * 512])
                        self.mm(self.pb[nt][:, 0:512],
                                self.v(WB + 1024 + c * 128, 128),
                                self.v(SP + 512, 512), False, c == 7,
                                waits=[ls_])
                        self.mm(self.pb[4 + nt][:, 0:512], self.v(WB + c * 128, 128),
                                self.v(SP + 512, 512), c == 0, c == 7)
                    w1last = self.tmark()
                    av = self.op("vector", "tensor_copy",
                                 self.v(AO + ob * 1024 + nt * 512, 512),
                                 self.pb[nt][:, 0:512], waits=[("TEN", w1last)])
                    self.op("vector", "tensor_copy",
                            self.v(BO + ob * 1024 + nt * 512, 512),
                            self.pb[2 + nt][:, 0:512])
                    bv = self.op("vector", "tensor_sub",
                                 self.v(BO + ob * 1024 + nt * 512, 512),
                                 self.v(BO + ob * 1024 + nt * 512, 512),
                                 self.pb[4 + nt][:, 0:512])
                    self.bank_free[nt] = ("VEC", av)
                    self.bank_free[2 + nt] = ("VEC", bv)
                    self.bank_free[4 + nt] = ("VEC", bv)
            self.barrier()
            for nb in range(8):
                for mt in range(2):
                    self.t_bank(0)
                    self.t_bank(1)
                    for oc in range(8):
                        lc_ = self.dma(
                            self.v(SP, 512),
                            self.d["dftC"].ap()[oc * 128:(oc + 1) * 128,
                                                mt * 512:(mt + 1) * 512],
                            after=("TEN", self.tmark())
                            if (oc + mt + nb) else None)
                        self.mm(self.pb[0][:, 0:512],
                                self.v(AO + oc * 1024 + nb * 128, 128),
                                self.v(SP, 512), oc == 0, oc == 7,
                                waits=[lc_])
                        ls_ = self.dma(
                            self.v(SP + 512, 512),
                            self.d["dftS"].ap()[oc * 128:(oc + 1) * 128,
                                                mt * 512:(mt + 1) * 512])
                        self.mm(self.pb[1][:, 0:512],
                                self.v(BO + oc * 1024 + nb * 128, 128),
                                self.v(SP + 512, 512), oc == 0, oc == 7,
                                waits=[ls_])
                    mv = self.tmark()
                    c1 = self.op("vector", "tensor_copy",
                                 self.v(MOFF + nb * 1024 + mt * 512, 512),
                                 self.pb[0][:, 0:512], waits=[("TEN", mv)])
                    sv = self.op("vector", "tensor_sub",
                                 self.v(MOFF + nb * 1024 + mt * 512, 512),
                                 self.v(MOFF + nb * 1024 + mt * 512, 512),
                                 self.pb[1][:, 0:512])
                    ev = self.op("scalar", "activation",
                                 self.v(MOFF + nb * 1024 + mt * 512, 512),
                                 self.v(MOFF + nb * 1024 + mt * 512, 512),
                                 AF.Copy, scale=1.0 / 1024.0,
                                 waits=[("VEC", sv)])
                    self.bank_free[0] = ("VEC", c1)
                    self.bank_free[1] = ("VEC", sv)
            self.wait("sync", "SCA", self.sem["SCA"].n)
            self.dma(self.mcache.ap()[:, :], self.v(MOFF, 8192))
            self.barrier()
        else:
            lv = self.dma(self.v(MOFF, 8192), self.mcache.ap()[:, :])
            self.wait("tensor", *lv)
            self.barrier()
        XN = R2 + 8192
        for tt in range(NTT):
            t0 = tt * TT
            gv = self.ln_tile(XS, XN, tt, 1.0 / D, True, XN)
            prev_e2 = [0, 0]
            for oc in range(8):
                bank = 2 + (oc % 4)
                self.t_bank(bank)
                for c in range(8):
                    self.mm(self.pb[bank][:, 0:TT],
                            self.v(MOFF + c * 1024 + oc * 128, 128),
                            self.v(XN + c * TT, TT), c == 0, c == 7,
                            waits=[("GPS", gv)] if (c == 0 and oc == 0) else ())
                mv = self.tmark()
                sl = oc % 2
                e1 = self.op("scalar", "activation", self.v(SP + sl * 512, TT),
                             self.pb[bank][:, 0:TT], AF.Identity,
                             bias=self.v(self.M_BIAS1 + oc, 1),
                             waits=[("TEN", mv), ("VEC", prev_e2[sl])])
                e2 = self.op("vector", "tensor_add",
                             self.v(XS + oc * T + t0, TT),
                             self.v(XS + oc * T + t0, TT),
                             self.v(SP + sl * 512, TT),
                             waits=[("SCA", e1)])
                prev_e2[sl] = e2
                self.bank_free[bank] = ("VEC", e2)
        self.barrier()

    def emit_init(self):
        self.dma(self.v(self.C_ONES, 128), self.d["ones"].ap())
        self.dma(self.v(self.C_ID, 128), self.d["id"].ap())
        self.dma(self.v(self.M_RB2EB, 1, 0, 8), self.d["rb2"].ap())
        self.dma(self.v(self.M_G0, 2), self.d["g0"].ap())
        self.dma(self.v(self.M_B0, 2), self.d["b0"].ap())
        self.dma(self.v(self.M_BIAS1, 8), self.d["bias1"].ap())
        self.dma(self.v(self.M_ALPHA1, 1, 0, 1), self.d["alpha"].ap())
        self.op("vector", "memset", self.v(self.M_EPS, 1), 1e-5)
        self.op("vector", "memset", self.v(self.M_LN18, 1), LN1_8)
        lv = self.dma(self.v(self.M_SC1, 1, 0, 8), self.d["eb"].ap())
        wall = [(f"LD{i}", self.sem[f"LD{i}"].n) for i in range(8)]
        self.op("vector", "tensor_add", self.v(self.M_RB2EB, 1, 0, 8),
                self.v(self.M_RB2EB, 1, 0, 8), self.v(self.M_SC1, 1, 0, 8),
                waits=wall)
        mv = self.mm(self.pb[7][:, 0:1],
                     self.heap[0:1, self.C_ONES:self.C_ONES + 128],
                     self.heap[0:1, self.M_ALPHA1:self.M_ALPHA1 + 1],
                     True, True, waits=wall, inc=True)
        self.op("vector", "tensor_copy", self.v(self.M_ALPHA, 1),
                self.pb[7][:, 0:1], waits=[("TEN", mv)])
        self.barrier()

    def emit_store(self, s):
        T, XS = self.T, self.OFF["XSB"]
        self.wait("sync", "VEC", self.sem["VEC"].n)
        self.wait("sync", "GPS", self.sem["GPS"].n)
        self.wait("sync", "SCA", self.sem["SCA"].n)
        for c in range(8):
            self.dma(self.yout.ap()[s, c * 128:(c + 1) * 128, :],
                     self.v(XS + c * T, T))
        for i in range(8):
            self.wait("sync", f"LD{i}", self.sem[f"LD{i}"].n)
        self.barrier()

    def build(self):
        self.rope_done = 0
        self.emit_init()
        for s in range(NSEQ):
            self.emit_l0a(s)
            if self.nblocks >= 2:
                self.emit_l0f(s)
            if self.nblocks >= 3:
                self.emit_l1a(s)
            if self.nblocks >= 4:
                self.emit_l1f(s)
            self.emit_store(s)
        nc, ops = self.nc, self.ops
        with nc.Block() as block:
            @block.sync
            def _(e):
                for fn in ops["sync"]:
                    fn(e)

            @block.tensor
            def _(e):
                for fn in ops["tensor"]:
                    fn(e)

            @block.vector
            def _(e):
                for fn in ops["vector"]:
                    fn(e)

            @block.scalar
            def _(e):
                for fn in ops["scalar"]:
                    fn(e)

            @block.gpsimd
            def _(e):
                for fn in ops["gpsimd"]:
                    fn(e)
        return nc


_BUILD_CACHE = {}


def build_program(T=2048, nblocks=4):
    key = (T, nblocks)
    if key not in _BUILD_CACHE:
        b = B(T, nblocks)
        _BUILD_CACHE[key] = b.build()
        _BUILD_CACHE[key]._mmlab = b.mmlab
    return _BUILD_CACHE[key]


def kernel(**inputs):
    T = SEQT
    x = np.ascontiguousarray(np.asarray(inputs["L0_attn_x"], np.float32))
    Bt, Tx, Dx = x.shape
    assert Dx == D and Tx == T, (x.shape, T)
    shared = _host_prep(inputs)
    shared.update(_host_consts(T))
    nc = build_program(T, NBLOCKS)
    in_maps = []
    for c in range(NCORES):
        m = dict(shared)
        m["x"] = np.ascontiguousarray(x[2 * c:2 * c + 2].transpose(0, 2, 1))
        in_maps.append(m)
    res = run_bass_kernel_spmd(nc, in_maps, core_ids=list(range(NCORES)))
    out = np.empty((Bt, T, D), np.float32)
    for c in range(NCORES):
        y = np.asarray(res.results[c]["y"])
        out[2 * c:2 * c + 2] = y.transpose(0, 2, 1)
    return out


# version 16
# speedup vs baseline: 1.0806x; 1.0267x over previous
import numpy as np
import concourse.bass as bass
import concourse.mybir as mybir
from concourse.bass_utils import run_bass_kernel_spmd

F32 = mybir.dt.float32
BF16 = mybir.dt.bfloat16
AF = mybir.ActivationFunctionType
ALU = mybir.AluOpType
AX = mybir.AxisListType

D = 1024
LATD = 256
KR = 512
NCORES = 8
NSEQ = 2
NEG = -1.0e30
LN1_8 = float(np.log(1.0 / 8.0))

SEQT = 2048
NBLOCKS = 4
DETECT = False
SYNC_SER = False



def _host_consts(T):
    c = {}
    j = np.arange(128)
    c["ones"] = np.ones((128, 128), np.float32)
    c["id"] = np.eye(128, dtype=np.float32)
    f = (j % 64) % 32
    inv = (1.0 / 10000.0 ** (f / 32.0))[:, None]
    ang = inv * np.arange(T)[None, :]
    c["ropeC"] = np.cos(ang).astype(np.float32)
    c["ropeS"] = np.sin(ang).astype(np.float32)
    Pm = np.zeros((128, 128), np.float32)
    for r in range(128):
        rb = r % 64
        base = r - rb
        if rb < 32:
            Pm[r, base + rb + 32] = -1.0
        else:
            Pm[r, base + rb - 32] = 1.0
    c["pm0T"] = np.ascontiguousarray(Pm.T)
    kk, qq = np.meshgrid(np.arange(128), np.arange(128), indexing="ij")
    m = np.zeros((128, 128), np.float32)
    m[kk > qq] = NEG
    c["mask"] = m
    E = np.zeros((8, 512), np.float32)
    for col in range(512):
        E[col // 64, col] = 1.0
    c["ec"] = E
    nv = (1.0 / np.arange(1, T + 1, dtype=np.float64)).astype(np.float32)
    c["nvinv"] = np.ascontiguousarray(np.broadcast_to(nv, (128, T)))
    n = np.arange(D, dtype=np.int64)
    th = 2.0 * np.pi * (np.outer(n, n) % D).astype(np.float64) / D
    c["dftC"] = np.cos(th).astype(np.float32)
    c["dftS"] = np.sin(th).astype(np.float32)
    return c


def _blk_lhsT(w, kb=128, mb=128):
    K, M = w.shape
    nk, nm = K // kb, M // mb
    a = w.reshape(nk, kb, nm, mb).transpose(2, 1, 0, 3).reshape(nm, kb, nk * mb)
    return np.ascontiguousarray(a)


def _flat_one(w):
    K, M = w.shape
    nk = K // 128
    a = w.reshape(nk, 128, M).transpose(1, 0, 2).reshape(128, nk * M)
    return np.ascontiguousarray(a)


def _host_prep(inputs):
    f32 = lambda a: np.ascontiguousarray(np.asarray(a, np.float32))
    iw = {}
    iw["w0q"] = _blk_lhsT(f32(inputs["L0_attn_W_q"]))
    iw["w0dkv"] = _flat_one(f32(inputs["L0_attn_W_dkv"]))
    iw["w0uk"] = _blk_lhsT(f32(inputs["L0_attn_W_uk"]))
    iw["w0uv"] = _blk_lhsT(f32(inputs["L0_attn_W_uv"]))
    iw["w0o"] = f32(inputs["L0_attn_W_o"])
    iw["g0"] = _flat_one(f32(inputs["L0_attn_ln_gamma"]).reshape(LATD, 1))
    iw["b0"] = _flat_one(f32(inputs["L0_attn_ln_beta"]).reshape(LATD, 1))
    iw["wfQ"] = _blk_lhsT(f32(inputs["L0_ffn_W_Q"]).T)
    iw["wfK"] = _blk_lhsT(f32(inputs["L0_ffn_W_K"]).T)
    iw["wfO"] = _blk_lhsT(f32(inputs["L0_ffn_W_O"]).T)
    iw["wfI"] = _blk_lhsT(f32(inputs["L0_ffn_W_inv"]).T)
    iw["wfr1"] = _blk_lhsT(f32(inputs["L0_ffn_router_w1"]).T)
    iw["wfr2"] = _blk_lhsT(f32(inputs["L0_ffn_router_w2"]).T, mb=8)
    vflat = lambda a: _blk_lhsT(f32(a).transpose(1, 0, 2).reshape(D, KR))
    uflat = lambda a: _blk_lhsT(f32(a).transpose(0, 2, 1).reshape(KR, D))
    iw["vfw"] = vflat(inputs["L0_ffn_V_fwd"])
    iw["wfw"] = vflat(inputs["L0_ffn_W_fwd"])
    iw["ufw"] = uflat(inputs["L0_ffn_U_fwd"])
    iw["vin"] = vflat(inputs["L0_ffn_W_inv_exp"])
    iw["win"] = vflat(inputs["L0_ffn_V_inv"])
    iw["uin"] = uflat(inputs["L0_ffn_U_inv"])
    iw["bfw"] = f32(inputs["L0_ffn_b_fwd"])
    iw["bin"] = f32(inputs["L0_ffn_b_inv"])
    iw["rb1"] = _flat_one(f32(inputs["L0_ffn_router_b1"]).reshape(D, 1))
    iw["rb2"] = f32(inputs["L0_ffn_router_b2"]).reshape(8, 1)
    iw["eb"] = f32(inputs["L0_ffn_expert_bias"]).reshape(8, 1)
    iw["alpha"] = f32(inputs["L0_ffn_alpha_bi"]).reshape(1, 1)
    iw["w1q"] = _blk_lhsT(f32(inputs["L1_attn_wq"]).T)
    iw["w1ka"] = _blk_lhsT(f32(inputs["L1_attn_wk_a"]).T, mb=256)
    iw["w1va"] = _blk_lhsT(f32(inputs["L1_attn_wv_a"]).T, mb=256)
    iw["w1kb"] = _blk_lhsT(f32(inputs["L1_attn_wk_b"]).T)
    iw["w1vb"] = _blk_lhsT(f32(inputs["L1_attn_wv_b"]).T)
    iw["w1o"] = np.ascontiguousarray(f32(inputs["L1_attn_wo"]).T)
    iw["wrB"] = _blk_lhsT(f32(inputs["L1_ffn_w_freq_real"]))
    iw["wiB"] = _blk_lhsT(f32(inputs["L1_ffn_w_freq_imag"]))
    iw["bias1"] = _flat_one(f32(inputs["L1_ffn_bias"]).reshape(D, 1))
    return iw


_SHAPES = {
    "w0q": [8, 128, 1024], "w0dkv": [128, 2048], "w0uk": [8, 128, 256],
    "w0uv": [8, 128, 256], "w0o": [1024, 1024], "g0": [128, 2], "b0": [128, 2],
    "wfQ": [8, 128, 1024], "wfK": [8, 128, 1024], "wfO": [8, 128, 1024],
    "wfI": [8, 128, 1024], "wfr1": [8, 128, 1024], "wfr2": [1, 128, 64],
    "vfw": [4, 128, 1024], "wfw": [4, 128, 1024], "ufw": [8, 128, 512],
    "vin": [4, 128, 1024], "win": [4, 128, 1024], "uin": [8, 128, 512],
    "bfw": [8, 1024], "bin": [8, 1024],
    "rb1": [128, 8], "rb2": [8, 1], "eb": [8, 1], "alpha": [1, 1],
    "w1q": [8, 128, 1024], "w1ka": [1, 128, 2048], "w1va": [1, 128, 2048],
    "w1kb": [8, 128, 256], "w1vb": [8, 128, 256], "w1o": [1024, 1024],
    "wrB": [8, 128, 1024], "wiB": [8, 128, 1024], "bias1": [128, 8],
}

ENGS = ("sync", "tensor", "vector", "scalar", "gpsimd")


class Sem:
    def __init__(self, h):
        self.h = h
        self.n = 0

    def inc(self, k=1):
        self.n += k
        return self.n


class B:
    def __init__(self, T, nblocks):
        self.T = T
        self.TT = min(512, T)
        self.NTT = T // self.TT
        self.NKT = T // 128
        self.NQT = T // self.TT
        self.nqsub = self.TT // 128
        self.nblocks = nblocks
        nc = bass.Bass("TRN2", target_bir_lowering=False,
                       detect_race_conditions=DETECT)
        self.nc = nc
        self.ops = {e: [] for e in ENGS}

        dr = lambda n, s, k: nc.dram_tensor(n, s, F32, kind=k)
        self.xin = dr("x", [NSEQ, D, T], "ExternalInput")
        self.yout = dr("y", [NSEQ, D, T], "ExternalOutput")
        self.d = {n: dr(n, s, "ExternalInput") for n, s in _SHAPES.items()}
        for n, s in (("ropeC", [128, T]), ("ropeS", [128, T]),
                     ("pm0T", [128, 128]), ("mask", [128, 128]),
                     ("ones", [128, 128]), ("id", [128, 128]),
                     ("ec", [8, 512]), ("nvinv", [128, T]),
                     ("dftC", [1024, 1024]), ("dftS", [1024, 1024])):
            self.d[n] = dr(n, s, "ExternalInput")
        self.mcache = dr("mcache", [128, 8192], "Internal")

        T8 = 8 * T
        self.OFF = {}
        cur = 0
        for name, n in (("XSB", T8), ("H1", max(T8, 32 * self.TT, 16384)),
                        ("R2", 14368),
                        ("WB", 2560), ("CB", 896), ("ST", 1024), ("SP", 1024)):
            self.OFF[name] = cur
            cur += n
        self.NCOLS = cur
        assert cur <= 53200, cur
        self.heap = nc.alloc_sbuf_tensor("heap", [128, self.NCOLS], F32)
        ha = nc.lookup_mloc(self.heap).addr
        TTl = self.TT
        self._bfregs = []
        for i, (st, ln) in enumerate((
                (self.OFF["R2"], 8 * TTl),
                (self.OFF["R2"] + 20 * TTl, 4 * TTl),
                (self.OFF["H1"], 8 * TTl),
                (self.OFF["H1"] + 8 * TTl, 8 * TTl),
                (self.OFF["H1"] + 16 * TTl, 8 * TTl),
                (self.OFF["H1"] + 24 * TTl, 8 * TTl),
                (self.OFF["WB"], 2048),
                (self.OFF["R2"] + 16 * TTl, 4 * TTl),
                (self.OFF["R2"] + 8 * TTl, 4 * TTl),
                (self.OFF["R2"] + 12 * TTl, 4 * TTl),
        )):
            t = nc.alloc_sbuf_tensor_at(f"ab{i}", [128, 2 * ln], BF16,
                                        offset=ha + 4 * st)
            self._bfregs.append((st, ln, t))
        self.pb = [nc.alloc_psum_tensor(f"pb{i}", [128, 512], F32) for i in range(8)]
        self.sem = {n: Sem(nc.alloc_semaphore(n)) for n in
                    ("TEN", "VEC", "SCA", "GPS", "BAR")}
        for i in range(8):
            self.sem[f"LD{i}"] = Sem(nc.alloc_semaphore(f"LD{i}"))
        self._dmai = 0
        self._ld_last = {}
        CB = self.OFF["CB"]
        self.C_ONES = CB + 0
        self.C_ID = CB + 128
        self.C_OVL = CB + 256
        self.C_MISC = CB + 768
        M = self.C_MISC
        self.M_RB2EB = M + 0
        self.M_ALPHA = M + 1
        self.M_G0 = M + 2
        self.M_B0 = M + 4
        self.M_BIAS1 = M + 6
        self.M_CARRY = M + 16
        self.M_NEGB = M + 24
        self.M_MQ = M + 25
        self.M_MK = M + 26
        self.M_SC1 = M + 27
        self.M_SC2 = M + 28
        self.M_RECIP = M + 29
        self.M_ALPHA1 = M + 33
        self.M_EPS = M + 34
        self.M_LN18 = M + 35
        self.bank_free = {i: None for i in range(8)}
        self._stage_busy = {}
        self._pend = 0
        self._pcell = None
        self.phase = "init"
        self.mmlab = []

    def v(self, col, n, p0=0, p1=128):
        return self.heap[p0:p1, col:col + n]

    def hap(self, col, dims, np_=128):
        return bass.AP(self.heap, col, [[self.NCOLS, np_]] + [list(d) for d in dims])

    def pap(self, pbt, col, dims, np_=128):
        return bass.AP(pbt, col, [[512, np_]] + [list(d) for d in dims])

    def _bfreg(self, col, n):
        for st, ln, t in self._bfregs:
            if st <= col and col + n <= st + ln:
                return st, t
        raise KeyError((col, n))

    def vb(self, col, n, p0=0, p1=128, lo=False):
        st, t = self._bfreg(col, n)
        base = 2 * (col - st) + (n if lo else 0)
        return t[p0:p1, base:base + n]

    def vbx(self, col, n, soff, slen, p0=0, p1=128, lo=False):
        st, t = self._bfreg(col, n)
        base = 2 * (col - st) + (n if lo else 0) + soff
        return t[p0:p1, base:base + slen]

    def split(self, col, n, src_ap, waits=(), p0=0, p1=128):
        h_ = self.vb(col, n, p0, p1)
        l_ = self.vb(col, n, p0, p1, lo=True)
        self.op("vector", "tensor_copy", h_, src_ap, waits=waits)
        return self.op("vector", "tensor_tensor", l_, src_ap, h_, ALU.subtract)

    def load_pair(self, pair_col, ncols, dram_ap, stage_col, after=None):
        if after is not None:
            self.wait("sync", after[0], after[1])
        sb = self._stage_busy.get(stage_col)
        if sb:
            self.wait("sync", sb[0], sb[1])
        lv = self.dma(self.v(stage_col, ncols), dram_ap)
        sv = self.split(pair_col, ncols, self.v(stage_col, ncols), waits=[lv])
        self._stage_busy[stage_col] = ("VEC", sv)
        return sv

    def mm3(self, psum_ap, wcol, nwe, wc, rcol, n_r, start, stop, waits=(),
            wspan=128, p0=0, p1=128):
        wst, wt = self._bfreg(wcol, nwe)
        b0 = 2 * (wcol - wst)
        lh = wt[p0:p1, b0 + wc * wspan:b0 + (wc + 1) * wspan]
        ll = wt[p0:p1, b0 + nwe + wc * wspan:b0 + nwe + (wc + 1) * wspan]
        rh = self.vb(rcol, n_r, p0, p1)
        rl = self.vb(rcol, n_r, p0, p1, lo=True)
        self.mm(psum_ap, lh, rh, start, False, waits=waits)
        self.mm(psum_ap, lh, rl, False, False)
        return self.mm(psum_ap, ll, rh, False, stop)

    def on(self, eng, fn):
        self.ops[eng].append(fn)

    def ten_flush(self):
        if self._pend:
            self._pcell[0] += self._pend
            self._pend = 0

    def tmark(self):
        self.ten_flush()
        return self.sem["TEN"].n

    def wait(self, eng, sem, val):
        if not val:
            return
        if eng == "tensor":
            self.ten_flush()
        self.on(eng, lambda e, s=self.sem[sem].h, v=val: e.wait_ge(s, v))

    def ready(self):
        return [("VEC", self.sem["VEC"].n), ("SCA", self.sem["SCA"].n),
                ("GPS", self.sem["GPS"].n)]

    def barrier(self):
        self.ten_flush()
        bar = self.sem["BAR"]
        v = bar.inc(5)
        for eng in ENGS:
            self.on(eng, lambda e, s=bar.h, v=v:
                    (e.nop().then_inc(s, 1), e.wait_ge(s, v)))
        for i in range(8):
            self.bank_free[i] = None

    def dma(self, dst_ap, src_ap, after=None):
        name = f"LD{self._dmai % 8}"
        self._dmai += 1
        LD = self.sem[name]
        if after is not None:
            self.wait("sync", after[0], after[1])
        if name in self._ld_last:
            self.wait("sync", name, self._ld_last[name])
        val = LD.inc(16)
        self._ld_last[name] = val
        self.on("sync", lambda e, d=dst_ap, s=src_ap, h=LD.h:
                e.dma_start(out=d, in_=s).then_inc(h, 16))
        if SYNC_SER:
            self.wait("sync", name, val)
        return (name, val)

    def t_bank(self, bank):
        bf = self.bank_free[bank]
        if bf is not None:
            self.wait("tensor", bf[0], bf[1])

    def mm(self, psum_ap, lhsT, rhs, start, stop, waits=(), tile_position=None,
           inc=False):
        for (s, v) in waits:
            self.wait("tensor", s, v)
        TEN = self.sem["TEN"]
        val = TEN.inc()
        try:
            shp = (lhsT.ap[0][1], lhsT.free_size(), rhs.free_size(),
                   "f32" if lhsT.tensor.dtype == F32 else "bf16")
        except (AttributeError, IndexError):
            shp = None
        self.mmlab.append((self.phase, shp))
        cell = [0]
        self._pend += 1
        self._pcell = cell
        if inc:
            self.ten_flush()
        def fn(e, o=psum_ap, l=lhsT, r=rhs, st=start, sp=stop, h=TEN.h,
               c=cell, tpos=tile_position):
            i = e.matmul(o, l, r, start=st, stop=sp, tile_position=tpos)
            if c[0]:
                i.then_inc(h, c[0])
        self.on("tensor", fn)
        return val

    def tp(self, out_ap, in_ap, waits=(), kdim=128, tile_position=None):
        for (s, v) in waits:
            self.wait("tensor", s, v)
        self.ten_flush()
        TEN = self.sem["TEN"]
        val = TEN.inc()
        self.mmlab.append((self.phase + "/tp", "tp"))
        def fn(e, o=out_ap, i_=in_ap,
               idn=self.heap[0:kdim, self.C_ID:self.C_ID + kdim], h=TEN.h,
               tpos=tile_position):
            e.transpose(o, i_, idn).then_inc(h, 1)
        self.on("tensor", fn)
        return val

    def op(self, eng, meth, *args, waits=(), **kw):
        S = {"vector": "VEC", "scalar": "SCA", "gpsimd": "GPS"}[eng]
        for (s, v) in waits:
            self.wait(eng, s, v)
        sem = self.sem[S]
        prev = sem.n
        val = sem.inc()
        def fn(e, m=meth, a=args, k=kw, h=sem.h, p=prev):
            i = getattr(e, m)(*a, **k).then_inc(h, 1)
            if p:
                i._wait_ge(h, p)
        self.on(eng, fn)
        return val

    def ln_tile(self, src_off, dst_off, tt, inv_d, dst_tile, sq_off):
        self.phase = self.phase.split("|")[0] + "|ln"
        """LayerNorm of one t-tile: dst = (src - mean)/sqrt(var+eps).
        src chunk c at src_off + c*T + tt*TT; dst chunk c at
        dst_off + c*(TT if dst_tile else T) (+ tt*TT if not dst_tile).
        sq_off: 8*TT scratch cols for squares. Returns GPS val of completion."""
        T, TT = self.T, self.TT
        ST = self.OFF["ST"]
        t0 = tt * TT
        meanB = self.v(ST, TT)
        rsd = self.v(ST + 512, TT)
        src3 = self.hap(src_off + t0, [[T, 8], [1, TT]])
        dst3 = self.hap(sq_off, [[TT, 8], [1, TT]])
        sq = self.op("scalar", "activation", dst3, src3, AF.Square,
                     waits=self.ready())
        for b_, off, step, w in ((0, src_off + t0, T, None), (1, sq_off, TT, sq)):
            self.t_bank(b_)
            for c in range(8):
                self.mm(self.pb[b_][:, 0:TT], self.v(self.C_ONES, 128),
                        self.v(off + c * step, TT), c == 0, c == 7,
                        waits=([("SCA", w)] if (w and c == 0) else ()))
        vsum = self.tmark()
        self.op("scalar", "activation", meanB, self.pb[0][:, 0:TT], AF.Copy,
                scale=inv_d, waits=[("TEN", vsum)])
        e2 = self.op("scalar", "activation", rsd, self.pb[1][:, 0:TT], AF.Copy,
                     scale=inv_d)
        m2 = self.op("vector", "tensor_mul", self.pb[0][:, 0:TT], meanB, meanB,
                     waits=[("SCA", e2)])
        v2 = self.op("vector", "tensor_sub", rsd, rsd, self.pb[0][:, 0:TT],
                     waits=[("VEC", m2)])
        self.op("scalar", "activation", rsd, rsd, AF.Ln, bias=self.v(self.M_EPS, 1),
                waits=[("VEC", v2)])
        r2 = self.op("scalar", "activation", rsd, rsd, AF.Exp, scale=-0.5)
        self.bank_free[0] = ("VEC", v2)
        self.bank_free[1] = ("SCA", e2)
        gv = None
        for c in range(8):
            xc = self.v(src_off + c * T + t0, TT)
            dc = self.v(dst_off + c * (TT if dst_tile else T) +
                        (0 if dst_tile else t0), TT)
            sv = self.op("vector", "tensor_sub", dc, xc, meanB,
                         waits=([("SCA", r2)] if c == 0 else ()))
            gv = self.op("gpsimd", "tensor_mul", dc, dc, rsd,
                         waits=[("VEC", sv)])
        return gv

    def preln(self, dst_off, sq_off):
        gv = None
        for tt in range(self.NTT):
            gv = self.ln_tile(self.OFF["XSB"], dst_off, tt, 1.0 / D, False, sq_off)
        self.barrier()
        return gv

    def bound_tile(self, src_off, acc_col, first, wv):
        TT = self.TT
        sq = self.v(self.OFF["ST"], TT)
        sv = self.op("scalar", "activation", sq, self.v(src_off, TT), AF.Square,
                     waits=[("VEC", wv)] if wv else ())
        self._last_sq = sv
        self.t_bank(5)
        mv = self.mm(self.pb[5][0:1, 0:TT], self.v(self.C_ONES, 1), sq,
                     True, True, waits=[("SCA", sv)], inc=True)
        if first:
            rv = self.op("vector", "reduce_max", self.v(acc_col, 1, 0, 1),
                         self.pb[5][0:1, 0:TT], AX.X, waits=[("TEN", mv)])
        else:
            self.op("vector", "reduce_max", self.v(self.M_SC1, 1, 0, 1),
                    self.pb[5][0:1, 0:TT], AX.X, waits=[("TEN", mv)])
            rv = self.op("vector", "tensor_max", self.v(acc_col, 1, 0, 1),
                         self.v(acc_col, 1, 0, 1), self.v(self.M_SC1, 1, 0, 1))
        self.bank_free[5] = ("VEC", rv)
        return rv

    def finish_bound(self):
        mq = self.v(self.M_MQ, 1, 0, 1)
        mk = self.v(self.M_MK, 1, 0, 1)
        self.op("scalar", "activation", mq, mq, AF.Ln, waits=self.ready())
        l2 = self.op("scalar", "activation", mk, mk, AF.Ln)
        a1 = self.op("vector", "tensor_add", mq, mq, mk, waits=[("SCA", l2)])
        e1 = self.op("scalar", "activation", mq, mq, AF.Exp, scale=0.5,
                     bias=self.v(self.M_LN18, 1, 0, 1), waits=[("VEC", a1)])
        n1 = self.op("vector", "tensor_scalar", mq, mq, -1.0, 25.0,
                     ALU.mult, ALU.add, waits=[("SCA", e1)])
        self.t_bank(5)
        mv = self.mm(self.pb[5][:, 0:1],
                     self.heap[0:1, self.C_ONES:self.C_ONES + 128], mq,
                     True, True, waits=[("VEC", n1)], inc=True)
        cv = self.op("vector", "tensor_copy", self.v(self.M_NEGB, 1),
                     self.pb[5][:, 0:1], waits=[("TEN", mv)])
        self.bank_free[5] = ("VEC", cv)
        return cv

    def attn_group(self, QG, KG, VG, causal, bound_wait, wo_cols, wo_ld,
                   s_pair=False):
        base = self.phase.split("|")[0]
        """attention for 2 heads + per-q-tile W_o partial accumulated into x.
        QG/KG feature-major [128 x T]; VG token-major aug [128 x 129*NKT]
        per k-tile laid out [V_h0(64)|ones|V_h1(64)].  PV is feature-major:
        O_aug^T [65 x TT] = V_augT.T @ P^T accumulated over k-tiles; the aug
        row gives the softmax denominator (row 64 for head 0, row 0 for head
        1).  Normalization multiplies by exp(-ln(denom)) broadcast across 64
        partitions via a K=1 matmul.  W_o is applied as two K=64 matmuls
        against AT[64 x 2*TT]; wo_cols = WB cols of the two 64-row halves."""
        T, TT, NKT, NQT, nqsub = self.T, self.TT, self.NKT, self.NQT, self.nqsub
        SP, ST, XS = self.OFF["SP"], self.OFF["ST"], self.OFF["XSB"]
        negb = self.v(self.M_NEGB, 1)
        AT = ST
        last_at = 0
        if causal:
            sbank, pslot = (0, 1), (SP, SP + 512)
        else:
            sbank, pslot = (0, 1, 3), (SP, SP + 512, self.C_OVL)
        nbank = len(sbank)
        for qt in range(NQT):
            at_done = 0
            for hl in range(2):
                kts = list(range(min(NKT, nqsub * qt + nqsub))) if causal \
                    else list(range(NKT))
                drow = 64
                vrow0 = 0
                last_pv = None
                prev_pv = {}
                rd = self.ready()
                self.t_bank(2)
                def emit_S(i, hl=hl, qt=qt, kts=kts, rd=rd):
                    kt = kts[i]
                    bank = sbank[i % nbank]
                    qoffb = max(0, kt - nqsub * qt) if causal else 0
                    self.t_bank(bank)
                    self.phase = base + "|S"
                    w0 = rd if i == 0 and hl == 0 else ()
                    if not s_pair:
                        return self.mm(
                            self.pb[bank][:, qoffb * 128:TT],
                            self.heap[hl * 64:hl * 64 + 64,
                                      KG + kt * 128:KG + kt * 128 + 128],
                            self.heap[hl * 64:hl * 64 + 64,
                                      QG + qt * TT + qoffb * 128:
                                      QG + (qt + 1) * TT],
                            True, True, waits=w0, inc=True)
                    p0, p1 = hl * 64, hl * 64 + 64
                    t0k = (kt * 128 // TT) * TT
                    sk = kt * 128 - t0k
                    sq_ = qoffb * 128
                    nq_ = TT - sq_
                    kh = self.vbx(KG + t0k, TT, sk, 128, p0, p1)
                    kl = self.vbx(KG + t0k, TT, sk, 128, p0, p1, lo=True)
                    qh = self.vbx(QG + qt * TT, TT, sq_, nq_, p0, p1)
                    ql = self.vbx(QG + qt * TT, TT, sq_, nq_, p0, p1, lo=True)
                    ps = self.pb[bank][:, sq_:TT]
                    self.mm(ps, kh, qh, True, False, waits=w0)
                    self.mm(ps, kh, ql, False, False)
                    return self.mm(ps, kl, qh, False, True, inc=True)
                svs = {}
                for i in range(min(nbank, len(kts))):
                    svs[i] = emit_S(i)
                for i, kt in enumerate(kts):
                    par = i % nbank
                    bank = sbank[par]
                    qoffb = max(0, kt - nqsub * qt) if causal else 0
                    n_q = TT - qoffb * 128
                    sv = svs[i]
                    mwait = [("TEN", sv)]
                    jd = kt - nqsub * qt
                    if causal and 0 <= jd < nqsub:
                        mv = self.op("vector", "tensor_add",
                                     self.pb[bank][:, jd * 128:jd * 128 + 128],
                                     self.pb[bank][:, jd * 128:jd * 128 + 128],
                                     self.v(self.C_OVL + 128, 128),
                                     waits=[("TEN", sv)])
                        mwait = [("VEC", mv)]
                    ew = list(mwait)
                    if i == 0:
                        ew.append(("VEC", bound_wait))
                    if i <= 1 and last_at:
                        ew.append(("VEC", last_at))
                    if par in prev_pv:
                        ew.append(("TEN", prev_pv[par]))
                    pexp = self.op("scalar", "activation",
                                   self.v(pslot[par], n_q),
                                   self.pb[bank][:, qoffb * 128:TT], AF.Exp,
                                   bias=negb, scale=0.125, waits=ew)
                    self.bank_free[bank] = ("SCA", pexp)
                    self.phase = base + "|PV"
                    last_pv = self.mm(
                        self.pb[2][0:65, qoffb * 128:TT],
                        self.hap(VG + kt * 130 + 65 * hl, [[1, 65]]),
                        self.v(pslot[par], n_q),
                        i == 0, i == len(kts) - 1,
                        waits=[("SCA", pexp)], inc=True)
                    prev_pv[par] = last_pv
                    if i + nbank < len(kts):
                        svs[i + nbank] = emit_S(i + nbank)
                u1 = self.op("scalar", "activation", self.v(SP, TT, 0, 1),
                             self.pb[2][drow:drow + 1, 0:TT], AF.Ln,
                             waits=[("TEN", last_pv)])
                u2 = self.op("scalar", "activation", self.v(SP, TT, 0, 1),
                             self.v(SP, TT, 0, 1), AF.Exp, scale=-1.0)
                self.t_bank(3)
                self.phase = base + "|norm"
                bc = self.mm(self.pb[3][0:64, 0:TT],
                             self.heap[0:1, self.C_ONES:self.C_ONES + 64],
                             self.v(SP, TT, 0, 1), True, True,
                             waits=[("SCA", u2)], inc=True)
                cpv = self.op("vector", "tensor_copy", self.v(SP + 512, TT, 0, 64),
                              self.pb[3][0:64, 0:TT], waits=[("TEN", bc)])
                at_done = self.op("vector", "scalar_tensor_tensor",
                                  self.heap[0:64, AT + hl * TT:AT + (hl + 1) * TT],
                                  self.pb[2][vrow0:vrow0 + 64, 0:TT], 1.0,
                                  self.v(SP + 512, TT, 0, 64),
                                  ALU.mult, ALU.mult, waits=[("VEC", cpv)])
                self.bank_free[2] = ("VEC", at_done)
                self.bank_free[3] = ("VEC", cpv)
                last_at = at_done
                self.attn_guard = [("TEN", last_pv), ("SCA", u2)]
            self.phase = base + "|WO"
            for oc in range(8):
                bank = 4 + (oc % 4)
                self.t_bank(bank)
                for hl in range(2):
                    w0 = []
                    if oc == 0 and hl == 0:
                        w0 = [("VEC", at_done)] + list(wo_ld)
                    mv = self.mm(self.pb[bank][:, 0:TT],
                                 self.heap[0:64, wo_cols[hl] + oc * 128:
                                           wo_cols[hl] + oc * 128 + 128],
                                 self.heap[0:64, AT + hl * TT:AT + (hl + 1) * TT],
                                 hl == 0, hl == 1, waits=w0, inc=(hl == 1))
                av = self.op("vector", "tensor_add",
                             self.v(XS + oc * T + qt * TT, TT),
                             self.v(XS + oc * T + qt * TT, TT),
                             self.pb[bank][:, 0:TT], waits=[("TEN", mv)])
                self.bank_free[bank] = ("VEC", av)

    def rope(self, off, lc1, lcs, src_ready):
        TT = self.TT
        SP, ST = self.OFF["SP"], self.OFF["ST"]
        self.t_bank(4)
        mv = self.mm(self.pb[4][:, 0:TT], self.v(self.C_OVL, 128),
                     self.v(off, TT), True, True, waits=[("VEC", src_ready)],
                     inc=True)
        tmp = self.v(ST, TT)
        v1 = self.op("vector", "tensor_mul", tmp, self.pb[4][:, 0:TT],
                     self.v(SP + 512, TT), waits=[("TEN", mv), lcs])
        self.bank_free[4] = ("VEC", v1)
        v2 = self.op("gpsimd", "tensor_mul", self.v(off, TT), self.v(off, TT),
                     self.v(SP, TT),
                     waits=[lc1, ("VEC", src_ready), ("TEN", mv)])
        v3 = self.op("vector", "tensor_add", self.v(off, TT), self.v(off, TT),
                     tmp, waits=[("GPS", v2)])
        return v3

    def emit_l0a(self, s):
        self.phase = "l0a.front"
        T, TT, NTT, NKT = self.T, self.TT, self.NTT, self.NKT
        XS, H1, R2, WB, SP = (self.OFF[k] for k in
                              ("XSB", "H1", "R2", "WB", "SP"))
        for c in range(8):
            lv = self.dma(self.v(XS + c * T, T),
                          self.xin.ap()[s, c * 128:(c + 1) * 128, :])
        self.dma(self.v(self.C_OVL, 128), self.d["pm0T"].ap())
        lov = self.dma(self.v(self.C_OVL + 128, 128), self.d["mask"].ap())
        wall = [(f"LD{i}", self.sem[f"LD{i}"].n) for i in range(8)]
        for eng in ("scalar", "vector", "tensor", "gpsimd"):
            for w in wall:
                self.wait(eng, *w)
        self.barrier()
        self.preln(H1, R2)
        LAT = R2
        SQ2 = R2 + 2 * T
        ldkv = self.dma(self.v(WB, 2048), self.d["w0dkv"].ap())
        for tt in range(NTT):
            t0 = tt * TT
            rd = self.ready()
            for lc in range(2):
                self.t_bank(lc)
                for c in range(8):
                    self.mm(self.pb[lc][:, 0:TT],
                            self.v(WB + c * 256 + lc * 128, 128),
                            self.v(H1 + c * T + t0, TT), c == 0, c == 7,
                            waits=(rd + [ldkv]) if (c == 0 and lc == 0) else ())
                ev = self.op("vector", "tensor_copy", self.v(LAT + lc * T + t0, TT),
                             self.pb[lc][:, 0:TT], waits=[("TEN", self.tmark())])
                self.bank_free[lc] = ("VEC", ev)
            lat_ev = ev
            lsq = self.op("scalar", "activation",
                          self.hap(SQ2, [[TT, 2], [1, TT]]),
                          self.hap(LAT + t0, [[T, 2], [1, TT]]),
                          AF.Square, waits=[("VEC", self.sem["VEC"].n)])
            for b_, off, step, w in ((2, LAT + t0, T, None), (3, SQ2, TT, lsq)):
                self.t_bank(b_)
                for lc in range(2):
                    self.mm(self.pb[b_][:, 0:TT], self.v(self.C_ONES, 128),
                            self.v(off + lc * step, TT), lc == 0, lc == 1,
                            waits=([("SCA", w)] if (w and lc == 0) else
                                   ([("VEC", lat_ev)] if lc == 0 else ())))
            vsum = self.tmark()
            ST = self.OFF["ST"]
            meanB = self.v(ST, TT)
            rsd = self.v(ST + 512, TT)
            self.op("scalar", "activation", meanB, self.pb[2][:, 0:TT], AF.Copy,
                    scale=1.0 / LATD, waits=[("TEN", vsum)])
            e2 = self.op("scalar", "activation", rsd, self.pb[3][:, 0:TT],
                         AF.Copy, scale=1.0 / LATD)
            m2 = self.op("vector", "tensor_mul", self.pb[2][:, 0:TT], meanB, meanB,
                         waits=[("SCA", e2)])
            v2 = self.op("vector", "tensor_sub", rsd, rsd, self.pb[2][:, 0:TT],
                         waits=[("VEC", m2)])
            self.op("scalar", "activation", rsd, rsd, AF.Ln, bias=self.v(self.M_EPS, 1),
                    waits=[("VEC", v2)])
            r2 = self.op("scalar", "activation", rsd, rsd, AF.Exp, scale=-0.5)
            self.bank_free[2] = ("VEC", v2)
            self.bank_free[3] = ("SCA", e2)
            for lc in range(2):
                la = self.v(LAT + lc * T + t0, TT)
                self.op("vector", "tensor_sub", la, la, meanB,
                        waits=([("SCA", r2)] if lc == 0 else ()))
                mv = self.op("vector", "tensor_mul", la, la, rsd)
                self.op("scalar", "activation", la, la, AF.Identity,
                        scale=self.v(self.M_G0 + lc, 1),
                        bias=self.v(self.M_B0 + lc, 1), waits=[("VEC", mv)])
        self.barrier()
        QG = R2 + 2 * T
        KG = QG + T
        VG = KG + T
        for g in range(8):
            self._l0a_group(g, LAT, QG, KG, VG)
        self.barrier()

    def _l0a_group(self, g, LAT, QG, KG, VG):
        self.phase = "l0a.proj"
        T, TT, NTT, NKT = self.T, self.TT, self.NTT, self.NKT
        H1, WB, SP = self.OFF["H1"], self.OFF["WB"], self.OFF["SP"]
        lq = self.dma(self.v(WB, 1024), self.d["w0q"].ap()[g])
        luk = self.dma(self.v(WB + 1024, 256), self.d["w0uk"].ap()[g])
        luv = self.dma(self.v(WB + 1280, 256), self.d["w0uv"].ap()[g])
        lwo = self.dma(self.heap[0:64, WB + 1536:WB + 2560],
                       self.d["w0o"].ap()[g * 128:g * 128 + 64, :])
        rd = self.ready()
        SCRQ = self.OFF["R2"] + 4 * T + 2080
        SCRK = SCRQ + 512
        for tt in range(NTT):
            t0 = tt * TT
            qb = 6 + (tt % 2)
            kb = 7 - (tt % 2)
            lc1 = self.dma(self.v(SP, TT), self.d["ropeC"].ap()[:, t0:t0 + TT],
                           after=("VEC", self.rope_done) if getattr(self, "rope_done", 0) else None)
            lcs = self.dma(self.v(SP + 512, TT), self.d["ropeS"].ap()[:, t0:t0 + TT])
            self.t_bank(qb)
            for c in range(8):
                self.mm(self.pb[qb][:, 0:TT], self.v(WB + c * 128, 128),
                        self.v(H1 + c * T + t0, TT), c == 0, c == 7,
                        waits=(rd + [lq]) if (c == 0 and tt == 0) else ())
            qw = [("TEN", self.tmark())]
            if tt > 0:
                qw.append(("SCA", self._last_sq))
            qv = self.op("vector", "tensor_copy", self.v(SCRQ, TT),
                         self.pb[qb][:, 0:TT], waits=qw)
            self.bank_free[qb] = ("VEC", qv)
            rq = self.rope(SCRQ, lc1, lcs, qv)
            self.bound_tile(SCRQ, self.M_MQ, tt == 0, rq)
            self.split(QG + t0, TT, self.v(SCRQ, TT))
            self.t_bank(kb)
            for lc in range(2):
                self.mm(self.pb[kb][:, 0:TT], self.v(WB + 1024 + lc * 128, 128),
                        self.v(LAT + lc * T + t0, TT), lc == 0, lc == 1,
                        waits=[luk] if lc == 0 else ())
            kw = [("TEN", self.tmark()), ("SCA", self._last_sq)]
            kv = self.op("vector", "tensor_copy", self.v(SCRK, TT),
                         self.pb[kb][:, 0:TT], waits=kw)
            self.bank_free[kb] = ("VEC", kv)
            rk = self.rope(SCRK, lc1, lcs, kv)
            self.rope_done = self.bound_tile(SCRK, self.M_MK, tt == 0, rk)
            self.split(KG + t0, TT, self.v(SCRK, TT))
        lwo2 = self.dma(self.heap[0:64, WB:WB + 1024],
                        self.d["w0o"].ap()[g * 128 + 64:g * 128 + 128, :],
                        after=("TEN", self.tmark()))
        bv = self.finish_bound()
        mz = self.op("vector", "memset",
                     self.hap(VG + 64, [[130, NKT], [65, 2]]), 1.0)
        for kt in range(NKT):
            vb_ = 6 + (kt % 2)
            self.t_bank(vb_)
            for hl in range(2):
                for lc in range(2):
                    self.mm(self.pb[vb_][:, hl * 64:hl * 64 + 64],
                            self.v(LAT + lc * T + kt * 128, 128),
                            self.v(WB + 1280 + lc * 128 + hl * 64, 64),
                            lc == 0, lc == 1,
                            waits=[luv] if (kt == 0 and hl == 0 and lc == 0) else ())
            ev = self.op("vector", "tensor_copy",
                         self.hap(VG + kt * 130, [[65, 2], [1, 64]]),
                         self.pb[vb_][:, 0:128],
                         waits=[("TEN", self.tmark()), ("VEC", mz)])
            self.bank_free[vb_] = ("VEC", ev)
        self.attn_group(QG, KG, VG, True, bv, (WB + 1536, WB), (lwo, lwo2),
                        s_pair=True)
        self.barrier()
        self.rope_done = 0

    def emit_l0f(self, s):
        lec = self.dma(self.v(self.C_OVL, 512, 0, 8), self.d["ec"].ap())
        self.wait("tensor", *lec)
        for tt in range(self.NTT):
            self._l0f_tile(tt)
        self.barrier()

    def ln_tile_pair(self, src_off, dst_pair, tt, scr_off):
        self.phase = "l0f.ln"
        """LayerNorm of one t-tile of chunked fp32 src into a bf16 PAIR at
        dst_pair (+ c*TT per chunk).  scr_off: 8*TT fp32 scratch."""
        T, TT = self.T, self.TT
        ST = self.OFF["ST"]
        t0 = tt * TT
        meanB = self.v(ST, TT)
        rsd = self.v(ST + 512, TT)
        src3 = self.hap(src_off + t0, [[T, 8], [1, TT]])
        dst3 = self.hap(scr_off, [[TT, 8], [1, TT]])
        sq = self.op("scalar", "activation", dst3, src3, AF.Square,
                     waits=self.ready())
        for b_, off, step, w in ((0, src_off + t0, T, None),
                                 (1, scr_off, TT, sq)):
            self.t_bank(b_)
            for c in range(8):
                self.mm(self.pb[b_][:, 0:TT], self.v(self.C_ONES, 128),
                        self.v(off + c * step, TT), c == 0, c == 7,
                        waits=([("SCA", w)] if (w and c == 0) else ()))
        vsum = self.tmark()
        self.op("scalar", "activation", meanB, self.pb[0][:, 0:TT], AF.Copy,
                scale=1.0 / D, waits=[("TEN", vsum)])
        e2 = self.op("scalar", "activation", rsd, self.pb[1][:, 0:TT], AF.Copy,
                     scale=1.0 / D)
        m2 = self.op("vector", "tensor_mul", self.pb[0][:, 0:TT], meanB, meanB,
                     waits=[("SCA", e2)])
        v2 = self.op("vector", "tensor_sub", rsd, rsd, self.pb[0][:, 0:TT],
                     waits=[("VEC", m2)])
        self.op("scalar", "activation", rsd, rsd, AF.Ln,
                bias=self.v(self.M_EPS, 1), waits=[("VEC", v2)])
        r2 = self.op("scalar", "activation", rsd, rsd, AF.Exp, scale=-0.5)
        self.bank_free[0] = ("VEC", v2)
        self.bank_free[1] = ("SCA", e2)
        gv = None
        for c in range(8):
            xc = self.v(src_off + c * T + t0, TT)
            uc = self.v(scr_off + c * TT, TT)
            sv = self.op("vector", "tensor_sub", uc, xc, meanB,
                         waits=([("SCA", r2)] if c == 0 else ()))
            gm = self.op("gpsimd", "tensor_mul", uc, uc, rsd,
                         waits=[("VEC", sv)])
            gv = self.split(dst_pair + c * TT, TT, uc, waits=[("GPS", gm)])
        return gv

    def _mmblk(self, dst_off, rhs_pair, wname, nblk, nk, blkelems, stage,
               extra=(), evac="pair", evac_args=None):
        TT = self.TT
        WB = self.OFF["WB"]
        self.phase = self.phase.split("|")[0] + "|mb." + wname
        evs = []
        for ob in range(nblk):
            sl = ob % 2
            after = ("TEN", self.wlast[sl]) if self.wlast[sl] else None
            wsv = self.load_pair(WB + sl * 1024, blkelems,
                                 self.d[wname].ap()[ob], stage, after=after)
            bank = 4 + sl
            self.t_bank(bank)
            w0 = (list(extra) if ob == 0 else []) + [("VEC", wsv)]
            for c in range(nk):
                self.mm3(self.pb[bank][:, 0:TT], WB + sl * 1024, blkelems, c,
                         rhs_pair + c * TT, TT, c == 0, c == nk - 1,
                         waits=w0 if c == 0 else ())
            self.wlast[sl] = self.tmark()
            mv = self.tmark()
            if evac == "pair":
                ev = self.split(dst_off + ob * TT, TT, self.pb[bank][:, 0:TT],
                                waits=[("TEN", mv)])
                self.bank_free[bank] = ("VEC", ev)
            elif evac == "copy":
                ev = self.op("vector", "tensor_copy",
                             self.v(dst_off + ob * TT, TT),
                             self.pb[bank][:, 0:TT], waits=[("TEN", mv)])
                self.bank_free[bank] = ("VEC", ev)
            elif evac == "gelu":
                ev = self.op("scalar", "activation",
                             self.v(dst_off + ob * TT, TT),
                             self.pb[bank][:, 0:TT], AF.Gelu,
                             bias=self.v(evac_args + ob, 1), waits=[("TEN", mv)])
                self.bank_free[bank] = ("SCA", ev)
            evs.append(ev)
        return evs

    def _l0f_tile(self, tt):
        T, TT = self.T, self.TT
        XS, H1, R2, WB, SP, ST = (self.OFF[k] for k in
                                  ("XSB", "H1", "R2", "WB", "SP", "ST"))
        t0 = tt * TT
        XNP = R2
        HH = R2 + 8 * TT
        XV = R2 + 16 * TT
        YW = R2 + 20 * TT
        CU = R2 + 24 * TT
        BF = R2 + 1024
        QQ, KK, IQ, IK = H1, H1 + 8 * TT, H1 + 16 * TT, H1 + 24 * TT
        GF, GI = KK, QQ
        CMB = IQ
        WF, WI = ST, ST + 512
        stageA = HH
        stageB = XNP
        self.wlast = [None, None]
        lrw2 = self.dma(self.v(WB + 2048, 64), self.d["wfr2"].ap()[0])
        lrb1 = self.dma(self.v(WB + 2112, 8), self.d["rb1"].ap())
        lnv = self.dma(self.v(SP, TT), self.d["nvinv"].ap()[:, t0:t0 + TT])
        gv = self.ln_tile_pair(XS, XNP, tt, HH)
        self.phase = "l0f"
        self._stage_busy[stageA] = ("VEC", gv)
        self._mmblk(QQ, XNP, "wfQ", 8, 8, 1024, stageA,
                    extra=[("VEC", gv), ("GPS", self.sem["GPS"].n)])
        self._mmblk(KK, XNP, "wfK", 8, 8, 1024, stageA)
        self._mmblk(IQ, QQ, "wfI", 8, 8, 1024, stageA)
        self._mmblk(IK, KK, "wfI", 8, 8, 1024, stageA)
        self._stage_busy[stageB] = ("TEN", self.tmark())
        self.phase = "l0f|rtr"
        for srcq, wdst in ((QQ, WF), (IQ, WI)):
            self._mmblk(HH, srcq, "wfr1", 8, 8, 1024, stageB,
                        evac="gelu", evac_args=WB + 2112)
            self.wait("scalar", *lrb1)
            self.t_bank(6)
            rd = self.ready()
            for c in range(8):
                self.mm(self.pb[6][0:8, 0:TT], self.v(WB + 2048 + c * 8, 8),
                        self.v(HH + c * TT, TT), c == 0, c == 7,
                        waits=(rd + [lrw2]) if c == 0 else ())
            lg = self.op("scalar", "activation", self.v(wdst, TT, 0, 8),
                         self.pb[6][0:8, 0:TT], AF.Identity,
                         bias=self.v(self.M_RB2EB, 1, 0, 8),
                         waits=[("TEN", self.tmark())])
            self.bank_free[6] = ("SCA", lg)
            wtm = self.v(SP + 512, 128)
            for j in range(TT // 128):
                self.t_bank(7)
                v1 = self.tp(self.pb[7][0:128, 0:8],
                             self.heap[0:8, wdst + j * 128:wdst + (j + 1) * 128],
                             waits=[("SCA", lg)], kdim=8)
                nm = self.op("vector", "tensor_reduce", self.v(self.M_SC1, 1),
                             self.pb[7][:, 0:8], AX.X, ALU.max, negate=True,
                             waits=[("TEN", v1)])
                e1 = self.op("scalar", "activation", wtm[:, 0:8],
                             self.pb[7][:, 0:8], AF.Exp,
                             bias=self.v(self.M_SC1, 1), waits=[("VEC", nm)])
                self.op("vector", "reduce_sum", self.v(self.M_SC1, 1),
                        wtm[:, 0:8], AX.X, waits=[("SCA", e1)])
                self.op("vector", "reciprocal", self.v(self.M_SC2, 1),
                        self.v(self.M_SC1, 1))
                m1 = self.op("vector", "tensor_scalar_mul", wtm[:, 0:8],
                             wtm[:, 0:8], self.v(self.M_SC2, 1))
                self.bank_free[7] = ("VEC", m1)
                self.t_bank(7)
                v2 = self.tp(self.pb[7][0:8, 0:128], wtm[:, 0:8],
                             waits=[("VEC", m1)])
                cv = self.op("vector", "tensor_copy",
                             self.heap[0:8, wdst + j * 128:wdst + (j + 1) * 128],
                             self.pb[7][0:8, 0:128], waits=[("TEN", v2)])
                self.bank_free[7] = ("VEC", cv)
        routers_done = self.tmark()
        lbf = self.dma(self.v(BF, 1024, 0, 8), self.d["bfw"].ap(),
                       after=("TEN", routers_done))
        lbi = self.dma(self.v(BF + 1024, 1024, 0, 8), self.d["bin"].ap())
        gps_prev = 0
        for br, (qsrc, ksrc, vn, wn, un, boff, wrow, c0) in enumerate(
                ((QQ, KK, "vfw", "wfw", "ufw", BF, WF, 0),
                 (IQ, IK, "vin", "win", "uin", BF + 1024, WI, 4))):
            self._mmblk(XV, qsrc, vn, 4, 8, 1024, stageB, evac="copy",
                        extra=[("TEN", self.tmark())])
            self._mmblk(YW, ksrc, wn, 4, 8, 1024, stageB, evac="copy",
                        extra=[("TEN", self.tmark())])
            self.phase = "l0f|scan"
            for krb in range(4):
                cc = self.v(self.M_CARRY + c0 + krb, 1)
                ini = 0.0 if tt == 0 else cc
                self.op("vector", "tensor_tensor_scan",
                        self.v(CU + krb * TT, TT), self.v(YW + krb * TT, TT),
                        self.v(YW + krb * TT, TT), ini, ALU.add, ALU.bypass,
                        waits=[("TEN", routers_done), ("GPS", gps_prev)])
                self.op("vector", "tensor_copy", cc,
                        self.v(CU + krb * TT + TT - 1, 1))
            for krb in range(4):
                p1 = self.op("gpsimd", "tensor_mul", self.v(XV + krb * TT, TT),
                             self.v(XV + krb * TT, TT), self.v(CU + krb * TT, TT),
                             waits=[("VEC", self.sem["VEC"].n)])
                gps_prev = p1
                self.t_bank(6)
                mv = self.mm(self.pb[6][:, 0:TT],
                             self.heap[0:8, self.C_OVL + krb * 128:
                                       self.C_OVL + (krb + 1) * 128],
                             self.v(wrow, TT, 0, 8), True, True,
                             waits=[("VEC", self.sem["VEC"].n)], inc=True)
                p2 = self.op("vector", "tensor_mul", self.v(XV + krb * TT, TT),
                             self.v(XV + krb * TT, TT), self.pb[6][:, 0:TT],
                             waits=[("TEN", mv), ("GPS", p1)])
                self.bank_free[6] = ("VEC", p2)
                self.split(YW + krb * TT, TT, self.v(XV + krb * TT, TT))
            pv = self.sem["VEC"].n
            gdst = GF if br == 0 else GI
            self.phase = "l0f|U"
            for oc in range(8):
                sl = oc % 2
                after = ("TEN", self.wlast[sl]) if self.wlast[sl] else None
                wsv = self.load_pair(WB + sl * 1024, 512, self.d[un].ap()[oc],
                                     stageB, after=after)
                bank = 4 + sl
                self.t_bank(bank)
                for krc in range(4):
                    self.mm3(self.pb[bank][:, 0:TT], WB + sl * 1024, 512, krc,
                             YW + krc * TT, TT, krc == 0, False,
                             waits=[("VEC", wsv), ("VEC", pv), lbf, lbi]
                             if krc == 0 else ())
                self.mm(self.pb[bank][:, 0:TT],
                        self.heap[0:8, boff + oc * 128:boff + oc * 128 + 128],
                        self.v(wrow, TT, 0, 8), False, True)
                self.wlast[sl] = self.tmark()
                ev = self.op("vector", "tensor_copy", self.v(gdst + oc * TT, TT),
                             self.pb[bank][:, 0:TT],
                             waits=[("TEN", self.tmark())])
                self.bank_free[bank] = ("VEC", ev)
        cvs = None
        for oc in range(8):
            self.op("vector", "scalar_tensor_tensor", self.v(GF + oc * TT, TT),
                    self.v(GI + oc * TT, TT), self.v(self.M_ALPHA, 1),
                    self.v(GF + oc * TT, TT), ALU.mult, ALU.add)
            cvs = self.split(CMB + oc * TT, TT, self.v(GF + oc * TT, TT))
        self.phase = "l0f|WO"
        for oc in range(8):
            sl = oc % 2
            after = ("TEN", self.wlast[sl]) if self.wlast[sl] else None
            wsv = self.load_pair(WB + sl * 1024, 1024, self.d["wfO"].ap()[oc],
                                 stageB, after=after)
            bank = 4 + sl
            self.t_bank(bank)
            for c in range(8):
                self.mm3(self.pb[bank][:, 0:TT], WB + sl * 1024, 1024, c,
                         CMB + c * TT, TT, c == 0, c == 7,
                         waits=[("VEC", wsv), ("VEC", cvs)] if c == 0 else ())
            self.wlast[sl] = self.tmark()
            mv = self.tmark()
            e1 = self.op("vector", "tensor_mul", self.v(GI + oc * TT, TT),
                         self.pb[bank][:, 0:TT], self.v(SP, TT),
                         waits=[("TEN", mv), lnv])
            self.op("vector", "tensor_add", self.v(XS + oc * T + t0, TT),
                    self.v(XS + oc * T + t0, TT), self.v(GI + oc * TT, TT))
            self.bank_free[bank] = ("VEC", e1)
        self.barrier()
        self._stage_busy = {}

    def emit_l1a(self, s):
        LATK, LATV, QG, KG, VG = self._l1a_front(s)
        for g in range(8):
            self._l1a_group(g, LATK, LATV, QG, KG, VG)
        self.barrier()

    def _l1a_front(self, s):
        self.phase = "l1a.front"
        T, TT, NTT, NKT = self.T, self.TT, self.NTT, self.NKT
        XS, H1, R2, WB = (self.OFF[k] for k in ("XSB", "H1", "R2", "WB"))
        self.preln(H1, R2)
        LATK, LATV = R2, R2 + 2 * T
        for wn, dsto in (("w1ka", LATK), ("w1va", LATV)):
            lv = self.dma(self.v(WB, 2048), self.d[wn].ap()[0])
            rd = self.ready()
            for tt in range(NTT):
                t0 = tt * TT
                for lc in range(2):
                    self.t_bank(lc)
                    for c in range(8):
                        self.mm(self.pb[lc][:, 0:TT],
                                self.v(WB + c * 256 + lc * 128, 128),
                                self.v(H1 + c * T + t0, TT), c == 0, c == 7,
                                waits=(rd + [lv]) if (c == 0 and tt == 0 and lc == 0) else ())
                    ev = self.op("vector", "tensor_copy",
                                 self.v(dsto + lc * T + t0, TT),
                                 self.pb[lc][:, 0:TT],
                                 waits=[("TEN", self.tmark())])
                    self.bank_free[lc] = ("VEC", ev)
            self.barrier()
        QG = R2 + 4 * T
        KG = QG + T
        VG = KG + T
        return LATK, LATV, QG, KG, VG

    def _l1a_group(self, g, LATK, LATV, QG, KG, VG):
        self.phase = "l1a.proj"
        T, TT, NTT, NKT = self.T, self.TT, self.NTT, self.NKT
        H1, WB = self.OFF["H1"], self.OFF["WB"]
        lq = self.dma(self.v(WB, 1024), self.d["w1q"].ap()[g])
        lkb = self.dma(self.v(WB + 1024, 256), self.d["w1kb"].ap()[g])
        lvb = self.dma(self.v(WB + 1280, 256), self.d["w1vb"].ap()[g])
        lwo = self.dma(self.heap[0:64, WB + 1536:WB + 2560],
                       self.d["w1o"].ap()[g * 128:g * 128 + 64, :])
        rd = self.ready()
        SCQ = self.C_OVL
        SCK = self.OFF["SP"]
        ag = getattr(self, "attn_guard", [])
        for tt in range(NTT):
            t0 = tt * TT
            qb = 6 + (tt % 2)
            kb = 7 - (tt % 2)
            self.t_bank(qb)
            for c in range(8):
                self.mm(self.pb[qb][:, 0:TT], self.v(WB + c * 128, 128),
                        self.v(H1 + c * T + t0, TT), c == 0, c == 7,
                        waits=(rd + [lq]) if (c == 0 and tt == 0) else ())
            qw = [("TEN", self.tmark())] + (list(ag) if tt == 0 else [])
            if tt > 0:
                qw.append(("SCA", self._last_sq))
            qv = self.op("vector", "tensor_copy", self.v(SCQ, TT),
                         self.pb[qb][:, 0:TT], waits=qw)
            self.bank_free[qb] = ("VEC", qv)
            self.bound_tile(SCQ, self.M_MQ, tt == 0, qv)
            self.split(QG + t0, TT, self.v(SCQ, TT))
            self.t_bank(kb)
            for lc in range(2):
                self.mm(self.pb[kb][:, 0:TT], self.v(WB + 1024 + lc * 128, 128),
                        self.v(LATK + lc * T + t0, TT), lc == 0, lc == 1,
                        waits=[lkb] if lc == 0 else ())
            kw = [("TEN", self.tmark())] + (list(ag) if tt == 0 else [])
            kw.append(("SCA", self._last_sq))
            kv = self.op("vector", "tensor_copy", self.v(SCK, TT),
                         self.pb[kb][:, 0:TT], waits=kw)
            self.bank_free[kb] = ("VEC", kv)
            self.bound_tile(SCK, self.M_MK, tt == 0, kv)
            self.split(KG + t0, TT, self.v(SCK, TT))
        lwo2 = self.dma(self.heap[0:64, WB:WB + 1024],
                        self.d["w1o"].ap()[g * 128 + 64:g * 128 + 128, :],
                        after=("TEN", self.tmark()))
        bv = self.finish_bound()
        mz = self.op("vector", "memset",
                     self.hap(VG + 64, [[130, NKT], [65, 2]]), 1.0)
        for kt in range(NKT):
            vb_ = 6 + (kt % 2)
            self.t_bank(vb_)
            for hl in range(2):
                for lc in range(2):
                    self.mm(self.pb[vb_][:, hl * 64:hl * 64 + 64],
                            self.v(LATV + lc * T + kt * 128, 128),
                            self.v(WB + 1280 + lc * 128 + hl * 64, 64),
                            lc == 0, lc == 1,
                            waits=[lvb] if (kt == 0 and hl == 0 and lc == 0) else ())
            ev = self.op("vector", "tensor_copy",
                         self.hap(VG + kt * 130, [[65, 2], [1, 64]]),
                         self.pb[vb_][:, 0:128],
                         waits=[("TEN", self.tmark()), ("VEC", mz)])
            self.bank_free[vb_] = ("VEC", ev)
        self.attn_group(QG, KG, VG, False, bv, (WB + 1536, WB), (lwo, lwo2),
                        s_pair=True)
        self.barrier()

    def emit_l1f(self, s):
        self.phase = "l1f"
        T, TT, NTT = self.T, self.TT, self.NTT
        XS, H1, R2, WB, SP = (self.OFF[k] for k in
                              ("XSB", "H1", "R2", "WB", "SP"))
        MOFF = R2
        if s == 0:
            AO, BO = H1, H1 + 8192
            w1last = None
            for ob in range(8):
                lwr = self.dma(self.v(WB, 1024), self.d["wrB"].ap()[ob],
                               after=("TEN", w1last) if w1last else None)
                lwi = self.dma(self.v(WB + 1024, 1024), self.d["wiB"].ap()[ob])
                for b in range(6):
                    self.t_bank(b)
                for nt in range(2):
                    for c in range(8):
                        lc_ = self.dma(
                            self.v(SP, 512),
                            self.d["dftC"].ap()[c * 128:(c + 1) * 128,
                                                nt * 512:(nt + 1) * 512],
                            after=("TEN", self.tmark())
                            if (c + nt + ob) else None)
                        self.mm(self.pb[nt][:, 0:512], self.v(WB + c * 128, 128),
                                self.v(SP, 512), c == 0, False,
                                waits=[lc_, lwr] if c == 0 else [lc_])
                        self.mm(self.pb[2 + nt][:, 0:512],
                                self.v(WB + 1024 + c * 128, 128),
                                self.v(SP, 512), c == 0, c == 7,
                                waits=[lwi] if c == 0 else ())
                        ls_ = self.dma(
                            self.v(SP + 512, 512),
                            self.d["dftS"].ap()[c * 128:(c + 1) * 128,
                                                nt * 512:(nt + 1) * 512])
                        self.mm(self.pb[nt][:, 0:512],
                                self.v(WB + 1024 + c * 128, 128),
                                self.v(SP + 512, 512), False, c == 7,
                                waits=[ls_])
                        self.mm(self.pb[4 + nt][:, 0:512], self.v(WB + c * 128, 128),
                                self.v(SP + 512, 512), c == 0, c == 7)
                    w1last = self.tmark()
                    av = self.op("vector", "tensor_copy",
                                 self.v(AO + ob * 1024 + nt * 512, 512),
                                 self.pb[nt][:, 0:512], waits=[("TEN", w1last)])
                    self.op("vector", "tensor_copy",
                            self.v(BO + ob * 1024 + nt * 512, 512),
                            self.pb[2 + nt][:, 0:512])
                    bv = self.op("vector", "tensor_sub",
                                 self.v(BO + ob * 1024 + nt * 512, 512),
                                 self.v(BO + ob * 1024 + nt * 512, 512),
                                 self.pb[4 + nt][:, 0:512])
                    self.bank_free[nt] = ("VEC", av)
                    self.bank_free[2 + nt] = ("VEC", bv)
                    self.bank_free[4 + nt] = ("VEC", bv)
            self.barrier()
            for nb in range(8):
                for mt in range(2):
                    self.t_bank(0)
                    self.t_bank(1)
                    for oc in range(8):
                        lc_ = self.dma(
                            self.v(SP, 512),
                            self.d["dftC"].ap()[oc * 128:(oc + 1) * 128,
                                                mt * 512:(mt + 1) * 512],
                            after=("TEN", self.tmark())
                            if (oc + mt + nb) else None)
                        self.mm(self.pb[0][:, 0:512],
                                self.v(AO + oc * 1024 + nb * 128, 128),
                                self.v(SP, 512), oc == 0, oc == 7,
                                waits=[lc_])
                        ls_ = self.dma(
                            self.v(SP + 512, 512),
                            self.d["dftS"].ap()[oc * 128:(oc + 1) * 128,
                                                mt * 512:(mt + 1) * 512])
                        self.mm(self.pb[1][:, 0:512],
                                self.v(BO + oc * 1024 + nb * 128, 128),
                                self.v(SP + 512, 512), oc == 0, oc == 7,
                                waits=[ls_])
                    mv = self.tmark()
                    c1 = self.op("vector", "tensor_copy",
                                 self.v(MOFF + nb * 1024 + mt * 512, 512),
                                 self.pb[0][:, 0:512], waits=[("TEN", mv)])
                    sv = self.op("vector", "tensor_sub",
                                 self.v(MOFF + nb * 1024 + mt * 512, 512),
                                 self.v(MOFF + nb * 1024 + mt * 512, 512),
                                 self.pb[1][:, 0:512])
                    ev = self.op("scalar", "activation",
                                 self.v(MOFF + nb * 1024 + mt * 512, 512),
                                 self.v(MOFF + nb * 1024 + mt * 512, 512),
                                 AF.Copy, scale=1.0 / 1024.0,
                                 waits=[("VEC", sv)])
                    self.bank_free[0] = ("VEC", c1)
                    self.bank_free[1] = ("VEC", sv)
            self.wait("sync", "SCA", self.sem["SCA"].n)
            self.dma(self.mcache.ap()[:, :], self.v(MOFF, 8192))
            self.barrier()
        else:
            lv = self.dma(self.v(MOFF, 8192), self.mcache.ap()[:, :])
            self.wait("tensor", *lv)
            self.barrier()
        XN = R2 + 8192
        for tt in range(NTT):
            t0 = tt * TT
            gv = self.ln_tile(XS, XN, tt, 1.0 / D, True, XN)
            prev_e2 = [0, 0]
            for oc in range(8):
                bank = 2 + (oc % 4)
                self.t_bank(bank)
                for c in range(8):
                    self.mm(self.pb[bank][:, 0:TT],
                            self.v(MOFF + c * 1024 + oc * 128, 128),
                            self.v(XN + c * TT, TT), c == 0, c == 7,
                            waits=[("GPS", gv)] if (c == 0 and oc == 0) else ())
                mv = self.tmark()
                sl = oc % 2
                e1 = self.op("scalar", "activation", self.v(SP + sl * 512, TT),
                             self.pb[bank][:, 0:TT], AF.Identity,
                             bias=self.v(self.M_BIAS1 + oc, 1),
                             waits=[("TEN", mv), ("VEC", prev_e2[sl])])
                e2 = self.op("vector", "tensor_add",
                             self.v(XS + oc * T + t0, TT),
                             self.v(XS + oc * T + t0, TT),
                             self.v(SP + sl * 512, TT),
                             waits=[("SCA", e1)])
                prev_e2[sl] = e2
                self.bank_free[bank] = ("VEC", e2)
        self.barrier()

    def emit_init(self):
        self.dma(self.v(self.C_ONES, 128), self.d["ones"].ap())
        self.dma(self.v(self.C_ID, 128), self.d["id"].ap())
        self.dma(self.v(self.M_RB2EB, 1, 0, 8), self.d["rb2"].ap())
        self.dma(self.v(self.M_G0, 2), self.d["g0"].ap())
        self.dma(self.v(self.M_B0, 2), self.d["b0"].ap())
        self.dma(self.v(self.M_BIAS1, 8), self.d["bias1"].ap())
        self.dma(self.v(self.M_ALPHA1, 1, 0, 1), self.d["alpha"].ap())
        self.op("vector", "memset", self.v(self.M_EPS, 1), 1e-5)
        self.op("vector", "memset", self.v(self.M_LN18, 1), LN1_8)
        lv = self.dma(self.v(self.M_SC1, 1, 0, 8), self.d["eb"].ap())
        wall = [(f"LD{i}", self.sem[f"LD{i}"].n) for i in range(8)]
        self.op("vector", "tensor_add", self.v(self.M_RB2EB, 1, 0, 8),
                self.v(self.M_RB2EB, 1, 0, 8), self.v(self.M_SC1, 1, 0, 8),
                waits=wall)
        mv = self.mm(self.pb[7][:, 0:1],
                     self.heap[0:1, self.C_ONES:self.C_ONES + 128],
                     self.heap[0:1, self.M_ALPHA1:self.M_ALPHA1 + 1],
                     True, True, waits=wall, inc=True)
        self.op("vector", "tensor_copy", self.v(self.M_ALPHA, 1),
                self.pb[7][:, 0:1], waits=[("TEN", mv)])
        self.barrier()

    def emit_store(self, s):
        T, XS = self.T, self.OFF["XSB"]
        self.wait("sync", "VEC", self.sem["VEC"].n)
        self.wait("sync", "GPS", self.sem["GPS"].n)
        self.wait("sync", "SCA", self.sem["SCA"].n)
        for c in range(8):
            self.dma(self.yout.ap()[s, c * 128:(c + 1) * 128, :],
                     self.v(XS + c * T, T))
        for i in range(8):
            self.wait("sync", f"LD{i}", self.sem[f"LD{i}"].n)
        self.barrier()

    def build(self):
        self.rope_done = 0
        self.emit_init()
        for s in range(NSEQ):
            self.emit_l0a(s)
            if self.nblocks >= 2:
                self.emit_l0f(s)
            if self.nblocks >= 3:
                self.emit_l1a(s)
            if self.nblocks >= 4:
                self.emit_l1f(s)
            self.emit_store(s)
        nc, ops = self.nc, self.ops
        with nc.Block() as block:
            @block.sync
            def _(e):
                for fn in ops["sync"]:
                    fn(e)

            @block.tensor
            def _(e):
                for fn in ops["tensor"]:
                    fn(e)

            @block.vector
            def _(e):
                for fn in ops["vector"]:
                    fn(e)

            @block.scalar
            def _(e):
                for fn in ops["scalar"]:
                    fn(e)

            @block.gpsimd
            def _(e):
                for fn in ops["gpsimd"]:
                    fn(e)
        return nc


_BUILD_CACHE = {}


def build_program(T=2048, nblocks=4):
    key = (T, nblocks)
    if key not in _BUILD_CACHE:
        b = B(T, nblocks)
        _BUILD_CACHE[key] = b.build()
        _BUILD_CACHE[key]._mmlab = b.mmlab
    return _BUILD_CACHE[key]


def kernel(**inputs):
    T = SEQT
    x = np.ascontiguousarray(np.asarray(inputs["L0_attn_x"], np.float32))
    Bt, Tx, Dx = x.shape
    assert Dx == D and Tx == T, (x.shape, T)
    shared = _host_prep(inputs)
    shared.update(_host_consts(T))
    nc = build_program(T, NBLOCKS)
    in_maps = []
    for c in range(NCORES):
        m = dict(shared)
        m["x"] = np.ascontiguousarray(x[2 * c:2 * c + 2].transpose(0, 2, 1))
        in_maps.append(m)
    res = run_bass_kernel_spmd(nc, in_maps, core_ids=list(range(NCORES)))
    out = np.empty((Bt, T, D), np.float32)
    for c in range(NCORES):
        y = np.asarray(res.results[c]["y"])
        out[2 * c:2 * c + 2] = y.transpose(0, 2, 1)
    return out


# version 17
# speedup vs baseline: 1.1096x; 1.0268x over previous
import numpy as np
import concourse.bass as bass
import concourse.mybir as mybir
from concourse.bass_utils import run_bass_kernel_spmd

F32 = mybir.dt.float32
BF16 = mybir.dt.bfloat16
AF = mybir.ActivationFunctionType
ALU = mybir.AluOpType
AX = mybir.AxisListType

D = 1024
LATD = 256
KR = 512
NCORES = 8
NSEQ = 2
NEG = -1.0e30
LN1_8 = float(np.log(1.0 / 8.0))

SEQT = 2048
NBLOCKS = 4
DETECT = False
SYNC_SER = False



def _host_consts(T):
    c = {}
    j = np.arange(128)
    c["ones"] = np.ones((128, 128), np.float32)
    c["id"] = np.eye(128, dtype=np.float32)
    f = (j % 64) % 32
    inv = (1.0 / 10000.0 ** (f / 32.0))[:, None]
    ang = inv * np.arange(T)[None, :]
    c["ropeC"] = np.cos(ang).astype(np.float32)
    c["ropeS"] = np.sin(ang).astype(np.float32)
    Pm = np.zeros((128, 128), np.float32)
    for r in range(128):
        rb = r % 64
        base = r - rb
        if rb < 32:
            Pm[r, base + rb + 32] = -1.0
        else:
            Pm[r, base + rb - 32] = 1.0
    c["pm0T"] = np.ascontiguousarray(Pm.T)
    kk, qq = np.meshgrid(np.arange(128), np.arange(128), indexing="ij")
    m = np.zeros((128, 128), np.float32)
    m[kk > qq] = NEG
    c["mask"] = m
    E = np.zeros((8, 512), np.float32)
    for col in range(512):
        E[col // 64, col] = 1.0
    c["ec"] = E
    nv = (1.0 / np.arange(1, T + 1, dtype=np.float64)).astype(np.float32)
    c["nvinv"] = np.ascontiguousarray(np.broadcast_to(nv, (128, T)))
    n = np.arange(D, dtype=np.int64)
    th = 2.0 * np.pi * (np.outer(n, n) % D).astype(np.float64) / D
    c["dftC"] = np.cos(th).astype(np.float32)
    c["dftS"] = np.sin(th).astype(np.float32)
    return c


def _blk_lhsT(w, kb=128, mb=128):
    K, M = w.shape
    nk, nm = K // kb, M // mb
    a = w.reshape(nk, kb, nm, mb).transpose(2, 1, 0, 3).reshape(nm, kb, nk * mb)
    return np.ascontiguousarray(a)


def _flat_one(w):
    K, M = w.shape
    nk = K // 128
    a = w.reshape(nk, 128, M).transpose(1, 0, 2).reshape(128, nk * M)
    return np.ascontiguousarray(a)


def _host_prep(inputs):
    f32 = lambda a: np.ascontiguousarray(np.asarray(a, np.float32))
    iw = {}
    iw["w0q"] = _blk_lhsT(f32(inputs["L0_attn_W_q"]))
    iw["w0dkv"] = _flat_one(f32(inputs["L0_attn_W_dkv"]))
    iw["w0uk"] = _blk_lhsT(f32(inputs["L0_attn_W_uk"]))
    iw["w0uv"] = _blk_lhsT(f32(inputs["L0_attn_W_uv"]))
    iw["w0o"] = f32(inputs["L0_attn_W_o"])
    iw["g0"] = _flat_one(f32(inputs["L0_attn_ln_gamma"]).reshape(LATD, 1))
    iw["b0"] = _flat_one(f32(inputs["L0_attn_ln_beta"]).reshape(LATD, 1))
    iw["wfQ"] = _blk_lhsT(f32(inputs["L0_ffn_W_Q"]).T)
    iw["wfK"] = _blk_lhsT(f32(inputs["L0_ffn_W_K"]).T)
    iw["wfO"] = _blk_lhsT(f32(inputs["L0_ffn_W_O"]).T)
    iw["wfI"] = _blk_lhsT(f32(inputs["L0_ffn_W_inv"]).T)
    iw["wfr1"] = _blk_lhsT(f32(inputs["L0_ffn_router_w1"]).T)
    iw["wfr2"] = _blk_lhsT(f32(inputs["L0_ffn_router_w2"]).T, mb=8)
    vflat = lambda a: _blk_lhsT(f32(a).transpose(1, 0, 2).reshape(D, KR))
    uflat = lambda a: _blk_lhsT(f32(a).transpose(0, 2, 1).reshape(KR, D))
    iw["vfw"] = vflat(inputs["L0_ffn_V_fwd"])
    iw["wfw"] = vflat(inputs["L0_ffn_W_fwd"])
    iw["ufw"] = uflat(inputs["L0_ffn_U_fwd"])
    iw["vin"] = vflat(inputs["L0_ffn_W_inv_exp"])
    iw["win"] = vflat(inputs["L0_ffn_V_inv"])
    iw["uin"] = uflat(inputs["L0_ffn_U_inv"])
    iw["bfw"] = f32(inputs["L0_ffn_b_fwd"])
    iw["bin"] = f32(inputs["L0_ffn_b_inv"])
    iw["rb1"] = _flat_one(f32(inputs["L0_ffn_router_b1"]).reshape(D, 1))
    iw["rb2"] = f32(inputs["L0_ffn_router_b2"]).reshape(8, 1)
    iw["eb"] = f32(inputs["L0_ffn_expert_bias"]).reshape(8, 1)
    iw["alpha"] = f32(inputs["L0_ffn_alpha_bi"]).reshape(1, 1)
    iw["w1q"] = _blk_lhsT(f32(inputs["L1_attn_wq"]).T)
    iw["w1ka"] = _blk_lhsT(f32(inputs["L1_attn_wk_a"]).T, mb=256)
    iw["w1va"] = _blk_lhsT(f32(inputs["L1_attn_wv_a"]).T, mb=256)
    iw["w1kb"] = _blk_lhsT(f32(inputs["L1_attn_wk_b"]).T)
    iw["w1vb"] = _blk_lhsT(f32(inputs["L1_attn_wv_b"]).T)
    iw["w1o"] = np.ascontiguousarray(f32(inputs["L1_attn_wo"]).T)
    iw["wrB"] = _blk_lhsT(f32(inputs["L1_ffn_w_freq_real"]))
    iw["wiB"] = _blk_lhsT(f32(inputs["L1_ffn_w_freq_imag"]))
    iw["bias1"] = _flat_one(f32(inputs["L1_ffn_bias"]).reshape(D, 1))
    return iw


_SHAPES = {
    "w0q": [8, 128, 1024], "w0dkv": [128, 2048], "w0uk": [8, 128, 256],
    "w0uv": [8, 128, 256], "w0o": [1024, 1024], "g0": [128, 2], "b0": [128, 2],
    "wfQ": [8, 128, 1024], "wfK": [8, 128, 1024], "wfO": [8, 128, 1024],
    "wfI": [8, 128, 1024], "wfr1": [8, 128, 1024], "wfr2": [1, 128, 64],
    "vfw": [4, 128, 1024], "wfw": [4, 128, 1024], "ufw": [8, 128, 512],
    "vin": [4, 128, 1024], "win": [4, 128, 1024], "uin": [8, 128, 512],
    "bfw": [8, 1024], "bin": [8, 1024],
    "rb1": [128, 8], "rb2": [8, 1], "eb": [8, 1], "alpha": [1, 1],
    "w1q": [8, 128, 1024], "w1ka": [1, 128, 2048], "w1va": [1, 128, 2048],
    "w1kb": [8, 128, 256], "w1vb": [8, 128, 256], "w1o": [1024, 1024],
    "wrB": [8, 128, 1024], "wiB": [8, 128, 1024], "bias1": [128, 8],
}

ENGS = ("sync", "tensor", "vector", "scalar", "gpsimd")


class Sem:
    def __init__(self, h):
        self.h = h
        self.n = 0

    def inc(self, k=1):
        self.n += k
        return self.n


class B:
    def __init__(self, T, nblocks):
        self.T = T
        self.TT = min(512, T)
        self.NTT = T // self.TT
        self.NKT = T // 128
        self.NQT = T // self.TT
        self.nqsub = self.TT // 128
        self.nblocks = nblocks
        nc = bass.Bass("TRN2", target_bir_lowering=False,
                       detect_race_conditions=DETECT)
        self.nc = nc
        self.ops = {e: [] for e in ENGS}

        dr = lambda n, s, k: nc.dram_tensor(n, s, F32, kind=k)
        self.xin = dr("x", [NSEQ, D, T], "ExternalInput")
        self.yout = dr("y", [NSEQ, D, T], "ExternalOutput")
        self.d = {n: dr(n, s, "ExternalInput") for n, s in _SHAPES.items()}
        for n, s in (("ropeC", [128, T]), ("ropeS", [128, T]),
                     ("pm0T", [128, 128]), ("mask", [128, 128]),
                     ("ones", [128, 128]), ("id", [128, 128]),
                     ("ec", [8, 512]), ("nvinv", [128, T]),
                     ("dftC", [1024, 1024]), ("dftS", [1024, 1024])):
            self.d[n] = dr(n, s, "ExternalInput")
        self.mcache = dr("mcache", [128, 8192], "Internal")

        T8 = 8 * T
        self.OFF = {}
        cur = 0
        for name, n in (("XSB", T8), ("H1", max(T8, 32 * self.TT, 16384)),
                        ("R2", 14368),
                        ("WB", 2560), ("CB", 896), ("ST", 1024), ("SP", 1024)):
            self.OFF[name] = cur
            cur += n
        self.NCOLS = cur
        assert cur <= 53200, cur
        self.heap = nc.alloc_sbuf_tensor("heap", [128, self.NCOLS], F32)
        ha = nc.lookup_mloc(self.heap).addr
        TTl = self.TT
        self._bfregs = []
        for i, (st, ln) in enumerate((
                (self.OFF["R2"], 8 * TTl),
                (self.OFF["R2"] + 20 * TTl, 4 * TTl),
                (self.OFF["H1"], 8 * TTl),
                (self.OFF["H1"] + 8 * TTl, 8 * TTl),
                (self.OFF["H1"] + 16 * TTl, 8 * TTl),
                (self.OFF["H1"] + 24 * TTl, 8 * TTl),
                (self.OFF["WB"], 2048),
                (self.OFF["R2"] + 16 * TTl, 4 * TTl),
                (self.OFF["R2"] + 8 * TTl, 4 * TTl),
                (self.OFF["R2"] + 12 * TTl, 4 * TTl),
        )):
            t = nc.alloc_sbuf_tensor_at(f"ab{i}", [128, 2 * ln], BF16,
                                        offset=ha + 4 * st)
            self._bfregs.append((st, ln, t))
        self.pb = [nc.alloc_psum_tensor(f"pb{i}", [128, 512], F32) for i in range(8)]
        self.sem = {n: Sem(nc.alloc_semaphore(n)) for n in
                    ("TEN", "VEC", "SCA", "GPS", "BAR")}
        for i in range(8):
            self.sem[f"LD{i}"] = Sem(nc.alloc_semaphore(f"LD{i}"))
        self._dmai = 0
        self._ld_last = {}
        CB = self.OFF["CB"]
        self.C_ONES = CB + 0
        self.C_ID = CB + 128
        self.C_OVL = CB + 256
        self.C_MISC = CB + 768
        M = self.C_MISC
        self.M_RB2EB = M + 0
        self.M_ALPHA = M + 1
        self.M_G0 = M + 2
        self.M_B0 = M + 4
        self.M_BIAS1 = M + 6
        self.M_CARRY = M + 16
        self.M_NEGB = M + 24
        self.M_MQ = M + 25
        self.M_MK = M + 26
        self.M_SC1 = M + 27
        self.M_SC2 = M + 28
        self.M_RECIP = M + 29
        self.M_ALPHA1 = M + 33
        self.M_EPS = M + 34
        self.M_LN18 = M + 35
        self.bank_free = {i: None for i in range(8)}
        self._stage_busy = {}
        self._pend = 0
        self._pcell = None
        self.phase = "init"
        self.mmlab = []

    def v(self, col, n, p0=0, p1=128):
        return self.heap[p0:p1, col:col + n]

    def hap(self, col, dims, np_=128):
        return bass.AP(self.heap, col, [[self.NCOLS, np_]] + [list(d) for d in dims])

    def pap(self, pbt, col, dims, np_=128):
        return bass.AP(pbt, col, [[512, np_]] + [list(d) for d in dims])

    def _bfreg(self, col, n):
        for st, ln, t in self._bfregs:
            if st <= col and col + n <= st + ln:
                return st, t
        raise KeyError((col, n))

    def vb(self, col, n, p0=0, p1=128, lo=False):
        st, t = self._bfreg(col, n)
        base = 2 * (col - st) + (n if lo else 0)
        return t[p0:p1, base:base + n]

    def vbx(self, col, n, soff, slen, p0=0, p1=128, lo=False):
        st, t = self._bfreg(col, n)
        base = 2 * (col - st) + (n if lo else 0) + soff
        return t[p0:p1, base:base + slen]

    def split(self, col, n, src_ap, waits=(), p0=0, p1=128):
        h_ = self.vb(col, n, p0, p1)
        l_ = self.vb(col, n, p0, p1, lo=True)
        self.op("vector", "tensor_copy", h_, src_ap, waits=waits)
        return self.op("vector", "tensor_tensor", l_, src_ap, h_, ALU.subtract)

    def load_pair(self, pair_col, ncols, dram_ap, stage_col, after=None):
        if after is not None:
            self.wait("sync", after[0], after[1])
        sb = self._stage_busy.get(stage_col)
        if sb:
            self.wait("sync", sb[0], sb[1])
        lv = self.dma(self.v(stage_col, ncols), dram_ap)
        sv = self.split(pair_col, ncols, self.v(stage_col, ncols), waits=[lv])
        self._stage_busy[stage_col] = ("VEC", sv)
        return sv

    def mm3(self, psum_ap, wcol, nwe, wc, rcol, n_r, start, stop, waits=(),
            wspan=128, p0=0, p1=128):
        wst, wt = self._bfreg(wcol, nwe)
        b0 = 2 * (wcol - wst)
        lh = wt[p0:p1, b0 + wc * wspan:b0 + (wc + 1) * wspan]
        ll = wt[p0:p1, b0 + nwe + wc * wspan:b0 + nwe + (wc + 1) * wspan]
        rh = self.vb(rcol, n_r, p0, p1)
        rl = self.vb(rcol, n_r, p0, p1, lo=True)
        self.mm(psum_ap, lh, rh, start, False, waits=waits)
        self.mm(psum_ap, lh, rl, False, False)
        return self.mm(psum_ap, ll, rh, False, stop)

    def on(self, eng, fn):
        self.ops[eng].append(fn)

    def ten_flush(self):
        if self._pend:
            self._pcell[0] += self._pend
            self._pend = 0

    def tmark(self):
        self.ten_flush()
        return self.sem["TEN"].n

    def wait(self, eng, sem, val):
        if not val:
            return
        if eng == "tensor":
            self.ten_flush()
        self.on(eng, lambda e, s=self.sem[sem].h, v=val: e.wait_ge(s, v))

    def ready(self):
        return [("VEC", self.sem["VEC"].n), ("SCA", self.sem["SCA"].n),
                ("GPS", self.sem["GPS"].n)]

    def barrier(self):
        self.ten_flush()
        bar = self.sem["BAR"]
        v = bar.inc(5)
        for eng in ENGS:
            self.on(eng, lambda e, s=bar.h, v=v:
                    (e.nop().then_inc(s, 1), e.wait_ge(s, v)))
        for i in range(8):
            self.bank_free[i] = None

    def dma(self, dst_ap, src_ap, after=None):
        name = f"LD{self._dmai % 8}"
        self._dmai += 1
        LD = self.sem[name]
        if after is not None:
            self.wait("sync", after[0], after[1])
        if name in self._ld_last:
            self.wait("sync", name, self._ld_last[name])
        val = LD.inc(16)
        self._ld_last[name] = val
        self.on("sync", lambda e, d=dst_ap, s=src_ap, h=LD.h:
                e.dma_start(out=d, in_=s).then_inc(h, 16))
        if SYNC_SER:
            self.wait("sync", name, val)
        return (name, val)

    def t_bank(self, bank):
        bf = self.bank_free[bank]
        if bf is not None:
            self.wait("tensor", bf[0], bf[1])

    def mm(self, psum_ap, lhsT, rhs, start, stop, waits=(), tile_position=None,
           inc=False):
        for (s, v) in waits:
            self.wait("tensor", s, v)
        TEN = self.sem["TEN"]
        val = TEN.inc()
        try:
            shp = (lhsT.ap[0][1], lhsT.free_size(), rhs.free_size(),
                   "f32" if lhsT.tensor.dtype == F32 else "bf16")
        except (AttributeError, IndexError):
            shp = None
        self.mmlab.append((self.phase, shp))
        cell = [0]
        self._pend += 1
        self._pcell = cell
        if inc:
            self.ten_flush()
        def fn(e, o=psum_ap, l=lhsT, r=rhs, st=start, sp=stop, h=TEN.h,
               c=cell, tpos=tile_position):
            i = e.matmul(o, l, r, start=st, stop=sp, tile_position=tpos)
            if c[0]:
                i.then_inc(h, c[0])
        self.on("tensor", fn)
        return val

    def tp(self, out_ap, in_ap, waits=(), kdim=128, tile_position=None):
        for (s, v) in waits:
            self.wait("tensor", s, v)
        self.ten_flush()
        TEN = self.sem["TEN"]
        val = TEN.inc()
        self.mmlab.append((self.phase + "/tp", "tp"))
        def fn(e, o=out_ap, i_=in_ap,
               idn=self.heap[0:kdim, self.C_ID:self.C_ID + kdim], h=TEN.h,
               tpos=tile_position):
            e.transpose(o, i_, idn).then_inc(h, 1)
        self.on("tensor", fn)
        return val

    def op(self, eng, meth, *args, waits=(), **kw):
        S = {"vector": "VEC", "scalar": "SCA", "gpsimd": "GPS"}[eng]
        for (s, v) in waits:
            self.wait(eng, s, v)
        sem = self.sem[S]
        prev = sem.n
        val = sem.inc()
        def fn(e, m=meth, a=args, k=kw, h=sem.h, p=prev):
            i = getattr(e, m)(*a, **k).then_inc(h, 1)
            if p:
                i._wait_ge(h, p)
        self.on(eng, fn)
        return val

    def ln_tile(self, src_off, dst_off, tt, inv_d, dst_tile, sq_off):
        self.phase = self.phase.split("|")[0] + "|ln"
        """LayerNorm of one t-tile: dst = (src - mean)/sqrt(var+eps).
        src chunk c at src_off + c*T + tt*TT; dst chunk c at
        dst_off + c*(TT if dst_tile else T) (+ tt*TT if not dst_tile).
        sq_off: 8*TT scratch cols for squares. Returns GPS val of completion."""
        T, TT = self.T, self.TT
        ST = self.OFF["ST"]
        t0 = tt * TT
        meanB = self.v(ST, TT)
        rsd = self.v(ST + 512, TT)
        src3 = self.hap(src_off + t0, [[T, 8], [1, TT]])
        dst3 = self.hap(sq_off, [[TT, 8], [1, TT]])
        sq = self.op("scalar", "activation", dst3, src3, AF.Square,
                     waits=self.ready())
        for b_, off, step, w in ((0, src_off + t0, T, None), (1, sq_off, TT, sq)):
            self.t_bank(b_)
            for c in range(8):
                self.mm(self.pb[b_][:, 0:TT], self.v(self.C_ONES, 128),
                        self.v(off + c * step, TT), c == 0, c == 7,
                        waits=([("SCA", w)] if (w and c == 0) else ()))
        vsum = self.tmark()
        self.op("scalar", "activation", meanB, self.pb[0][:, 0:TT], AF.Copy,
                scale=inv_d, waits=[("TEN", vsum)])
        e2 = self.op("scalar", "activation", rsd, self.pb[1][:, 0:TT], AF.Copy,
                     scale=inv_d)
        m2 = self.op("vector", "tensor_mul", self.pb[0][:, 0:TT], meanB, meanB,
                     waits=[("SCA", e2)])
        v2 = self.op("vector", "tensor_sub", rsd, rsd, self.pb[0][:, 0:TT],
                     waits=[("VEC", m2)])
        self.op("scalar", "activation", rsd, rsd, AF.Ln, bias=self.v(self.M_EPS, 1),
                waits=[("VEC", v2)])
        r2 = self.op("scalar", "activation", rsd, rsd, AF.Exp, scale=-0.5)
        self.bank_free[0] = ("VEC", v2)
        self.bank_free[1] = ("SCA", e2)
        gv = None
        for c in range(8):
            xc = self.v(src_off + c * T + t0, TT)
            dc = self.v(dst_off + c * (TT if dst_tile else T) +
                        (0 if dst_tile else t0), TT)
            sv = self.op("vector", "tensor_sub", dc, xc, meanB,
                         waits=([("SCA", r2)] if c == 0 else ()))
            gv = self.op("gpsimd", "tensor_mul", dc, dc, rsd,
                         waits=[("VEC", sv)])
        return gv

    def preln(self, dst_off, sq_off):
        gv = None
        for tt in range(self.NTT):
            gv = self.ln_tile(self.OFF["XSB"], dst_off, tt, 1.0 / D, False, sq_off)
        self.barrier()
        return gv

    def bound_tile(self, src_off, acc_col, first, wv):
        TT = self.TT
        sq = self.v(self.OFF["ST"], TT)
        sv = self.op("scalar", "activation", sq, self.v(src_off, TT), AF.Square,
                     waits=[("VEC", wv)] if wv else ())
        self._last_sq = sv
        self.t_bank(5)
        mv = self.mm(self.pb[5][0:1, 0:TT], self.v(self.C_ONES, 1), sq,
                     True, True, waits=[("SCA", sv)], inc=True)
        if first:
            rv = self.op("vector", "reduce_max", self.v(acc_col, 1, 0, 1),
                         self.pb[5][0:1, 0:TT], AX.X, waits=[("TEN", mv)])
        else:
            self.op("vector", "reduce_max", self.v(self.M_SC1, 1, 0, 1),
                    self.pb[5][0:1, 0:TT], AX.X, waits=[("TEN", mv)])
            rv = self.op("vector", "tensor_max", self.v(acc_col, 1, 0, 1),
                         self.v(acc_col, 1, 0, 1), self.v(self.M_SC1, 1, 0, 1))
        self.bank_free[5] = ("VEC", rv)
        return rv

    def finish_bound(self):
        mq = self.v(self.M_MQ, 1, 0, 1)
        mk = self.v(self.M_MK, 1, 0, 1)
        self.op("scalar", "activation", mq, mq, AF.Ln, waits=self.ready())
        l2 = self.op("scalar", "activation", mk, mk, AF.Ln)
        a1 = self.op("vector", "tensor_add", mq, mq, mk, waits=[("SCA", l2)])
        e1 = self.op("scalar", "activation", mq, mq, AF.Exp, scale=0.5,
                     bias=self.v(self.M_LN18, 1, 0, 1), waits=[("VEC", a1)])
        n1 = self.op("vector", "tensor_scalar", mq, mq, -1.0, 25.0,
                     ALU.mult, ALU.add, waits=[("SCA", e1)])
        self.t_bank(5)
        mv = self.mm(self.pb[5][:, 0:1],
                     self.heap[0:1, self.C_ONES:self.C_ONES + 128], mq,
                     True, True, waits=[("VEC", n1)], inc=True)
        cv = self.op("vector", "tensor_copy", self.v(self.M_NEGB, 1),
                     self.pb[5][:, 0:1], waits=[("TEN", mv)])
        self.bank_free[5] = ("VEC", cv)
        return cv

    def attn_group(self, QG, KG, VG, causal, bound_wait, wo_cols, wo_ld,
                   s_pair=False):
        base = self.phase.split("|")[0]
        """attention for 2 heads + per-q-tile W_o partial accumulated into x.
        QG/KG feature-major [128 x T]; VG token-major aug [128 x 129*NKT]
        per k-tile laid out [V_h0(64)|ones|V_h1(64)].  PV is feature-major:
        O_aug^T [65 x TT] = V_augT.T @ P^T accumulated over k-tiles; the aug
        row gives the softmax denominator (row 64 for head 0, row 0 for head
        1).  Normalization multiplies by exp(-ln(denom)) broadcast across 64
        partitions via a K=1 matmul.  W_o is applied as two K=64 matmuls
        against AT[64 x 2*TT]; wo_cols = WB cols of the two 64-row halves."""
        T, TT, NKT, NQT, nqsub = self.T, self.TT, self.NKT, self.NQT, self.nqsub
        SP, ST, XS = self.OFF["SP"], self.OFF["ST"], self.OFF["XSB"]
        negb = self.v(self.M_NEGB, 1)
        AT = ST
        last_at = 0
        if causal:
            sbank, pslot = (0, 1), (SP, SP + 512)
        else:
            sbank, pslot = (0, 1, 3), (SP, SP + 512, self.C_OVL)
        nbank = len(sbank)
        for qt in range(NQT):
            at_done = 0
            for hl in range(2):
                kts = list(range(min(NKT, nqsub * qt + nqsub))) if causal \
                    else list(range(NKT))
                drow = 64
                vrow0 = 0
                last_pv = None
                prev_pv = {}
                rd = self.ready()
                self.t_bank(2)
                def emit_S(i, hl=hl, qt=qt, kts=kts, rd=rd):
                    kt = kts[i]
                    bank = sbank[i % nbank]
                    qoffb = max(0, kt - nqsub * qt) if causal else 0
                    self.t_bank(bank)
                    self.phase = base + "|S"
                    w0 = rd if i == 0 and hl == 0 else ()
                    if not s_pair:
                        return self.mm(
                            self.pb[bank][:, qoffb * 128:TT],
                            self.heap[hl * 64:hl * 64 + 64,
                                      KG + kt * 128:KG + kt * 128 + 128],
                            self.heap[hl * 64:hl * 64 + 64,
                                      QG + qt * TT + qoffb * 128:
                                      QG + (qt + 1) * TT],
                            True, True, waits=w0, inc=True)
                    p0, p1 = hl * 64, hl * 64 + 64
                    t0k = (kt * 128 // TT) * TT
                    sk = kt * 128 - t0k
                    sq_ = qoffb * 128
                    nq_ = TT - sq_
                    kh = self.vbx(KG + t0k, TT, sk, 128, p0, p1)
                    kl = self.vbx(KG + t0k, TT, sk, 128, p0, p1, lo=True)
                    qh = self.vbx(QG + qt * TT, TT, sq_, nq_, p0, p1)
                    ql = self.vbx(QG + qt * TT, TT, sq_, nq_, p0, p1, lo=True)
                    ps = self.pb[bank][:, sq_:TT]
                    self.mm(ps, kh, qh, True, False, waits=w0)
                    self.mm(ps, kh, ql, False, False)
                    return self.mm(ps, kl, qh, False, True, inc=True)
                svs = {}
                for i in range(min(nbank, len(kts))):
                    svs[i] = emit_S(i)
                for i, kt in enumerate(kts):
                    par = i % nbank
                    bank = sbank[par]
                    qoffb = max(0, kt - nqsub * qt) if causal else 0
                    n_q = TT - qoffb * 128
                    sv = svs[i]
                    mwait = [("TEN", sv)]
                    jd = kt - nqsub * qt
                    if causal and 0 <= jd < nqsub:
                        mv = self.op("vector", "tensor_add",
                                     self.pb[bank][:, jd * 128:jd * 128 + 128],
                                     self.pb[bank][:, jd * 128:jd * 128 + 128],
                                     self.v(self.C_OVL + 128, 128),
                                     waits=[("TEN", sv)])
                        mwait = [("VEC", mv)]
                    ew = list(mwait)
                    if i == 0:
                        ew.append(("VEC", bound_wait))
                    if i <= 1 and last_at:
                        ew.append(("VEC", last_at))
                    if par in prev_pv:
                        ew.append(("TEN", prev_pv[par]))
                    pexp = self.op("scalar", "activation",
                                   self.v(pslot[par], n_q),
                                   self.pb[bank][:, qoffb * 128:TT], AF.Exp,
                                   bias=negb, scale=0.125, waits=ew)
                    self.bank_free[bank] = ("SCA", pexp)
                    self.phase = base + "|PV"
                    last_pv = self.mm(
                        self.pb[2][0:65, qoffb * 128:TT],
                        self.hap(VG + kt * 130 + 65 * hl, [[1, 65]]),
                        self.v(pslot[par], n_q),
                        i == 0, i == len(kts) - 1,
                        waits=[("SCA", pexp)], inc=True)
                    prev_pv[par] = last_pv
                    if i + nbank < len(kts):
                        svs[i + nbank] = emit_S(i + nbank)
                u1 = self.op("scalar", "activation", self.v(SP, TT, 0, 1),
                             self.pb[2][drow:drow + 1, 0:TT], AF.Ln,
                             waits=[("TEN", last_pv)])
                u2 = self.op("scalar", "activation", self.v(SP, TT, 0, 1),
                             self.v(SP, TT, 0, 1), AF.Exp, scale=-1.0)
                self.t_bank(3)
                self.phase = base + "|norm"
                bc = self.mm(self.pb[3][0:64, 0:TT],
                             self.heap[0:1, self.C_ONES:self.C_ONES + 64],
                             self.v(SP, TT, 0, 1), True, True,
                             waits=[("SCA", u2)], inc=True)
                cpv = self.op("vector", "tensor_copy", self.v(SP + 512, TT, 0, 64),
                              self.pb[3][0:64, 0:TT], waits=[("TEN", bc)])
                at_done = self.op("vector", "scalar_tensor_tensor",
                                  self.heap[0:64, AT + hl * TT:AT + (hl + 1) * TT],
                                  self.pb[2][vrow0:vrow0 + 64, 0:TT], 1.0,
                                  self.v(SP + 512, TT, 0, 64),
                                  ALU.mult, ALU.mult, waits=[("VEC", cpv)])
                self.bank_free[2] = ("VEC", at_done)
                self.bank_free[3] = ("VEC", cpv)
                last_at = at_done
                self.attn_guard = [("TEN", last_pv), ("SCA", u2)]
            self.phase = base + "|WO"
            for oc in range(8):
                bank = 4 + (oc % 4)
                self.t_bank(bank)
                for hl in range(2):
                    w0 = []
                    if oc == 0 and hl == 0:
                        w0 = [("VEC", at_done)] + list(wo_ld)
                    mv = self.mm(self.pb[bank][:, 0:TT],
                                 self.heap[0:64, wo_cols[hl] + oc * 128:
                                           wo_cols[hl] + oc * 128 + 128],
                                 self.heap[0:64, AT + hl * TT:AT + (hl + 1) * TT],
                                 hl == 0, hl == 1, waits=w0, inc=(hl == 1))
                av = self.op("vector", "tensor_add",
                             self.v(XS + oc * T + qt * TT, TT),
                             self.v(XS + oc * T + qt * TT, TT),
                             self.pb[bank][:, 0:TT], waits=[("TEN", mv)])
                self.bank_free[bank] = ("VEC", av)

    def rope(self, off, lc1, lcs, src_ready):
        TT = self.TT
        SP, ST = self.OFF["SP"], self.OFF["ST"]
        self.t_bank(4)
        mv = self.mm(self.pb[4][:, 0:TT], self.v(self.C_OVL, 128),
                     self.v(off, TT), True, True, waits=[("VEC", src_ready)],
                     inc=True)
        tmp = self.v(ST, TT)
        v1 = self.op("vector", "tensor_mul", tmp, self.pb[4][:, 0:TT],
                     self.v(SP + 512, TT), waits=[("TEN", mv), lcs])
        self.bank_free[4] = ("VEC", v1)
        v2 = self.op("gpsimd", "tensor_mul", self.v(off, TT), self.v(off, TT),
                     self.v(SP, TT),
                     waits=[lc1, ("VEC", src_ready), ("TEN", mv)])
        v3 = self.op("vector", "tensor_add", self.v(off, TT), self.v(off, TT),
                     tmp, waits=[("GPS", v2)])
        return v3

    def emit_l0a(self, s):
        self.phase = "l0a.front"
        T, TT, NTT, NKT = self.T, self.TT, self.NTT, self.NKT
        XS, H1, R2, WB, SP = (self.OFF[k] for k in
                              ("XSB", "H1", "R2", "WB", "SP"))
        for c in range(8):
            lv = self.dma(self.v(XS + c * T, T),
                          self.xin.ap()[s, c * 128:(c + 1) * 128, :])
        self.dma(self.v(self.C_OVL, 128), self.d["pm0T"].ap())
        lov = self.dma(self.v(self.C_OVL + 128, 128), self.d["mask"].ap())
        wall = [(f"LD{i}", self.sem[f"LD{i}"].n) for i in range(8)]
        for eng in ("scalar", "vector", "tensor", "gpsimd"):
            for w in wall:
                self.wait(eng, *w)
        self.barrier()
        self.preln(H1, R2)
        LAT = R2
        SQ2 = R2 + 2 * T
        ldkv = self.dma(self.v(WB, 2048), self.d["w0dkv"].ap())
        for tt in range(NTT):
            t0 = tt * TT
            rd = self.ready()
            for lc in range(2):
                self.t_bank(lc)
                for c in range(8):
                    self.mm(self.pb[lc][:, 0:TT],
                            self.v(WB + c * 256 + lc * 128, 128),
                            self.v(H1 + c * T + t0, TT), c == 0, c == 7,
                            waits=(rd + [ldkv]) if (c == 0 and lc == 0) else ())
                ev = self.op("vector", "tensor_copy", self.v(LAT + lc * T + t0, TT),
                             self.pb[lc][:, 0:TT], waits=[("TEN", self.tmark())])
                self.bank_free[lc] = ("VEC", ev)
            lat_ev = ev
            lsq = self.op("scalar", "activation",
                          self.hap(SQ2, [[TT, 2], [1, TT]]),
                          self.hap(LAT + t0, [[T, 2], [1, TT]]),
                          AF.Square, waits=[("VEC", self.sem["VEC"].n)])
            for b_, off, step, w in ((2, LAT + t0, T, None), (3, SQ2, TT, lsq)):
                self.t_bank(b_)
                for lc in range(2):
                    self.mm(self.pb[b_][:, 0:TT], self.v(self.C_ONES, 128),
                            self.v(off + lc * step, TT), lc == 0, lc == 1,
                            waits=([("SCA", w)] if (w and lc == 0) else
                                   ([("VEC", lat_ev)] if lc == 0 else ())))
            vsum = self.tmark()
            ST = self.OFF["ST"]
            meanB = self.v(ST, TT)
            rsd = self.v(ST + 512, TT)
            self.op("scalar", "activation", meanB, self.pb[2][:, 0:TT], AF.Copy,
                    scale=1.0 / LATD, waits=[("TEN", vsum)])
            e2 = self.op("scalar", "activation", rsd, self.pb[3][:, 0:TT],
                         AF.Copy, scale=1.0 / LATD)
            m2 = self.op("vector", "tensor_mul", self.pb[2][:, 0:TT], meanB, meanB,
                         waits=[("SCA", e2)])
            v2 = self.op("vector", "tensor_sub", rsd, rsd, self.pb[2][:, 0:TT],
                         waits=[("VEC", m2)])
            self.op("scalar", "activation", rsd, rsd, AF.Ln, bias=self.v(self.M_EPS, 1),
                    waits=[("VEC", v2)])
            r2 = self.op("scalar", "activation", rsd, rsd, AF.Exp, scale=-0.5)
            self.bank_free[2] = ("VEC", v2)
            self.bank_free[3] = ("SCA", e2)
            for lc in range(2):
                la = self.v(LAT + lc * T + t0, TT)
                self.op("vector", "tensor_sub", la, la, meanB,
                        waits=([("SCA", r2)] if lc == 0 else ()))
                mv = self.op("vector", "tensor_mul", la, la, rsd)
                self.op("scalar", "activation", la, la, AF.Identity,
                        scale=self.v(self.M_G0 + lc, 1),
                        bias=self.v(self.M_B0 + lc, 1), waits=[("VEC", mv)])
        self.barrier()
        QG = R2 + 2 * T
        KG = QG + T
        VG = KG + T
        for g in range(8):
            self._l0a_group(g, LAT, QG, KG, VG)
        self.barrier()

    def _l0a_group(self, g, LAT, QG, KG, VG):
        self.phase = "l0a.proj"
        T, TT, NTT, NKT = self.T, self.TT, self.NTT, self.NKT
        H1, WB, SP = self.OFF["H1"], self.OFF["WB"], self.OFF["SP"]
        lq = self.dma(self.v(WB, 1024), self.d["w0q"].ap()[g])
        luk = self.dma(self.v(WB + 1024, 256), self.d["w0uk"].ap()[g])
        luv = self.dma(self.v(WB + 1280, 256), self.d["w0uv"].ap()[g])
        lwo = self.dma(self.heap[0:64, WB + 1536:WB + 2560],
                       self.d["w0o"].ap()[g * 128:g * 128 + 64, :])
        rd = self.ready()
        SCRQ = self.OFF["R2"] + 4 * T + 2080
        SCRK = SCRQ + 512
        for tt in range(NTT):
            t0 = tt * TT
            qb = 6 + (tt % 2)
            kb = 7 - (tt % 2)
            lc1 = self.dma(self.v(SP, TT), self.d["ropeC"].ap()[:, t0:t0 + TT],
                           after=("VEC", self.rope_done) if getattr(self, "rope_done", 0) else None)
            lcs = self.dma(self.v(SP + 512, TT), self.d["ropeS"].ap()[:, t0:t0 + TT])
            self.t_bank(qb)
            for c in range(8):
                self.mm(self.pb[qb][:, 0:TT], self.v(WB + c * 128, 128),
                        self.v(H1 + c * T + t0, TT), c == 0, c == 7,
                        waits=(rd + [lq]) if (c == 0 and tt == 0) else ())
            qw = [("TEN", self.tmark())]
            if tt > 0:
                qw.append(("SCA", self._last_sq))
            qv = self.op("vector", "tensor_copy", self.v(SCRQ, TT),
                         self.pb[qb][:, 0:TT], waits=qw)
            self.bank_free[qb] = ("VEC", qv)
            rq = self.rope(SCRQ, lc1, lcs, qv)
            self.bound_tile(SCRQ, self.M_MQ, tt == 0, rq)
            self.split(QG + t0, TT, self.v(SCRQ, TT))
            self.t_bank(kb)
            for lc in range(2):
                self.mm(self.pb[kb][:, 0:TT], self.v(WB + 1024 + lc * 128, 128),
                        self.v(LAT + lc * T + t0, TT), lc == 0, lc == 1,
                        waits=[luk] if lc == 0 else ())
            kw = [("TEN", self.tmark()), ("SCA", self._last_sq)]
            kv = self.op("vector", "tensor_copy", self.v(SCRK, TT),
                         self.pb[kb][:, 0:TT], waits=kw)
            self.bank_free[kb] = ("VEC", kv)
            rk = self.rope(SCRK, lc1, lcs, kv)
            self.rope_done = self.bound_tile(SCRK, self.M_MK, tt == 0, rk)
            self.split(KG + t0, TT, self.v(SCRK, TT))
        lwo2 = self.dma(self.heap[0:64, WB:WB + 1024],
                        self.d["w0o"].ap()[g * 128 + 64:g * 128 + 128, :],
                        after=("TEN", self.tmark()))
        bv = self.finish_bound()
        mz = self.op("vector", "memset",
                     self.hap(VG + 64, [[130, NKT], [65, 2]]), 1.0)
        for kt in range(NKT):
            vb_ = 6 + (kt % 2)
            self.t_bank(vb_)
            for lc in range(2):
                self.mm(self.pb[vb_][:, 0:128],
                        self.v(LAT + lc * T + kt * 128, 128),
                        self.v(WB + 1280 + lc * 128, 128),
                        lc == 0, lc == 1,
                        waits=[luv] if (kt == 0 and lc == 0) else ())
            ev = self.op("vector", "tensor_copy",
                         self.hap(VG + kt * 130, [[65, 2], [1, 64]]),
                         self.pb[vb_][:, 0:128],
                         waits=[("TEN", self.tmark()), ("VEC", mz)])
            self.bank_free[vb_] = ("VEC", ev)
        self.attn_group(QG, KG, VG, True, bv, (WB + 1536, WB), (lwo, lwo2),
                        s_pair=True)
        self.barrier()
        self.rope_done = 0

    def emit_l0f(self, s):
        lec = self.dma(self.v(self.C_OVL, 512, 0, 8), self.d["ec"].ap())
        self.wait("tensor", *lec)
        for tt in range(self.NTT):
            self._l0f_tile(tt)
        self.barrier()

    def ln_tile_pair(self, src_off, dst_pair, tt, scr_off):
        self.phase = "l0f.ln"
        """LayerNorm of one t-tile of chunked fp32 src into a bf16 PAIR at
        dst_pair (+ c*TT per chunk).  scr_off: 8*TT fp32 scratch."""
        T, TT = self.T, self.TT
        ST = self.OFF["ST"]
        t0 = tt * TT
        meanB = self.v(ST, TT)
        rsd = self.v(ST + 512, TT)
        src3 = self.hap(src_off + t0, [[T, 8], [1, TT]])
        dst3 = self.hap(scr_off, [[TT, 8], [1, TT]])
        sq = self.op("scalar", "activation", dst3, src3, AF.Square,
                     waits=self.ready())
        for b_, off, step, w in ((0, src_off + t0, T, None),
                                 (1, scr_off, TT, sq)):
            self.t_bank(b_)
            for c in range(8):
                self.mm(self.pb[b_][:, 0:TT], self.v(self.C_ONES, 128),
                        self.v(off + c * step, TT), c == 0, c == 7,
                        waits=([("SCA", w)] if (w and c == 0) else ()))
        vsum = self.tmark()
        self.op("scalar", "activation", meanB, self.pb[0][:, 0:TT], AF.Copy,
                scale=1.0 / D, waits=[("TEN", vsum)])
        e2 = self.op("scalar", "activation", rsd, self.pb[1][:, 0:TT], AF.Copy,
                     scale=1.0 / D)
        m2 = self.op("vector", "tensor_mul", self.pb[0][:, 0:TT], meanB, meanB,
                     waits=[("SCA", e2)])
        v2 = self.op("vector", "tensor_sub", rsd, rsd, self.pb[0][:, 0:TT],
                     waits=[("VEC", m2)])
        self.op("scalar", "activation", rsd, rsd, AF.Ln,
                bias=self.v(self.M_EPS, 1), waits=[("VEC", v2)])
        r2 = self.op("scalar", "activation", rsd, rsd, AF.Exp, scale=-0.5)
        self.bank_free[0] = ("VEC", v2)
        self.bank_free[1] = ("SCA", e2)
        gv = None
        for c in range(8):
            xc = self.v(src_off + c * T + t0, TT)
            uc = self.v(scr_off + c * TT, TT)
            sv = self.op("vector", "tensor_sub", uc, xc, meanB,
                         waits=([("SCA", r2)] if c == 0 else ()))
            gm = self.op("gpsimd", "tensor_mul", uc, uc, rsd,
                         waits=[("VEC", sv)])
            gv = self.split(dst_pair + c * TT, TT, uc, waits=[("GPS", gm)])
        return gv

    def _mmblk(self, dst_off, rhs_pair, wname, nblk, nk, blkelems, stage,
               extra=(), evac="pair", evac_args=None):
        TT = self.TT
        WB = self.OFF["WB"]
        self.phase = self.phase.split("|")[0] + "|mb." + wname
        evs = []
        for ob in range(nblk):
            sl = ob % 2
            after = ("TEN", self.wlast[sl]) if self.wlast[sl] else None
            wsv = self.load_pair(WB + sl * 1024, blkelems,
                                 self.d[wname].ap()[ob], stage, after=after)
            bank = 4 + sl
            self.t_bank(bank)
            w0 = (list(extra) if ob == 0 else []) + [("VEC", wsv)]
            for c in range(nk):
                self.mm3(self.pb[bank][:, 0:TT], WB + sl * 1024, blkelems, c,
                         rhs_pair + c * TT, TT, c == 0, c == nk - 1,
                         waits=w0 if c == 0 else ())
            self.wlast[sl] = self.tmark()
            mv = self.tmark()
            if evac == "pair":
                ev = self.split(dst_off + ob * TT, TT, self.pb[bank][:, 0:TT],
                                waits=[("TEN", mv)])
                self.bank_free[bank] = ("VEC", ev)
            elif evac == "copy":
                ev = self.op("vector", "tensor_copy",
                             self.v(dst_off + ob * TT, TT),
                             self.pb[bank][:, 0:TT], waits=[("TEN", mv)])
                self.bank_free[bank] = ("VEC", ev)
            elif evac == "gelu":
                ev = self.op("scalar", "activation",
                             self.v(dst_off + ob * TT, TT),
                             self.pb[bank][:, 0:TT], AF.Gelu,
                             bias=self.v(evac_args + ob, 1), waits=[("TEN", mv)])
                self.bank_free[bank] = ("SCA", ev)
            evs.append(ev)
        return evs

    def _l0f_tile(self, tt):
        T, TT = self.T, self.TT
        XS, H1, R2, WB, SP, ST = (self.OFF[k] for k in
                                  ("XSB", "H1", "R2", "WB", "SP", "ST"))
        t0 = tt * TT
        XNP = R2
        HH = R2 + 8 * TT
        XV = R2 + 16 * TT
        YW = R2 + 20 * TT
        CU = R2 + 24 * TT
        BF = R2 + 1024
        QQ, KK, IQ, IK = H1, H1 + 8 * TT, H1 + 16 * TT, H1 + 24 * TT
        GF, GI = KK, QQ
        CMB = IQ
        WF, WI = ST, ST + 512
        stageA = HH
        stageB = XNP
        self.wlast = [None, None]
        lrw2 = self.dma(self.v(WB + 2048, 64), self.d["wfr2"].ap()[0])
        lrb1 = self.dma(self.v(WB + 2112, 8), self.d["rb1"].ap())
        lnv = self.dma(self.v(SP, TT), self.d["nvinv"].ap()[:, t0:t0 + TT])
        gv = self.ln_tile_pair(XS, XNP, tt, HH)
        self.phase = "l0f"
        self._stage_busy[stageA] = ("VEC", gv)
        self._mmblk(QQ, XNP, "wfQ", 8, 8, 1024, stageA,
                    extra=[("VEC", gv), ("GPS", self.sem["GPS"].n)])
        self._mmblk(KK, XNP, "wfK", 8, 8, 1024, stageA)
        self._mmblk(IQ, QQ, "wfI", 8, 8, 1024, stageA)
        self._mmblk(IK, KK, "wfI", 8, 8, 1024, stageA)
        self._stage_busy[stageB] = ("TEN", self.tmark())
        self.phase = "l0f|rtr"
        for srcq, wdst in ((QQ, WF), (IQ, WI)):
            self._mmblk(HH, srcq, "wfr1", 8, 8, 1024, stageB,
                        evac="gelu", evac_args=WB + 2112)
            self.wait("scalar", *lrb1)
            self.t_bank(6)
            rd = self.ready()
            for c in range(8):
                self.mm(self.pb[6][0:8, 0:TT], self.v(WB + 2048 + c * 8, 8),
                        self.v(HH + c * TT, TT), c == 0, c == 7,
                        waits=(rd + [lrw2]) if c == 0 else ())
            lg = self.op("scalar", "activation", self.v(wdst, TT, 0, 8),
                         self.pb[6][0:8, 0:TT], AF.Identity,
                         bias=self.v(self.M_RB2EB, 1, 0, 8),
                         waits=[("TEN", self.tmark())])
            self.bank_free[6] = ("SCA", lg)
            wtm = self.v(SP + 512, 128)
            for j in range(TT // 128):
                self.t_bank(7)
                v1 = self.tp(self.pb[7][0:128, 0:8],
                             self.heap[0:8, wdst + j * 128:wdst + (j + 1) * 128],
                             waits=[("SCA", lg)], kdim=8)
                nm = self.op("vector", "tensor_reduce", self.v(self.M_SC1, 1),
                             self.pb[7][:, 0:8], AX.X, ALU.max, negate=True,
                             waits=[("TEN", v1)])
                e1 = self.op("scalar", "activation", wtm[:, 0:8],
                             self.pb[7][:, 0:8], AF.Exp,
                             bias=self.v(self.M_SC1, 1), waits=[("VEC", nm)])
                self.op("vector", "reduce_sum", self.v(self.M_SC1, 1),
                        wtm[:, 0:8], AX.X, waits=[("SCA", e1)])
                self.op("vector", "reciprocal", self.v(self.M_SC2, 1),
                        self.v(self.M_SC1, 1))
                m1 = self.op("vector", "tensor_scalar_mul", wtm[:, 0:8],
                             wtm[:, 0:8], self.v(self.M_SC2, 1))
                self.bank_free[7] = ("VEC", m1)
                self.t_bank(7)
                v2 = self.tp(self.pb[7][0:8, 0:128], wtm[:, 0:8],
                             waits=[("VEC", m1)])
                cv = self.op("vector", "tensor_copy",
                             self.heap[0:8, wdst + j * 128:wdst + (j + 1) * 128],
                             self.pb[7][0:8, 0:128], waits=[("TEN", v2)])
                self.bank_free[7] = ("VEC", cv)
        routers_done = self.tmark()
        lbf = self.dma(self.v(BF, 1024, 0, 8), self.d["bfw"].ap(),
                       after=("TEN", routers_done))
        lbi = self.dma(self.v(BF + 1024, 1024, 0, 8), self.d["bin"].ap())
        gps_prev = 0
        for br, (qsrc, ksrc, vn, wn, un, boff, wrow, c0) in enumerate(
                ((QQ, KK, "vfw", "wfw", "ufw", BF, WF, 0),
                 (IQ, IK, "vin", "win", "uin", BF + 1024, WI, 4))):
            self._mmblk(XV, qsrc, vn, 4, 8, 1024, stageB, evac="copy",
                        extra=[("TEN", self.tmark())])
            self._mmblk(YW, ksrc, wn, 4, 8, 1024, stageB, evac="copy",
                        extra=[("TEN", self.tmark())])
            self.phase = "l0f|scan"
            for krb in range(4):
                cc = self.v(self.M_CARRY + c0 + krb, 1)
                ini = 0.0 if tt == 0 else cc
                self.op("vector", "tensor_tensor_scan",
                        self.v(CU + krb * TT, TT), self.v(YW + krb * TT, TT),
                        self.v(YW + krb * TT, TT), ini, ALU.add, ALU.bypass,
                        waits=[("TEN", routers_done), ("GPS", gps_prev)])
                self.op("vector", "tensor_copy", cc,
                        self.v(CU + krb * TT + TT - 1, 1))
            for krb in range(4):
                p1 = self.op("gpsimd", "tensor_mul", self.v(XV + krb * TT, TT),
                             self.v(XV + krb * TT, TT), self.v(CU + krb * TT, TT),
                             waits=[("VEC", self.sem["VEC"].n)])
                gps_prev = p1
                self.t_bank(6)
                mv = self.mm(self.pb[6][:, 0:TT],
                             self.heap[0:8, self.C_OVL + krb * 128:
                                       self.C_OVL + (krb + 1) * 128],
                             self.v(wrow, TT, 0, 8), True, True,
                             waits=[("VEC", self.sem["VEC"].n)], inc=True)
                p2 = self.op("vector", "tensor_mul", self.v(XV + krb * TT, TT),
                             self.v(XV + krb * TT, TT), self.pb[6][:, 0:TT],
                             waits=[("TEN", mv), ("GPS", p1)])
                self.bank_free[6] = ("VEC", p2)
                self.split(YW + krb * TT, TT, self.v(XV + krb * TT, TT))
            pv = self.sem["VEC"].n
            gdst = GF if br == 0 else GI
            self.phase = "l0f|U"
            for oc in range(8):
                sl = oc % 2
                after = ("TEN", self.wlast[sl]) if self.wlast[sl] else None
                wsv = self.load_pair(WB + sl * 1024, 512, self.d[un].ap()[oc],
                                     stageB, after=after)
                bank = 4 + sl
                self.t_bank(bank)
                for krc in range(4):
                    self.mm3(self.pb[bank][:, 0:TT], WB + sl * 1024, 512, krc,
                             YW + krc * TT, TT, krc == 0, False,
                             waits=[("VEC", wsv), ("VEC", pv), lbf, lbi]
                             if krc == 0 else ())
                self.mm(self.pb[bank][:, 0:TT],
                        self.heap[0:8, boff + oc * 128:boff + oc * 128 + 128],
                        self.v(wrow, TT, 0, 8), False, True)
                self.wlast[sl] = self.tmark()
                ev = self.op("vector", "tensor_copy", self.v(gdst + oc * TT, TT),
                             self.pb[bank][:, 0:TT],
                             waits=[("TEN", self.tmark())])
                self.bank_free[bank] = ("VEC", ev)
        cvs = None
        for oc in range(8):
            self.op("vector", "scalar_tensor_tensor", self.v(GF + oc * TT, TT),
                    self.v(GI + oc * TT, TT), self.v(self.M_ALPHA, 1),
                    self.v(GF + oc * TT, TT), ALU.mult, ALU.add)
            cvs = self.split(CMB + oc * TT, TT, self.v(GF + oc * TT, TT))
        self.phase = "l0f|WO"
        for oc in range(8):
            sl = oc % 2
            after = ("TEN", self.wlast[sl]) if self.wlast[sl] else None
            wsv = self.load_pair(WB + sl * 1024, 1024, self.d["wfO"].ap()[oc],
                                 stageB, after=after)
            bank = 4 + sl
            self.t_bank(bank)
            for c in range(8):
                self.mm3(self.pb[bank][:, 0:TT], WB + sl * 1024, 1024, c,
                         CMB + c * TT, TT, c == 0, c == 7,
                         waits=[("VEC", wsv), ("VEC", cvs)] if c == 0 else ())
            self.wlast[sl] = self.tmark()
            mv = self.tmark()
            e1 = self.op("vector", "tensor_mul", self.v(GI + oc * TT, TT),
                         self.pb[bank][:, 0:TT], self.v(SP, TT),
                         waits=[("TEN", mv), lnv])
            self.op("vector", "tensor_add", self.v(XS + oc * T + t0, TT),
                    self.v(XS + oc * T + t0, TT), self.v(GI + oc * TT, TT))
            self.bank_free[bank] = ("VEC", e1)
        self.barrier()
        self._stage_busy = {}

    def emit_l1a(self, s):
        LATK, LATV, QG, KG, VG = self._l1a_front(s)
        for g in range(8):
            self._l1a_group(g, LATK, LATV, QG, KG, VG)
        self.barrier()

    def _l1a_front(self, s):
        self.phase = "l1a.front"
        T, TT, NTT, NKT = self.T, self.TT, self.NTT, self.NKT
        XS, H1, R2, WB = (self.OFF[k] for k in ("XSB", "H1", "R2", "WB"))
        self.preln(H1, R2)
        LATK, LATV = R2, R2 + 2 * T
        for wn, dsto in (("w1ka", LATK), ("w1va", LATV)):
            lv = self.dma(self.v(WB, 2048), self.d[wn].ap()[0])
            rd = self.ready()
            for tt in range(NTT):
                t0 = tt * TT
                for lc in range(2):
                    self.t_bank(lc)
                    for c in range(8):
                        self.mm(self.pb[lc][:, 0:TT],
                                self.v(WB + c * 256 + lc * 128, 128),
                                self.v(H1 + c * T + t0, TT), c == 0, c == 7,
                                waits=(rd + [lv]) if (c == 0 and tt == 0 and lc == 0) else ())
                    ev = self.op("vector", "tensor_copy",
                                 self.v(dsto + lc * T + t0, TT),
                                 self.pb[lc][:, 0:TT],
                                 waits=[("TEN", self.tmark())])
                    self.bank_free[lc] = ("VEC", ev)
            self.barrier()
        QG = R2 + 4 * T
        KG = QG + T
        VG = KG + T
        return LATK, LATV, QG, KG, VG

    def _l1a_group(self, g, LATK, LATV, QG, KG, VG):
        self.phase = "l1a.proj"
        T, TT, NTT, NKT = self.T, self.TT, self.NTT, self.NKT
        H1, WB = self.OFF["H1"], self.OFF["WB"]
        lq = self.dma(self.v(WB, 1024), self.d["w1q"].ap()[g])
        lkb = self.dma(self.v(WB + 1024, 256), self.d["w1kb"].ap()[g])
        lvb = self.dma(self.v(WB + 1280, 256), self.d["w1vb"].ap()[g])
        lwo = self.dma(self.heap[0:64, WB + 1536:WB + 2560],
                       self.d["w1o"].ap()[g * 128:g * 128 + 64, :])
        rd = self.ready()
        SCQ = self.C_OVL
        SCK = self.OFF["SP"]
        ag = getattr(self, "attn_guard", [])
        for tt in range(NTT):
            t0 = tt * TT
            qb = 6 + (tt % 2)
            kb = 7 - (tt % 2)
            self.t_bank(qb)
            for c in range(8):
                self.mm(self.pb[qb][:, 0:TT], self.v(WB + c * 128, 128),
                        self.v(H1 + c * T + t0, TT), c == 0, c == 7,
                        waits=(rd + [lq]) if (c == 0 and tt == 0) else ())
            qw = [("TEN", self.tmark())] + (list(ag) if tt == 0 else [])
            if tt > 0:
                qw.append(("SCA", self._last_sq))
            qv = self.op("vector", "tensor_copy", self.v(SCQ, TT),
                         self.pb[qb][:, 0:TT], waits=qw)
            self.bank_free[qb] = ("VEC", qv)
            self.bound_tile(SCQ, self.M_MQ, tt == 0, qv)
            self.split(QG + t0, TT, self.v(SCQ, TT))
            self.t_bank(kb)
            for lc in range(2):
                self.mm(self.pb[kb][:, 0:TT], self.v(WB + 1024 + lc * 128, 128),
                        self.v(LATK + lc * T + t0, TT), lc == 0, lc == 1,
                        waits=[lkb] if lc == 0 else ())
            kw = [("TEN", self.tmark())] + (list(ag) if tt == 0 else [])
            kw.append(("SCA", self._last_sq))
            kv = self.op("vector", "tensor_copy", self.v(SCK, TT),
                         self.pb[kb][:, 0:TT], waits=kw)
            self.bank_free[kb] = ("VEC", kv)
            self.bound_tile(SCK, self.M_MK, tt == 0, kv)
            self.split(KG + t0, TT, self.v(SCK, TT))
        lwo2 = self.dma(self.heap[0:64, WB:WB + 1024],
                        self.d["w1o"].ap()[g * 128 + 64:g * 128 + 128, :],
                        after=("TEN", self.tmark()))
        bv = self.finish_bound()
        mz = self.op("vector", "memset",
                     self.hap(VG + 64, [[130, NKT], [65, 2]]), 1.0)
        for kt in range(NKT):
            vb_ = 6 + (kt % 2)
            self.t_bank(vb_)
            for lc in range(2):
                self.mm(self.pb[vb_][:, 0:128],
                        self.v(LATV + lc * T + kt * 128, 128),
                        self.v(WB + 1280 + lc * 128, 128),
                        lc == 0, lc == 1,
                        waits=[lvb] if (kt == 0 and lc == 0) else ())
            ev = self.op("vector", "tensor_copy",
                         self.hap(VG + kt * 130, [[65, 2], [1, 64]]),
                         self.pb[vb_][:, 0:128],
                         waits=[("TEN", self.tmark()), ("VEC", mz)])
            self.bank_free[vb_] = ("VEC", ev)
        self.attn_group(QG, KG, VG, False, bv, (WB + 1536, WB), (lwo, lwo2),
                        s_pair=True)
        self.barrier()

    def emit_l1f(self, s):
        self.phase = "l1f"
        T, TT, NTT = self.T, self.TT, self.NTT
        XS, H1, R2, WB, SP = (self.OFF[k] for k in
                              ("XSB", "H1", "R2", "WB", "SP"))
        MOFF = R2
        if s == 0:
            AO, BO = H1, H1 + 8192
            w1last = None
            for ob in range(8):
                lwr = self.dma(self.v(WB, 1024), self.d["wrB"].ap()[ob],
                               after=("TEN", w1last) if w1last else None)
                lwi = self.dma(self.v(WB + 1024, 1024), self.d["wiB"].ap()[ob])
                for b in range(6):
                    self.t_bank(b)
                for nt in range(2):
                    for c in range(8):
                        lc_ = self.dma(
                            self.v(SP, 512),
                            self.d["dftC"].ap()[c * 128:(c + 1) * 128,
                                                nt * 512:(nt + 1) * 512],
                            after=("TEN", self.tmark())
                            if (c + nt + ob) else None)
                        self.mm(self.pb[nt][:, 0:512], self.v(WB + c * 128, 128),
                                self.v(SP, 512), c == 0, False,
                                waits=[lc_, lwr] if c == 0 else [lc_])
                        self.mm(self.pb[2 + nt][:, 0:512],
                                self.v(WB + 1024 + c * 128, 128),
                                self.v(SP, 512), c == 0, c == 7,
                                waits=[lwi] if c == 0 else ())
                        ls_ = self.dma(
                            self.v(SP + 512, 512),
                            self.d["dftS"].ap()[c * 128:(c + 1) * 128,
                                                nt * 512:(nt + 1) * 512])
                        self.mm(self.pb[nt][:, 0:512],
                                self.v(WB + 1024 + c * 128, 128),
                                self.v(SP + 512, 512), False, c == 7,
                                waits=[ls_])
                        self.mm(self.pb[4 + nt][:, 0:512], self.v(WB + c * 128, 128),
                                self.v(SP + 512, 512), c == 0, c == 7)
                    w1last = self.tmark()
                    av = self.op("vector", "tensor_copy",
                                 self.v(AO + ob * 1024 + nt * 512, 512),
                                 self.pb[nt][:, 0:512], waits=[("TEN", w1last)])
                    self.op("vector", "tensor_copy",
                            self.v(BO + ob * 1024 + nt * 512, 512),
                            self.pb[2 + nt][:, 0:512])
                    bv = self.op("vector", "tensor_sub",
                                 self.v(BO + ob * 1024 + nt * 512, 512),
                                 self.v(BO + ob * 1024 + nt * 512, 512),
                                 self.pb[4 + nt][:, 0:512])
                    self.bank_free[nt] = ("VEC", av)
                    self.bank_free[2 + nt] = ("VEC", bv)
                    self.bank_free[4 + nt] = ("VEC", bv)
            self.barrier()
            for nb in range(8):
                for mt in range(2):
                    self.t_bank(0)
                    self.t_bank(1)
                    for oc in range(8):
                        lc_ = self.dma(
                            self.v(SP, 512),
                            self.d["dftC"].ap()[oc * 128:(oc + 1) * 128,
                                                mt * 512:(mt + 1) * 512],
                            after=("TEN", self.tmark())
                            if (oc + mt + nb) else None)
                        self.mm(self.pb[0][:, 0:512],
                                self.v(AO + oc * 1024 + nb * 128, 128),
                                self.v(SP, 512), oc == 0, oc == 7,
                                waits=[lc_])
                        ls_ = self.dma(
                            self.v(SP + 512, 512),
                            self.d["dftS"].ap()[oc * 128:(oc + 1) * 128,
                                                mt * 512:(mt + 1) * 512])
                        self.mm(self.pb[1][:, 0:512],
                                self.v(BO + oc * 1024 + nb * 128, 128),
                                self.v(SP + 512, 512), oc == 0, oc == 7,
                                waits=[ls_])
                    mv = self.tmark()
                    c1 = self.op("vector", "tensor_copy",
                                 self.v(MOFF + nb * 1024 + mt * 512, 512),
                                 self.pb[0][:, 0:512], waits=[("TEN", mv)])
                    sv = self.op("vector", "tensor_sub",
                                 self.v(MOFF + nb * 1024 + mt * 512, 512),
                                 self.v(MOFF + nb * 1024 + mt * 512, 512),
                                 self.pb[1][:, 0:512])
                    ev = self.op("scalar", "activation",
                                 self.v(MOFF + nb * 1024 + mt * 512, 512),
                                 self.v(MOFF + nb * 1024 + mt * 512, 512),
                                 AF.Copy, scale=1.0 / 1024.0,
                                 waits=[("VEC", sv)])
                    self.bank_free[0] = ("VEC", c1)
                    self.bank_free[1] = ("VEC", sv)
            self.wait("sync", "SCA", self.sem["SCA"].n)
            self.dma(self.mcache.ap()[:, :], self.v(MOFF, 8192))
            self.barrier()
        else:
            lv = self.dma(self.v(MOFF, 8192), self.mcache.ap()[:, :])
            self.wait("tensor", *lv)
            self.barrier()
        XN = R2 + 8192
        for tt in range(NTT):
            t0 = tt * TT
            gv = self.ln_tile(XS, XN, tt, 1.0 / D, True, XN)
            prev_e2 = [0, 0]
            for oc in range(8):
                bank = 2 + (oc % 4)
                self.t_bank(bank)
                for c in range(8):
                    self.mm(self.pb[bank][:, 0:TT],
                            self.v(MOFF + c * 1024 + oc * 128, 128),
                            self.v(XN + c * TT, TT), c == 0, c == 7,
                            waits=[("GPS", gv)] if (c == 0 and oc == 0) else ())
                mv = self.tmark()
                sl = oc % 2
                e1 = self.op("scalar", "activation", self.v(SP + sl * 512, TT),
                             self.pb[bank][:, 0:TT], AF.Identity,
                             bias=self.v(self.M_BIAS1 + oc, 1),
                             waits=[("TEN", mv), ("VEC", prev_e2[sl])])
                e2 = self.op("vector", "tensor_add",
                             self.v(XS + oc * T + t0, TT),
                             self.v(XS + oc * T + t0, TT),
                             self.v(SP + sl * 512, TT),
                             waits=[("SCA", e1)])
                prev_e2[sl] = e2
                self.bank_free[bank] = ("VEC", e2)
        self.barrier()

    def emit_init(self):
        self.dma(self.v(self.C_ONES, 128), self.d["ones"].ap())
        self.dma(self.v(self.C_ID, 128), self.d["id"].ap())
        self.dma(self.v(self.M_RB2EB, 1, 0, 8), self.d["rb2"].ap())
        self.dma(self.v(self.M_G0, 2), self.d["g0"].ap())
        self.dma(self.v(self.M_B0, 2), self.d["b0"].ap())
        self.dma(self.v(self.M_BIAS1, 8), self.d["bias1"].ap())
        self.dma(self.v(self.M_ALPHA1, 1, 0, 1), self.d["alpha"].ap())
        self.op("vector", "memset", self.v(self.M_EPS, 1), 1e-5)
        self.op("vector", "memset", self.v(self.M_LN18, 1), LN1_8)
        lv = self.dma(self.v(self.M_SC1, 1, 0, 8), self.d["eb"].ap())
        wall = [(f"LD{i}", self.sem[f"LD{i}"].n) for i in range(8)]
        self.op("vector", "tensor_add", self.v(self.M_RB2EB, 1, 0, 8),
                self.v(self.M_RB2EB, 1, 0, 8), self.v(self.M_SC1, 1, 0, 8),
                waits=wall)
        mv = self.mm(self.pb[7][:, 0:1],
                     self.heap[0:1, self.C_ONES:self.C_ONES + 128],
                     self.heap[0:1, self.M_ALPHA1:self.M_ALPHA1 + 1],
                     True, True, waits=wall, inc=True)
        self.op("vector", "tensor_copy", self.v(self.M_ALPHA, 1),
                self.pb[7][:, 0:1], waits=[("TEN", mv)])
        self.barrier()

    def emit_store(self, s):
        T, XS = self.T, self.OFF["XSB"]
        self.wait("sync", "VEC", self.sem["VEC"].n)
        self.wait("sync", "GPS", self.sem["GPS"].n)
        self.wait("sync", "SCA", self.sem["SCA"].n)
        for c in range(8):
            self.dma(self.yout.ap()[s, c * 128:(c + 1) * 128, :],
                     self.v(XS + c * T, T))
        for i in range(8):
            self.wait("sync", f"LD{i}", self.sem[f"LD{i}"].n)
        self.barrier()

    def build(self):
        self.rope_done = 0
        self.emit_init()
        for s in range(NSEQ):
            self.emit_l0a(s)
            if self.nblocks >= 2:
                self.emit_l0f(s)
            if self.nblocks >= 3:
                self.emit_l1a(s)
            if self.nblocks >= 4:
                self.emit_l1f(s)
            self.emit_store(s)
        nc, ops = self.nc, self.ops
        with nc.Block() as block:
            @block.sync
            def _(e):
                for fn in ops["sync"]:
                    fn(e)

            @block.tensor
            def _(e):
                for fn in ops["tensor"]:
                    fn(e)

            @block.vector
            def _(e):
                for fn in ops["vector"]:
                    fn(e)

            @block.scalar
            def _(e):
                for fn in ops["scalar"]:
                    fn(e)

            @block.gpsimd
            def _(e):
                for fn in ops["gpsimd"]:
                    fn(e)
        return nc


_BUILD_CACHE = {}


def build_program(T=2048, nblocks=4):
    key = (T, nblocks)
    if key not in _BUILD_CACHE:
        b = B(T, nblocks)
        _BUILD_CACHE[key] = b.build()
        _BUILD_CACHE[key]._mmlab = b.mmlab
    return _BUILD_CACHE[key]


def kernel(**inputs):
    T = SEQT
    x = np.ascontiguousarray(np.asarray(inputs["L0_attn_x"], np.float32))
    Bt, Tx, Dx = x.shape
    assert Dx == D and Tx == T, (x.shape, T)
    shared = _host_prep(inputs)
    shared.update(_host_consts(T))
    nc = build_program(T, NBLOCKS)
    in_maps = []
    for c in range(NCORES):
        m = dict(shared)
        m["x"] = np.ascontiguousarray(x[2 * c:2 * c + 2].transpose(0, 2, 1))
        in_maps.append(m)
    res = run_bass_kernel_spmd(nc, in_maps, core_ids=list(range(NCORES)))
    out = np.empty((Bt, T, D), np.float32)
    for c in range(NCORES):
        y = np.asarray(res.results[c]["y"])
        out[2 * c:2 * c + 2] = y.transpose(0, 2, 1)
    return out


# version 18
# speedup vs baseline: 1.1724x; 1.0566x over previous
import numpy as np
import concourse.bass as bass
import concourse.mybir as mybir
from concourse.bass_utils import run_bass_kernel_spmd

F32 = mybir.dt.float32
BF16 = mybir.dt.bfloat16
AF = mybir.ActivationFunctionType
ALU = mybir.AluOpType
AX = mybir.AxisListType

D = 1024
LATD = 256
KR = 512
NCORES = 8
NSEQ = 2
NEG = -1.0e30
LN1_8 = float(np.log(1.0 / 8.0))

SEQT = 2048
NBLOCKS = 4
DETECT = False
SYNC_SER = False



def _host_consts(T):
    c = {}
    j = np.arange(128)
    c["ones"] = np.ones((128, 128), np.float32)
    c["id"] = np.eye(128, dtype=np.float32)
    f = (j % 64) % 32
    inv = (1.0 / 10000.0 ** (f / 32.0))[:, None]
    ang = inv * np.arange(T)[None, :]
    c["ropeC"] = np.cos(ang).astype(np.float32)
    c["ropeS"] = np.sin(ang).astype(np.float32)
    Pm = np.zeros((128, 128), np.float32)
    for r in range(128):
        rb = r % 64
        base = r - rb
        if rb < 32:
            Pm[r, base + rb + 32] = -1.0
        else:
            Pm[r, base + rb - 32] = 1.0
    c["pm0T"] = np.ascontiguousarray(Pm.T)
    kk, qq = np.meshgrid(np.arange(128), np.arange(128), indexing="ij")
    m = np.zeros((128, 128), np.float32)
    m[kk > qq] = NEG
    c["mask"] = m
    E = np.zeros((8, 512), np.float32)
    for col in range(512):
        E[col // 64, col] = 1.0
    c["ec"] = E
    nv = (1.0 / np.arange(1, T + 1, dtype=np.float64)).astype(np.float32)
    c["nvinv"] = np.ascontiguousarray(np.broadcast_to(nv, (128, T)))
    n = np.arange(D, dtype=np.int64)
    th = 2.0 * np.pi * (np.outer(n, n) % D).astype(np.float64) / D
    c["dftC"] = np.cos(th).astype(np.float32)
    c["dftS"] = np.sin(th).astype(np.float32)
    return c


def _blk_lhsT(w, kb=128, mb=128):
    K, M = w.shape
    nk, nm = K // kb, M // mb
    a = w.reshape(nk, kb, nm, mb).transpose(2, 1, 0, 3).reshape(nm, kb, nk * mb)
    return np.ascontiguousarray(a)


def _flat_one(w):
    K, M = w.shape
    nk = K // 128
    a = w.reshape(nk, 128, M).transpose(1, 0, 2).reshape(128, nk * M)
    return np.ascontiguousarray(a)


def _host_prep(inputs):
    f32 = lambda a: np.ascontiguousarray(np.asarray(a, np.float32))
    iw = {}
    iw["w0q"] = _blk_lhsT(f32(inputs["L0_attn_W_q"]))
    iw["w0dkv"] = _flat_one(f32(inputs["L0_attn_W_dkv"]))
    iw["w0uk"] = _blk_lhsT(f32(inputs["L0_attn_W_uk"]))
    iw["w0uv"] = _blk_lhsT(f32(inputs["L0_attn_W_uv"]))
    iw["w0o"] = f32(inputs["L0_attn_W_o"])
    iw["g0"] = _flat_one(f32(inputs["L0_attn_ln_gamma"]).reshape(LATD, 1))
    iw["b0"] = _flat_one(f32(inputs["L0_attn_ln_beta"]).reshape(LATD, 1))
    iw["wfQ"] = _blk_lhsT(f32(inputs["L0_ffn_W_Q"]).T)
    iw["wfK"] = _blk_lhsT(f32(inputs["L0_ffn_W_K"]).T)
    iw["wfO"] = _blk_lhsT(f32(inputs["L0_ffn_W_O"]).T)
    iw["wfI"] = _blk_lhsT(f32(inputs["L0_ffn_W_inv"]).T)
    iw["wfr1"] = _blk_lhsT(f32(inputs["L0_ffn_router_w1"]).T)
    iw["wfr2"] = _blk_lhsT(f32(inputs["L0_ffn_router_w2"]).T, mb=8)
    vflat = lambda a: _blk_lhsT(f32(a).transpose(1, 0, 2).reshape(D, KR))
    uflat = lambda a: _blk_lhsT(f32(a).transpose(0, 2, 1).reshape(KR, D))
    iw["vfw"] = vflat(inputs["L0_ffn_V_fwd"])
    iw["wfw"] = vflat(inputs["L0_ffn_W_fwd"])
    iw["ufw"] = uflat(inputs["L0_ffn_U_fwd"])
    iw["vin"] = vflat(inputs["L0_ffn_W_inv_exp"])
    iw["win"] = vflat(inputs["L0_ffn_V_inv"])
    iw["uin"] = uflat(inputs["L0_ffn_U_inv"])
    iw["bfw"] = f32(inputs["L0_ffn_b_fwd"])
    iw["bin"] = f32(inputs["L0_ffn_b_inv"])
    iw["rb1"] = _flat_one(f32(inputs["L0_ffn_router_b1"]).reshape(D, 1))
    iw["rb2"] = f32(inputs["L0_ffn_router_b2"]).reshape(8, 1)
    iw["eb"] = f32(inputs["L0_ffn_expert_bias"]).reshape(8, 1)
    iw["alpha"] = f32(inputs["L0_ffn_alpha_bi"]).reshape(1, 1)
    iw["w1q"] = _blk_lhsT(f32(inputs["L1_attn_wq"]).T)
    iw["w1ka"] = _blk_lhsT(f32(inputs["L1_attn_wk_a"]).T, mb=256)
    iw["w1va"] = _blk_lhsT(f32(inputs["L1_attn_wv_a"]).T, mb=256)
    iw["w1kb"] = _blk_lhsT(f32(inputs["L1_attn_wk_b"]).T)
    iw["w1vb"] = _blk_lhsT(f32(inputs["L1_attn_wv_b"]).T)
    iw["w1o"] = np.ascontiguousarray(f32(inputs["L1_attn_wo"]).T)
    iw["wrB"] = _blk_lhsT(f32(inputs["L1_ffn_w_freq_real"]))
    iw["wiB"] = _blk_lhsT(f32(inputs["L1_ffn_w_freq_imag"]))
    iw["bias1"] = _flat_one(f32(inputs["L1_ffn_bias"]).reshape(D, 1))
    return iw


_SHAPES = {
    "w0q": [8, 128, 1024], "w0dkv": [128, 2048], "w0uk": [8, 128, 256],
    "w0uv": [8, 128, 256], "w0o": [1024, 1024], "g0": [128, 2], "b0": [128, 2],
    "wfQ": [8, 128, 1024], "wfK": [8, 128, 1024], "wfO": [8, 128, 1024],
    "wfI": [8, 128, 1024], "wfr1": [8, 128, 1024], "wfr2": [1, 128, 64],
    "vfw": [4, 128, 1024], "wfw": [4, 128, 1024], "ufw": [8, 128, 512],
    "vin": [4, 128, 1024], "win": [4, 128, 1024], "uin": [8, 128, 512],
    "bfw": [8, 1024], "bin": [8, 1024],
    "rb1": [128, 8], "rb2": [8, 1], "eb": [8, 1], "alpha": [1, 1],
    "w1q": [8, 128, 1024], "w1ka": [1, 128, 2048], "w1va": [1, 128, 2048],
    "w1kb": [8, 128, 256], "w1vb": [8, 128, 256], "w1o": [1024, 1024],
    "wrB": [8, 128, 1024], "wiB": [8, 128, 1024], "bias1": [128, 8],
}

ENGS = ("sync", "tensor", "vector", "scalar", "gpsimd")


class Sem:
    def __init__(self, h):
        self.h = h
        self.n = 0

    def inc(self, k=1):
        self.n += k
        return self.n


class B:
    def __init__(self, T, nblocks):
        self.T = T
        self.TT = min(512, T)
        self.NTT = T // self.TT
        self.NKT = T // 128
        self.NQT = T // self.TT
        self.nqsub = self.TT // 128
        self.nblocks = nblocks
        nc = bass.Bass("TRN2", target_bir_lowering=False,
                       detect_race_conditions=DETECT)
        self.nc = nc
        self.ops = {e: [] for e in ENGS}

        dr = lambda n, s, k: nc.dram_tensor(n, s, F32, kind=k)
        self.xin = dr("x", [NSEQ, D, T], "ExternalInput")
        self.yout = dr("y", [NSEQ, D, T], "ExternalOutput")
        self.d = {n: dr(n, s, "ExternalInput") for n, s in _SHAPES.items()}
        for n, s in (("ropeC", [128, T]), ("ropeS", [128, T]),
                     ("pm0T", [128, 128]), ("mask", [128, 128]),
                     ("ones", [128, 128]), ("id", [128, 128]),
                     ("ec", [8, 512]), ("nvinv", [128, T]),
                     ("dftC", [1024, 1024]), ("dftS", [1024, 1024])):
            self.d[n] = dr(n, s, "ExternalInput")
        self.mcache = dr("mcache", [128, 8192], "Internal")

        T8 = 8 * T
        self.OFF = {}
        cur = 0
        for name, n in (("XSB", T8), ("H1", max(T8, 32 * self.TT, 16384)),
                        ("R2", 14368),
                        ("WB", 2560), ("CB", 896), ("ST", 1024), ("SP", 1024)):
            self.OFF[name] = cur
            cur += n
        self.NCOLS = cur
        assert cur <= 53200, cur
        self.heap = nc.alloc_sbuf_tensor("heap", [128, self.NCOLS], F32)
        ha = nc.lookup_mloc(self.heap).addr
        TTl = self.TT
        self._bfregs = []
        for i, (st, ln) in enumerate((
                (self.OFF["R2"], 8 * TTl),
                (self.OFF["R2"] + 20 * TTl, 4 * TTl),
                (self.OFF["H1"], 8 * TTl),
                (self.OFF["H1"] + 8 * TTl, 8 * TTl),
                (self.OFF["H1"] + 16 * TTl, 8 * TTl),
                (self.OFF["H1"] + 24 * TTl, 8 * TTl),
                (self.OFF["WB"], 2048),
                (self.OFF["R2"] + 16 * TTl, 4 * TTl),
                (self.OFF["R2"] + 8 * TTl, 4 * TTl),
                (self.OFF["R2"] + 12 * TTl, 4 * TTl),
        )):
            t = nc.alloc_sbuf_tensor_at(f"ab{i}", [128, 2 * ln], BF16,
                                        offset=ha + 4 * st)
            self._bfregs.append((st, ln, t))
        self.pb = [nc.alloc_psum_tensor(f"pb{i}", [128, 512], F32) for i in range(8)]
        self.sem = {n: Sem(nc.alloc_semaphore(n)) for n in
                    ("TEN", "VEC", "SCA", "GPS", "BAR")}
        for i in range(8):
            self.sem[f"LD{i}"] = Sem(nc.alloc_semaphore(f"LD{i}"))
        self._dmai = 0
        self._ld_last = {}
        CB = self.OFF["CB"]
        self.C_ONES = CB + 0
        self.C_ID = CB + 128
        self.C_OVL = CB + 256
        self.C_MISC = CB + 768
        M = self.C_MISC
        self.M_RB2EB = M + 0
        self.M_ALPHA = M + 1
        self.M_G0 = M + 2
        self.M_B0 = M + 4
        self.M_BIAS1 = M + 6
        self.M_CARRY = M + 16
        self.M_NEGB = M + 24
        self.M_MQ = M + 25
        self.M_MK = M + 26
        self.M_SC1 = M + 27
        self.M_SC2 = M + 28
        self.M_RECIP = M + 29
        self.M_ALPHA1 = M + 33
        self.M_EPS = M + 34
        self.M_LN18 = M + 35
        self.bank_free = {i: None for i in range(8)}
        self._stage_busy = {}
        self._pend = 0
        self._pcell = None
        self.phase = "init"
        self.mmlab = []

    def v(self, col, n, p0=0, p1=128):
        return self.heap[p0:p1, col:col + n]

    def hap(self, col, dims, np_=128):
        return bass.AP(self.heap, col, [[self.NCOLS, np_]] + [list(d) for d in dims])

    def pap(self, pbt, col, dims, np_=128):
        return bass.AP(pbt, col, [[512, np_]] + [list(d) for d in dims])

    def _bfreg(self, col, n):
        for st, ln, t in self._bfregs:
            if st <= col and col + n <= st + ln:
                return st, t
        raise KeyError((col, n))

    def vb(self, col, n, p0=0, p1=128, lo=False):
        st, t = self._bfreg(col, n)
        base = 2 * (col - st) + (n if lo else 0)
        return t[p0:p1, base:base + n]

    def vbx(self, col, n, soff, slen, p0=0, p1=128, lo=False):
        st, t = self._bfreg(col, n)
        base = 2 * (col - st) + (n if lo else 0) + soff
        return t[p0:p1, base:base + slen]

    def split(self, col, n, src_ap, waits=(), p0=0, p1=128):
        h_ = self.vb(col, n, p0, p1)
        l_ = self.vb(col, n, p0, p1, lo=True)
        self.op("vector", "tensor_copy", h_, src_ap, waits=waits)
        return self.op("vector", "tensor_tensor", l_, src_ap, h_, ALU.subtract)

    def load_pair(self, pair_col, ncols, dram_ap, stage_col, after=None):
        if after is not None:
            self.wait("sync", after[0], after[1])
        sb = self._stage_busy.get(stage_col)
        if sb:
            self.wait("sync", sb[0], sb[1])
        lv = self.dma(self.v(stage_col, ncols), dram_ap)
        sv = self.split(pair_col, ncols, self.v(stage_col, ncols), waits=[lv])
        self._stage_busy[stage_col] = ("VEC", sv)
        return sv

    def mm3(self, psum_ap, wcol, nwe, wc, rcol, n_r, start, stop, waits=(),
            wspan=128, p0=0, p1=128):
        wst, wt = self._bfreg(wcol, nwe)
        b0 = 2 * (wcol - wst)
        lh = wt[p0:p1, b0 + wc * wspan:b0 + (wc + 1) * wspan]
        ll = wt[p0:p1, b0 + nwe + wc * wspan:b0 + nwe + (wc + 1) * wspan]
        rh = self.vb(rcol, n_r, p0, p1)
        rl = self.vb(rcol, n_r, p0, p1, lo=True)
        self.mm(psum_ap, lh, rh, start, False, waits=waits)
        self.mm(psum_ap, lh, rl, False, False)
        return self.mm(psum_ap, ll, rh, False, stop)

    def on(self, eng, fn):
        self.ops[eng].append(fn)

    def ten_flush(self):
        if self._pend:
            self._pcell[0] += self._pend
            self._pend = 0

    def tmark(self):
        self.ten_flush()
        return self.sem["TEN"].n

    def wait(self, eng, sem, val):
        if not val:
            return
        if eng == "tensor":
            self.ten_flush()
        self.on(eng, lambda e, s=self.sem[sem].h, v=val: e.wait_ge(s, v))

    def ready(self):
        return [("VEC", self.sem["VEC"].n), ("SCA", self.sem["SCA"].n),
                ("GPS", self.sem["GPS"].n)]

    def barrier(self):
        self.ten_flush()
        bar = self.sem["BAR"]
        v = bar.inc(5)
        for eng in ENGS:
            self.on(eng, lambda e, s=bar.h, v=v:
                    (e.nop().then_inc(s, 1), e.wait_ge(s, v)))
        for i in range(8):
            self.bank_free[i] = None

    def dma(self, dst_ap, src_ap, after=None):
        name = f"LD{self._dmai % 8}"
        self._dmai += 1
        LD = self.sem[name]
        if after is not None:
            self.wait("sync", after[0], after[1])
        if name in self._ld_last:
            self.wait("sync", name, self._ld_last[name])
        val = LD.inc(16)
        self._ld_last[name] = val
        self.on("sync", lambda e, d=dst_ap, s=src_ap, h=LD.h:
                e.dma_start(out=d, in_=s).then_inc(h, 16))
        if SYNC_SER:
            self.wait("sync", name, val)
        return (name, val)

    def t_bank(self, bank):
        bf = self.bank_free[bank]
        if bf is not None:
            self.wait("tensor", bf[0], bf[1])

    def mm(self, psum_ap, lhsT, rhs, start, stop, waits=(), tile_position=None,
           inc=False):
        for (s, v) in waits:
            self.wait("tensor", s, v)
        TEN = self.sem["TEN"]
        val = TEN.inc()
        try:
            shp = (lhsT.ap[0][1], lhsT.free_size(), rhs.free_size(),
                   "f32" if lhsT.tensor.dtype == F32 else "bf16")
        except (AttributeError, IndexError):
            shp = None
        self.mmlab.append((self.phase, shp))
        cell = [0]
        self._pend += 1
        self._pcell = cell
        if inc:
            self.ten_flush()
        def fn(e, o=psum_ap, l=lhsT, r=rhs, st=start, sp=stop, h=TEN.h,
               c=cell, tpos=tile_position):
            i = e.matmul(o, l, r, start=st, stop=sp, tile_position=tpos)
            if c[0]:
                i.then_inc(h, c[0])
        self.on("tensor", fn)
        return val

    def tp(self, out_ap, in_ap, waits=(), kdim=128, tile_position=None):
        for (s, v) in waits:
            self.wait("tensor", s, v)
        self.ten_flush()
        TEN = self.sem["TEN"]
        val = TEN.inc()
        self.mmlab.append((self.phase + "/tp", "tp"))
        def fn(e, o=out_ap, i_=in_ap,
               idn=self.heap[0:kdim, self.C_ID:self.C_ID + kdim], h=TEN.h,
               tpos=tile_position):
            e.transpose(o, i_, idn).then_inc(h, 1)
        self.on("tensor", fn)
        return val

    def op(self, eng, meth, *args, waits=(), **kw):
        S = {"vector": "VEC", "scalar": "SCA", "gpsimd": "GPS"}[eng]
        for (s, v) in waits:
            self.wait(eng, s, v)
        sem = self.sem[S]
        prev = sem.n
        val = sem.inc()
        def fn(e, m=meth, a=args, k=kw, h=sem.h, p=prev):
            i = getattr(e, m)(*a, **k).then_inc(h, 1)
            if p:
                i._wait_ge(h, p)
        self.on(eng, fn)
        return val

    def ln_tile(self, src_off, dst_off, tt, inv_d, dst_tile, sq_off):
        self.phase = self.phase.split("|")[0] + "|ln"
        """LayerNorm of one t-tile: dst = (src - mean)/sqrt(var+eps).
        src chunk c at src_off + c*T + tt*TT; dst chunk c at
        dst_off + c*(TT if dst_tile else T) (+ tt*TT if not dst_tile).
        sq_off: 8*TT scratch cols for squares. Returns GPS val of completion."""
        T, TT = self.T, self.TT
        ST = self.OFF["ST"]
        t0 = tt * TT
        meanB = self.v(ST, TT)
        rsd = self.v(ST + 512, TT)
        src3 = self.hap(src_off + t0, [[T, 8], [1, TT]])
        dst3 = self.hap(sq_off, [[TT, 8], [1, TT]])
        sq = self.op("scalar", "activation", dst3, src3, AF.Square,
                     waits=self.ready())
        for b_, off, step, w in ((0, src_off + t0, T, None), (1, sq_off, TT, sq)):
            self.t_bank(b_)
            for c in range(8):
                self.mm(self.pb[b_][:, 0:TT], self.v(self.C_ONES, 128),
                        self.v(off + c * step, TT), c == 0, c == 7,
                        waits=([("SCA", w)] if (w and c == 0) else ()))
        vsum = self.tmark()
        self.op("scalar", "activation", meanB, self.pb[0][:, 0:TT], AF.Copy,
                scale=inv_d, waits=[("TEN", vsum)])
        e2 = self.op("scalar", "activation", rsd, self.pb[1][:, 0:TT], AF.Copy,
                     scale=inv_d)
        m2 = self.op("vector", "tensor_mul", self.pb[0][:, 0:TT], meanB, meanB,
                     waits=[("SCA", e2)])
        v2 = self.op("vector", "tensor_sub", rsd, rsd, self.pb[0][:, 0:TT],
                     waits=[("VEC", m2)])
        self.op("scalar", "activation", rsd, rsd, AF.Ln, bias=self.v(self.M_EPS, 1),
                waits=[("VEC", v2)])
        r2 = self.op("scalar", "activation", rsd, rsd, AF.Exp, scale=-0.5)
        self.bank_free[0] = ("VEC", v2)
        self.bank_free[1] = ("SCA", e2)
        gv = None
        for c in range(8):
            xc = self.v(src_off + c * T + t0, TT)
            dc = self.v(dst_off + c * (TT if dst_tile else T) +
                        (0 if dst_tile else t0), TT)
            sv = self.op("vector", "tensor_sub", dc, xc, meanB,
                         waits=([("SCA", r2)] if c == 0 else ()))
            gv = self.op("gpsimd", "tensor_mul", dc, dc, rsd,
                         waits=[("VEC", sv)])
        return gv

    def preln(self, dst_off, sq_off):
        gv = None
        for tt in range(self.NTT):
            gv = self.ln_tile(self.OFF["XSB"], dst_off, tt, 1.0 / D, False, sq_off)
        self.barrier()
        return gv

    def bound_tile(self, src_off, acc_col, first, wv):
        TT = self.TT
        sq = self.v(self.OFF["ST"], TT)
        sv = self.op("scalar", "activation", sq, self.v(src_off, TT), AF.Square,
                     waits=[("VEC", wv)] if wv else ())
        self._last_sq = sv
        self.t_bank(5)
        mv = self.mm(self.pb[5][0:1, 0:TT], self.v(self.C_ONES, 1), sq,
                     True, True, waits=[("SCA", sv)], inc=True)
        if first:
            rv = self.op("vector", "reduce_max", self.v(acc_col, 1, 0, 1),
                         self.pb[5][0:1, 0:TT], AX.X, waits=[("TEN", mv)])
        else:
            self.op("vector", "reduce_max", self.v(self.M_SC1, 1, 0, 1),
                    self.pb[5][0:1, 0:TT], AX.X, waits=[("TEN", mv)])
            rv = self.op("vector", "tensor_max", self.v(acc_col, 1, 0, 1),
                         self.v(acc_col, 1, 0, 1), self.v(self.M_SC1, 1, 0, 1))
        self.bank_free[5] = ("VEC", rv)
        return rv

    def finish_bound(self):
        mq = self.v(self.M_MQ, 1, 0, 1)
        mk = self.v(self.M_MK, 1, 0, 1)
        self.op("scalar", "activation", mq, mq, AF.Ln, waits=self.ready())
        l2 = self.op("scalar", "activation", mk, mk, AF.Ln)
        a1 = self.op("vector", "tensor_add", mq, mq, mk, waits=[("SCA", l2)])
        e1 = self.op("scalar", "activation", mq, mq, AF.Exp, scale=0.5,
                     bias=self.v(self.M_LN18, 1, 0, 1), waits=[("VEC", a1)])
        n1 = self.op("vector", "tensor_scalar", mq, mq, -1.0, 25.0,
                     ALU.mult, ALU.add, waits=[("SCA", e1)])
        self.t_bank(5)
        mv = self.mm(self.pb[5][:, 0:1],
                     self.heap[0:1, self.C_ONES:self.C_ONES + 128], mq,
                     True, True, waits=[("VEC", n1)], inc=True)
        cv = self.op("vector", "tensor_copy", self.v(self.M_NEGB, 1),
                     self.pb[5][:, 0:1], waits=[("TEN", mv)])
        self.bank_free[5] = ("VEC", cv)
        return cv

    def attn_group(self, QG, KG, VG, causal, bound_wait, wo_cols, wo_ld,
                   s_pair=False):
        base = self.phase.split("|")[0]
        """attention for 2 heads + per-q-tile W_o partial accumulated into x.
        QG/KG feature-major [128 x T]; VG token-major aug [128 x 129*NKT]
        per k-tile laid out [V_h0(64)|ones|V_h1(64)].  PV is feature-major:
        O_aug^T [65 x TT] = V_augT.T @ P^T accumulated over k-tiles; the aug
        row gives the softmax denominator (row 64 for head 0, row 0 for head
        1).  Normalization multiplies by exp(-ln(denom)) broadcast across 64
        partitions via a K=1 matmul.  W_o is applied as two K=64 matmuls
        against AT[64 x 2*TT]; wo_cols = WB cols of the two 64-row halves."""
        T, TT, NKT, NQT, nqsub = self.T, self.TT, self.NKT, self.NQT, self.nqsub
        SP, ST, XS = self.OFF["SP"], self.OFF["ST"], self.OFF["XSB"]
        negb = self.v(self.M_NEGB, 1)
        AT = ST
        last_at = 0
        if causal:
            sbank, pslot = (0, 1), (SP, SP + 512)
        else:
            sbank, pslot = (0, 1, 3), (SP, SP + 512, self.C_OVL)
        nbank = len(sbank)
        for qt in range(NQT):
            at_done = 0
            for hl in range(2):
                kts = list(range(min(NKT, nqsub * qt + nqsub))) if causal \
                    else list(range(NKT))
                drow = 64
                vrow0 = 0
                last_pv = None
                prev_pv = {}
                rd = self.ready()
                self.t_bank(2)
                def emit_S(i, hl=hl, qt=qt, kts=kts, rd=rd):
                    kt = kts[i]
                    bank = sbank[i % nbank]
                    qoffb = max(0, kt - nqsub * qt) if causal else 0
                    self.t_bank(bank)
                    self.phase = base + "|S"
                    w0 = rd if i == 0 and hl == 0 else ()
                    if not s_pair:
                        return self.mm(
                            self.pb[bank][:, qoffb * 128:TT],
                            self.heap[hl * 64:hl * 64 + 64,
                                      KG + kt * 128:KG + kt * 128 + 128],
                            self.heap[hl * 64:hl * 64 + 64,
                                      QG + qt * TT + qoffb * 128:
                                      QG + (qt + 1) * TT],
                            True, True, waits=w0, inc=True)
                    p0, p1 = hl * 64, hl * 64 + 64
                    t0k = (kt * 128 // TT) * TT
                    sk = kt * 128 - t0k
                    sq_ = qoffb * 128
                    nq_ = TT - sq_
                    kh = self.vbx(KG + t0k, TT, sk, 128, p0, p1)
                    kl = self.vbx(KG + t0k, TT, sk, 128, p0, p1, lo=True)
                    qh = self.vbx(QG + qt * TT, TT, sq_, nq_, p0, p1)
                    ql = self.vbx(QG + qt * TT, TT, sq_, nq_, p0, p1, lo=True)
                    ps = self.pb[bank][:, sq_:TT]
                    self.mm(ps, kh, qh, True, False, waits=w0)
                    self.mm(ps, kh, ql, False, False)
                    return self.mm(ps, kl, qh, False, True, inc=True)
                svs = {}
                for i in range(min(nbank, len(kts))):
                    svs[i] = emit_S(i)
                for i, kt in enumerate(kts):
                    par = i % nbank
                    bank = sbank[par]
                    qoffb = max(0, kt - nqsub * qt) if causal else 0
                    n_q = TT - qoffb * 128
                    sv = svs[i]
                    mwait = [("TEN", sv)]
                    jd = kt - nqsub * qt
                    if causal and 0 <= jd < nqsub:
                        mv = self.op("vector", "tensor_add",
                                     self.pb[bank][:, jd * 128:jd * 128 + 128],
                                     self.pb[bank][:, jd * 128:jd * 128 + 128],
                                     self.v(self.C_OVL + 128, 128),
                                     waits=[("TEN", sv)])
                        mwait = [("VEC", mv)]
                    ew = list(mwait)
                    if i == 0:
                        ew.append(("VEC", bound_wait))
                    if i <= 1 and last_at:
                        ew.append(("VEC", last_at))
                    if par in prev_pv:
                        ew.append(("TEN", prev_pv[par]))
                    pexp = self.op("scalar", "activation",
                                   self.v(pslot[par], n_q),
                                   self.pb[bank][:, qoffb * 128:TT], AF.Exp,
                                   bias=negb, scale=0.125, waits=ew)
                    self.bank_free[bank] = ("SCA", pexp)
                    self.phase = base + "|PV"
                    last_pv = self.mm(
                        self.pb[2][0:65, qoffb * 128:TT],
                        self.hap(VG + kt * 130 + 65 * hl, [[1, 65]]),
                        self.v(pslot[par], n_q),
                        i == 0, i == len(kts) - 1,
                        waits=[("SCA", pexp)], inc=True)
                    prev_pv[par] = last_pv
                    if i + nbank < len(kts):
                        svs[i + nbank] = emit_S(i + nbank)
                u1 = self.op("scalar", "activation", self.v(SP, TT, 0, 1),
                             self.pb[2][drow:drow + 1, 0:TT], AF.Ln,
                             waits=[("TEN", last_pv)])
                u2 = self.op("scalar", "activation", self.v(SP, TT, 0, 1),
                             self.v(SP, TT, 0, 1), AF.Exp, scale=-1.0)
                self.t_bank(3)
                self.phase = base + "|norm"
                bc = self.mm(self.pb[3][0:64, 0:TT],
                             self.heap[0:1, self.C_ONES:self.C_ONES + 64],
                             self.v(SP, TT, 0, 1), True, True,
                             waits=[("SCA", u2)], inc=True)
                cpv = self.op("vector", "tensor_copy", self.v(SP + 512, TT, 0, 64),
                              self.pb[3][0:64, 0:TT], waits=[("TEN", bc)])
                at_done = self.op("vector", "scalar_tensor_tensor",
                                  self.heap[0:64, AT + hl * TT:AT + (hl + 1) * TT],
                                  self.pb[2][vrow0:vrow0 + 64, 0:TT], 1.0,
                                  self.v(SP + 512, TT, 0, 64),
                                  ALU.mult, ALU.mult, waits=[("VEC", cpv)])
                self.bank_free[2] = ("VEC", at_done)
                self.bank_free[3] = ("VEC", cpv)
                last_at = at_done
                self.attn_guard = [("TEN", last_pv), ("SCA", u2)]
            self.phase = base + "|WO"
            for oc in range(8):
                bank = 4 + (oc % 4)
                self.t_bank(bank)
                for hl in range(2):
                    w0 = []
                    if oc == 0 and hl == 0:
                        w0 = [("VEC", at_done)] + list(wo_ld)
                    mv = self.mm(self.pb[bank][:, 0:TT],
                                 self.heap[0:64, wo_cols[hl] + oc * 128:
                                           wo_cols[hl] + oc * 128 + 128],
                                 self.heap[0:64, AT + hl * TT:AT + (hl + 1) * TT],
                                 hl == 0, hl == 1, waits=w0, inc=(hl == 1))
                av = self.op("vector", "tensor_add",
                             self.v(XS + oc * T + qt * TT, TT),
                             self.v(XS + oc * T + qt * TT, TT),
                             self.pb[bank][:, 0:TT], waits=[("TEN", mv)])
                self.bank_free[bank] = ("VEC", av)

    def rope(self, off, lc1, lcs, src_ready):
        TT = self.TT
        SP, ST = self.OFF["SP"], self.OFF["ST"]
        self.t_bank(4)
        mv = self.mm(self.pb[4][:, 0:TT], self.v(self.C_OVL, 128),
                     self.v(off, TT), True, True, waits=[("VEC", src_ready)],
                     inc=True)
        tmp = self.v(ST, TT)
        v1 = self.op("vector", "tensor_mul", tmp, self.pb[4][:, 0:TT],
                     self.v(SP + 512, TT), waits=[("TEN", mv), lcs])
        self.bank_free[4] = ("VEC", v1)
        v2 = self.op("gpsimd", "tensor_mul", self.v(off, TT), self.v(off, TT),
                     self.v(SP, TT),
                     waits=[lc1, ("VEC", src_ready), ("TEN", mv)])
        v3 = self.op("vector", "tensor_add", self.v(off, TT), self.v(off, TT),
                     tmp, waits=[("GPS", v2)])
        return v3

    def emit_l0a(self, s):
        self.phase = "l0a.front"
        T, TT, NTT, NKT = self.T, self.TT, self.NTT, self.NKT
        XS, H1, R2, WB, SP = (self.OFF[k] for k in
                              ("XSB", "H1", "R2", "WB", "SP"))
        for c in range(8):
            lv = self.dma(self.v(XS + c * T, T),
                          self.xin.ap()[s, c * 128:(c + 1) * 128, :])
        self.dma(self.v(self.C_OVL, 128), self.d["pm0T"].ap())
        lov = self.dma(self.v(self.C_OVL + 128, 128), self.d["mask"].ap())
        wall = [(f"LD{i}", self.sem[f"LD{i}"].n) for i in range(8)]
        for eng in ("scalar", "vector", "tensor", "gpsimd"):
            for w in wall:
                self.wait(eng, *w)
        self.barrier()
        self.preln(H1, R2)
        LAT = R2
        SQ2 = R2 + 2 * T
        ldkv = self.dma(self.v(WB, 2048), self.d["w0dkv"].ap())
        for tt in range(NTT):
            t0 = tt * TT
            rd = self.ready()
            for lc in range(2):
                self.t_bank(lc)
                for c in range(8):
                    self.mm(self.pb[lc][:, 0:TT],
                            self.v(WB + c * 256 + lc * 128, 128),
                            self.v(H1 + c * T + t0, TT), c == 0, c == 7,
                            waits=(rd + [ldkv]) if (c == 0 and lc == 0) else ())
                ev = self.op("vector", "tensor_copy", self.v(LAT + lc * T + t0, TT),
                             self.pb[lc][:, 0:TT], waits=[("TEN", self.tmark())])
                self.bank_free[lc] = ("VEC", ev)
            lat_ev = ev
            lsq = self.op("scalar", "activation",
                          self.hap(SQ2, [[TT, 2], [1, TT]]),
                          self.hap(LAT + t0, [[T, 2], [1, TT]]),
                          AF.Square, waits=[("VEC", self.sem["VEC"].n)])
            for b_, off, step, w in ((2, LAT + t0, T, None), (3, SQ2, TT, lsq)):
                self.t_bank(b_)
                for lc in range(2):
                    self.mm(self.pb[b_][:, 0:TT], self.v(self.C_ONES, 128),
                            self.v(off + lc * step, TT), lc == 0, lc == 1,
                            waits=([("SCA", w)] if (w and lc == 0) else
                                   ([("VEC", lat_ev)] if lc == 0 else ())))
            vsum = self.tmark()
            ST = self.OFF["ST"]
            meanB = self.v(ST, TT)
            rsd = self.v(ST + 512, TT)
            self.op("scalar", "activation", meanB, self.pb[2][:, 0:TT], AF.Copy,
                    scale=1.0 / LATD, waits=[("TEN", vsum)])
            e2 = self.op("scalar", "activation", rsd, self.pb[3][:, 0:TT],
                         AF.Copy, scale=1.0 / LATD)
            m2 = self.op("vector", "tensor_mul", self.pb[2][:, 0:TT], meanB, meanB,
                         waits=[("SCA", e2)])
            v2 = self.op("vector", "tensor_sub", rsd, rsd, self.pb[2][:, 0:TT],
                         waits=[("VEC", m2)])
            self.op("scalar", "activation", rsd, rsd, AF.Ln, bias=self.v(self.M_EPS, 1),
                    waits=[("VEC", v2)])
            r2 = self.op("scalar", "activation", rsd, rsd, AF.Exp, scale=-0.5)
            self.bank_free[2] = ("VEC", v2)
            self.bank_free[3] = ("SCA", e2)
            for lc in range(2):
                la = self.v(LAT + lc * T + t0, TT)
                self.op("vector", "tensor_sub", la, la, meanB,
                        waits=([("SCA", r2)] if lc == 0 else ()))
                mv = self.op("vector", "tensor_mul", la, la, rsd)
                self.op("scalar", "activation", la, la, AF.Identity,
                        scale=self.v(self.M_G0 + lc, 1),
                        bias=self.v(self.M_B0 + lc, 1), waits=[("VEC", mv)])
        self.barrier()
        QG = R2 + 2 * T
        KG = QG + T
        VG = KG + T
        for g in range(8):
            self._l0a_group(g, LAT, QG, KG, VG)
        self.barrier()

    def _l0a_group(self, g, LAT, QG, KG, VG):
        self.phase = "l0a.proj"
        T, TT, NTT, NKT = self.T, self.TT, self.NTT, self.NKT
        H1, WB, SP = self.OFF["H1"], self.OFF["WB"], self.OFF["SP"]
        lq = self.dma(self.v(WB, 1024), self.d["w0q"].ap()[g])
        luk = self.dma(self.v(WB + 1024, 256), self.d["w0uk"].ap()[g])
        luv = self.dma(self.v(WB + 1280, 256), self.d["w0uv"].ap()[g])
        lwo = self.dma(self.heap[0:64, WB + 1536:WB + 2560],
                       self.d["w0o"].ap()[g * 128:g * 128 + 64, :])
        rd = self.ready()
        SCRQ = self.OFF["R2"] + 4 * T + 2080
        SCRK = SCRQ + 512
        for tt in range(NTT):
            t0 = tt * TT
            qb = 6 + (tt % 2)
            kb = 7 - (tt % 2)
            lc1 = self.dma(self.v(SP, TT), self.d["ropeC"].ap()[:, t0:t0 + TT],
                           after=("VEC", self.rope_done) if getattr(self, "rope_done", 0) else None)
            lcs = self.dma(self.v(SP + 512, TT), self.d["ropeS"].ap()[:, t0:t0 + TT])
            self.t_bank(qb)
            for c in range(8):
                self.mm(self.pb[qb][:, 0:TT], self.v(WB + c * 128, 128),
                        self.v(H1 + c * T + t0, TT), c == 0, c == 7,
                        waits=(rd + [lq]) if (c == 0 and tt == 0) else ())
            qw = [("TEN", self.tmark())]
            if tt > 0:
                qw.append(("SCA", self._last_sq))
            qv = self.op("vector", "tensor_copy", self.v(SCRQ, TT),
                         self.pb[qb][:, 0:TT], waits=qw)
            self.bank_free[qb] = ("VEC", qv)
            rq = self.rope(SCRQ, lc1, lcs, qv)
            self.bound_tile(SCRQ, self.M_MQ, tt == 0, rq)
            self.split(QG + t0, TT, self.v(SCRQ, TT))
            self.t_bank(kb)
            for lc in range(2):
                self.mm(self.pb[kb][:, 0:TT], self.v(WB + 1024 + lc * 128, 128),
                        self.v(LAT + lc * T + t0, TT), lc == 0, lc == 1,
                        waits=[luk] if lc == 0 else ())
            kw = [("TEN", self.tmark()), ("SCA", self._last_sq)]
            kv = self.op("vector", "tensor_copy", self.v(SCRK, TT),
                         self.pb[kb][:, 0:TT], waits=kw)
            self.bank_free[kb] = ("VEC", kv)
            rk = self.rope(SCRK, lc1, lcs, kv)
            self.rope_done = self.bound_tile(SCRK, self.M_MK, tt == 0, rk)
            self.split(KG + t0, TT, self.v(SCRK, TT))
        lwo2 = self.dma(self.heap[0:64, WB:WB + 1024],
                        self.d["w0o"].ap()[g * 128 + 64:g * 128 + 128, :],
                        after=("TEN", self.tmark()))
        bv = self.finish_bound()
        mz = self.op("vector", "memset",
                     self.hap(VG + 64, [[130, NKT], [65, 2]]), 1.0)
        for kt in range(NKT):
            vb_ = 6 + (kt % 2)
            self.t_bank(vb_)
            for lc in range(2):
                self.mm(self.pb[vb_][:, 0:128],
                        self.v(LAT + lc * T + kt * 128, 128),
                        self.v(WB + 1280 + lc * 128, 128),
                        lc == 0, lc == 1,
                        waits=[luv] if (kt == 0 and lc == 0) else ())
            ev = self.op("vector", "tensor_copy",
                         self.hap(VG + kt * 130, [[65, 2], [1, 64]]),
                         self.pb[vb_][:, 0:128],
                         waits=[("TEN", self.tmark()), ("VEC", mz)])
            self.bank_free[vb_] = ("VEC", ev)
        self.attn_group(QG, KG, VG, True, bv, (WB + 1536, WB), (lwo, lwo2),
                        s_pair=True)
        self.barrier()
        self.rope_done = 0

    def emit_l0f(self, s):
        lec = self.dma(self.v(self.C_OVL, 512, 0, 8), self.d["ec"].ap())
        self.wait("tensor", *lec)
        for tt in range(self.NTT):
            self._l0f_tile(tt)
        self.barrier()

    def ln_tile_pair(self, src_off, dst_pair, tt, scr_off):
        self.phase = "l0f.ln"
        """LayerNorm of one t-tile of chunked fp32 src into a bf16 PAIR at
        dst_pair (+ c*TT per chunk).  scr_off: 8*TT fp32 scratch."""
        T, TT = self.T, self.TT
        ST = self.OFF["ST"]
        t0 = tt * TT
        meanB = self.v(ST, TT)
        rsd = self.v(ST + 512, TT)
        src3 = self.hap(src_off + t0, [[T, 8], [1, TT]])
        dst3 = self.hap(scr_off, [[TT, 8], [1, TT]])
        sq = self.op("scalar", "activation", dst3, src3, AF.Square,
                     waits=self.ready())
        for b_, off, step, w in ((0, src_off + t0, T, None),
                                 (1, scr_off, TT, sq)):
            self.t_bank(b_)
            for c in range(8):
                self.mm(self.pb[b_][:, 0:TT], self.v(self.C_ONES, 128),
                        self.v(off + c * step, TT), c == 0, c == 7,
                        waits=([("SCA", w)] if (w and c == 0) else ()))
        vsum = self.tmark()
        self.op("scalar", "activation", meanB, self.pb[0][:, 0:TT], AF.Copy,
                scale=1.0 / D, waits=[("TEN", vsum)])
        e2 = self.op("scalar", "activation", rsd, self.pb[1][:, 0:TT], AF.Copy,
                     scale=1.0 / D)
        m2 = self.op("vector", "tensor_mul", self.pb[0][:, 0:TT], meanB, meanB,
                     waits=[("SCA", e2)])
        v2 = self.op("vector", "tensor_sub", rsd, rsd, self.pb[0][:, 0:TT],
                     waits=[("VEC", m2)])
        self.op("scalar", "activation", rsd, rsd, AF.Ln,
                bias=self.v(self.M_EPS, 1), waits=[("VEC", v2)])
        r2 = self.op("scalar", "activation", rsd, rsd, AF.Exp, scale=-0.5)
        self.bank_free[0] = ("VEC", v2)
        self.bank_free[1] = ("SCA", e2)
        gv = None
        for c in range(8):
            xc = self.v(src_off + c * T + t0, TT)
            uc = self.v(scr_off + c * TT, TT)
            sv = self.op("vector", "tensor_sub", uc, xc, meanB,
                         waits=([("SCA", r2)] if c == 0 else ()))
            gm = self.op("gpsimd", "tensor_mul", uc, uc, rsd,
                         waits=[("VEC", sv)])
            gv = self.split(dst_pair + c * TT, TT, uc, waits=[("GPS", gm)])
        return gv

    def _mmblk(self, dst_off, rhs_pair, wname, nblk, nk, blkelems, stage,
               extra=(), evac="pair", evac_args=None):
        TT = self.TT
        WB = self.OFF["WB"]
        self.phase = self.phase.split("|")[0] + "|mb." + wname
        evs = []

        def lp(ob):
            sl = ob % 2
            after = ("TEN", self.wlast[sl]) if self.wlast[sl] else None
            return self.load_pair(WB + sl * 1024, blkelems,
                                  self.d[wname].ap()[ob],
                                  stage[sl], after=after)
        wsv = lp(0)
        for ob in range(nblk):
            sl = ob % 2
            wsv_next = lp(ob + 1) if ob + 1 < nblk else None
            bank = 4 + sl
            self.t_bank(bank)
            w0 = (list(extra) if ob == 0 else []) + [("VEC", wsv)]
            for c in range(nk):
                self.mm3(self.pb[bank][:, 0:TT], WB + sl * 1024, blkelems, c,
                         rhs_pair + c * TT, TT, c == 0, c == nk - 1,
                         waits=w0 if c == 0 else ())
            self.wlast[sl] = self.tmark()
            mv = self.tmark()
            if evac == "pair":
                ev = self.split(dst_off + ob * TT, TT, self.pb[bank][:, 0:TT],
                                waits=[("TEN", mv)])
                self.bank_free[bank] = ("VEC", ev)
            elif evac == "copy":
                ev = self.op("vector", "tensor_copy",
                             self.v(dst_off + ob * TT, TT),
                             self.pb[bank][:, 0:TT], waits=[("TEN", mv)])
                self.bank_free[bank] = ("VEC", ev)
            elif evac == "gelu":
                ev = self.op("scalar", "activation",
                             self.v(dst_off + ob * TT, TT),
                             self.pb[bank][:, 0:TT], AF.Gelu,
                             bias=self.v(evac_args + ob, 1), waits=[("TEN", mv)])
                self.bank_free[bank] = ("SCA", ev)
            evs.append(ev)
            wsv = wsv_next
        return evs

    def _l0f_tile(self, tt):
        T, TT = self.T, self.TT
        XS, H1, R2, WB, SP, ST = (self.OFF[k] for k in
                                  ("XSB", "H1", "R2", "WB", "SP", "ST"))
        t0 = tt * TT
        XNP = R2
        HH = R2 + 8 * TT
        XV = R2 + 16 * TT
        YW = R2 + 20 * TT
        CU = R2 + 24 * TT
        BF = R2 + 1024
        QQ, KK, IQ, IK = H1, H1 + 8 * TT, H1 + 16 * TT, H1 + 24 * TT
        GF, GI = KK, QQ
        CMB = IQ
        WF, WI = ST, ST + 512
        stageA = (HH, HH + 1024)
        stageRT = (XNP, XNP + 1024)
        stageP = (XNP, HH + 4 * TT)
        stageB = XNP
        self.wlast = [None, None]
        lrw2 = self.dma(self.v(WB + 2048, 64), self.d["wfr2"].ap()[0])
        lrb1 = self.dma(self.v(WB + 2112, 8), self.d["rb1"].ap())
        lnv = self.dma(self.v(SP, TT), self.d["nvinv"].ap()[:, t0:t0 + TT])
        gv = self.ln_tile_pair(XS, XNP, tt, HH)
        self.phase = "l0f"
        self._stage_busy[stageA[0]] = ("VEC", gv)
        self._stage_busy[stageA[1]] = ("VEC", gv)
        self._mmblk(QQ, XNP, "wfQ", 8, 8, 1024, stageA,
                    extra=[("VEC", gv), ("GPS", self.sem["GPS"].n)])
        self._mmblk(KK, XNP, "wfK", 8, 8, 1024, stageA)
        self._mmblk(IQ, QQ, "wfI", 8, 8, 1024, stageA)
        self._mmblk(IK, KK, "wfI", 8, 8, 1024, stageA)
        self._stage_busy[stageRT[0]] = ("TEN", self.tmark())
        self._stage_busy[stageRT[1]] = ("TEN", self.tmark())
        self.phase = "l0f|rtr"
        for srcq, wdst in ((QQ, WF), (IQ, WI)):
            self._mmblk(HH, srcq, "wfr1", 8, 8, 1024, stageRT,
                        evac="gelu", evac_args=WB + 2112)
            self.wait("scalar", *lrb1)
            self.t_bank(6)
            rd = self.ready()
            for c in range(8):
                self.mm(self.pb[6][0:8, 0:TT], self.v(WB + 2048 + c * 8, 8),
                        self.v(HH + c * TT, TT), c == 0, c == 7,
                        waits=(rd + [lrw2]) if c == 0 else ())
            lg = self.op("scalar", "activation", self.v(wdst, TT, 0, 8),
                         self.pb[6][0:8, 0:TT], AF.Identity,
                         bias=self.v(self.M_RB2EB, 1, 0, 8),
                         waits=[("TEN", self.tmark())])
            self.bank_free[6] = ("SCA", lg)
            wtm = self.v(SP + 512, 128)
            for j in range(TT // 128):
                self.t_bank(7)
                v1 = self.tp(self.pb[7][0:128, 0:8],
                             self.heap[0:8, wdst + j * 128:wdst + (j + 1) * 128],
                             waits=[("SCA", lg)], kdim=8)
                nm = self.op("vector", "tensor_reduce", self.v(self.M_SC1, 1),
                             self.pb[7][:, 0:8], AX.X, ALU.max, negate=True,
                             waits=[("TEN", v1)])
                e1 = self.op("scalar", "activation", wtm[:, 0:8],
                             self.pb[7][:, 0:8], AF.Exp,
                             bias=self.v(self.M_SC1, 1), waits=[("VEC", nm)])
                self.op("vector", "reduce_sum", self.v(self.M_SC1, 1),
                        wtm[:, 0:8], AX.X, waits=[("SCA", e1)])
                self.op("vector", "reciprocal", self.v(self.M_SC2, 1),
                        self.v(self.M_SC1, 1))
                m1 = self.op("vector", "tensor_scalar_mul", wtm[:, 0:8],
                             wtm[:, 0:8], self.v(self.M_SC2, 1))
                self.bank_free[7] = ("VEC", m1)
                self.t_bank(7)
                v2 = self.tp(self.pb[7][0:8, 0:128], wtm[:, 0:8],
                             waits=[("VEC", m1)])
                cv = self.op("vector", "tensor_copy",
                             self.heap[0:8, wdst + j * 128:wdst + (j + 1) * 128],
                             self.pb[7][0:8, 0:128], waits=[("TEN", v2)])
                self.bank_free[7] = ("VEC", cv)
        routers_done = self.tmark()
        self._stage_busy[stageP[1]] = ("TEN", routers_done)
        lbf = self.dma(self.v(BF, 1024, 0, 8), self.d["bfw"].ap(),
                       after=("TEN", routers_done))
        lbi = self.dma(self.v(BF + 1024, 1024, 0, 8), self.d["bin"].ap())
        gps_prev = 0
        for br, (qsrc, ksrc, vn, wn, un, boff, wrow, c0) in enumerate(
                ((QQ, KK, "vfw", "wfw", "ufw", BF, WF, 0),
                 (IQ, IK, "vin", "win", "uin", BF + 1024, WI, 4))):
            self._mmblk(XV, qsrc, vn, 4, 8, 1024, stageP, evac="copy",
                        extra=[("TEN", self.tmark())])
            self._mmblk(YW, ksrc, wn, 4, 8, 1024, stageP, evac="copy",
                        extra=[("TEN", self.tmark())])
            self.phase = "l0f|scan"
            for krb in range(4):
                cc = self.v(self.M_CARRY + c0 + krb, 1)
                ini = 0.0 if tt == 0 else cc
                self.op("vector", "tensor_tensor_scan",
                        self.v(CU + krb * TT, TT), self.v(YW + krb * TT, TT),
                        self.v(YW + krb * TT, TT), ini, ALU.add, ALU.bypass,
                        waits=[("TEN", routers_done), ("GPS", gps_prev)])
                self.op("vector", "tensor_copy", cc,
                        self.v(CU + krb * TT + TT - 1, 1))
            for krb in range(4):
                p1 = self.op("gpsimd", "tensor_mul", self.v(XV + krb * TT, TT),
                             self.v(XV + krb * TT, TT), self.v(CU + krb * TT, TT),
                             waits=[("VEC", self.sem["VEC"].n)])
                gps_prev = p1
                self.t_bank(6)
                mv = self.mm(self.pb[6][:, 0:TT],
                             self.heap[0:8, self.C_OVL + krb * 128:
                                       self.C_OVL + (krb + 1) * 128],
                             self.v(wrow, TT, 0, 8), True, True,
                             waits=[("VEC", self.sem["VEC"].n)], inc=True)
                p2 = self.op("vector", "tensor_mul", self.v(XV + krb * TT, TT),
                             self.v(XV + krb * TT, TT), self.pb[6][:, 0:TT],
                             waits=[("TEN", mv), ("GPS", p1)])
                self.bank_free[6] = ("VEC", p2)
                self.split(YW + krb * TT, TT, self.v(XV + krb * TT, TT))
            pv = self.sem["VEC"].n
            gdst = GF if br == 0 else GI
            self.phase = "l0f|U"
            for oc in range(8):
                sl = oc % 2
                after = ("TEN", self.wlast[sl]) if self.wlast[sl] else None
                wsv = self.load_pair(WB + sl * 1024, 512, self.d[un].ap()[oc],
                                     stageB, after=after)
                bank = 4 + sl
                self.t_bank(bank)
                for krc in range(4):
                    self.mm3(self.pb[bank][:, 0:TT], WB + sl * 1024, 512, krc,
                             YW + krc * TT, TT, krc == 0, False,
                             waits=[("VEC", wsv), ("VEC", pv), lbf, lbi]
                             if krc == 0 else ())
                self.mm(self.pb[bank][:, 0:TT],
                        self.heap[0:8, boff + oc * 128:boff + oc * 128 + 128],
                        self.v(wrow, TT, 0, 8), False, True)
                self.wlast[sl] = self.tmark()
                ev = self.op("vector", "tensor_copy", self.v(gdst + oc * TT, TT),
                             self.pb[bank][:, 0:TT],
                             waits=[("TEN", self.tmark())])
                self.bank_free[bank] = ("VEC", ev)
        cvs = None
        for oc in range(8):
            self.op("vector", "scalar_tensor_tensor", self.v(GF + oc * TT, TT),
                    self.v(GI + oc * TT, TT), self.v(self.M_ALPHA, 1),
                    self.v(GF + oc * TT, TT), ALU.mult, ALU.add)
            cvs = self.split(CMB + oc * TT, TT, self.v(GF + oc * TT, TT))
        self.phase = "l0f|WO"
        for oc in range(8):
            sl = oc % 2
            after = ("TEN", self.wlast[sl]) if self.wlast[sl] else None
            wsv = self.load_pair(WB + sl * 1024, 1024, self.d["wfO"].ap()[oc],
                                 stageB, after=after)
            bank = 4 + sl
            self.t_bank(bank)
            for c in range(8):
                self.mm3(self.pb[bank][:, 0:TT], WB + sl * 1024, 1024, c,
                         CMB + c * TT, TT, c == 0, c == 7,
                         waits=[("VEC", wsv), ("VEC", cvs)] if c == 0 else ())
            self.wlast[sl] = self.tmark()
            mv = self.tmark()
            e1 = self.op("vector", "tensor_mul", self.v(GI + oc * TT, TT),
                         self.pb[bank][:, 0:TT], self.v(SP, TT),
                         waits=[("TEN", mv), lnv])
            self.op("vector", "tensor_add", self.v(XS + oc * T + t0, TT),
                    self.v(XS + oc * T + t0, TT), self.v(GI + oc * TT, TT))
            self.bank_free[bank] = ("VEC", e1)
        self.barrier()
        self._stage_busy = {}

    def emit_l1a(self, s):
        LATK, LATV, QG, KG, VG = self._l1a_front(s)
        for g in range(8):
            self._l1a_group(g, LATK, LATV, QG, KG, VG)
        self.barrier()

    def _l1a_front(self, s):
        self.phase = "l1a.front"
        T, TT, NTT, NKT = self.T, self.TT, self.NTT, self.NKT
        XS, H1, R2, WB = (self.OFF[k] for k in ("XSB", "H1", "R2", "WB"))
        self.preln(H1, R2)
        LATK, LATV = R2, R2 + 2 * T
        for wn, dsto in (("w1ka", LATK), ("w1va", LATV)):
            lv = self.dma(self.v(WB, 2048), self.d[wn].ap()[0])
            rd = self.ready()
            for tt in range(NTT):
                t0 = tt * TT
                for lc in range(2):
                    self.t_bank(lc)
                    for c in range(8):
                        self.mm(self.pb[lc][:, 0:TT],
                                self.v(WB + c * 256 + lc * 128, 128),
                                self.v(H1 + c * T + t0, TT), c == 0, c == 7,
                                waits=(rd + [lv]) if (c == 0 and tt == 0 and lc == 0) else ())
                    ev = self.op("vector", "tensor_copy",
                                 self.v(dsto + lc * T + t0, TT),
                                 self.pb[lc][:, 0:TT],
                                 waits=[("TEN", self.tmark())])
                    self.bank_free[lc] = ("VEC", ev)
            self.barrier()
        QG = R2 + 4 * T
        KG = QG + T
        VG = KG + T
        return LATK, LATV, QG, KG, VG

    def _l1a_group(self, g, LATK, LATV, QG, KG, VG):
        self.phase = "l1a.proj"
        T, TT, NTT, NKT = self.T, self.TT, self.NTT, self.NKT
        H1, WB = self.OFF["H1"], self.OFF["WB"]
        lq = self.dma(self.v(WB, 1024), self.d["w1q"].ap()[g])
        lkb = self.dma(self.v(WB + 1024, 256), self.d["w1kb"].ap()[g])
        lvb = self.dma(self.v(WB + 1280, 256), self.d["w1vb"].ap()[g])
        lwo = self.dma(self.heap[0:64, WB + 1536:WB + 2560],
                       self.d["w1o"].ap()[g * 128:g * 128 + 64, :])
        rd = self.ready()
        SCQ = self.C_OVL
        SCK = self.OFF["SP"]
        ag = getattr(self, "attn_guard", [])
        for tt in range(NTT):
            t0 = tt * TT
            qb = 6 + (tt % 2)
            kb = 7 - (tt % 2)
            self.t_bank(qb)
            for c in range(8):
                self.mm(self.pb[qb][:, 0:TT], self.v(WB + c * 128, 128),
                        self.v(H1 + c * T + t0, TT), c == 0, c == 7,
                        waits=(rd + [lq]) if (c == 0 and tt == 0) else ())
            qw = [("TEN", self.tmark())] + (list(ag) if tt == 0 else [])
            if tt > 0:
                qw.append(("SCA", self._last_sq))
            qv = self.op("vector", "tensor_copy", self.v(SCQ, TT),
                         self.pb[qb][:, 0:TT], waits=qw)
            self.bank_free[qb] = ("VEC", qv)
            self.bound_tile(SCQ, self.M_MQ, tt == 0, qv)
            self.split(QG + t0, TT, self.v(SCQ, TT))
            self.t_bank(kb)
            for lc in range(2):
                self.mm(self.pb[kb][:, 0:TT], self.v(WB + 1024 + lc * 128, 128),
                        self.v(LATK + lc * T + t0, TT), lc == 0, lc == 1,
                        waits=[lkb] if lc == 0 else ())
            kw = [("TEN", self.tmark())] + (list(ag) if tt == 0 else [])
            kw.append(("SCA", self._last_sq))
            kv = self.op("vector", "tensor_copy", self.v(SCK, TT),
                         self.pb[kb][:, 0:TT], waits=kw)
            self.bank_free[kb] = ("VEC", kv)
            self.bound_tile(SCK, self.M_MK, tt == 0, kv)
            self.split(KG + t0, TT, self.v(SCK, TT))
        lwo2 = self.dma(self.heap[0:64, WB:WB + 1024],
                        self.d["w1o"].ap()[g * 128 + 64:g * 128 + 128, :],
                        after=("TEN", self.tmark()))
        bv = self.finish_bound()
        mz = self.op("vector", "memset",
                     self.hap(VG + 64, [[130, NKT], [65, 2]]), 1.0)
        for kt in range(NKT):
            vb_ = 6 + (kt % 2)
            self.t_bank(vb_)
            for lc in range(2):
                self.mm(self.pb[vb_][:, 0:128],
                        self.v(LATV + lc * T + kt * 128, 128),
                        self.v(WB + 1280 + lc * 128, 128),
                        lc == 0, lc == 1,
                        waits=[lvb] if (kt == 0 and lc == 0) else ())
            ev = self.op("vector", "tensor_copy",
                         self.hap(VG + kt * 130, [[65, 2], [1, 64]]),
                         self.pb[vb_][:, 0:128],
                         waits=[("TEN", self.tmark()), ("VEC", mz)])
            self.bank_free[vb_] = ("VEC", ev)
        self.attn_group(QG, KG, VG, False, bv, (WB + 1536, WB), (lwo, lwo2),
                        s_pair=True)
        self.barrier()

    def emit_l1f(self, s):
        self.phase = "l1f"
        T, TT, NTT = self.T, self.TT, self.NTT
        XS, H1, R2, WB, SP = (self.OFF[k] for k in
                              ("XSB", "H1", "R2", "WB", "SP"))
        MOFF = R2
        if s == 0:
            AO, BO = H1, H1 + 8192
            w1last = None
            for ob in range(8):
                lwr = self.dma(self.v(WB, 1024), self.d["wrB"].ap()[ob],
                               after=("TEN", w1last) if w1last else None)
                lwi = self.dma(self.v(WB + 1024, 1024), self.d["wiB"].ap()[ob])
                for b in range(6):
                    self.t_bank(b)
                for nt in range(2):
                    for c in range(8):
                        lc_ = self.dma(
                            self.v(SP, 512),
                            self.d["dftC"].ap()[c * 128:(c + 1) * 128,
                                                nt * 512:(nt + 1) * 512],
                            after=("TEN", self.tmark())
                            if (c + nt + ob) else None)
                        self.mm(self.pb[nt][:, 0:512], self.v(WB + c * 128, 128),
                                self.v(SP, 512), c == 0, False,
                                waits=[lc_, lwr] if c == 0 else [lc_])
                        self.mm(self.pb[2 + nt][:, 0:512],
                                self.v(WB + 1024 + c * 128, 128),
                                self.v(SP, 512), c == 0, c == 7,
                                waits=[lwi] if c == 0 else ())
                        ls_ = self.dma(
                            self.v(SP + 512, 512),
                            self.d["dftS"].ap()[c * 128:(c + 1) * 128,
                                                nt * 512:(nt + 1) * 512])
                        self.mm(self.pb[nt][:, 0:512],
                                self.v(WB + 1024 + c * 128, 128),
                                self.v(SP + 512, 512), False, c == 7,
                                waits=[ls_])
                        self.mm(self.pb[4 + nt][:, 0:512], self.v(WB + c * 128, 128),
                                self.v(SP + 512, 512), c == 0, c == 7)
                    w1last = self.tmark()
                    av = self.op("vector", "tensor_copy",
                                 self.v(AO + ob * 1024 + nt * 512, 512),
                                 self.pb[nt][:, 0:512], waits=[("TEN", w1last)])
                    self.op("vector", "tensor_copy",
                            self.v(BO + ob * 1024 + nt * 512, 512),
                            self.pb[2 + nt][:, 0:512])
                    bv = self.op("vector", "tensor_sub",
                                 self.v(BO + ob * 1024 + nt * 512, 512),
                                 self.v(BO + ob * 1024 + nt * 512, 512),
                                 self.pb[4 + nt][:, 0:512])
                    self.bank_free[nt] = ("VEC", av)
                    self.bank_free[2 + nt] = ("VEC", bv)
                    self.bank_free[4 + nt] = ("VEC", bv)
            self.barrier()
            for nb in range(8):
                for mt in range(2):
                    self.t_bank(0)
                    self.t_bank(1)
                    for oc in range(8):
                        lc_ = self.dma(
                            self.v(SP, 512),
                            self.d["dftC"].ap()[oc * 128:(oc + 1) * 128,
                                                mt * 512:(mt + 1) * 512],
                            after=("TEN", self.tmark())
                            if (oc + mt + nb) else None)
                        self.mm(self.pb[0][:, 0:512],
                                self.v(AO + oc * 1024 + nb * 128, 128),
                                self.v(SP, 512), oc == 0, oc == 7,
                                waits=[lc_])
                        ls_ = self.dma(
                            self.v(SP + 512, 512),
                            self.d["dftS"].ap()[oc * 128:(oc + 1) * 128,
                                                mt * 512:(mt + 1) * 512])
                        self.mm(self.pb[1][:, 0:512],
                                self.v(BO + oc * 1024 + nb * 128, 128),
                                self.v(SP + 512, 512), oc == 0, oc == 7,
                                waits=[ls_])
                    mv = self.tmark()
                    c1 = self.op("vector", "tensor_copy",
                                 self.v(MOFF + nb * 1024 + mt * 512, 512),
                                 self.pb[0][:, 0:512], waits=[("TEN", mv)])
                    sv = self.op("vector", "tensor_sub",
                                 self.v(MOFF + nb * 1024 + mt * 512, 512),
                                 self.v(MOFF + nb * 1024 + mt * 512, 512),
                                 self.pb[1][:, 0:512])
                    ev = self.op("scalar", "activation",
                                 self.v(MOFF + nb * 1024 + mt * 512, 512),
                                 self.v(MOFF + nb * 1024 + mt * 512, 512),
                                 AF.Copy, scale=1.0 / 1024.0,
                                 waits=[("VEC", sv)])
                    self.bank_free[0] = ("VEC", c1)
                    self.bank_free[1] = ("VEC", sv)
            self.wait("sync", "SCA", self.sem["SCA"].n)
            self.dma(self.mcache.ap()[:, :], self.v(MOFF, 8192))
            self.barrier()
        else:
            lv = self.dma(self.v(MOFF, 8192), self.mcache.ap()[:, :])
            self.wait("tensor", *lv)
            self.barrier()
        XN = R2 + 8192
        for tt in range(NTT):
            t0 = tt * TT
            gv = self.ln_tile(XS, XN, tt, 1.0 / D, True, XN)
            prev_e2 = [0, 0]
            for oc in range(8):
                bank = 2 + (oc % 4)
                self.t_bank(bank)
                for c in range(8):
                    self.mm(self.pb[bank][:, 0:TT],
                            self.v(MOFF + c * 1024 + oc * 128, 128),
                            self.v(XN + c * TT, TT), c == 0, c == 7,
                            waits=[("GPS", gv)] if (c == 0 and oc == 0) else ())
                mv = self.tmark()
                sl = oc % 2
                e1 = self.op("scalar", "activation", self.v(SP + sl * 512, TT),
                             self.pb[bank][:, 0:TT], AF.Identity,
                             bias=self.v(self.M_BIAS1 + oc, 1),
                             waits=[("TEN", mv), ("VEC", prev_e2[sl])])
                e2 = self.op("vector", "tensor_add",
                             self.v(XS + oc * T + t0, TT),
                             self.v(XS + oc * T + t0, TT),
                             self.v(SP + sl * 512, TT),
                             waits=[("SCA", e1)])
                prev_e2[sl] = e2
                self.bank_free[bank] = ("VEC", e2)
        self.barrier()

    def emit_init(self):
        self.dma(self.v(self.C_ONES, 128), self.d["ones"].ap())
        self.dma(self.v(self.C_ID, 128), self.d["id"].ap())
        self.dma(self.v(self.M_RB2EB, 1, 0, 8), self.d["rb2"].ap())
        self.dma(self.v(self.M_G0, 2), self.d["g0"].ap())
        self.dma(self.v(self.M_B0, 2), self.d["b0"].ap())
        self.dma(self.v(self.M_BIAS1, 8), self.d["bias1"].ap())
        self.dma(self.v(self.M_ALPHA1, 1, 0, 1), self.d["alpha"].ap())
        self.op("vector", "memset", self.v(self.M_EPS, 1), 1e-5)
        self.op("vector", "memset", self.v(self.M_LN18, 1), LN1_8)
        lv = self.dma(self.v(self.M_SC1, 1, 0, 8), self.d["eb"].ap())
        wall = [(f"LD{i}", self.sem[f"LD{i}"].n) for i in range(8)]
        self.op("vector", "tensor_add", self.v(self.M_RB2EB, 1, 0, 8),
                self.v(self.M_RB2EB, 1, 0, 8), self.v(self.M_SC1, 1, 0, 8),
                waits=wall)
        mv = self.mm(self.pb[7][:, 0:1],
                     self.heap[0:1, self.C_ONES:self.C_ONES + 128],
                     self.heap[0:1, self.M_ALPHA1:self.M_ALPHA1 + 1],
                     True, True, waits=wall, inc=True)
        self.op("vector", "tensor_copy", self.v(self.M_ALPHA, 1),
                self.pb[7][:, 0:1], waits=[("TEN", mv)])
        self.barrier()

    def emit_store(self, s):
        T, XS = self.T, self.OFF["XSB"]
        self.wait("sync", "VEC", self.sem["VEC"].n)
        self.wait("sync", "GPS", self.sem["GPS"].n)
        self.wait("sync", "SCA", self.sem["SCA"].n)
        for c in range(8):
            self.dma(self.yout.ap()[s, c * 128:(c + 1) * 128, :],
                     self.v(XS + c * T, T))
        for i in range(8):
            self.wait("sync", f"LD{i}", self.sem[f"LD{i}"].n)
        self.barrier()

    def build(self):
        self.rope_done = 0
        self.emit_init()
        for s in range(NSEQ):
            self.emit_l0a(s)
            if self.nblocks >= 2:
                self.emit_l0f(s)
            if self.nblocks >= 3:
                self.emit_l1a(s)
            if self.nblocks >= 4:
                self.emit_l1f(s)
            self.emit_store(s)
        nc, ops = self.nc, self.ops
        with nc.Block() as block:
            @block.sync
            def _(e):
                for fn in ops["sync"]:
                    fn(e)

            @block.tensor
            def _(e):
                for fn in ops["tensor"]:
                    fn(e)

            @block.vector
            def _(e):
                for fn in ops["vector"]:
                    fn(e)

            @block.scalar
            def _(e):
                for fn in ops["scalar"]:
                    fn(e)

            @block.gpsimd
            def _(e):
                for fn in ops["gpsimd"]:
                    fn(e)
        return nc


_BUILD_CACHE = {}


def build_program(T=2048, nblocks=4):
    key = (T, nblocks)
    if key not in _BUILD_CACHE:
        b = B(T, nblocks)
        _BUILD_CACHE[key] = b.build()
        _BUILD_CACHE[key]._mmlab = b.mmlab
    return _BUILD_CACHE[key]


def kernel(**inputs):
    T = SEQT
    x = np.ascontiguousarray(np.asarray(inputs["L0_attn_x"], np.float32))
    Bt, Tx, Dx = x.shape
    assert Dx == D and Tx == T, (x.shape, T)
    shared = _host_prep(inputs)
    shared.update(_host_consts(T))
    nc = build_program(T, NBLOCKS)
    in_maps = []
    for c in range(NCORES):
        m = dict(shared)
        m["x"] = np.ascontiguousarray(x[2 * c:2 * c + 2].transpose(0, 2, 1))
        in_maps.append(m)
    res = run_bass_kernel_spmd(nc, in_maps, core_ids=list(range(NCORES)))
    out = np.empty((Bt, T, D), np.float32)
    for c in range(NCORES):
        y = np.asarray(res.results[c]["y"])
        out[2 * c:2 * c + 2] = y.transpose(0, 2, 1)
    return out


# version 19
# speedup vs baseline: 1.1954x; 1.0196x over previous
import numpy as np
import concourse.bass as bass
import concourse.mybir as mybir
from concourse.bass_utils import run_bass_kernel_spmd

F32 = mybir.dt.float32
BF16 = mybir.dt.bfloat16
AF = mybir.ActivationFunctionType
ALU = mybir.AluOpType
AX = mybir.AxisListType

D = 1024
LATD = 256
KR = 512
NCORES = 8
NSEQ = 2
NEG = -1.0e30
LN1_8 = float(np.log(1.0 / 8.0))

SEQT = 2048
NBLOCKS = 4
DETECT = False
SYNC_SER = False



def _host_consts(T):
    c = {}
    j = np.arange(128)
    c["ones"] = np.ones((128, 128), np.float32)
    c["id"] = np.eye(128, dtype=np.float32)
    f = (j % 64) % 32
    inv = (1.0 / 10000.0 ** (f / 32.0))[:, None]
    ang = inv * np.arange(T)[None, :]
    c["ropeC"] = np.cos(ang).astype(np.float32)
    c["ropeS"] = np.sin(ang).astype(np.float32)
    Pm = np.zeros((128, 128), np.float32)
    for r in range(128):
        rb = r % 64
        base = r - rb
        if rb < 32:
            Pm[r, base + rb + 32] = -1.0
        else:
            Pm[r, base + rb - 32] = 1.0
    c["pm0T"] = np.ascontiguousarray(Pm.T)
    kk, qq = np.meshgrid(np.arange(128), np.arange(128), indexing="ij")
    m = np.zeros((128, 128), np.float32)
    m[kk > qq] = NEG
    c["mask"] = m
    E = np.zeros((8, 512), np.float32)
    for col in range(512):
        E[col // 64, col] = 1.0
    c["ec"] = E
    nv = (1.0 / np.arange(1, T + 1, dtype=np.float64)).astype(np.float32)
    c["nvinv"] = np.ascontiguousarray(np.broadcast_to(nv, (128, T)))
    n = np.arange(D, dtype=np.int64)
    th = 2.0 * np.pi * (np.outer(n, n) % D).astype(np.float64) / D
    c["dftC"] = np.cos(th).astype(np.float32)
    c["dftS"] = np.sin(th).astype(np.float32)
    return c


def _blk_lhsT(w, kb=128, mb=128):
    K, M = w.shape
    nk, nm = K // kb, M // mb
    a = w.reshape(nk, kb, nm, mb).transpose(2, 1, 0, 3).reshape(nm, kb, nk * mb)
    return np.ascontiguousarray(a)


def _flat_one(w):
    K, M = w.shape
    nk = K // 128
    a = w.reshape(nk, 128, M).transpose(1, 0, 2).reshape(128, nk * M)
    return np.ascontiguousarray(a)


def _host_prep(inputs):
    f32 = lambda a: np.ascontiguousarray(np.asarray(a, np.float32))
    iw = {}
    iw["w0q"] = _blk_lhsT(f32(inputs["L0_attn_W_q"]))
    iw["w0dkv"] = _flat_one(f32(inputs["L0_attn_W_dkv"]))
    iw["w0uk"] = _blk_lhsT(f32(inputs["L0_attn_W_uk"]))
    iw["w0uv"] = _blk_lhsT(f32(inputs["L0_attn_W_uv"]))
    iw["w0o"] = f32(inputs["L0_attn_W_o"])
    iw["g0"] = _flat_one(f32(inputs["L0_attn_ln_gamma"]).reshape(LATD, 1))
    iw["b0"] = _flat_one(f32(inputs["L0_attn_ln_beta"]).reshape(LATD, 1))
    iw["wfQ"] = _blk_lhsT(f32(inputs["L0_ffn_W_Q"]).T)
    iw["wfK"] = _blk_lhsT(f32(inputs["L0_ffn_W_K"]).T)
    iw["wfO"] = _blk_lhsT(f32(inputs["L0_ffn_W_O"]).T)
    iw["wfI"] = _blk_lhsT(f32(inputs["L0_ffn_W_inv"]).T)
    iw["wfr1"] = _blk_lhsT(f32(inputs["L0_ffn_router_w1"]).T)
    iw["wfr2"] = _blk_lhsT(f32(inputs["L0_ffn_router_w2"]).T, mb=8)
    vflat = lambda a: _blk_lhsT(f32(a).transpose(1, 0, 2).reshape(D, KR))
    uflat = lambda a: _blk_lhsT(f32(a).transpose(0, 2, 1).reshape(KR, D))
    iw["vfw"] = vflat(inputs["L0_ffn_V_fwd"])
    iw["wfw"] = vflat(inputs["L0_ffn_W_fwd"])
    iw["ufw"] = uflat(inputs["L0_ffn_U_fwd"])
    iw["vin"] = vflat(inputs["L0_ffn_W_inv_exp"])
    iw["win"] = vflat(inputs["L0_ffn_V_inv"])
    iw["uin"] = uflat(inputs["L0_ffn_U_inv"])
    iw["bfw"] = f32(inputs["L0_ffn_b_fwd"])
    iw["bin"] = f32(inputs["L0_ffn_b_inv"])
    iw["rb1"] = _flat_one(f32(inputs["L0_ffn_router_b1"]).reshape(D, 1))
    iw["rb2"] = f32(inputs["L0_ffn_router_b2"]).reshape(8, 1)
    iw["eb"] = f32(inputs["L0_ffn_expert_bias"]).reshape(8, 1)
    iw["alpha"] = f32(inputs["L0_ffn_alpha_bi"]).reshape(1, 1)
    iw["w1q"] = _blk_lhsT(f32(inputs["L1_attn_wq"]).T)
    iw["w1ka"] = _blk_lhsT(f32(inputs["L1_attn_wk_a"]).T, mb=256)
    iw["w1va"] = _blk_lhsT(f32(inputs["L1_attn_wv_a"]).T, mb=256)
    iw["w1kb"] = _blk_lhsT(f32(inputs["L1_attn_wk_b"]).T)
    iw["w1vb"] = _blk_lhsT(f32(inputs["L1_attn_wv_b"]).T)
    iw["w1o"] = np.ascontiguousarray(f32(inputs["L1_attn_wo"]).T)
    iw["wrB"] = _blk_lhsT(f32(inputs["L1_ffn_w_freq_real"]))
    iw["wiB"] = _blk_lhsT(f32(inputs["L1_ffn_w_freq_imag"]))
    iw["bias1"] = _flat_one(f32(inputs["L1_ffn_bias"]).reshape(D, 1))
    return iw


_SHAPES = {
    "w0q": [8, 128, 1024], "w0dkv": [128, 2048], "w0uk": [8, 128, 256],
    "w0uv": [8, 128, 256], "w0o": [1024, 1024], "g0": [128, 2], "b0": [128, 2],
    "wfQ": [8, 128, 1024], "wfK": [8, 128, 1024], "wfO": [8, 128, 1024],
    "wfI": [8, 128, 1024], "wfr1": [8, 128, 1024], "wfr2": [1, 128, 64],
    "vfw": [4, 128, 1024], "wfw": [4, 128, 1024], "ufw": [8, 128, 512],
    "vin": [4, 128, 1024], "win": [4, 128, 1024], "uin": [8, 128, 512],
    "bfw": [8, 1024], "bin": [8, 1024],
    "rb1": [128, 8], "rb2": [8, 1], "eb": [8, 1], "alpha": [1, 1],
    "w1q": [8, 128, 1024], "w1ka": [1, 128, 2048], "w1va": [1, 128, 2048],
    "w1kb": [8, 128, 256], "w1vb": [8, 128, 256], "w1o": [1024, 1024],
    "wrB": [8, 128, 1024], "wiB": [8, 128, 1024], "bias1": [128, 8],
}

ENGS = ("sync", "tensor", "vector", "scalar", "gpsimd")


class Sem:
    def __init__(self, h):
        self.h = h
        self.n = 0

    def inc(self, k=1):
        self.n += k
        return self.n


class B:
    def __init__(self, T, nblocks):
        self.T = T
        self.TT = min(512, T)
        self.NTT = T // self.TT
        self.NKT = T // 128
        self.NQT = T // self.TT
        self.nqsub = self.TT // 128
        self.nblocks = nblocks
        nc = bass.Bass("TRN2", target_bir_lowering=False,
                       detect_race_conditions=DETECT)
        self.nc = nc
        self.ops = {e: [] for e in ENGS}

        dr = lambda n, s, k: nc.dram_tensor(n, s, F32, kind=k)
        self.xin = dr("x", [NSEQ, D, T], "ExternalInput")
        self.yout = dr("y", [NSEQ, D, T], "ExternalOutput")
        self.d = {n: dr(n, s, "ExternalInput") for n, s in _SHAPES.items()}
        for n, s in (("ropeC", [128, T]), ("ropeS", [128, T]),
                     ("pm0T", [128, 128]), ("mask", [128, 128]),
                     ("ones", [128, 128]), ("id", [128, 128]),
                     ("ec", [8, 512]), ("nvinv", [128, T]),
                     ("dftC", [1024, 1024]), ("dftS", [1024, 1024])):
            self.d[n] = dr(n, s, "ExternalInput")
        self.mcache = dr("mcache", [128, 8192], "Internal")

        T8 = 8 * T
        self.OFF = {}
        cur = 0
        for name, n in (("XSB", T8), ("H1", max(T8, 32 * self.TT, 16384)),
                        ("R2", 14368),
                        ("WB", 2560), ("CB", 896), ("ST", 1024), ("SP", 1024)):
            self.OFF[name] = cur
            cur += n
        self.NCOLS = cur
        assert cur <= 53200, cur
        self.heap = nc.alloc_sbuf_tensor("heap", [128, self.NCOLS], F32)
        ha = nc.lookup_mloc(self.heap).addr
        TTl = self.TT
        self._bfregs = []
        for i, (st, ln) in enumerate((
                (self.OFF["R2"], 8 * TTl),
                (self.OFF["R2"] + 20 * TTl, 4 * TTl),
                (self.OFF["H1"], 8 * TTl),
                (self.OFF["H1"] + 8 * TTl, 8 * TTl),
                (self.OFF["H1"] + 16 * TTl, 8 * TTl),
                (self.OFF["H1"] + 24 * TTl, 8 * TTl),
                (self.OFF["WB"], 2048),
                (self.OFF["R2"] + 16 * TTl, 4 * TTl),
                (self.OFF["R2"] + 8 * TTl, 4 * TTl),
                (self.OFF["R2"] + 12 * TTl, 4 * TTl),
        )):
            t = nc.alloc_sbuf_tensor_at(f"ab{i}", [128, 2 * ln], BF16,
                                        offset=ha + 4 * st)
            self._bfregs.append((st, ln, t))
        self.pb = [nc.alloc_psum_tensor(f"pb{i}", [128, 512], F32) for i in range(8)]
        self.sem = {n: Sem(nc.alloc_semaphore(n)) for n in
                    ("TEN", "VEC", "SCA", "GPS", "BAR")}
        for i in range(8):
            self.sem[f"LD{i}"] = Sem(nc.alloc_semaphore(f"LD{i}"))
        self._dmai = 0
        self._ld_last = {}
        CB = self.OFF["CB"]
        self.C_ONES = CB + 0
        self.C_ID = CB + 128
        self.C_OVL = CB + 256
        self.C_MISC = CB + 768
        M = self.C_MISC
        self.M_RB2EB = M + 0
        self.M_ALPHA = M + 1
        self.M_G0 = M + 2
        self.M_B0 = M + 4
        self.M_BIAS1 = M + 6
        self.M_CARRY = M + 16
        self.M_NEGB = M + 24
        self.M_MQ = M + 25
        self.M_MK = M + 26
        self.M_SC1 = M + 27
        self.M_SC2 = M + 28
        self.M_RECIP = M + 29
        self.M_ALPHA1 = M + 33
        self.M_EPS = M + 34
        self.M_LN18 = M + 35
        self.bank_free = {i: None for i in range(8)}
        self._stage_busy = {}
        self._pend = 0
        self._pcell = None
        self.phase = "init"
        self.mmlab = []

    def v(self, col, n, p0=0, p1=128):
        return self.heap[p0:p1, col:col + n]

    def hap(self, col, dims, np_=128):
        return bass.AP(self.heap, col, [[self.NCOLS, np_]] + [list(d) for d in dims])

    def pap(self, pbt, col, dims, np_=128):
        return bass.AP(pbt, col, [[512, np_]] + [list(d) for d in dims])

    def _bfreg(self, col, n):
        for st, ln, t in self._bfregs:
            if st <= col and col + n <= st + ln:
                return st, t
        raise KeyError((col, n))

    def vb(self, col, n, p0=0, p1=128, lo=False):
        st, t = self._bfreg(col, n)
        base = 2 * (col - st) + (n if lo else 0)
        return t[p0:p1, base:base + n]

    def vbx(self, col, n, soff, slen, p0=0, p1=128, lo=False):
        st, t = self._bfreg(col, n)
        base = 2 * (col - st) + (n if lo else 0) + soff
        return t[p0:p1, base:base + slen]

    def split(self, col, n, src_ap, waits=(), p0=0, p1=128):
        h_ = self.vb(col, n, p0, p1)
        l_ = self.vb(col, n, p0, p1, lo=True)
        self.op("vector", "tensor_copy", h_, src_ap, waits=waits)
        return self.op("vector", "tensor_tensor", l_, src_ap, h_, ALU.subtract)

    def load_pair(self, pair_col, ncols, dram_ap, stage_col, after=None):
        if after is not None:
            self.wait("sync", after[0], after[1])
        sb = self._stage_busy.get(stage_col)
        if sb:
            self.wait("sync", sb[0], sb[1])
        lv = self.dma(self.v(stage_col, ncols), dram_ap)
        sv = self.split(pair_col, ncols, self.v(stage_col, ncols), waits=[lv])
        self._stage_busy[stage_col] = ("VEC", sv)
        return sv

    def mm3(self, psum_ap, wcol, nwe, wc, rcol, n_r, start, stop, waits=(),
            wspan=128, p0=0, p1=128):
        wst, wt = self._bfreg(wcol, nwe)
        b0 = 2 * (wcol - wst)
        lh = wt[p0:p1, b0 + wc * wspan:b0 + (wc + 1) * wspan]
        ll = wt[p0:p1, b0 + nwe + wc * wspan:b0 + nwe + (wc + 1) * wspan]
        rh = self.vb(rcol, n_r, p0, p1)
        rl = self.vb(rcol, n_r, p0, p1, lo=True)
        self.mm(psum_ap, lh, rh, start, False, waits=waits)
        self.mm(psum_ap, lh, rl, False, False)
        return self.mm(psum_ap, ll, rh, False, stop)

    def on(self, eng, fn):
        self.ops[eng].append(fn)

    def ten_flush(self):
        if self._pend:
            self._pcell[0] += self._pend
            self._pend = 0

    def tmark(self):
        self.ten_flush()
        return self.sem["TEN"].n

    def wait(self, eng, sem, val):
        if not val:
            return
        if eng == "tensor":
            self.ten_flush()
        self.on(eng, lambda e, s=self.sem[sem].h, v=val: e.wait_ge(s, v))

    def ready(self):
        return [("VEC", self.sem["VEC"].n), ("SCA", self.sem["SCA"].n),
                ("GPS", self.sem["GPS"].n)]

    def barrier(self):
        self.ten_flush()
        bar = self.sem["BAR"]
        v = bar.inc(5)
        for eng in ENGS:
            self.on(eng, lambda e, s=bar.h, v=v:
                    (e.nop().then_inc(s, 1), e.wait_ge(s, v)))
        for i in range(8):
            self.bank_free[i] = None

    def dma(self, dst_ap, src_ap, after=None):
        name = f"LD{self._dmai % 8}"
        self._dmai += 1
        LD = self.sem[name]
        if after is not None:
            self.wait("sync", after[0], after[1])
        if name in self._ld_last:
            self.wait("sync", name, self._ld_last[name])
        val = LD.inc(16)
        self._ld_last[name] = val
        self.on("sync", lambda e, d=dst_ap, s=src_ap, h=LD.h:
                e.dma_start(out=d, in_=s).then_inc(h, 16))
        if SYNC_SER:
            self.wait("sync", name, val)
        return (name, val)

    def t_bank(self, bank):
        bf = self.bank_free[bank]
        if bf is not None:
            self.wait("tensor", bf[0], bf[1])

    def mm(self, psum_ap, lhsT, rhs, start, stop, waits=(), tile_position=None,
           inc=False):
        for (s, v) in waits:
            self.wait("tensor", s, v)
        TEN = self.sem["TEN"]
        val = TEN.inc()
        try:
            shp = (lhsT.ap[0][1], lhsT.free_size(), rhs.free_size(),
                   "f32" if lhsT.tensor.dtype == F32 else "bf16")
        except (AttributeError, IndexError):
            shp = None
        self.mmlab.append((self.phase, shp))
        cell = [0]
        self._pend += 1
        self._pcell = cell
        if inc:
            self.ten_flush()
        def fn(e, o=psum_ap, l=lhsT, r=rhs, st=start, sp=stop, h=TEN.h,
               c=cell, tpos=tile_position):
            i = e.matmul(o, l, r, start=st, stop=sp, tile_position=tpos)
            if c[0]:
                i.then_inc(h, c[0])
        self.on("tensor", fn)
        return val

    def tp(self, out_ap, in_ap, waits=(), kdim=128, tile_position=None):
        for (s, v) in waits:
            self.wait("tensor", s, v)
        self.ten_flush()
        TEN = self.sem["TEN"]
        val = TEN.inc()
        self.mmlab.append((self.phase + "/tp", "tp"))
        def fn(e, o=out_ap, i_=in_ap,
               idn=self.heap[0:kdim, self.C_ID:self.C_ID + kdim], h=TEN.h,
               tpos=tile_position):
            e.transpose(o, i_, idn).then_inc(h, 1)
        self.on("tensor", fn)
        return val

    def op(self, eng, meth, *args, waits=(), **kw):
        S = {"vector": "VEC", "scalar": "SCA", "gpsimd": "GPS"}[eng]
        for (s, v) in waits:
            self.wait(eng, s, v)
        sem = self.sem[S]
        prev = sem.n
        val = sem.inc()
        def fn(e, m=meth, a=args, k=kw, h=sem.h, p=prev):
            i = getattr(e, m)(*a, **k).then_inc(h, 1)
            if p:
                i._wait_ge(h, p)
        self.on(eng, fn)
        return val

    def ln_tile(self, src_off, dst_off, tt, inv_d, dst_tile, sq_off):
        self.phase = self.phase.split("|")[0] + "|ln"
        """LayerNorm of one t-tile: dst = (src - mean)/sqrt(var+eps).
        src chunk c at src_off + c*T + tt*TT; dst chunk c at
        dst_off + c*(TT if dst_tile else T) (+ tt*TT if not dst_tile).
        sq_off: 8*TT scratch cols for squares. Returns GPS val of completion."""
        T, TT = self.T, self.TT
        ST = self.OFF["ST"]
        t0 = tt * TT
        meanB = self.v(ST, TT)
        rsd = self.v(ST + 512, TT)
        src3 = self.hap(src_off + t0, [[T, 8], [1, TT]])
        dst3 = self.hap(sq_off, [[TT, 8], [1, TT]])
        sq = self.op("scalar", "activation", dst3, src3, AF.Square,
                     waits=self.ready())
        for b_, off, step, w in ((0, src_off + t0, T, None), (1, sq_off, TT, sq)):
            self.t_bank(b_)
            for c in range(8):
                self.mm(self.pb[b_][:, 0:TT], self.v(self.C_ONES, 128),
                        self.v(off + c * step, TT), c == 0, c == 7,
                        waits=([("SCA", w)] if (w and c == 0) else ()))
        vsum = self.tmark()
        self.op("scalar", "activation", meanB, self.pb[0][:, 0:TT], AF.Copy,
                scale=inv_d, waits=[("TEN", vsum)])
        e2 = self.op("scalar", "activation", rsd, self.pb[1][:, 0:TT], AF.Copy,
                     scale=inv_d)
        m2 = self.op("vector", "tensor_mul", self.pb[0][:, 0:TT], meanB, meanB,
                     waits=[("SCA", e2)])
        v2 = self.op("vector", "tensor_sub", rsd, rsd, self.pb[0][:, 0:TT],
                     waits=[("VEC", m2)])
        self.op("scalar", "activation", rsd, rsd, AF.Ln, bias=self.v(self.M_EPS, 1),
                waits=[("VEC", v2)])
        r2 = self.op("scalar", "activation", rsd, rsd, AF.Exp, scale=-0.5)
        self.bank_free[0] = ("VEC", v2)
        self.bank_free[1] = ("SCA", e2)
        gv = None
        for c in range(8):
            xc = self.v(src_off + c * T + t0, TT)
            dc = self.v(dst_off + c * (TT if dst_tile else T) +
                        (0 if dst_tile else t0), TT)
            sv = self.op("vector", "tensor_sub", dc, xc, meanB,
                         waits=([("SCA", r2)] if c == 0 else ()))
            gv = self.op("gpsimd", "tensor_mul", dc, dc, rsd,
                         waits=[("VEC", sv)])
        return gv

    def preln(self, dst_off, sq_off):
        gv = None
        for tt in range(self.NTT):
            gv = self.ln_tile(self.OFF["XSB"], dst_off, tt, 1.0 / D, False, sq_off)
        self.barrier()
        return gv

    def bound_tile(self, src_off, acc_col, first, wv):
        TT = self.TT
        sq = self.v(self.OFF["ST"], TT)
        sv = self.op("scalar", "activation", sq, self.v(src_off, TT), AF.Square,
                     waits=[("VEC", wv)] if wv else ())
        self._last_sq = sv
        self.t_bank(5)
        mv = self.mm(self.pb[5][0:1, 0:TT], self.v(self.C_ONES, 1), sq,
                     True, True, waits=[("SCA", sv)], inc=True)
        if first:
            rv = self.op("vector", "reduce_max", self.v(acc_col, 1, 0, 1),
                         self.pb[5][0:1, 0:TT], AX.X, waits=[("TEN", mv)])
        else:
            self.op("vector", "reduce_max", self.v(self.M_SC1, 1, 0, 1),
                    self.pb[5][0:1, 0:TT], AX.X, waits=[("TEN", mv)])
            rv = self.op("vector", "tensor_max", self.v(acc_col, 1, 0, 1),
                         self.v(acc_col, 1, 0, 1), self.v(self.M_SC1, 1, 0, 1))
        self.bank_free[5] = ("VEC", rv)
        return rv

    def finish_bound(self):
        mq = self.v(self.M_MQ, 1, 0, 1)
        mk = self.v(self.M_MK, 1, 0, 1)
        self.op("scalar", "activation", mq, mq, AF.Ln, waits=self.ready())
        l2 = self.op("scalar", "activation", mk, mk, AF.Ln)
        a1 = self.op("vector", "tensor_add", mq, mq, mk, waits=[("SCA", l2)])
        e1 = self.op("scalar", "activation", mq, mq, AF.Exp, scale=0.5,
                     bias=self.v(self.M_LN18, 1, 0, 1), waits=[("VEC", a1)])
        n1 = self.op("vector", "tensor_scalar", mq, mq, -1.0, 25.0,
                     ALU.mult, ALU.add, waits=[("SCA", e1)])
        self.t_bank(5)
        mv = self.mm(self.pb[5][:, 0:1],
                     self.heap[0:1, self.C_ONES:self.C_ONES + 128], mq,
                     True, True, waits=[("VEC", n1)], inc=True)
        cv = self.op("vector", "tensor_copy", self.v(self.M_NEGB, 1),
                     self.pb[5][:, 0:1], waits=[("TEN", mv)])
        self.bank_free[5] = ("VEC", cv)
        return cv

    def attn_group(self, QG, KG, VG, causal, bound_wait, wo_cols, wo_ld,
                   s_pair=False):
        base = self.phase.split("|")[0]
        """attention for 2 heads + per-q-tile W_o partial accumulated into x.
        QG/KG feature-major [128 x T]; VG token-major aug [128 x 129*NKT]
        per k-tile laid out [V_h0(64)|ones|V_h1(64)].  PV is feature-major:
        O_aug^T [65 x TT] = V_augT.T @ P^T accumulated over k-tiles; the aug
        row gives the softmax denominator (row 64 for head 0, row 0 for head
        1).  Normalization multiplies by exp(-ln(denom)) broadcast across 64
        partitions via a K=1 matmul.  W_o is applied as two K=64 matmuls
        against AT[64 x 2*TT]; wo_cols = WB cols of the two 64-row halves."""
        T, TT, NKT, NQT, nqsub = self.T, self.TT, self.NKT, self.NQT, self.nqsub
        SP, ST, XS = self.OFF["SP"], self.OFF["ST"], self.OFF["XSB"]
        negb = self.v(self.M_NEGB, 1)
        AT = ST
        last_at = 0
        if causal:
            sbank, pslot = (0, 1), (SP, SP + 512)
        else:
            sbank, pslot = (0, 1, 3), (SP, SP + 512, self.C_OVL)
        nbank = len(sbank)
        for qt in range(NQT):
            at_done = 0
            for hl in range(2):
                kts = list(range(min(NKT, nqsub * qt + nqsub))) if causal \
                    else list(range(NKT))
                drow = 64
                vrow0 = 0
                last_pv = None
                prev_pv = {}
                rd = self.ready()
                self.t_bank(2)
                def emit_S(i, hl=hl, qt=qt, kts=kts, rd=rd):
                    kt = kts[i]
                    bank = sbank[i % nbank]
                    qoffb = max(0, kt - nqsub * qt) if causal else 0
                    self.t_bank(bank)
                    self.phase = base + "|S"
                    w0 = rd if i == 0 and hl == 0 else ()
                    if not s_pair:
                        return self.mm(
                            self.pb[bank][:, qoffb * 128:TT],
                            self.heap[hl * 64:hl * 64 + 64,
                                      KG + kt * 128:KG + kt * 128 + 128],
                            self.heap[hl * 64:hl * 64 + 64,
                                      QG + qt * TT + qoffb * 128:
                                      QG + (qt + 1) * TT],
                            True, True, waits=w0, inc=True)
                    p0, p1 = hl * 64, hl * 64 + 64
                    t0k = (kt * 128 // TT) * TT
                    sk = kt * 128 - t0k
                    sq_ = qoffb * 128
                    nq_ = TT - sq_
                    kh = self.vbx(KG + t0k, TT, sk, 128, p0, p1)
                    kl = self.vbx(KG + t0k, TT, sk, 128, p0, p1, lo=True)
                    qh = self.vbx(QG + qt * TT, TT, sq_, nq_, p0, p1)
                    ql = self.vbx(QG + qt * TT, TT, sq_, nq_, p0, p1, lo=True)
                    ps = self.pb[bank][:, sq_:TT]
                    self.mm(ps, kh, qh, True, False, waits=w0)
                    self.mm(ps, kh, ql, False, False)
                    return self.mm(ps, kl, qh, False, True, inc=True)
                svs = {}
                for i in range(min(nbank, len(kts))):
                    svs[i] = emit_S(i)
                for i, kt in enumerate(kts):
                    par = i % nbank
                    bank = sbank[par]
                    qoffb = max(0, kt - nqsub * qt) if causal else 0
                    n_q = TT - qoffb * 128
                    sv = svs[i]
                    mwait = [("TEN", sv)]
                    jd = kt - nqsub * qt
                    if causal and 0 <= jd < nqsub:
                        mv = self.op("vector", "tensor_add",
                                     self.pb[bank][:, jd * 128:jd * 128 + 128],
                                     self.pb[bank][:, jd * 128:jd * 128 + 128],
                                     self.v(self.C_OVL + 128, 128),
                                     waits=[("TEN", sv)])
                        mwait = [("VEC", mv)]
                    ew = list(mwait)
                    if i == 0:
                        ew.append(("VEC", bound_wait))
                    if i <= 1 and last_at:
                        ew.append(("VEC", last_at))
                    if par in prev_pv:
                        ew.append(("TEN", prev_pv[par]))
                    pexp = self.op("scalar", "activation",
                                   self.v(pslot[par], n_q),
                                   self.pb[bank][:, qoffb * 128:TT], AF.Exp,
                                   bias=negb, scale=0.125, waits=ew)
                    self.bank_free[bank] = ("SCA", pexp)
                    self.phase = base + "|PV"
                    last_pv = self.mm(
                        self.pb[2][0:65, qoffb * 128:TT],
                        self.hap(VG + kt * 130 + 65 * hl, [[1, 65]]),
                        self.v(pslot[par], n_q),
                        i == 0, i == len(kts) - 1,
                        waits=[("SCA", pexp)], inc=True)
                    prev_pv[par] = last_pv
                    if i + nbank < len(kts):
                        svs[i + nbank] = emit_S(i + nbank)
                u1 = self.op("scalar", "activation", self.v(SP, TT, 0, 1),
                             self.pb[2][drow:drow + 1, 0:TT], AF.Ln,
                             waits=[("TEN", last_pv)])
                u2 = self.op("scalar", "activation", self.v(SP, TT, 0, 1),
                             self.v(SP, TT, 0, 1), AF.Exp, scale=-1.0)
                self.t_bank(3)
                self.phase = base + "|norm"
                bc = self.mm(self.pb[3][0:64, 0:TT],
                             self.heap[0:1, self.C_ONES:self.C_ONES + 64],
                             self.v(SP, TT, 0, 1), True, True,
                             waits=[("SCA", u2)], inc=True)
                cpv = self.op("vector", "tensor_copy", self.v(SP + 512, TT, 0, 64),
                              self.pb[3][0:64, 0:TT], waits=[("TEN", bc)])
                at_done = self.op("vector", "scalar_tensor_tensor",
                                  self.heap[0:64, AT + hl * TT:AT + (hl + 1) * TT],
                                  self.pb[2][vrow0:vrow0 + 64, 0:TT], 1.0,
                                  self.v(SP + 512, TT, 0, 64),
                                  ALU.mult, ALU.mult, waits=[("VEC", cpv)])
                self.bank_free[2] = ("VEC", at_done)
                self.bank_free[3] = ("VEC", cpv)
                last_at = at_done
                self.attn_guard = [("TEN", last_pv), ("SCA", u2)]
            self.phase = base + "|WO"
            for oc in range(8):
                bank = 4 + (oc % 4)
                self.t_bank(bank)
                for hl in range(2):
                    w0 = []
                    if oc == 0 and hl == 0:
                        w0 = [("VEC", at_done)] + list(wo_ld)
                    mv = self.mm(self.pb[bank][:, 0:TT],
                                 self.heap[0:64, wo_cols[hl] + oc * 128:
                                           wo_cols[hl] + oc * 128 + 128],
                                 self.heap[0:64, AT + hl * TT:AT + (hl + 1) * TT],
                                 hl == 0, hl == 1, waits=w0, inc=(hl == 1))
                av = self.op("vector", "tensor_add",
                             self.v(XS + oc * T + qt * TT, TT),
                             self.v(XS + oc * T + qt * TT, TT),
                             self.pb[bank][:, 0:TT], waits=[("TEN", mv)])
                self.bank_free[bank] = ("VEC", av)

    def rope(self, off, lc1, lcs, src_ready):
        TT = self.TT
        SP, ST = self.OFF["SP"], self.OFF["ST"]
        self.t_bank(4)
        mv = self.mm(self.pb[4][:, 0:TT], self.v(self.C_OVL, 128),
                     self.v(off, TT), True, True, waits=[("VEC", src_ready)],
                     inc=True)
        tmp = self.v(ST, TT)
        v1 = self.op("vector", "tensor_mul", tmp, self.pb[4][:, 0:TT],
                     self.v(SP + 512, TT), waits=[("TEN", mv), lcs])
        self.bank_free[4] = ("VEC", v1)
        v2 = self.op("gpsimd", "tensor_mul", self.v(off, TT), self.v(off, TT),
                     self.v(SP, TT),
                     waits=[lc1, ("VEC", src_ready), ("TEN", mv)])
        v3 = self.op("vector", "tensor_add", self.v(off, TT), self.v(off, TT),
                     tmp, waits=[("GPS", v2)])
        return v3

    def emit_l0a(self, s):
        self.phase = "l0a.front"
        T, TT, NTT, NKT = self.T, self.TT, self.NTT, self.NKT
        XS, H1, R2, WB, SP = (self.OFF[k] for k in
                              ("XSB", "H1", "R2", "WB", "SP"))
        for c in range(8):
            lv = self.dma(self.v(XS + c * T, T),
                          self.xin.ap()[s, c * 128:(c + 1) * 128, :])
        self.dma(self.v(self.C_OVL, 128), self.d["pm0T"].ap())
        lov = self.dma(self.v(self.C_OVL + 128, 128), self.d["mask"].ap())
        wall = [(f"LD{i}", self.sem[f"LD{i}"].n) for i in range(8)]
        for eng in ("scalar", "vector", "tensor", "gpsimd"):
            for w in wall:
                self.wait(eng, *w)
        self.barrier()
        self.preln(H1, R2)
        LAT = R2
        SQ2 = R2 + 2 * T
        ldkv = self.dma(self.v(WB, 2048), self.d["w0dkv"].ap())
        for tt in range(NTT):
            t0 = tt * TT
            rd = self.ready()
            for lc in range(2):
                self.t_bank(lc)
                for c in range(8):
                    self.mm(self.pb[lc][:, 0:TT],
                            self.v(WB + c * 256 + lc * 128, 128),
                            self.v(H1 + c * T + t0, TT), c == 0, c == 7,
                            waits=(rd + [ldkv]) if (c == 0 and lc == 0) else ())
                ev = self.op("vector", "tensor_copy", self.v(LAT + lc * T + t0, TT),
                             self.pb[lc][:, 0:TT], waits=[("TEN", self.tmark())])
                self.bank_free[lc] = ("VEC", ev)
            lat_ev = ev
            lsq = self.op("scalar", "activation",
                          self.hap(SQ2, [[TT, 2], [1, TT]]),
                          self.hap(LAT + t0, [[T, 2], [1, TT]]),
                          AF.Square, waits=[("VEC", self.sem["VEC"].n)])
            for b_, off, step, w in ((2, LAT + t0, T, None), (3, SQ2, TT, lsq)):
                self.t_bank(b_)
                for lc in range(2):
                    self.mm(self.pb[b_][:, 0:TT], self.v(self.C_ONES, 128),
                            self.v(off + lc * step, TT), lc == 0, lc == 1,
                            waits=([("SCA", w)] if (w and lc == 0) else
                                   ([("VEC", lat_ev)] if lc == 0 else ())))
            vsum = self.tmark()
            ST = self.OFF["ST"]
            meanB = self.v(ST, TT)
            rsd = self.v(ST + 512, TT)
            self.op("scalar", "activation", meanB, self.pb[2][:, 0:TT], AF.Copy,
                    scale=1.0 / LATD, waits=[("TEN", vsum)])
            e2 = self.op("scalar", "activation", rsd, self.pb[3][:, 0:TT],
                         AF.Copy, scale=1.0 / LATD)
            m2 = self.op("vector", "tensor_mul", self.pb[2][:, 0:TT], meanB, meanB,
                         waits=[("SCA", e2)])
            v2 = self.op("vector", "tensor_sub", rsd, rsd, self.pb[2][:, 0:TT],
                         waits=[("VEC", m2)])
            self.op("scalar", "activation", rsd, rsd, AF.Ln, bias=self.v(self.M_EPS, 1),
                    waits=[("VEC", v2)])
            r2 = self.op("scalar", "activation", rsd, rsd, AF.Exp, scale=-0.5)
            self.bank_free[2] = ("VEC", v2)
            self.bank_free[3] = ("SCA", e2)
            for lc in range(2):
                la = self.v(LAT + lc * T + t0, TT)
                self.op("vector", "tensor_sub", la, la, meanB,
                        waits=([("SCA", r2)] if lc == 0 else ()))
                mv = self.op("vector", "tensor_mul", la, la, rsd)
                self.op("scalar", "activation", la, la, AF.Identity,
                        scale=self.v(self.M_G0 + lc, 1),
                        bias=self.v(self.M_B0 + lc, 1), waits=[("VEC", mv)])
        self.barrier()
        QG = R2 + 2 * T
        KG = QG + T
        VG = KG + T
        for g in range(8):
            self._l0a_group(g, LAT, QG, KG, VG)
        self.barrier()

    def _l0a_group(self, g, LAT, QG, KG, VG):
        self.phase = "l0a.proj"
        T, TT, NTT, NKT = self.T, self.TT, self.NTT, self.NKT
        H1, WB, SP = self.OFF["H1"], self.OFF["WB"], self.OFF["SP"]
        lq = self.dma(self.v(WB, 1024), self.d["w0q"].ap()[g])
        luk = self.dma(self.v(WB + 1024, 256), self.d["w0uk"].ap()[g])
        luv = self.dma(self.v(WB + 1280, 256), self.d["w0uv"].ap()[g])
        lwo = self.dma(self.heap[0:64, WB + 1536:WB + 2560],
                       self.d["w0o"].ap()[g * 128:g * 128 + 64, :])
        rd = self.ready()
        SCRQ = self.OFF["R2"] + 4 * T + 2080
        SCRK = SCRQ + 512
        for tt in range(NTT):
            t0 = tt * TT
            qb = 6 + (tt % 2)
            kb = 7 - (tt % 2)
            lc1 = self.dma(self.v(SP, TT), self.d["ropeC"].ap()[:, t0:t0 + TT],
                           after=("VEC", self.rope_done) if getattr(self, "rope_done", 0) else None)
            lcs = self.dma(self.v(SP + 512, TT), self.d["ropeS"].ap()[:, t0:t0 + TT])
            self.t_bank(qb)
            for c in range(8):
                self.mm(self.pb[qb][:, 0:TT], self.v(WB + c * 128, 128),
                        self.v(H1 + c * T + t0, TT), c == 0, c == 7,
                        waits=(rd + [lq]) if (c == 0 and tt == 0) else ())
            qw = [("TEN", self.tmark())]
            if tt > 0:
                qw.append(("SCA", self._last_sq))
            qv = self.op("vector", "tensor_copy", self.v(SCRQ, TT),
                         self.pb[qb][:, 0:TT], waits=qw)
            self.bank_free[qb] = ("VEC", qv)
            rq = self.rope(SCRQ, lc1, lcs, qv)
            self.bound_tile(SCRQ, self.M_MQ, tt == 0, rq)
            self.split(QG + t0, TT, self.v(SCRQ, TT))
            self.t_bank(kb)
            for lc in range(2):
                self.mm(self.pb[kb][:, 0:TT], self.v(WB + 1024 + lc * 128, 128),
                        self.v(LAT + lc * T + t0, TT), lc == 0, lc == 1,
                        waits=[luk] if lc == 0 else ())
            kw = [("TEN", self.tmark()), ("SCA", self._last_sq)]
            kv = self.op("vector", "tensor_copy", self.v(SCRK, TT),
                         self.pb[kb][:, 0:TT], waits=kw)
            self.bank_free[kb] = ("VEC", kv)
            rk = self.rope(SCRK, lc1, lcs, kv)
            self.rope_done = self.bound_tile(SCRK, self.M_MK, tt == 0, rk)
            self.split(KG + t0, TT, self.v(SCRK, TT))
        lwo2 = self.dma(self.heap[0:64, WB:WB + 1024],
                        self.d["w0o"].ap()[g * 128 + 64:g * 128 + 128, :],
                        after=("TEN", self.tmark()))
        bv = self.finish_bound()
        mz = self.op("vector", "memset",
                     self.hap(VG + 64, [[130, NKT], [65, 2]]), 1.0)
        for kt in range(NKT):
            vb_ = 6 + (kt % 2)
            self.t_bank(vb_)
            for lc in range(2):
                self.mm(self.pb[vb_][:, 0:128],
                        self.v(LAT + lc * T + kt * 128, 128),
                        self.v(WB + 1280 + lc * 128, 128),
                        lc == 0, lc == 1,
                        waits=[luv] if (kt == 0 and lc == 0) else ())
            ev = self.op("vector", "tensor_copy",
                         self.hap(VG + kt * 130, [[65, 2], [1, 64]]),
                         self.pb[vb_][:, 0:128],
                         waits=[("TEN", self.tmark()), ("VEC", mz)])
            self.bank_free[vb_] = ("VEC", ev)
        self.attn_group(QG, KG, VG, True, bv, (WB + 1536, WB), (lwo, lwo2),
                        s_pair=True)
        self.barrier()
        self.rope_done = 0

    def emit_l0f(self, s):
        lec = self.dma(self.v(self.C_OVL, 512, 0, 8), self.d["ec"].ap())
        self.wait("tensor", *lec)
        for tt in range(self.NTT):
            self._l0f_tile(tt)
        self.barrier()

    def ln_tile_pair(self, src_off, dst_pair, tt, scr_off):
        self.phase = "l0f.ln"
        """LayerNorm of one t-tile of chunked fp32 src into a bf16 PAIR at
        dst_pair (+ c*TT per chunk).  scr_off: 8*TT fp32 scratch."""
        T, TT = self.T, self.TT
        ST = self.OFF["ST"]
        t0 = tt * TT
        meanB = self.v(ST, TT)
        rsd = self.v(ST + 512, TT)
        src3 = self.hap(src_off + t0, [[T, 8], [1, TT]])
        dst3 = self.hap(scr_off, [[TT, 8], [1, TT]])
        sq = self.op("scalar", "activation", dst3, src3, AF.Square,
                     waits=self.ready())
        for b_, off, step, w in ((0, src_off + t0, T, None),
                                 (1, scr_off, TT, sq)):
            self.t_bank(b_)
            for c in range(8):
                self.mm(self.pb[b_][:, 0:TT], self.v(self.C_ONES, 128),
                        self.v(off + c * step, TT), c == 0, c == 7,
                        waits=([("SCA", w)] if (w and c == 0) else ()))
        vsum = self.tmark()
        self.op("scalar", "activation", meanB, self.pb[0][:, 0:TT], AF.Copy,
                scale=1.0 / D, waits=[("TEN", vsum)])
        e2 = self.op("scalar", "activation", rsd, self.pb[1][:, 0:TT], AF.Copy,
                     scale=1.0 / D)
        m2 = self.op("vector", "tensor_mul", self.pb[0][:, 0:TT], meanB, meanB,
                     waits=[("SCA", e2)])
        v2 = self.op("vector", "tensor_sub", rsd, rsd, self.pb[0][:, 0:TT],
                     waits=[("VEC", m2)])
        self.op("scalar", "activation", rsd, rsd, AF.Ln,
                bias=self.v(self.M_EPS, 1), waits=[("VEC", v2)])
        r2 = self.op("scalar", "activation", rsd, rsd, AF.Exp, scale=-0.5)
        self.bank_free[0] = ("VEC", v2)
        self.bank_free[1] = ("SCA", e2)
        gv = None
        for c in range(8):
            xc = self.v(src_off + c * T + t0, TT)
            uc = self.v(scr_off + c * TT, TT)
            sv = self.op("vector", "tensor_sub", uc, xc, meanB,
                         waits=([("SCA", r2)] if c == 0 else ()))
            gm = self.op("gpsimd", "tensor_mul", uc, uc, rsd,
                         waits=[("VEC", sv)])
            gv = self.split(dst_pair + c * TT, TT, uc, waits=[("GPS", gm)])
        return gv

    def _mmblk(self, dst_off, rhs_pair, wname, nblk, nk, blkelems, stage,
               extra=(), evac="pair", evac_args=None):
        TT = self.TT
        WB = self.OFF["WB"]
        self.phase = self.phase.split("|")[0] + "|mb." + wname
        evs = []

        def lp(ob):
            sl = ob % 2
            after = ("TEN", self.wlast[sl]) if self.wlast[sl] else None
            return self.load_pair(WB + sl * 1024, blkelems,
                                  self.d[wname].ap()[ob],
                                  stage[sl], after=after)
        wsv = lp(0)
        for ob in range(nblk):
            sl = ob % 2
            wsv_next = lp(ob + 1) if ob + 1 < nblk else None
            bank = 4 + sl
            self.t_bank(bank)
            w0 = (list(extra) if ob == 0 else []) + [("VEC", wsv)]
            for c in range(nk):
                self.mm3(self.pb[bank][:, 0:TT], WB + sl * 1024, blkelems, c,
                         rhs_pair + c * TT, TT, c == 0, c == nk - 1,
                         waits=w0 if c == 0 else ())
            self.wlast[sl] = self.tmark()
            mv = self.tmark()
            if evac == "pair":
                ev = self.split(dst_off + ob * TT, TT, self.pb[bank][:, 0:TT],
                                waits=[("TEN", mv)])
                self.bank_free[bank] = ("VEC", ev)
            elif evac == "copy":
                ev = self.op("vector", "tensor_copy",
                             self.v(dst_off + ob * TT, TT),
                             self.pb[bank][:, 0:TT], waits=[("TEN", mv)])
                self.bank_free[bank] = ("VEC", ev)
            elif evac == "gelu":
                ev = self.op("scalar", "activation",
                             self.v(dst_off + ob * TT, TT),
                             self.pb[bank][:, 0:TT], AF.Gelu,
                             bias=self.v(evac_args + ob, 1), waits=[("TEN", mv)])
                self.bank_free[bank] = ("SCA", ev)
            evs.append(ev)
            wsv = wsv_next
        return evs

    def _l0f_tile(self, tt):
        T, TT = self.T, self.TT
        XS, H1, R2, WB, SP, ST = (self.OFF[k] for k in
                                  ("XSB", "H1", "R2", "WB", "SP", "ST"))
        t0 = tt * TT
        XNP = R2
        HH = R2 + 8 * TT
        XV = R2 + 16 * TT
        YW = R2 + 20 * TT
        CU = R2 + 24 * TT
        BF = R2 + 1024
        QQ, KK, IQ, IK = H1, H1 + 8 * TT, H1 + 16 * TT, H1 + 24 * TT
        GF, GI = KK, QQ
        CMB = IQ
        WF, WI = ST, ST + 512
        stageA = (HH, HH + 1024)
        stageRT = (XNP, XNP + 1024)
        stageP = (XNP, HH + 4 * TT)
        stageB = XNP
        self.wlast = [None, None]
        lrw2 = self.dma(self.v(WB + 2048, 64), self.d["wfr2"].ap()[0])
        lrb1 = self.dma(self.v(WB + 2112, 8), self.d["rb1"].ap())
        lnv = self.dma(self.v(SP, TT), self.d["nvinv"].ap()[:, t0:t0 + TT])
        gv = self.ln_tile_pair(XS, XNP, tt, HH)
        self.phase = "l0f"
        self._stage_busy[stageA[0]] = ("VEC", gv)
        self._stage_busy[stageA[1]] = ("VEC", gv)
        self._mmblk(QQ, XNP, "wfQ", 8, 8, 1024, stageA,
                    extra=[("VEC", gv), ("GPS", self.sem["GPS"].n)])
        self._mmblk(KK, XNP, "wfK", 8, 8, 1024, stageA)
        self._mmblk(IQ, QQ, "wfI", 8, 8, 1024, stageA)
        self._mmblk(IK, KK, "wfI", 8, 8, 1024, stageA)
        self._stage_busy[stageRT[0]] = ("TEN", self.tmark())
        self._stage_busy[stageRT[1]] = ("TEN", self.tmark())
        self.phase = "l0f|rtr"
        for srcq, wdst in ((QQ, WF), (IQ, WI)):
            self._mmblk(HH, srcq, "wfr1", 8, 8, 1024, stageRT,
                        evac="gelu", evac_args=WB + 2112)
            self.wait("scalar", *lrb1)
            self.t_bank(6)
            rd = self.ready()
            for c in range(8):
                self.mm(self.pb[6][0:8, 0:TT], self.v(WB + 2048 + c * 8, 8),
                        self.v(HH + c * TT, TT), c == 0, c == 7,
                        waits=(rd + [lrw2]) if c == 0 else ())
            lg = self.op("scalar", "activation", self.v(wdst, TT, 0, 8),
                         self.pb[6][0:8, 0:TT], AF.Identity,
                         bias=self.v(self.M_RB2EB, 1, 0, 8),
                         waits=[("TEN", self.tmark())])
            self.bank_free[6] = ("SCA", lg)
            wtm = self.v(SP + 512, 128)
            for j in range(TT // 128):
                self.t_bank(7)
                v1 = self.tp(self.pb[7][0:128, 0:8],
                             self.heap[0:8, wdst + j * 128:wdst + (j + 1) * 128],
                             waits=[("SCA", lg)], kdim=8)
                nm = self.op("vector", "tensor_reduce", self.v(self.M_SC1, 1),
                             self.pb[7][:, 0:8], AX.X, ALU.max, negate=True,
                             waits=[("TEN", v1)])
                e1 = self.op("scalar", "activation", wtm[:, 0:8],
                             self.pb[7][:, 0:8], AF.Exp,
                             bias=self.v(self.M_SC1, 1), waits=[("VEC", nm)])
                self.op("vector", "reduce_sum", self.v(self.M_SC1, 1),
                        wtm[:, 0:8], AX.X, waits=[("SCA", e1)])
                self.op("vector", "reciprocal", self.v(self.M_SC2, 1),
                        self.v(self.M_SC1, 1))
                m1 = self.op("vector", "tensor_scalar_mul", wtm[:, 0:8],
                             wtm[:, 0:8], self.v(self.M_SC2, 1))
                self.bank_free[7] = ("VEC", m1)
                self.t_bank(7)
                v2 = self.tp(self.pb[7][0:8, 0:128], wtm[:, 0:8],
                             waits=[("VEC", m1)])
                cv = self.op("vector", "tensor_copy",
                             self.heap[0:8, wdst + j * 128:wdst + (j + 1) * 128],
                             self.pb[7][0:8, 0:128], waits=[("TEN", v2)])
                self.bank_free[7] = ("VEC", cv)
        routers_done = self.tmark()
        self._stage_busy[stageP[1]] = ("TEN", routers_done)
        lbf = self.dma(self.v(BF, 1024, 0, 8), self.d["bfw"].ap(),
                       after=("TEN", routers_done))
        lbi = self.dma(self.v(BF + 1024, 1024, 0, 8), self.d["bin"].ap())
        gps_prev = 0
        for br, (qsrc, ksrc, vn, wn, un, boff, wrow, c0) in enumerate(
                ((QQ, KK, "vfw", "wfw", "ufw", BF, WF, 0),
                 (IQ, IK, "vin", "win", "uin", BF + 1024, WI, 4))):
            self._mmblk(XV, qsrc, vn, 4, 8, 1024, stageP, evac="copy",
                        extra=[("TEN", self.tmark())])
            self._mmblk(YW, ksrc, wn, 4, 8, 1024, stageP, evac="copy",
                        extra=[("TEN", self.tmark())])
            self.phase = "l0f|scan"
            for krb in range(4):
                cc = self.v(self.M_CARRY + c0 + krb, 1)
                ini = 0.0 if tt == 0 else cc
                self.op("vector", "tensor_tensor_scan",
                        self.v(CU + krb * TT, TT), self.v(YW + krb * TT, TT),
                        self.v(YW + krb * TT, TT), ini, ALU.add, ALU.bypass,
                        waits=[("TEN", routers_done), ("GPS", gps_prev)])
                self.op("vector", "tensor_copy", cc,
                        self.v(CU + krb * TT + TT - 1, 1))
            for krb in range(4):
                p1 = self.op("gpsimd", "tensor_mul", self.v(XV + krb * TT, TT),
                             self.v(XV + krb * TT, TT), self.v(CU + krb * TT, TT),
                             waits=[("VEC", self.sem["VEC"].n)])
                gps_prev = p1
                self.t_bank(6)
                mv = self.mm(self.pb[6][:, 0:TT],
                             self.heap[0:8, self.C_OVL + krb * 128:
                                       self.C_OVL + (krb + 1) * 128],
                             self.v(wrow, TT, 0, 8), True, True,
                             waits=[("VEC", self.sem["VEC"].n)], inc=True)
                p2 = self.op("vector", "tensor_mul", self.v(XV + krb * TT, TT),
                             self.v(XV + krb * TT, TT), self.pb[6][:, 0:TT],
                             waits=[("TEN", mv), ("GPS", p1)])
                self.bank_free[6] = ("VEC", p2)
                self.split(YW + krb * TT, TT, self.v(XV + krb * TT, TT))
            pv = self.sem["VEC"].n
            gdst = GF if br == 0 else GI
            self.phase = "l0f|U"

            def lpu(oc, un=un):
                sl = oc % 2
                after = ("TEN", self.wlast[sl]) if self.wlast[sl] else None
                return self.load_pair(WB + sl * 1024, 512, self.d[un].ap()[oc],
                                      stageP[sl], after=after)
            wsv = lpu(0)
            for oc in range(8):
                sl = oc % 2
                wsv_next = lpu(oc + 1) if oc < 7 else None
                bank = 4 + sl
                self.t_bank(bank)
                for krc in range(4):
                    self.mm3(self.pb[bank][:, 0:TT], WB + sl * 1024, 512, krc,
                             YW + krc * TT, TT, krc == 0, False,
                             waits=[("VEC", wsv), ("VEC", pv), lbf, lbi]
                             if krc == 0 else ())
                wsv = wsv_next
                self.mm(self.pb[bank][:, 0:TT],
                        self.heap[0:8, boff + oc * 128:boff + oc * 128 + 128],
                        self.v(wrow, TT, 0, 8), False, True)
                self.wlast[sl] = self.tmark()
                ev = self.op("vector", "tensor_copy", self.v(gdst + oc * TT, TT),
                             self.pb[bank][:, 0:TT],
                             waits=[("TEN", self.tmark())])
                self.bank_free[bank] = ("VEC", ev)
        cvs = None
        for oc in range(8):
            self.op("vector", "scalar_tensor_tensor", self.v(GF + oc * TT, TT),
                    self.v(GI + oc * TT, TT), self.v(self.M_ALPHA, 1),
                    self.v(GF + oc * TT, TT), ALU.mult, ALU.add)
            cvs = self.split(CMB + oc * TT, TT, self.v(GF + oc * TT, TT))
        self.phase = "l0f|WO"

        def lpo(oc):
            sl = oc % 2
            after = ("TEN", self.wlast[sl]) if self.wlast[sl] else None
            return self.load_pair(WB + sl * 1024, 1024, self.d["wfO"].ap()[oc],
                                  stageP[sl], after=after)
        wsv = lpo(0)
        for oc in range(8):
            sl = oc % 2
            wsv_next = lpo(oc + 1) if oc < 7 else None
            bank = 4 + sl
            self.t_bank(bank)
            for c in range(8):
                self.mm3(self.pb[bank][:, 0:TT], WB + sl * 1024, 1024, c,
                         CMB + c * TT, TT, c == 0, c == 7,
                         waits=[("VEC", wsv), ("VEC", cvs)] if c == 0 else ())
            wsv = wsv_next
            self.wlast[sl] = self.tmark()
            mv = self.tmark()
            e1 = self.op("vector", "tensor_mul", self.v(GI + oc * TT, TT),
                         self.pb[bank][:, 0:TT], self.v(SP, TT),
                         waits=[("TEN", mv), lnv])
            self.op("vector", "tensor_add", self.v(XS + oc * T + t0, TT),
                    self.v(XS + oc * T + t0, TT), self.v(GI + oc * TT, TT))
            self.bank_free[bank] = ("VEC", e1)
        self.barrier()
        self._stage_busy = {}

    def emit_l1a(self, s):
        LATK, LATV, QG, KG, VG = self._l1a_front(s)
        for g in range(8):
            self._l1a_group(g, LATK, LATV, QG, KG, VG)
        self.barrier()

    def _l1a_front(self, s):
        self.phase = "l1a.front"
        T, TT, NTT, NKT = self.T, self.TT, self.NTT, self.NKT
        XS, H1, R2, WB = (self.OFF[k] for k in ("XSB", "H1", "R2", "WB"))
        self.preln(H1, R2)
        LATK, LATV = R2, R2 + 2 * T
        for wn, dsto in (("w1ka", LATK), ("w1va", LATV)):
            lv = self.dma(self.v(WB, 2048), self.d[wn].ap()[0])
            rd = self.ready()
            for tt in range(NTT):
                t0 = tt * TT
                for lc in range(2):
                    self.t_bank(lc)
                    for c in range(8):
                        self.mm(self.pb[lc][:, 0:TT],
                                self.v(WB + c * 256 + lc * 128, 128),
                                self.v(H1 + c * T + t0, TT), c == 0, c == 7,
                                waits=(rd + [lv]) if (c == 0 and tt == 0 and lc == 0) else ())
                    ev = self.op("vector", "tensor_copy",
                                 self.v(dsto + lc * T + t0, TT),
                                 self.pb[lc][:, 0:TT],
                                 waits=[("TEN", self.tmark())])
                    self.bank_free[lc] = ("VEC", ev)
            self.barrier()
        QG = R2 + 4 * T
        KG = QG + T
        VG = KG + T
        return LATK, LATV, QG, KG, VG

    def _l1a_group(self, g, LATK, LATV, QG, KG, VG):
        self.phase = "l1a.proj"
        T, TT, NTT, NKT = self.T, self.TT, self.NTT, self.NKT
        H1, WB = self.OFF["H1"], self.OFF["WB"]
        lq = self.dma(self.v(WB, 1024), self.d["w1q"].ap()[g])
        lkb = self.dma(self.v(WB + 1024, 256), self.d["w1kb"].ap()[g])
        lvb = self.dma(self.v(WB + 1280, 256), self.d["w1vb"].ap()[g])
        lwo = self.dma(self.heap[0:64, WB + 1536:WB + 2560],
                       self.d["w1o"].ap()[g * 128:g * 128 + 64, :])
        rd = self.ready()
        SCQ = self.C_OVL
        SCK = self.OFF["SP"]
        ag = getattr(self, "attn_guard", [])
        for tt in range(NTT):
            t0 = tt * TT
            qb = 6 + (tt % 2)
            kb = 7 - (tt % 2)
            self.t_bank(qb)
            for c in range(8):
                self.mm(self.pb[qb][:, 0:TT], self.v(WB + c * 128, 128),
                        self.v(H1 + c * T + t0, TT), c == 0, c == 7,
                        waits=(rd + [lq]) if (c == 0 and tt == 0) else ())
            qw = [("TEN", self.tmark())] + (list(ag) if tt == 0 else [])
            if tt > 0:
                qw.append(("SCA", self._last_sq))
            qv = self.op("vector", "tensor_copy", self.v(SCQ, TT),
                         self.pb[qb][:, 0:TT], waits=qw)
            self.bank_free[qb] = ("VEC", qv)
            self.bound_tile(SCQ, self.M_MQ, tt == 0, qv)
            self.split(QG + t0, TT, self.v(SCQ, TT))
            self.t_bank(kb)
            for lc in range(2):
                self.mm(self.pb[kb][:, 0:TT], self.v(WB + 1024 + lc * 128, 128),
                        self.v(LATK + lc * T + t0, TT), lc == 0, lc == 1,
                        waits=[lkb] if lc == 0 else ())
            kw = [("TEN", self.tmark())] + (list(ag) if tt == 0 else [])
            kw.append(("SCA", self._last_sq))
            kv = self.op("vector", "tensor_copy", self.v(SCK, TT),
                         self.pb[kb][:, 0:TT], waits=kw)
            self.bank_free[kb] = ("VEC", kv)
            self.bound_tile(SCK, self.M_MK, tt == 0, kv)
            self.split(KG + t0, TT, self.v(SCK, TT))
        lwo2 = self.dma(self.heap[0:64, WB:WB + 1024],
                        self.d["w1o"].ap()[g * 128 + 64:g * 128 + 128, :],
                        after=("TEN", self.tmark()))
        bv = self.finish_bound()
        mz = self.op("vector", "memset",
                     self.hap(VG + 64, [[130, NKT], [65, 2]]), 1.0)
        for kt in range(NKT):
            vb_ = 6 + (kt % 2)
            self.t_bank(vb_)
            for lc in range(2):
                self.mm(self.pb[vb_][:, 0:128],
                        self.v(LATV + lc * T + kt * 128, 128),
                        self.v(WB + 1280 + lc * 128, 128),
                        lc == 0, lc == 1,
                        waits=[lvb] if (kt == 0 and lc == 0) else ())
            ev = self.op("vector", "tensor_copy",
                         self.hap(VG + kt * 130, [[65, 2], [1, 64]]),
                         self.pb[vb_][:, 0:128],
                         waits=[("TEN", self.tmark()), ("VEC", mz)])
            self.bank_free[vb_] = ("VEC", ev)
        self.attn_group(QG, KG, VG, False, bv, (WB + 1536, WB), (lwo, lwo2),
                        s_pair=True)
        self.barrier()

    def emit_l1f(self, s):
        self.phase = "l1f"
        T, TT, NTT = self.T, self.TT, self.NTT
        XS, H1, R2, WB, SP = (self.OFF[k] for k in
                              ("XSB", "H1", "R2", "WB", "SP"))
        MOFF = R2
        if s == 0:
            AO, BO = H1, H1 + 8192
            w1last = None
            for ob in range(8):
                lwr = self.dma(self.v(WB, 1024), self.d["wrB"].ap()[ob],
                               after=("TEN", w1last) if w1last else None)
                lwi = self.dma(self.v(WB + 1024, 1024), self.d["wiB"].ap()[ob])
                for b in range(6):
                    self.t_bank(b)
                for nt in range(2):
                    for c in range(8):
                        lc_ = self.dma(
                            self.v(SP, 512),
                            self.d["dftC"].ap()[c * 128:(c + 1) * 128,
                                                nt * 512:(nt + 1) * 512],
                            after=("TEN", self.tmark())
                            if (c + nt + ob) else None)
                        self.mm(self.pb[nt][:, 0:512], self.v(WB + c * 128, 128),
                                self.v(SP, 512), c == 0, False,
                                waits=[lc_, lwr] if c == 0 else [lc_])
                        self.mm(self.pb[2 + nt][:, 0:512],
                                self.v(WB + 1024 + c * 128, 128),
                                self.v(SP, 512), c == 0, c == 7,
                                waits=[lwi] if c == 0 else ())
                        ls_ = self.dma(
                            self.v(SP + 512, 512),
                            self.d["dftS"].ap()[c * 128:(c + 1) * 128,
                                                nt * 512:(nt + 1) * 512])
                        self.mm(self.pb[nt][:, 0:512],
                                self.v(WB + 1024 + c * 128, 128),
                                self.v(SP + 512, 512), False, c == 7,
                                waits=[ls_])
                        self.mm(self.pb[4 + nt][:, 0:512], self.v(WB + c * 128, 128),
                                self.v(SP + 512, 512), c == 0, c == 7)
                    w1last = self.tmark()
                    av = self.op("vector", "tensor_copy",
                                 self.v(AO + ob * 1024 + nt * 512, 512),
                                 self.pb[nt][:, 0:512], waits=[("TEN", w1last)])
                    self.op("vector", "tensor_copy",
                            self.v(BO + ob * 1024 + nt * 512, 512),
                            self.pb[2 + nt][:, 0:512])
                    bv = self.op("vector", "tensor_sub",
                                 self.v(BO + ob * 1024 + nt * 512, 512),
                                 self.v(BO + ob * 1024 + nt * 512, 512),
                                 self.pb[4 + nt][:, 0:512])
                    self.bank_free[nt] = ("VEC", av)
                    self.bank_free[2 + nt] = ("VEC", bv)
                    self.bank_free[4 + nt] = ("VEC", bv)
            self.barrier()
            for nb in range(8):
                for mt in range(2):
                    self.t_bank(0)
                    self.t_bank(1)
                    for oc in range(8):
                        lc_ = self.dma(
                            self.v(SP, 512),
                            self.d["dftC"].ap()[oc * 128:(oc + 1) * 128,
                                                mt * 512:(mt + 1) * 512],
                            after=("TEN", self.tmark())
                            if (oc + mt + nb) else None)
                        self.mm(self.pb[0][:, 0:512],
                                self.v(AO + oc * 1024 + nb * 128, 128),
                                self.v(SP, 512), oc == 0, oc == 7,
                                waits=[lc_])
                        ls_ = self.dma(
                            self.v(SP + 512, 512),
                            self.d["dftS"].ap()[oc * 128:(oc + 1) * 128,
                                                mt * 512:(mt + 1) * 512])
                        self.mm(self.pb[1][:, 0:512],
                                self.v(BO + oc * 1024 + nb * 128, 128),
                                self.v(SP + 512, 512), oc == 0, oc == 7,
                                waits=[ls_])
                    mv = self.tmark()
                    c1 = self.op("vector", "tensor_copy",
                                 self.v(MOFF + nb * 1024 + mt * 512, 512),
                                 self.pb[0][:, 0:512], waits=[("TEN", mv)])
                    sv = self.op("vector", "tensor_sub",
                                 self.v(MOFF + nb * 1024 + mt * 512, 512),
                                 self.v(MOFF + nb * 1024 + mt * 512, 512),
                                 self.pb[1][:, 0:512])
                    ev = self.op("scalar", "activation",
                                 self.v(MOFF + nb * 1024 + mt * 512, 512),
                                 self.v(MOFF + nb * 1024 + mt * 512, 512),
                                 AF.Copy, scale=1.0 / 1024.0,
                                 waits=[("VEC", sv)])
                    self.bank_free[0] = ("VEC", c1)
                    self.bank_free[1] = ("VEC", sv)
            self.wait("sync", "SCA", self.sem["SCA"].n)
            self.dma(self.mcache.ap()[:, :], self.v(MOFF, 8192))
            self.barrier()
        else:
            lv = self.dma(self.v(MOFF, 8192), self.mcache.ap()[:, :])
            self.wait("tensor", *lv)
            self.barrier()
        XN = R2 + 8192
        for tt in range(NTT):
            t0 = tt * TT
            gv = self.ln_tile(XS, XN, tt, 1.0 / D, True, XN)
            prev_e2 = [0, 0]
            for oc in range(8):
                bank = 2 + (oc % 4)
                self.t_bank(bank)
                for c in range(8):
                    self.mm(self.pb[bank][:, 0:TT],
                            self.v(MOFF + c * 1024 + oc * 128, 128),
                            self.v(XN + c * TT, TT), c == 0, c == 7,
                            waits=[("GPS", gv)] if (c == 0 and oc == 0) else ())
                mv = self.tmark()
                sl = oc % 2
                e1 = self.op("scalar", "activation", self.v(SP + sl * 512, TT),
                             self.pb[bank][:, 0:TT], AF.Identity,
                             bias=self.v(self.M_BIAS1 + oc, 1),
                             waits=[("TEN", mv), ("VEC", prev_e2[sl])])
                e2 = self.op("vector", "tensor_add",
                             self.v(XS + oc * T + t0, TT),
                             self.v(XS + oc * T + t0, TT),
                             self.v(SP + sl * 512, TT),
                             waits=[("SCA", e1)])
                prev_e2[sl] = e2
                self.bank_free[bank] = ("VEC", e2)
        self.barrier()

    def emit_init(self):
        self.dma(self.v(self.C_ONES, 128), self.d["ones"].ap())
        self.dma(self.v(self.C_ID, 128), self.d["id"].ap())
        self.dma(self.v(self.M_RB2EB, 1, 0, 8), self.d["rb2"].ap())
        self.dma(self.v(self.M_G0, 2), self.d["g0"].ap())
        self.dma(self.v(self.M_B0, 2), self.d["b0"].ap())
        self.dma(self.v(self.M_BIAS1, 8), self.d["bias1"].ap())
        self.dma(self.v(self.M_ALPHA1, 1, 0, 1), self.d["alpha"].ap())
        self.op("vector", "memset", self.v(self.M_EPS, 1), 1e-5)
        self.op("vector", "memset", self.v(self.M_LN18, 1), LN1_8)
        lv = self.dma(self.v(self.M_SC1, 1, 0, 8), self.d["eb"].ap())
        wall = [(f"LD{i}", self.sem[f"LD{i}"].n) for i in range(8)]
        self.op("vector", "tensor_add", self.v(self.M_RB2EB, 1, 0, 8),
                self.v(self.M_RB2EB, 1, 0, 8), self.v(self.M_SC1, 1, 0, 8),
                waits=wall)
        mv = self.mm(self.pb[7][:, 0:1],
                     self.heap[0:1, self.C_ONES:self.C_ONES + 128],
                     self.heap[0:1, self.M_ALPHA1:self.M_ALPHA1 + 1],
                     True, True, waits=wall, inc=True)
        self.op("vector", "tensor_copy", self.v(self.M_ALPHA, 1),
                self.pb[7][:, 0:1], waits=[("TEN", mv)])
        self.barrier()

    def emit_store(self, s):
        T, XS = self.T, self.OFF["XSB"]
        self.wait("sync", "VEC", self.sem["VEC"].n)
        self.wait("sync", "GPS", self.sem["GPS"].n)
        self.wait("sync", "SCA", self.sem["SCA"].n)
        for c in range(8):
            self.dma(self.yout.ap()[s, c * 128:(c + 1) * 128, :],
                     self.v(XS + c * T, T))
        for i in range(8):
            self.wait("sync", f"LD{i}", self.sem[f"LD{i}"].n)
        self.barrier()

    def build(self):
        self.rope_done = 0
        self.emit_init()
        for s in range(NSEQ):
            self.emit_l0a(s)
            if self.nblocks >= 2:
                self.emit_l0f(s)
            if self.nblocks >= 3:
                self.emit_l1a(s)
            if self.nblocks >= 4:
                self.emit_l1f(s)
            self.emit_store(s)
        nc, ops = self.nc, self.ops
        with nc.Block() as block:
            @block.sync
            def _(e):
                for fn in ops["sync"]:
                    fn(e)

            @block.tensor
            def _(e):
                for fn in ops["tensor"]:
                    fn(e)

            @block.vector
            def _(e):
                for fn in ops["vector"]:
                    fn(e)

            @block.scalar
            def _(e):
                for fn in ops["scalar"]:
                    fn(e)

            @block.gpsimd
            def _(e):
                for fn in ops["gpsimd"]:
                    fn(e)
        return nc


_BUILD_CACHE = {}


def build_program(T=2048, nblocks=4):
    key = (T, nblocks)
    if key not in _BUILD_CACHE:
        b = B(T, nblocks)
        _BUILD_CACHE[key] = b.build()
        _BUILD_CACHE[key]._mmlab = b.mmlab
    return _BUILD_CACHE[key]


def kernel(**inputs):
    T = SEQT
    x = np.ascontiguousarray(np.asarray(inputs["L0_attn_x"], np.float32))
    Bt, Tx, Dx = x.shape
    assert Dx == D and Tx == T, (x.shape, T)
    shared = _host_prep(inputs)
    shared.update(_host_consts(T))
    nc = build_program(T, NBLOCKS)
    in_maps = []
    for c in range(NCORES):
        m = dict(shared)
        m["x"] = np.ascontiguousarray(x[2 * c:2 * c + 2].transpose(0, 2, 1))
        in_maps.append(m)
    res = run_bass_kernel_spmd(nc, in_maps, core_ids=list(range(NCORES)))
    out = np.empty((Bt, T, D), np.float32)
    for c in range(NCORES):
        y = np.asarray(res.results[c]["y"])
        out[2 * c:2 * c + 2] = y.transpose(0, 2, 1)
    return out
